# version 01
# baseline (speedup 1.0000x reference)
import contextlib
import numpy as np
import ml_dtypes
import concourse.bass as bass
import concourse.mybir as mybir
from concourse.bass_utils import run_bass_kernel_spmd




"""Raw-bass emission framework: per-engine queues, auto semaphore edges.

Tokens are (sem_name, value). Every op declares `after=[tokens...]`; the
framework emits wait_ge on the consuming engine (deduped by monotone
per-engine wait tracking) and puts then_inc on the producing instruction.

Rules encoded here (learned from sim/device probes):
- same-engine RAW/WAW edges need semaphores too (race detector / HW).
- DMA sems: SWDGE (pool) and HWDGE (sync/scalar) must not share a sem.
  A sem shared by several DMAs may only be waited at full cumulative value;
  we give each logical channel its own sem and consumers wait the channel's
  cumulative count at emission time.
- then_inc rides the real producing instruction.
"""

ENGINES = ("pe", "act", "dve", "pool", "sp")
ENG_ATTR = {"pe": "tensor", "act": "scalar", "dve": "vector", "pool": "gpsimd", "sp": "sync"}


class Em:
    def __init__(self, nc):
        self.nc = nc
        self.stack = contextlib.ExitStack()
        self.scopes = []
        self.q = {e: [] for e in ENGINES}
        self.sems = {}
        self.cnt = {}
        self.waited = {e: {} for e in ENGINES}
        self.is_chan = set()
        for e in ENGINES:
            self._mksem("c_" + e)

    def _mksem(self, name):
        self.sems[name] = self.stack.enter_context(self.nc.semaphore(name))
        self.cnt[name] = 0
        return name

    def chan(self, name):
        if name not in self.sems:
            self._mksem(name)
            self.is_chan.add(name)
        return name

    def sb(self, name, shape, dtype):
        st = self.scopes[-1] if self.scopes else self.stack
        return st.enter_context(self.nc.sbuf_tensor(name, shape, dtype))

    def ps(self, name, shape, dtype):
        st = self.scopes[-1] if self.scopes else self.stack
        return st.enter_context(self.nc.psum_tensor(name, shape, dtype))

    def push_scope(self):
        self.scopes.append(contextlib.ExitStack())

    def pop_scope(self):
        self.flush()
        self.scopes.pop().close()

    def flush(self):
        with self.nc.Block() as block:
            for e in ENGINES:
                if not self.q[e]:
                    continue
                def run(engobj, e=e):
                    for f in self.q[e]:
                        f(engobj)
                getattr(block, ENG_ATTR[e])(run)
        self.q = {e: [] for e in ENGINES}

    def pscratch(self):
        if not hasattr(self, "_pscr"):
            import concourse.mybir as mybir
            self._pscr = self.stack.enter_context(
                self.nc.psum_tensor("pscratch", [128, 16], mybir.dt.float32))
        return self._pscr

    def op(self, eng, build, after=(), sem=None, inc=1):
        semname = sem if sem is not None else ("c_" + eng)
        waits = []
        for tok in after:
            if tok is None:
                continue
            psem, pval = tok
            if psem in self.is_chan:
                pval = self.cnt[psem]
            cur = self.waited[eng].get(psem, 0)
            if pval > cur:
                self.waited[eng][psem] = pval
                waits.append((self.sems[psem], pval))
        self.cnt[semname] += inc
        val = self.cnt[semname]
        semh = self.sems[semname]

        def emit(engobj, waits=waits, build=build, semh=semh, inc=inc):
            for s, v in waits:
                engobj.wait_ge(s, v)
            ins = build(engobj)
            ins.then_inc(semh, inc)

        self.q[eng].append(emit)
        return (semname, val)

    def dma(self, eng, chan, build, after=()):
        return self.op(eng, build, after, sem=self.chan(chan), inc=16)

    def wait(self, eng, tokens):
        def emit(engobj, tokens=tuple(t for t in tokens if t is not None)):
            for psem, pval in tokens:
                engobj.wait_ge(self.sems[psem], pval)
        toks = []
        for tok in tokens:
            if tok is None:
                continue
            psem, pval = tok
            if psem in self.is_chan:
                pval = self.cnt[psem]
            cur = self.waited[eng].get(psem, 0)
            if pval > cur:
                self.waited[eng][psem] = pval
                toks.append(tok)

        def emit2(engobj, toks=toks):
            for psem, pval in toks:
                engobj.wait_ge(self.sems[psem], pval)
        self.q[eng].append(emit2)

    def finalize(self):
        self.flush()
        while self.scopes:
            self.scopes.pop().close()
        self.stack.close()



"""Dense MoE pass (L0 ffn): out = res + sum_e w_e * (silu(xn@Wg_e) * (xn@Wu_e)) @ Wd_e

Layouts:
- xn DRAM [T, D] bf16 token-major; read feature-major via HWDGE transpose-DMA.
- Wg, Wu [E, D, F] f32; Wd [E, F, D] f32; Wr [D, E] f32 (cast bf16 during SWDGE DMA)
- res_in (optional) DRAM [T, D] f32; out DRAM [T, D] f32 token-major.
"""

F32 = mybir.dt.float32
BF16 = mybir.dt.bfloat16
AF = mybir.ActivationFunctionType
ALU = mybir.AluOpType
X = mybir.AxisListType.X


def emit_moe(nc, em, cfg, xn, wr, wg, wu, wd, out, res_in=None, name="moe"):
    T, D, F, E, ST = cfg["T"], cfg["D"], cfg["F"], cfg["E"], cfg["ST"]
    n_dc = D // 128
    n_fc = F // 128
    FQ = min(512, F)
    n_fq = F // FQ
    fq_fc = FQ // 128
    n_st = T // ST
    TBW = min(512, ST)
    n_tb = ST // TBW
    n_tc = ST // 128
    WDH = min(512, F)
    n_wdh = F // WDH
    wdh_fc = WDH // 128
    NDW = min(512, D)
    n_nd = D // NDW

    sb, ps = em.sb, em.ps
    xnT = sb(name + "_xnT", [128, n_dc * ST], BF16)
    acc = sb(name + "_acc", [128, n_tc * D], F32)
    fuse = sb(name + "_fuse", [128, n_fc * ST], BF16)
    n_gu_slots = 2 * n_fq + 1
    gu = [sb(f"{name}_gu{i}", [128, n_dc * FQ], BF16) for i in range(n_gu_slots)]
    n_wd_slots = min(n_wdh + 1, 6)
    wdsb = [sb(f"{name}_wd{i}", [128, wdh_fc * D], BF16) for i in range(n_wd_slots)]
    wrb = sb(name + "_wrb", [128, n_dc * E], BF16)
    slog = sb(name + "_slog", [128, n_tc * E], F32)
    sstat = sb(name + "_sstat", [128, 8], F32)
    sse = sb(name + "_sse", [128, 2 * E], F32)
    wful = sb(name + "_wful", [128, n_tc * E], F32)
    sact = [sb(f"{name}_sact{i}", [128, TBW], F32) for i in range(2)]

    pg = ps(name + "_pg", [128, 512], F32)
    pu = ps(name + "_pu", [128, 512], F32)
    pg2 = ps(name + "_pg2", [128, 512], F32)
    pu2 = ps(name + "_pu2", [128, 512], F32)
    pd = ps(name + "_pd", [128, D], F32)
    pd2 = ps(name + "_pd2", [128, D], F32)
    pfree = {}

    t_wr = em.dma("pool", name + "_wr", lambda e: e.dma_start(
        out=wrb[:].rearrange("p (c e) -> p c e", c=n_dc),
        in_=wr.rearrange("(c p) e -> p c e", p=128)))

    def load_gu(slot, e_, kind, fq, after):
        src = (wg if kind == 0 else wu)[e_, :, :].rearrange("(c p) f -> p c f", p=128)
        src = src[:, :, fq * FQ:(fq + 1) * FQ]
        return em.dma("pool", f"{name}_gu{slot}", lambda e: e.dma_start(
            out=gu[slot][:].rearrange("p (c f) -> p c f", c=n_dc), in_=src), after=after)

    def load_wd(slot, e_, h, after):
        src = wd[e_, h * WDH:(h + 1) * WDH, :].rearrange("(c p) d -> p c d", p=128)
        return em.dma("pool", f"{name}_wd{slot}", lambda e: e.dma_start(
            out=wdsb[slot][:].rearrange("p (c d) -> p c d", c=wdh_fc), in_=src), after=after)

    units = []
    wd_units = []
    for st in range(n_st):
        for e_ in range(E):
            for fq in range(n_fq):
                units.append((e_, 0, fq))
                units.append((e_, 1, fq))
            for h in range(n_wdh):
                wd_units.append((e_, h))

    gu_last_use = [None] * n_gu_slots
    wd_last_use = [None] * n_wd_slots
    gu_tok = {}
    wd_tok = {}
    state = {"next_gu": 0, "next_wd": 0}

    def ensure_gu(upto):
        while state["next_gu"] <= min(upto, len(units) - 1):
            i = state["next_gu"]
            slot = i % n_gu_slots
            e_, kind, fq = units[i]
            tok = load_gu(slot, e_, kind, fq, after=[gu_last_use[slot]])
            gu_tok[i] = (slot, tok)
            state["next_gu"] = i + 1

    def ensure_wd(upto):
        while state["next_wd"] <= min(upto, len(wd_units) - 1):
            i = state["next_wd"]
            slot = i % n_wd_slots
            e_, h = wd_units[i]
            tok = load_wd(slot, e_, h, after=[wd_last_use[slot]])
            wd_tok[i] = (slot, tok)
            state["next_wd"] = i + 1

    t_xnT_prev_users = []
    t_acc_store = None
    gu_idx = 0
    wd_idx = 0
    sact_free = [None, None]
    prev_t17 = None

    for st in range(n_st):
        for dc in range(n_dc):
            t_xnT = em.dma("sp", name + "_xnT", lambda e, st=st, dc=dc: e.dma_start(
                out=xnT[:, dc * ST:(dc + 1) * ST],
                in_=xn[st * ST:(st + 1) * ST, dc * 128:(dc + 1) * 128],
                transpose=True), after=t_xnT_prev_users)

        if res_in is not None:
            t_acc0 = em.dma("sp", name + "_accld", lambda e, st=st: e.dma_start(
                out=acc[:].rearrange("p (c d) -> p c d", c=n_tc),
                in_=res_in[st * ST:(st + 1) * ST, :].rearrange("(c p) d -> p c d", p=128)),
                after=[t_acc_store])
        else:
            t_acc0 = em.op("dve", lambda e: e.memset(acc[:], 0.0), after=[t_acc_store])

        rt_first = True
        for tc in range(n_tc):
            for dc in range(n_dc):
                t_rt = em.op("pe", lambda e, tc=tc, dc=dc, first=rt_first: e.matmul(
                    pg[:, tc * E:(tc + 1) * E],
                    xnT[:, dc * ST + tc * 128: dc * ST + (tc + 1) * 128],
                    wrb[:, dc * E:(dc + 1) * E],
                    start=first, stop=(tc == n_tc - 1 and dc == n_dc - 1),
                    skip_group_check=True),
                    after=[t_xnT, t_wr, pfree.get("pg")])
                rt_first = False
        t_slog = em.op("dve", lambda e: e.tensor_copy(slog[:, 0:n_tc * E], pg[:, 0:n_tc * E]),
                       after=[t_rt, prev_t17])
        pfree["pg"] = t_slog

        for tc in range(n_tc):
            sl = slog[:, tc * E:(tc + 1) * E]
            t1 = em.op("dve", lambda e, sl=sl: e.reduce_max(sstat[:, 0:1], sl, axis=X, negate=True),
                       after=[t_slog, prev_t17])
            t2 = em.op("act", lambda e, sl=sl: e.activation(sse[:, 0:E], sl, AF.Exp, bias=sstat[:, 0:1]),
                       after=[t1, prev_t17])
            t3 = em.op("dve", lambda e: e.reduce_sum(sstat[:, 1:2], sse[:, 0:E], axis=X), after=[t2])
            t4 = em.op("dve", lambda e: e.reciprocal(sstat[:, 2:3], sstat[:, 1:2]), after=[t3])
            t5 = em.op("dve", lambda e: e.tensor_scalar(sse[:, E:2 * E], sse[:, 0:E], sstat[:, 2:3], None, ALU.mult), after=[t4])
            t6 = em.op("dve", lambda e: e.reduce_max(sstat[:, 3:4], sse[:, E:2 * E], axis=X), after=[t5])
            t7 = em.op("dve", lambda e: e.tensor_scalar(sse[:, 0:E], sse[:, E:2 * E], sstat[:, 3:4], None, ALU.is_ge), after=[t6])
            t8 = em.op("dve", lambda e: e.tensor_sub(sse[:, 0:E], sse[:, E:2 * E], sse[:, 0:E]), after=[t7])
            t9 = em.op("dve", lambda e: e.reduce_max(sstat[:, 4:5], sse[:, 0:E], axis=X), after=[t8])
            t10 = em.op("dve", lambda e: e.tensor_scalar(sse[:, 0:E], sse[:, E:2 * E], sstat[:, 4:5], None, ALU.is_ge), after=[t9])
            t11 = em.op("act", lambda e: e.activation(sse[:, E:2 * E], sse[:, E:2 * E], AF.Exp), after=[t10])
            t12 = em.op("act", lambda e: e.activation(sstat[:, 5:6], sstat[:, 3:4], AF.Exp), after=[t9])
            t13 = em.op("act", lambda e: e.activation(sstat[:, 6:7], sstat[:, 4:5], AF.Exp), after=[t12])
            t14 = em.op("dve", lambda e: e.tensor_add(sstat[:, 5:6], sstat[:, 5:6], sstat[:, 6:7]), after=[t13])
            t15 = em.op("dve", lambda e: e.reciprocal(sstat[:, 7:8], sstat[:, 5:6]), after=[t14])
            t16 = em.op("dve", lambda e, tc=tc: e.tensor_mul(wful[:, tc * E:(tc + 1) * E], sse[:, E:2 * E], sse[:, 0:E]), after=[t11, t15])
            t17 = em.op("dve", lambda e, tc=tc: e.tensor_scalar(wful[:, tc * E:(tc + 1) * E], wful[:, tc * E:(tc + 1) * E], sstat[:, 7:8], None, ALU.mult), after=[t16])
            prev_t17 = t17
        t_wful = prev_t17

        last_acc = [t_acc0] * n_tc
        prev_down_last = None
        for e_ in range(E):
            ensure_gu(gu_idx + 2 * n_fq)
            ensure_wd(wd_idx + n_wdh)
            fuse_last = None
            for fq in range(n_fq):
                gi, ui = gu_idx, gu_idx + 1
                gu_idx += 2
                gslot, gtok = gu_tok[gi]
                uslot, utok = gu_tok[ui]
                unit_last = None
                for fc in range(fq_fc):
                    fci = fq * fq_fc + fc
                    for tb in range(n_tb):
                        rot = (fci * n_tb + tb) % 2
                        pga, pua = (pg, pu) if rot == 0 else (pg2, pu2)
                        pga_nm = "pg" if rot == 0 else "pg2"
                        pua_nm = "pu" if rot == 0 else "pu2"
                        sa = sact[rot]
                        mm_last = None
                        for dc in range(n_dc):
                            mm_last = em.op("pe", lambda e, gslot=gslot, dc=dc, fc=fc, tb=tb, pga=pga: e.matmul(
                                pga[:, 0:TBW],
                                gu[gslot][:, dc * FQ + fc * 128: dc * FQ + (fc + 1) * 128],
                                xnT[:, dc * ST + tb * TBW: dc * ST + (tb + 1) * TBW],
                                start=(dc == 0), stop=(dc == n_dc - 1)),
                                after=[gtok, t_xnT, pfree.get(pga_nm) if dc == 0 else None])
                        mmg_last = mm_last
                        for dc in range(n_dc):
                            mm_last = em.op("pe", lambda e, uslot=uslot, dc=dc, fc=fc, tb=tb, pua=pua: e.matmul(
                                pua[:, 0:TBW],
                                gu[uslot][:, dc * FQ + fc * 128: dc * FQ + (fc + 1) * 128],
                                xnT[:, dc * ST + tb * TBW: dc * ST + (tb + 1) * TBW],
                                start=(dc == 0), stop=(dc == n_dc - 1)),
                                after=[utok, t_xnT, pfree.get(pua_nm) if dc == 0 else None])
                        ts1 = em.op("act", lambda e, pga=pga, sa=sa: e.activation(sa[:, 0:TBW], pga[:, 0:TBW], AF.Silu),
                                    after=[mmg_last, sact_free[rot]])
                        pfree[pga_nm] = ts1
                        ts2 = em.op("dve", lambda e, pua=pua, sa=sa, fci=fci, tb=tb: e.tensor_mul(
                            fuse[:, fci * ST + tb * TBW: fci * ST + (tb + 1) * TBW], pua[:, 0:TBW], sa[:, 0:TBW]),
                            after=[mm_last, ts1, prev_down_last])
                        pfree[pua_nm] = ts2
                        sact_free[rot] = ts2
                        unit_last = ts2
                        fuse_last = ts2
                gu_last_use[gslot] = unit_last
                gu_last_use[uslot] = unit_last
            t_fuse = fuse_last

            for tc in range(n_tc):
                pda = pd if tc % 2 == 0 else pd2
                pda_nm = "pd" if tc % 2 == 0 else "pd2"
                mmd_last = None
                for h in range(n_wdh):
                    wslot, wtok = wd_tok[wd_idx + h]
                    for fc in range(wdh_fc):
                        fci = h * wdh_fc + fc
                        for nd in range(n_nd):
                            mmd_last = em.op("pe", lambda e, fci=fci, tc=tc, wslot=wslot, fc=fc, nd=nd, pda=pda: e.matmul(
                                pda[:, nd * NDW:(nd + 1) * NDW],
                                fuse[:, fci * ST + tc * 128: fci * ST + (tc + 1) * 128],
                                wdsb[wslot][:, fc * D + nd * NDW: fc * D + (nd + 1) * NDW],
                                start=(fci == 0), stop=(fci == n_fc - 1)),
                                after=[wtok, t_fuse, pfree.get(pda_nm) if (fci == 0 and nd == 0) else None])
                tev = em.op("dve", lambda e, pda=pda, tc=tc, e_=e_: e.scalar_tensor_tensor(
                    acc[:, tc * D:(tc + 1) * D], pda[:, 0:D], wful[:, tc * E + e_: tc * E + e_ + 1],
                    acc[:, tc * D:(tc + 1) * D], ALU.mult, ALU.add),
                    after=[mmd_last, t_wful, last_acc[tc], t_acc0])
                last_acc[tc] = tev
                pfree[pda_nm] = tev
                for h in range(n_wdh):
                    wd_last_use[wd_tok[wd_idx + h][0]] = tev
                prev_down_last = mmd_last
            wd_idx += n_wdh

        t_acc_store = em.dma("sp", name + "_st", lambda e, st=st: e.dma_start(
            out=out[st * ST:(st + 1) * ST, :].rearrange("(c p) d -> p c d", p=128),
            in_=acc[:].rearrange("p (c d) -> p c d", c=n_tc)), after=last_acc)
        t_xnT_prev_users = last_acc[-1:]
    em.wait("sp", [t_acc_store])
    return t_acc_store



"""Bitnet FFN pass (L1 ffn):
  Wq = absmean-ternary(W);  h = gelu_exact(xn @ Wq_fc^T);  hn = LN(h)*ln_w + ln_b
  out = res + hn @ Wq_proj^T
LN folded into proj:  out = res + r*G - (r*m)*c1 + c2
  G = h @ Wpl (Wpl = lnw*Wq_projT), c1 = colsum(Wpl), c2 = lnb^T @ Wq_projT... note:
  hn@WqP^T = ((h-m)*r*lnw + lnb)@WqP^T = r*(h@Wpl) - r*m*colsum(Wpl) + lnb@WqP^T
  c2 uses the UNSCALED Wq_projT -> needs lnb^T@Wq_projT: we compute c2 from Wpl with
  lnb/lnw? lnw may be ~0 -> instead quantize stores BOTH: here we take c2 via a second
  lhsT column using lnb_over row trick: store Wq_projT unscaled, scale rows of G?!
  Simpler: quantizer writes Wpl (lnw-scaled) AND we get c2 = (lnb/lnw)ak... NO.
  => quantizer writes UNSCALED wprq; lnw folded into G's *rhs* here is impossible
  (lnw is per h = per contraction row) -> fold lnw into h instead? lnw per hh row =
  per-PARTITION in hT! hn_part = h*lnw is a per-partition scale on hbf... but stats
  need raw h. We apply lnw as per-partition scale DURING the gelu evac? stats use
  raw h though. Final choice: hbf stores gelu(h) raw; G-matmul lhsT = hbf, rhs = wprq
  (unscaled); fold lnw into... lnw multiplies rows of Wq_projT = contraction rows =
  lhsT-partition rows of rhs operand... rhs[k=hh, d]: scale per k row: pre-scale rhs
  tiles ONCE per load: wpl[slot] = wprq_tile * lnw_col (1 DVE per loaded tile, cheap).
  c1 = colsum(wpl) = ones^T@wpl, c2 = lnb_over_lnw... NO: c2 = lnb^T @ wprq (raw).
  Both c1, c2 via the [ones|lnb] K-column trick on raw wprq needs lnw-scaled for c1:
  c1 = lnw^T-weighted colsum of wprq = (lnw)^T @ wprq. So c1 = lnw^T@wprq, c2 =
  lnb^T@wprq -> one K=2-ish matmul per chunk with lhsT = [lnw_col | lnb_col]!
"""

F32 = mybir.dt.float32
BF16 = mybir.dt.bfloat16
AF = mybir.ActivationFunctionType
ALU = mybir.AluOpType
X = mybir.AxisListType.X


def emit_quant(nc, em, src, dst, R, C, name):
    sb, ps = em.sb, em.ps
    n_rc = R // 128
    CT = min(C, 1024)
    n_ct = C // CT
    bufs = [sb(name + "_w0", [128, CT], F32), sb(name + "_w1", [128, CT], F32)]
    accv = sb(name + "_accv", [128, 1], F32)
    red = sb(name + "_red", [128, 2], F32)
    ones = sb(name + "_ones", [128, 1], F32)
    onesrow = sb(name + "_onesrow", [1, 128], F32)
    gcol = sb(name + "_gcol", [128, 2], F32)
    qts = [sb(name + "_qt0", [128, CT], BF16), sb(name + "_qt1", [128, CT], BF16)]
    qneg = sb(name + "_qneg", [128, CT], F32)
    pq = em.pscratch()

    t_ones = em.op("dve", lambda e: e.memset(ones[:], 1.0))
    t_or = em.op("dve", lambda e: e.memset(onesrow[:], 1.0))
    t_acc = em.op("dve", lambda e: e.memset(accv[:], 0.0))
    tiles = [(rc, ct) for rc in range(n_rc) for ct in range(n_ct)]
    buf_free = [None, None]
    for i, (rc, ct) in enumerate(tiles):
        b = i % 2
        t_ld = em.dma("sp", f"{name}_ld{b}", lambda e, rc=rc, ct=ct, b=b: e.dma_start(
            out=bufs[b][:], in_=src[rc * 128:(rc + 1) * 128, ct * CT:(ct + 1) * CT]),
            after=[buf_free[b]])
        t_r = em.op("dve", lambda e, b=b: e.reduce_sum(red[:, 0:1], bufs[b][:], axis=X,
                                                       apply_absolute_value=True),
                    after=[t_ld, t_acc])
        t_acc = em.op("dve", lambda e: e.tensor_add(accv[:], accv[:], red[:, 0:1]), after=[t_r])
        buf_free[b] = t_acc
    t_tot = em.op("pe", lambda e: e.matmul(pq[0:1, 0:1], accv[:], ones[:], start=True, stop=True),
                  after=[t_acc, t_ones])
    t_g = em.op("dve", lambda e: e.tensor_scalar(red[0:1, 0:1], pq[0:1, 0:1], 0.5 / (R * C), 0.5e-6,
                                                 ALU.mult, ALU.add), after=[t_tot])
    t_g2 = em.op("dve", lambda e: e.tensor_scalar(red[0:1, 1:2], red[0:1, 0:1], -1.0, None, ALU.mult),
                 after=[t_g])
    t_gb = em.op("pe", lambda e: e.matmul(pq[:, 2:4], onesrow[0:1, :], red[0:1, 0:2], start=True, stop=True),
                 after=[t_g2, t_or])
    t_gc = em.op("dve", lambda e: e.tensor_copy(gcol[:], pq[:, 2:4]), after=[t_gb])

    last_st = None
    qfree = [None, None]
    for i, (rc, ct) in enumerate(tiles):
        b = i % 2
        t_ld2 = em.dma("sp", f"{name}_ld{b}", lambda e, rc=rc, ct=ct, b=b: e.dma_start(
            out=bufs[b][:], in_=src[rc * 128:(rc + 1) * 128, ct * CT:(ct + 1) * CT]),
            after=[buf_free[b]])
        tq1 = em.op("dve", lambda e, b=b: e.tensor_scalar(qneg[:], bufs[b][:], gcol[:, 1:2], None, ALU.is_lt),
                    after=[t_ld2, t_gc, qfree[b]])
        tq2 = em.op("dve", lambda e, b=b: e.tensor_scalar(qts[b][:], bufs[b][:], gcol[:, 0:1], None, ALU.is_gt),
                    after=[tq1])
        buf_free[b] = tq2
        tq3 = em.op("dve", lambda e, b=b: e.tensor_sub(qts[b][:], qts[b][:], qneg[:]), after=[tq2])
        last_st = em.dma("sp", name + "_qst", lambda e, rc=rc, ct=ct, b=b: e.dma_start(
            out=dst[rc * 128:(rc + 1) * 128, ct * CT:(ct + 1) * CT], in_=qts[b][:]), after=[tq3])
        qfree[b] = last_st
    em.wait("sp", [last_st])
    return last_st


def emit_ffn(nc, em, cfg, xn, wfcq, wprq, lnw, lnb, res, out, cst, after_w=(), name="ffn"):
    T, D, Hh = cfg["T"], cfg["D"], cfg["Hh"]
    EPS_COL = cfg["eps_col"]
    n_dc = D // 128
    n_hc = Hh // 128
    TT = min(512, T)
    n_tt = T // TT
    n_tc = TT // 128
    NDW = min(512, D)
    n_nd = D // NDW

    sb, ps = em.sb, em.ps
    xnT = sb(name + "_xnT", [128, n_dc * TT], BF16)
    hbf = sb(name + "_hbf", [128, n_hc * TT], BF16)
    hsq = sb(name + "_hsq", [128, TT], F32)
    wfc = [sb(f"{name}_wfc{i}", [128, n_dc * 128], BF16) for i in range(3)]
    wprf = sb(name + "_wprf", [128, n_hc * D], BF16)
    lnwc = sb(name + "_lnwc", [128, n_hc], F32)
    lnbc = sb(name + "_lnbc", [128, n_hc], F32)
    onescol = sb(name + "_onescol", [128, 1], F32)
    onescb = sb(name + "_onescb", [128, 1], BF16)
    onesrow = sb(name + "_onesrow", [1, 128], F32)
    ident = sb(name + "_ident", [128, 128], F32)
    c12 = sb(name + "_c12", [128, 2 * D], F32)
    c12r = sb(name + "_c12r", [2, D], F32)
    lcol = sb(name + "_lcol", [128, 2], BF16)
    statr = sb(name + "_statr", [1, 2 * TT], F32)
    statc = sb(name + "_statc", [128, 8], F32)
    rcols = sb(name + "_rcols", [128, 2 * n_tc], F32)
    btmp = sb(name + "_btmp", [128, D], F32)
    rest = sb(name + "_rest", [128, n_tc * D], F32)
    btmp2 = sb(name + "_btmp2", [128, D], F32)

    pA = [ps(name + "_pA0", [128, TT], F32), ps(name + "_pA1", [128, TT], F32)]
    pB = [ps(name + "_pB0", [128, D], F32)]
    pS1 = ps(name + "_pS1", [1, 512], F32)
    pS2 = ps(name + "_pS2", [1, 512], F32)
    pT = em.pscratch()
    pfree = {}

    t_ones = em.op("dve", lambda e: e.memset(onescol[:], 1.0))
    t_oneb = em.op("dve", lambda e: e.memset(onescb[:], 1.0))
    t_or = em.op("dve", lambda e: e.memset(onesrow[:], 1.0))
    t_id = em.dma("sp", name + "_id", lambda e: e.dma_start(out=ident[:], in_=cfg["ident_dram"]))
    t_lnw = em.dma("sp", name + "_lnw", lambda e: e.dma_start(
        out=lnwc[:], in_=lnw.rearrange("(c p) -> p c", p=128), allow_slow_non_contiguous=True))
    t_lnb = em.dma("sp", name + "_lnb", lambda e: e.dma_start(
        out=lnbc[:], in_=lnb.rearrange("(c p) -> p c", p=128), allow_slow_non_contiguous=True))

    t_wf = em.dma("pool", name + "_wprf", lambda e: e.dma_start(
        out=wprf[:].rearrange("p (c d) -> p c d", c=n_hc),
        in_=wprq.rearrange("(c p) d -> p c d", p=128)), after=list(after_w))
    t_cmm = None
    for hc in range(n_hc):
        t_l2 = em.op("dve", lambda e, hc=hc: e.tensor_copy(lcol[:, 0:1], lnwc[:, hc:hc + 1]),
                     after=[t_lnw, t_cmm])
        t_l3 = em.op("dve", lambda e, hc=hc: e.tensor_copy(lcol[:, 1:2], lnbc[:, hc:hc + 1]),
                     after=[t_lnb, t_l2])
        for nd in range(n_nd):
            t_cmm = em.op("pe", lambda e, nd=nd, hc=hc: e.matmul(
                pB[0][0:2, nd * NDW:(nd + 1) * NDW],
                lcol[:, 0:2], wprf[:, hc * D + nd * NDW: hc * D + (nd + 1) * NDW],
                start=(hc == 0), stop=(hc == n_hc - 1),
                skip_group_check=True),
                after=[t_wf, t_l3])
    t_wpl = None
    for hc in range(n_hc):
        t_wpl = em.op("dve", lambda e, hc=hc: e.tensor_scalar(
            wprf[:, hc * D:(hc + 1) * D], wprf[:, hc * D:(hc + 1) * D],
            lnwc[:, hc:hc + 1], None, ALU.mult), after=[t_wf, t_lnw, t_cmm])
    t_ce = em.op("dve", lambda e: e.tensor_copy(c12r[:], pB[0][0:2, 0:D]), after=[t_cmm])
    sel2 = sb(name + "_sel2", [2, 256], F32)
    t_sel = em.dma("sp", name + "_sel", lambda e: e.dma_start(out=sel2[:], in_=cfg["sel2_dram"]))
    t_cb = None
    for nd in range(n_nd):
        t_cb = em.op("pe", lambda e, nd=nd: e.matmul(
            pB[0][:, nd * NDW:(nd + 1) * NDW],
            sel2[:, 0:128], c12r[0:2, nd * NDW:(nd + 1) * NDW],
            start=True, stop=True, skip_group_check=True),
            after=[t_ce, t_sel])
    t_c1e = em.op("dve", lambda e: e.tensor_copy(c12[:, 0:D], pB[0][:, 0:D]), after=[t_cb])
    t_cb2 = None
    for nd in range(n_nd):
        t_cb2 = em.op("pe", lambda e, nd=nd: e.matmul(
            pB[0][:, nd * NDW:(nd + 1) * NDW],
            sel2[:, 128:256], c12r[0:2, nd * NDW:(nd + 1) * NDW],
            start=True, stop=True, skip_group_check=True),
            after=[t_c1e, t_sel])
    t_c2e = em.op("dve", lambda e: e.tensor_copy(c12[:, D:2 * D], pB[0][:, 0:D]), after=[t_cb2])
    pfree["pB0"] = t_c2e

    wfc_tok = {}
    wfc_use = {}
    state = {"next": 0}

    def ensure_wfc(upto):
        while state["next"] <= min(upto, n_hc * n_tt - 1):
            i = state["next"]
            hc = i % n_hc
            slot = i % 3
            prev = wfc_use.get(i - 3)
            tok = em.dma("pool", f"{name}_wfc{slot}", lambda e, hc=hc, slot=slot: e.dma_start(
                out=wfc[slot][:].rearrange("p (c h) -> p c h", c=n_dc),
                in_=wfcq[:, hc * 128:(hc + 1) * 128].rearrange("(c p) h -> p c h", p=128)),
                after=[prev] + list(after_w))
            wfc_tok[i] = (slot, tok)
            state["next"] = i + 1

    t_xnT_prev = []
    t_hbf_free = None
    t_statr_free = None
    last_store = None
    for tt in range(n_tt):
        for dc in range(n_dc):
            t_xnT = em.dma("sp", name + "_xnT", lambda e, tt=tt, dc=dc: e.dma_start(
                out=xnT[:, dc * TT:(dc + 1) * TT],
                in_=xn[tt * TT:(tt + 1) * TT, dc * 128:(dc + 1) * 128],
                transpose=True), after=t_xnT_prev)
        t_res = em.dma("sp", name + "_res", lambda e, tt=tt: e.dma_start(
            out=rest[:].rearrange("p (c d) -> p c d", c=n_tc),
            in_=res[tt * TT:(tt + 1) * TT, :].rearrange("(c p) d -> p c d", p=128)),
            after=[last_store])

        tm1 = tm2 = None
        for hc in range(n_hc):
            ensure_wfc(tt * n_hc + hc + 2)
            pha = pA[hc % 2]
            pha_nm = f"pA{hc % 2}"
            slot, wtok = wfc_tok[tt * n_hc + hc]
            mm = None
            for dc in range(n_dc):
                mm = em.op("pe", lambda e, hc=hc, dc=dc, slot=slot, pha=pha: e.matmul(
                    pha[:, 0:TT],
                    wfc[slot][:, dc * 128:(dc + 1) * 128],
                    xnT[:, dc * TT:(dc + 1) * TT],
                    start=(dc == 0), stop=(dc == n_dc - 1)),
                    after=[wtok, t_xnT, pfree.get(pha_nm) if dc == 0 else None])
            wfc_use[tt * n_hc + hc] = mm
            tg = em.op("act", lambda e, hc=hc, pha=pha: e.activation(
                hbf[:, hc * TT:(hc + 1) * TT], pha[:, 0:TT], AF.Gelu),
                after=[mm, t_hbf_free])
            pfree[pha_nm] = tg
            tsq = em.op("act", lambda e, hc=hc: e.activation(
                hsq[:, 0:TT], hbf[:, hc * TT:(hc + 1) * TT], AF.Square),
                after=[tg, pfree.get("hsq")])
            tm1 = em.op("pe", lambda e, hc=hc: e.matmul(
                pS1[0:1, 0:TT], onescb[:], hbf[:, hc * TT:(hc + 1) * TT],
                start=(hc == 0), stop=(hc == n_hc - 1), skip_group_check=True),
                after=[tg, t_oneb, pfree.get("pS") if hc == 0 else None])
            tm2 = em.op("pe", lambda e, hc=hc: e.matmul(
                pS2[0:1, 0:TT], onescol[:], hsq[:, 0:TT],
                start=(hc == 0), stop=(hc == n_hc - 1), skip_group_check=True),
                after=[tsq, t_ones])
            pfree["hsq"] = tm2
        t_st0 = em.op("dve", lambda e: e.tensor_copy(statr[:, 0:TT], pS1[0:1, 0:TT]),
                      after=[tm1, t_statr_free])
        t_st = em.op("dve", lambda e: e.tensor_copy(statr[:, TT:2 * TT], pS2[0:1, 0:TT]),
                     after=[tm2, t_st0])
        pfree["pS"] = t_st
        t_rc = []
        for tc in range(n_tc):
            ttr0 = em.op("pe", lambda e, tc=tc: e.transpose(
                pT[:, 0:1], statr[0:1, tc * 128:(tc + 1) * 128], ident[0:1, 0:1]),
                after=[t_st, pfree.get("pT"), t_id])
            ttr = em.op("pe", lambda e, tc=tc: e.transpose(
                pT[:, 1:2], statr[0:1, TT + tc * 128: TT + (tc + 1) * 128], ident[0:1, 0:1]),
                after=[ttr0])
            a1 = em.op("dve", lambda e: e.tensor_scalar(statc[:, 0:2], pT[:, 0:2], 1.0 / Hh, None, ALU.mult),
                       after=[ttr] + (t_rc[-1:] if t_rc else []))
            pfree["pT"] = a1
            a2 = em.op("dve", lambda e: e.tensor_mul(statc[:, 2:3], statc[:, 0:1], statc[:, 0:1]),
                       after=[a1])
            a3 = em.op("dve", lambda e: e.tensor_sub(statc[:, 3:4], statc[:, 1:2], statc[:, 2:3]),
                       after=[a2])
            a4 = em.op("act", lambda e: e.activation(statc[:, 4:5], statc[:, 3:4], AF.Sqrt,
                                                     bias=cst[:, EPS_COL:EPS_COL + 1]),
                       after=[a3, cfg.get("cst_tok")])
            a5 = em.op("dve", lambda e, tc=tc: e.reciprocal(rcols[:, 2 * tc:2 * tc + 1], statc[:, 4:5]),
                       after=[a4])
            a6 = em.op("dve", lambda e, tc=tc: e.tensor_scalar(
                rcols[:, 2 * tc + 1:2 * tc + 2], statc[:, 0:1], rcols[:, 2 * tc:2 * tc + 1], -1.0,
                ALU.mult, ALU.mult), after=[a5])
            t_rc.append(a6)
        ev_toks = []
        for tc in range(n_tc):
            pb = pB[0]
            pb_nm = "pB0"
            mmp = None
            for hc in range(n_hc):
                for nd in range(n_nd):
                    mmp = em.op("pe", lambda e, hc=hc, tc=tc, nd=nd, pb=pb: e.matmul(
                        pb[:, nd * NDW:(nd + 1) * NDW],
                        hbf[:, hc * TT + tc * 128: hc * TT + (tc + 1) * 128],
                        wprf[:, hc * D + nd * NDW: hc * D + (nd + 1) * NDW],
                        start=(hc == 0), stop=(hc == n_hc - 1)),
                        after=[t_wpl, tg, pfree.get(pb_nm) if (hc == 0 and nd == 0) else None])
            tb1 = em.op("dve", lambda e, tc=tc: e.scalar_tensor_tensor(
                btmp[:], c12[:, 0:D], rcols[:, 2 * tc + 1:2 * tc + 2], c12[:, D:2 * D],
                ALU.mult, ALU.add), after=[t_rc[tc], t_c1e, t_c2e] + ([ev_toks[-1]] if ev_toks else []))
            tb2 = em.op("dve", lambda e, tc=tc, pb=pb: e.scalar_tensor_tensor(
                btmp2[:], pb[:, 0:D], rcols[:, 2 * tc:2 * tc + 1], btmp[:],
                ALU.mult, ALU.add), after=[mmp, tb1, last_store])
            pfree[pb_nm] = tb2
            tb3 = em.op("dve", lambda e, tc=tc: e.tensor_add(
                rest[:, tc * D:(tc + 1) * D], rest[:, tc * D:(tc + 1) * D], btmp2[:]),
                after=[tb2, t_res])
            ev_toks.append(tb3)
        last_store = em.dma("sp", name + "_st", lambda e, tt=tt: e.dma_start(
            out=out[tt * TT:(tt + 1) * TT, :].rearrange("(c p) d -> p c d", p=128),
            in_=rest[:].rearrange("p (c d) -> p c d", c=n_tc)), after=ev_toks)
        t_xnT_prev = [tm1]
        t_hbf_free = ev_toks[-1]
        t_statr_free = t_rc[-1]
    em.wait("sp", [last_store])
    return last_store



"""MLA-style attention pass (L1 attn), causal, per-sequence.

q = [c_Q@W_UQ | (c_Q@W_QR_pad)*scale]; k = [c_KV@W_UK | (xn@W_KR)*scale bcast];
v = c_KV@W_UV;  att = softmax(q.k/sqrt(96)) causal;  out = res + (att@v)@W_O.

Normalizer factored: P = exp(St*scl [+mask]), l = colsum(P), y = (V^T@P)*(1/l).
Host supplies:
  wqr_pad [DCH, H*64] f32: W_QR with each head's 32 cols zero-padded to 64.
  scale_dup [H*64, T2] f32: row (h*64+j) = scaleT[j, t] for j<32 else 0.
  masks [4, 128, 512] f32 additive causal masks for partial tiles.
Structure per sequence b: for ti: projections(ti) -> per-head attention(qt=ti)
over k-tiles 0..ti -> output tile. K/V persist per b; q transient per ti.
cst: consts sbuf col SCL_COL holds 1/sqrt(96).
"""

F32 = mybir.dt.float32
BF16 = mybir.dt.bfloat16
AF = mybir.ActivationFunctionType
ALU = mybir.AluOpType

SCL = 1.0 / (96.0 ** 0.5)


def emit_attn(nc, em, cfg, xn, wdq, wuq, wdkv, wuk, wuv, wqr_pad, wkr, wo,
              scale_dup, masks, res, out, cst, name="att"):
    T, D, NB, H = cfg["T"], cfg["D"], cfg["NB"], cfg["H"]
    T2 = T // NB
    DCH = cfg["DCH"]
    DKV = cfg["DKV"]
    SCL_COL = cfg["scl_col"]
    n_dc = D // 128
    n_qc = DCH // 128
    n_kv = DKV // 128
    n_hc = H * 64 // 128
    QT = min(512, T2)
    n_qt = T2 // QT
    n_tc = QT // 128
    PQ = QT
    n_pq = QT // PQ
    n_tcp = PQ // 128
    NDW = min(512, D)
    n_nd = D // NDW

    sb, ps = em.sb, em.ps
    xnT = sb(name + "_xnT", [128, n_dc * PQ], BF16)
    cqT = sb(name + "_cqT", [128, n_qc * PQ], BF16)
    ckvT = sb(name + "_ckvT", [128, n_kv * PQ], BF16)
    qCt = sb(name + "_qCt", [128, n_hc * QT], BF16)
    qRt = sb(name + "_qRt", [128, n_hc * QT], BF16)
    kC = sb(name + "_kC", [128, n_hc * T2], BF16)
    kR = sb(name + "_kR", [128, T2], BF16)
    vT = sb(name + "_vT", [128, (T2 // 128) * D], BF16)
    yTt = sb(name + "_yTt", [128, n_hc * QT], BF16)
    sclt = sb(name + "_sclt", [128, n_hc * QT], BF16)
    sclk = sb(name + "_sclk", [64, T2], BF16)
    MEXT = QT + (n_tc - 1) * 128
    msk = sb(name + "_msk", [128, MEXT], F32)
    sexp = sb(name + "_sexp", [128, QT], F32)
    pbf = [sb(f"{name}_pbf{i}", [128, QT], BF16) for i in range(2)]
    srow = sb(name + "_srow", [1, 2 * QT], F32)
    grow = sb(name + "_grow", [1, QT], BF16)
    gbuf = sb(name + "_gb", [128, QT], F32)
    onescb = sb(name + "_onescb", [128, 1], BF16)
    onesrb = sb(name + "_onesrb", [1, 128], BF16)
    outt = sb(name + "_outt", [128, D], F32)
    wu_dq = [sb(f"{name}_wdq{i}", [128, n_dc * 128], BF16) for i in range(3)]
    wu_uqr = [sb(f"{name}_wuqr{i}", [128, n_qc * 128], BF16) for i in range(2)]
    w_dkv = sb(name + "_wdkv", [128, n_dc * DKV], BF16)
    w_uk = sb(name + "_wuk", [128, n_kv * D], BF16)
    w_uv = sb(name + "_wuv", [128, n_kv * D], BF16)
    w_kr = sb(name + "_wkr", [128, n_dc * 32], BF16)
    w_o = [sb(f"{name}_wo{i}", [128, D], BF16) for i in range(3)]

    pS = [ps(name + "_pS0", [128, 512], F32), ps(name + "_pS1", [128, 512], F32)]
    pP0 = ps(name + "_pP0", [128, 512], F32)
    pL = ps(name + "_pL", [1, 512], F32)
    pY = ps(name + "_pY", [128, 512], F32)
    pO = ps(name + "_pO", [128, D], F32)
    pfree = {}
    t_free = {}

    t_ones = em.op("dve", lambda e: e.memset(onescb[:], 1.0))
    t_onr = em.op("dve", lambda e: e.memset(onesrb[:], 1.0))
    t_sclk = em.dma("pool", name + "_sclk", lambda e: e.dma_start(
        out=sclk[:], in_=scale_dup[0:64, :]))
    t_msk = em.dma("sp", name + "_msk", lambda e: e.dma_start(out=msk[:], in_=masks))
    t_wdkv = em.dma("pool", name + "_wdkv", lambda e: e.dma_start(
        out=w_dkv[:].rearrange("p (c n) -> p c n", c=n_dc),
        in_=wdkv.rearrange("(c p) n -> p c n", p=128)))
    t_wuk = em.dma("pool", name + "_wuk", lambda e: e.dma_start(
        out=w_uk[:].rearrange("p (c n) -> p c n", c=n_kv),
        in_=wuk.rearrange("(c p) n -> p c n", p=128)))
    t_wuv = em.dma("pool", name + "_wuv", lambda e: e.dma_start(
        out=w_uv[:].rearrange("p (c n) -> p c n", c=n_kv),
        in_=wuv.rearrange("(c p) n -> p c n", p=128)))
    t_wkr = em.dma("pool", name + "_wkr", lambda e: e.dma_start(
        out=w_kr[:].rearrange("p (c n) -> p c n", c=n_dc),
        in_=wkr.rearrange("(c p) n -> p c n", p=128)))
    wo_use = {}

    def mk_stream(wdram, n_k, n_oc, slots, tag):
        toks, uses, st = {}, {}, {"next": 0}

        def ensure(upto, total):
            while st["next"] <= min(upto, total - 1):
                i = st["next"]
                oc = i % n_oc
                slot = i % len(slots)
                prev = uses.get(i - len(slots))
                tok = em.dma("pool", f"{name}_{tag}{slot}", lambda e, oc=oc, slot=slot: e.dma_start(
                    out=slots[slot][:].rearrange("p (c n) -> p c n", c=n_k),
                    in_=wdram[:, oc * 128:(oc + 1) * 128].rearrange("(c p) n -> p c n", p=128)),
                    after=[prev])
                toks[i] = (slot, tok)
                st["next"] = i + 1
        return toks, uses, ensure

    dq_toks, dq_uses, dq_ensure = mk_stream(wdq, n_dc, n_qc, wu_dq, "sdq")

    last_store = None
    for b in range(NB):
        t0 = b * T2
        for ti in range(n_qt):
            trng = (t0 + ti * QT, t0 + (ti + 1) * QT)
            uqr_tok = {}
            for sub in range(n_pq):
                ts0 = trng[0] + sub * PQ
                for dc in range(n_dc):
                    t_xnT = em.dma("sp", name + "_xnT", lambda e, ts0=ts0, dc=dc: e.dma_start(
                        out=xnT[:, dc * PQ:(dc + 1) * PQ],
                        in_=xn[ts0:ts0 + PQ, dc * 128:(dc + 1) * 128],
                        transpose=True), after=[t_free.get("xnT")])
                if sub == 0:
                    t_sclt = em.dma("pool", name + "_sclt", lambda e, ti=ti: e.dma_start(
                        out=sclt[:].rearrange("p (c t) -> p c t", c=n_hc),
                        in_=scale_dup[:, ti * QT:(ti + 1) * QT].rearrange("(c p) t -> p c t", p=128)),
                        after=[t_free.get("sclt")])
                base_q = ((b * n_qt + ti) * n_pq + sub) * n_qc
                t_cq = None
                for oc in range(n_qc):
                    dq_ensure(base_q + oc + 2, NB * n_qt * n_pq * n_qc)
                    slot, wtok = dq_toks[base_q + oc]
                    mm = None
                    for dc in range(n_dc):
                        mm = em.op("pe", lambda e, oc=oc, dc=dc, slot=slot: e.matmul(
                            pS[oc % 2][:, 0:PQ],
                            wu_dq[slot][:, dc * 128:(dc + 1) * 128],
                            xnT[:, dc * PQ:(dc + 1) * PQ],
                            start=(dc == 0), stop=(dc == n_dc - 1)),
                            after=[wtok, t_xnT, pfree.get(f"pS{oc % 2}") if dc == 0 else None])
                    dq_uses[base_q + oc] = mm
                    t_cq = em.op("dve", lambda e, oc=oc: e.tensor_copy(
                        cqT[:, oc * PQ:(oc + 1) * PQ], pS[oc % 2][:, 0:PQ]),
                        after=[mm, t_free.get("cqT")])
                    pfree[f"pS{oc % 2}"] = t_cq
                for fam in range(2):
                    wsrc = wuq if fam == 0 else wqr_pad
                    for oc in range(n_hc):
                        ui = fam * n_hc + oc
                        slot = ui % 2
                        prev = uqr_tok.get(("use", ui - 3))
                        if ("ld", ui) not in uqr_tok or sub == 0:
                            pass
                        if sub == 0:
                            wtok = em.dma("pool", f"{name}_suqr{slot}", lambda e, oc=oc, slot=slot, wsrc=wsrc: e.dma_start(
                                out=wu_uqr[slot][:].rearrange("p (c n) -> p c n", c=n_qc),
                                in_=wsrc[:, oc * 128:(oc + 1) * 128].rearrange("(c p) n -> p c n", p=128)),
                                after=[uqr_tok.get(("use", ui - 2)), t_free.get(f"uqr{slot}")])
                            uqr_tok[("ld", ui)] = wtok
                        else:
                            wtok = uqr_tok[("ld", ui)]
                        mm = None
                        for kc in range(n_qc):
                            mm = em.op("pe", lambda e, oc=oc, kc=kc, slot=slot: e.matmul(
                                pS[oc % 2][:, 0:PQ],
                                wu_uqr[slot][:, kc * 128:(kc + 1) * 128],
                                cqT[:, kc * PQ:(kc + 1) * PQ],
                                start=(kc == 0), stop=(kc == n_qc - 1)),
                                after=[wtok, t_cq, pfree.get(f"pS{oc % 2}") if kc == 0 else None])
                        uqr_tok[("use", ui)] = mm
                        if fam == 0:
                            tev = em.op("dve", lambda e, oc=oc, sub=sub: e.tensor_copy(
                                qCt[:, oc * QT + sub * PQ: oc * QT + sub * PQ + PQ],
                                pS[oc % 2][:, 0:PQ]),
                                after=[mm, t_free.get("qCt")])
                        else:
                            tev = em.op("dve", lambda e, oc=oc, sub=sub: e.tensor_tensor(
                                qRt[:, oc * QT + sub * PQ: oc * QT + sub * PQ + PQ],
                                pS[oc % 2][:, 0:PQ],
                                sclt[:, oc * QT + sub * PQ: oc * QT + sub * PQ + PQ], ALU.mult),
                                after=[mm, t_sclt, t_free.get("qRt")])
                        pfree[f"pS{oc % 2}"] = tev
                t_ckv = None
                for oc in range(n_kv):
                    mm = None
                    for dc in range(n_dc):
                        mm = em.op("pe", lambda e, oc=oc, dc=dc: e.matmul(
                            pP0[:, 0:PQ],
                            w_dkv[:, dc * DKV + oc * 128: dc * DKV + (oc + 1) * 128],
                            xnT[:, dc * PQ:(dc + 1) * PQ],
                            start=(dc == 0), stop=(dc == n_dc - 1)),
                            after=[t_wdkv, t_xnT, pfree.get("pP0") if dc == 0 else None])
                    t_ckv = em.op("dve", lambda e, oc=oc: e.tensor_copy(
                        ckvT[:, oc * PQ:(oc + 1) * PQ], pP0[:, 0:PQ]),
                        after=[mm, t_free.get("ckvT")])
                    pfree["pP0"] = t_ckv
                for oc in range(n_hc):
                    mm = None
                    for kc in range(n_kv):
                        mm = em.op("pe", lambda e, oc=oc, kc=kc: e.matmul(
                            pS[oc % 2][:, 0:PQ],
                            w_uk[:, kc * D + oc * 128: kc * D + (oc + 1) * 128],
                            ckvT[:, kc * PQ:(kc + 1) * PQ],
                            start=(kc == 0), stop=(kc == n_kv - 1)),
                            after=[t_wuk, t_ckv, pfree.get(f"pS{oc % 2}") if kc == 0 else None])
                    tev = em.op("dve", lambda e, oc=oc, ti=ti, sub=sub: e.tensor_copy(
                        kC[:, oc * T2 + ti * QT + sub * PQ: oc * T2 + ti * QT + sub * PQ + PQ],
                        pS[oc % 2][:, 0:PQ]),
                        after=[mm, t_free.get("kC")])
                    pfree[f"pS{oc % 2}"] = tev
                mm = None
                for dc in range(n_dc):
                    mm = em.op("pe", lambda e, dc=dc: e.matmul(
                        pP0[0:32, 0:PQ],
                        w_kr[:, dc * 32:(dc + 1) * 32],
                        xnT[:, dc * PQ:(dc + 1) * PQ],
                        start=(dc == 0), stop=(dc == n_dc - 1)),
                        after=[t_wkr, t_xnT, pfree.get("pP0")])
                kslice = slice(ti * QT + sub * PQ, ti * QT + sub * PQ + PQ)
                tz = em.op("dve", lambda e, kslice=kslice: e.memset(kR[:, kslice], 0.0),
                           after=[t_free.get("kR")])
                t_kr0 = em.op("dve", lambda e, kslice=kslice: e.tensor_tensor(
                    kR[0:32, kslice], pP0[0:32, 0:PQ], sclk[0:32, kslice], ALU.mult),
                    after=[mm, tz, t_sclk])
                t_kr = em.op("dve", lambda e, kslice=kslice: e.tensor_tensor(
                    kR[64:96, kslice], pP0[0:32, 0:PQ], sclk[0:32, kslice], ALU.mult),
                    after=[t_kr0])
                pfree["pP0"] = t_kr
                t_v = None
                for tc in range(n_tcp):
                    mm = None
                    for kc in range(n_kv):
                        for nd in range(n_nd):
                            mm = em.op("pe", lambda e, tc=tc, kc=kc, nd=nd: e.matmul(
                                pO[:, nd * NDW:(nd + 1) * NDW],
                                ckvT[:, kc * PQ + tc * 128: kc * PQ + (tc + 1) * 128],
                                w_uv[:, kc * D + nd * NDW: kc * D + (nd + 1) * NDW],
                                start=(kc == 0), stop=(kc == n_kv - 1)),
                                after=[t_wuv, t_ckv, pfree.get("pO") if (kc == 0 and nd == 0) else None])
                    vchunk = ti * n_tc + sub * n_tcp + tc
                    t_v = em.op("dve", lambda e, vchunk=vchunk: e.tensor_copy(
                        vT[:, vchunk * D:(vchunk + 1) * D], pO[:, 0:D]),
                        after=[mm, t_free.get("vT")])
                    pfree["pO"] = t_v
                t_free["xnT"] = t_v
                t_free["cqT"] = uqr_tok[("use", 2 * n_hc - 1)]
                t_free["ckvT"] = t_v
            t_proj = t_v


            n_kc = n_tc * (ti + 1)
            y_last = None
            for h in range(H):
                hq = (h * 64) // 128
                hb = (h * 64) % 128
                t_pl = t_py = None
                for kc in range(n_kc):
                    rot = kc % 2
                    pSa = pS[rot]
                    m1 = em.op("pe", lambda e, kc=kc, ti=ti, hq=hq, hb=hb, pSa=pSa: e.matmul(
                        pSa[:, 0:QT],
                        kC[hb:hb + 64, hq * T2 + kc * 128: hq * T2 + (kc + 1) * 128],
                        qCt[hb:hb + 64, hq * QT: hq * QT + QT],
                        start=True, stop=False),
                        after=[t_proj, pfree.get(f"pS{rot}")])
                    m2 = em.op("pe", lambda e, kc=kc, hq=hq, hb=hb, pSa=pSa: e.matmul(
                        pSa[:, 0:QT],
                        kR[hb:hb + 64, kc * 128:(kc + 1) * 128],
                        qRt[hb:hb + 64, hq * QT: hq * QT + QT],
                        start=False, stop=True), after=[])
                    if kc >= n_tc * ti:
                        p_ = kc - n_tc * ti
                        tm = em.op("dve", lambda e, p_=p_, pSa=pSa: e.scalar_tensor_tensor(
                            sexp[:, 0:QT], pSa[:, 0:QT], SCL, msk[:, (n_tc - 1 - p_) * 128:(n_tc - 1 - p_) * 128 + QT],
                            ALU.mult, ALU.add), after=[m2, t_msk, pfree.get("sexp")])
                        te = em.op("act", lambda e, rot=rot: e.activation(
                            pbf[rot][:, 0:QT], sexp[:, 0:QT], AF.Exp),
                            after=[tm, t_free.get(f"pbf{rot}")])
                        pfree["sexp"] = te
                    else:
                        te = em.op("act", lambda e, rot=rot, pSa=pSa: e.activation(
                            pbf[rot][:, 0:QT], pSa[:, 0:QT], AF.Exp,
                            scale=cst[:, SCL_COL:SCL_COL + 1]),
                            after=[m2, t_free.get(f"pbf{rot}"), cfg.get("cst_tok")])
                    pfree[f"pS{rot}"] = te
                    t_pl = em.op("pe", lambda e, kc=kc, rot=rot, n_kc=n_kc: e.matmul(
                        pL[0:1, 0:QT], onescb[:], pbf[rot][:, 0:QT],
                        start=(kc == 0), stop=(kc == n_kc - 1), skip_group_check=True),
                        after=[te, t_ones, pfree.get("pL") if kc == 0 else None])
                    t_py = em.op("pe", lambda e, kc=kc, h=h, rot=rot, n_kc=n_kc: e.matmul(
                        pY[0:64, 0:QT], vT[:, kc * D + h * 64: kc * D + (h + 1) * 64],
                        pbf[rot][:, 0:QT],
                        start=(kc == 0), stop=(kc == n_kc - 1), skip_group_check=True),
                        after=[te, pfree.get("pY") if kc == 0 else None])
                    t_free[f"pbf{rot}"] = t_py
                f1 = em.op("dve", lambda e: e.tensor_copy(srow[0:1, 0:QT], pL[0:1, 0:QT]),
                           after=[t_pl, pfree.get("srow")])
                pfree["pL"] = f1
                f2 = em.op("dve", lambda e: e.reciprocal(srow[0:1, QT:2 * QT], srow[0:1, 0:QT]),
                           after=[f1])
                f3 = em.op("dve", lambda e: e.tensor_copy(grow[0:1, 0:QT], srow[0:1, QT:2 * QT]),
                           after=[f2, pfree.get("grow")])
                pfree["srow"] = f3
                f4 = em.op("pe", lambda e: e.matmul(
                    pP0[:, 0:QT], onesrb[0:1, :], grow[0:1, 0:QT], start=True, stop=True),
                    after=[f3, t_onr, pfree.get("pP0")])
                pfree["grow"] = f4
                f5 = em.op("act", lambda e: e.activation(gbuf[:, 0:QT], pP0[:, 0:QT], AF.Copy),
                           after=[f4, pfree.get("gbuf")])
                pfree["pP0"] = f5
                f6 = em.op("dve", lambda e, h=h, hq=hq, hb=hb: e.tensor_tensor(
                    yTt[hb:hb + 64, hq * QT: hq * QT + QT],
                    pY[0:64, 0:QT], gbuf[0:64, 0:QT], ALU.mult),
                    after=[t_py, f5, t_free.get("yTt")])
                pfree["pY"] = f6
                pfree["gbuf"] = f6
                y_last = f6
            t_res = em.dma("sp", name + "_res", lambda e, trng=trng: e.dma_start(
                out=outt[:, 0:D],
                in_=res[trng[0]:trng[0] + 128, :]), after=[last_store, pfree.get("outt")])
            o_toks = []
            wo_toks = {}
            for kc in range(n_hc):
                slot = kc % 3
                wo_toks[kc] = em.dma("pool", f"{name}_wo{slot}", lambda e, kc=kc, slot=slot: e.dma_start(
                    out=w_o[slot][:], in_=wo[kc * 128:(kc + 1) * 128, :]),
                    after=[wo_use.get((b, ti, kc - 3) if False else None), wo_use.get(kc - 3)])
            for tc in range(n_tc):
                if tc > 0:
                    t_res = em.dma("sp", name + "_res", lambda e, trng=trng, tc=tc: e.dma_start(
                        out=outt[:, 0:D], in_=res[trng[0] + tc * 128: trng[0] + (tc + 1) * 128, :]),
                        after=[last_store, pfree.get("outt")])
                mm = None
                for kc in range(n_hc):
                    for nd in range(n_nd):
                        mm = em.op("pe", lambda e, kc=kc, nd=nd, tc=tc: e.matmul(
                            pO[:, nd * NDW:(nd + 1) * NDW],
                            yTt[:, kc * QT + tc * 128: kc * QT + (tc + 1) * 128],
                            w_o[kc % 3][:, nd * NDW:(nd + 1) * NDW],
                            start=(kc == 0), stop=(kc == n_hc - 1)),
                            after=[wo_toks[kc], y_last, pfree.get("pO") if (kc == 0 and nd == 0) else None])
                        if tc == n_tc - 1:
                            wo_use[kc] = mm
                tev = em.op("dve", lambda e, tc=tc: e.tensor_add(
                    outt[:], pO[:, 0:D], outt[:]),
                    after=[mm, t_res])
                pfree["pO"] = tev
                o_toks.append(tev)
                last_store = em.dma("sp", name + "_st", lambda e, trng=trng, tc=tc: e.dma_start(
                    out=out[trng[0] + tc * 128: trng[0] + (tc + 1) * 128, :], in_=outt[:]),
                    after=[tev])
                pfree["outt"] = last_store
            t_free["xnT"] = t_proj
            t_free["qCt"] = y_last
            t_free["qRt"] = y_last
            for sl in range(2):
                t_free[f"uqr{sl}"] = uqr_tok[("use", 2 * n_hc - 1)]
            t_free["ckvT"] = t_proj
            t_free["sclt"] = y_last
            t_free["yTt"] = o_toks[-1]
        t_free["kC"] = y_last
        t_free["kR"] = y_last
        t_free["vT"] = y_last
    em.wait("sp", [last_store])
    return last_store



"""Mamba-3 mixer pass (L0 attn branch), chunked SSD formulation, v2.

Two sub-passes (separate Blocks via em scopes):
  A) in_proj streamer: xn @ w_in^T -> DRAM proj [T, 4224] bf16 (z|x|B token-major)
     plus feature-major B^T/C^T -> bT/cT [128, T] bf16 and smallT -> smT [128, T] f32.
  B) per sequence: pass1 (small-family, lambda, theta->r0) then pass2 (chunk scan).

scr dict (internal DRAM): proj, bT, cT, smT, r0d [128, 8*T] bf16,
tld/trd [1, 32*T] f32 (per-b offset b*32*T2).
"""

F32 = mybir.dt.float32
BF16 = mybir.dt.bfloat16
I32 = mybir.dt.int32
AF = mybir.ActivationFunctionType
ALU = mybir.AluOpType

ZC, XC, BRC, CRC, DTC, TRC = 0, 2048, 4096, 4224, 4352, 4416
INV2PI = float(1.0 / (2.0 * 3.14159265358979323846))
PI = 3.14159265358979323846


def emit_mamba(nc, em, cfg, xn, w_inT, w_outT, dtb_col, d_dup, sel32f, sel32b, selh,
               selp4, mask4, identf, identb, res, out, cst, scr, name="mb"):
    T, D, NB = cfg["T"], cfg["D"], cfg["NB"]
    T2 = T // NB
    H, DI = 32, 2048
    ONE_COL, TPI_COL = cfg["one_col"], cfg["tpi_col"]
    n_dc = D // 128
    Q = 128
    n_c = T2 // Q
    GQ = min(2, n_c)
    n_g = n_c // GQ
    GW = GQ * Q
    cstk = cfg["cst_tok"]
    proj, bTd, cTd, smTd, r0d, tld, trd = (
        scr["proj"], scr["bT"], scr["cT"], scr["smT"], scr["r0d"], scr["tld"], scr["trd"])

    sb, ps = em.sb, em.ps
    ddup = sb(name + "_ddup", [128, DI], F32)
    s32f = sb(name + "_s32f", [32, 128], F32)
    s32b = sb(name + "_s32b", [32, 128], BF16)
    shh = sb(name + "_shh", [32, 8 * 128], F32)
    sp4 = sb(name + "_sp4", [4, 128], F32)
    m4 = sb(name + "_m4", [128, 512], F32)
    idf = sb(name + "_idf", [128, 128], F32)
    idb = sb(name + "_idb", [128, 128], BF16)
    dtb = sb(name + "_dtb", [32, 1], F32)
    onesr = sb(name + "_onesr", [1, 128], F32)
    bg_tok = sb(name + "_bgtok", [128, 16 * 96], F32)
    pit_tok = sb(name + "_pittok", [128, 16 * 32], F32)
    arow = sb(name + "_arow", [1, 16 * 32], F32)
    a4 = sb(name + "_a4", [4, 8], F32)
    a8 = sb(name + "_a8", [8, 4], F32)
    s_cos = sb(name + "_scos", [128, 512], F32)
    s_sin = sb(name + "_ssin", [128, 512], F32)
    s_un = sb(name + "_sun", [96, 2048], F32)
    s_cosb = sb(name + "_scosb", [128, 512], BF16)
    s_sinb = sb(name + "_ssinb", [128, 512], BF16)
    s_unb = sb(name + "_sunb", [96, 2048], BF16)
    w_os = [sb(f"{name}_wos{i}", [128, 1024], BF16) for i in range(3)]

    PA = ps(name + "_PA", [128, 512], F32)
    PB = ps(name + "_PB", [128, 512], F32)
    PC = ps(name + "_PC", [128, 512], F32)
    PD = ps(name + "_PD", [128, 512], F32)
    PE1 = ps(name + "_PE1", [128, 512], F32)
    PF = ps(name + "_PF", [128, 512], F32)
    ptr = ps(name + "_ptr", [128, 128], BF16)
    pfree = {}
    t_free = {}

    em.dma("sp", name + "_cd", lambda e: e.dma_start(out=ddup[:], in_=d_dup))
    em.dma("sp", name + "_cd", lambda e: e.dma_start(out=s32f[:], in_=sel32f))
    em.dma("sp", name + "_cd", lambda e: e.dma_start(out=s32b[:], in_=sel32b))
    em.dma("sp", name + "_cd", lambda e: e.dma_start(out=shh[:], in_=selh))
    em.dma("sp", name + "_cd", lambda e: e.dma_start(out=sp4[:], in_=selp4))
    em.dma("sp", name + "_cd", lambda e: e.dma_start(out=m4[:], in_=mask4))
    em.dma("sp", name + "_cd", lambda e: e.dma_start(out=idf[:], in_=identf))
    em.dma("sp", name + "_cd", lambda e: e.dma_start(out=idb[:], in_=identb))
    t_cd = em.dma("sp", name + "_cd", lambda e: e.dma_start(out=dtb[:], in_=dtb_col))
    t_or = em.op("dve", lambda e: e.memset(onesr[:], 1.0))

    em.push_scope()
    xnT = sb(name + "_xnT", [128, n_dc * 512], BF16)
    wu = [sb(f"{name}_wu{i}", [128, n_dc * 512], BF16) for i in range(3)]
    pbuf = sb(name + "_pbuf", [128, 4 * 4224], BF16)
    fbuf = sb(name + "_fbuf", [128, 2 * 512], BF16)
    fbuf2 = sb(name + "_fbuf2", [128, 512], F32)
    units = [(ZC + 512 * k, 512, "tok") for k in range(4)] + \
            [(XC + 512 * k, 512, "tok") for k in range(4)] + \
            [(BRC, 128, "both"), (CRC, 128, "feat"), (DTC, 128, "feat")]
    n_ttA = T // 512
    wu_tok = {}
    wu_use = {}
    wu_state = {"next": 0}

    def ensure_wu(upto):
        while wu_state["next"] <= min(upto, n_ttA * len(units) - 1):
            i = wu_state["next"]
            ui = i % len(units)
            co, nw, kind = units[ui]
            slot = i % 3
            prev = wu_use.get(i - 3)
            tok = em.dma("pool", f"{name}_wu{slot}", lambda e, co=co, nw=nw, slot=slot: e.dma_start(
                out=wu[slot][:, 0:n_dc * nw].rearrange("p (c n) -> p c n", c=n_dc),
                in_=w_inT[:, co:co + nw].rearrange("(c p) n -> p c n", p=128)),
                after=[prev])
            wu_tok[i] = (slot, tok)
            wu_state["next"] = i + 1

    t_pst = None
    t_fst = None
    for tt in range(n_ttA):
        for dc in range(n_dc):
            t_xn = em.dma("sp", name + "_xnT", lambda e, tt=tt, dc=dc: e.dma_start(
                out=xnT[:, dc * 512:(dc + 1) * 512],
                in_=xn[tt * 512:(tt + 1) * 512, dc * 128:(dc + 1) * 128],
                transpose=True), after=[t_free.get("xnT")])
        ev_toks = []
        for ui, (co, nw, kind) in enumerate(units):
            gi = tt * len(units) + ui
            ensure_wu(gi + 2)
            slot, wtok = wu_tok[gi]
            if kind in ("tok", "both"):
                for tc in range(4):
                    pband = PA if tc % 2 == 0 else PB
                    pnm = "PA" if tc % 2 == 0 else "PB"
                    mm = None
                    for dc in range(n_dc):
                        mm = em.op("pe", lambda e, nw=nw, dc=dc, tc=tc, slot=slot, pband=pband: e.matmul(
                            pband[:, 0:nw],
                            xnT[:, dc * 512 + tc * 128: dc * 512 + (tc + 1) * 128],
                            wu[slot][:, dc * nw:(dc + 1) * nw],
                            start=(dc == 0), stop=(dc == n_dc - 1)),
                            after=[wtok, t_xn, pfree.get(pnm) if dc == 0 else None])
                    tev = em.op("dve", lambda e, co=co, nw=nw, tc=tc, pband=pband: e.tensor_copy(
                        pbuf[:, tc * 4224 + co: tc * 4224 + co + nw], pband[:, 0:nw]),
                        after=[mm, t_pst])
                    pfree[pnm] = tev
                    ev_toks.append(tev)
            if kind in ("feat", "both"):
                mm = None
                for dc in range(n_dc):
                    mm = em.op("pe", lambda e, dc=dc, slot=slot: e.matmul(
                        PC[:, 0:512],
                        wu[slot][:, dc * 128:(dc + 1) * 128],
                        xnT[:, dc * 512:(dc + 1) * 512],
                        start=(dc == 0), stop=(dc == n_dc - 1)),
                        after=[wtok, t_xn, pfree.get("PC") if dc == 0 else None])
                if co == DTC:
                    tev = em.op("dve", lambda e: e.tensor_copy(fbuf2[:], PC[:, 0:512]),
                                after=[mm, t_fst])
                else:
                    fo = 0 if co == BRC else 512
                    tev = em.op("dve", lambda e, fo=fo: e.tensor_copy(
                        fbuf[:, fo:fo + 512], PC[:, 0:512]), after=[mm, t_fst])
                pfree["PC"] = tev
                ev_toks.append(tev)
            wu_use[gi] = ev_toks[-1]
        for tc in range(4):
            t_pst = em.dma("sp", name + "_pst", lambda e, tt=tt, tc=tc: e.dma_start(
                out=proj[tt * 512 + tc * 128: tt * 512 + (tc + 1) * 128, :],
                in_=pbuf[:, tc * 4224:(tc + 1) * 4224]), after=ev_toks)
        em.dma("sp", name + "_fst", lambda e, tt=tt: e.dma_start(
            out=bTd[:, tt * 512:(tt + 1) * 512], in_=fbuf[:, 0:512]), after=ev_toks)
        em.dma("sp", name + "_fst", lambda e, tt=tt: e.dma_start(
            out=cTd[:, tt * 512:(tt + 1) * 512], in_=fbuf[:, 512:1024]), after=ev_toks)
        t_fst = em.dma("sp", name + "_fst", lambda e, tt=tt: e.dma_start(
            out=smTd[:, tt * 512:(tt + 1) * 512], in_=fbuf2[:]), after=ev_toks)
        t_free["xnT"] = ev_toks[-1]
    em.wait("sp", [t_pst, t_fst])
    em.pop_scope()

    wos_tok = {}
    wos_use = {}
    wos_state = {"next": 0}

    def ensure_wos(upto, total):
        while wos_state["next"] <= min(upto, total - 1):
            i = wos_state["next"]
            kc = i % 16
            slot = i % 3
            prev = wos_use.get(i - 3)
            tok = em.dma("pool", f"{name}_wos{slot}", lambda e, kc=kc, slot=slot: e.dma_start(
                out=w_os[slot][:, 0:D], in_=w_outT[kc * 128:(kc + 1) * 128, :]), after=[prev])
            wos_tok[i] = (slot, tok)
            wos_state["next"] = i + 1

    last_store = None
    for b in range(NB):
        t0 = b * T2
        em.push_scope()
        smA = sb(name + f"{b}_smA", [64, T2], F32)
        smB = sb(name + f"{b}_smB", [64, T2], F32)
        dtT = sb(name + f"{b}_dtT", [32, T2], F32)
        aT = sb(name + f"{b}_aTT", [32, T2], F32)
        gaT = sb(name + f"{b}_gaT", [32, T2], F32)
        beT = sb(name + f"{b}_beT", [32, T2], F32)
        linc = sb(name + f"{b}_linc", [32, T2], F32)
        alT = sb(name + f"{b}_alT", [32, T2], F32)
        sp1 = sb(name + f"{b}_sp1", [32, T2], F32)
        sp2 = sb(name + f"{b}_sp2", [32, T2], F32)
        tanT = sb(name + f"{b}_tanT", [32, T2], F32)
        lam = sb(name + f"{b}_lam", [32, T2], F32)
        lneg = sb(name + f"{b}_lneg", [32, T2], F32)
        dT = sb(name + f"{b}_dTT", [32, T2], F32)
        piT = sb(name + f"{b}_piT", [32, T2], F32)
        aab = sb(name + f"{b}_aab", [32, 64], F32)
        on128 = sb(name + f"{b}_on128", [128, T2], F32)
        thb = sb(name + f"{b}_thb", [128, T2], F32)
        thc = sb(name + f"{b}_thc", [128, T2], F32)
        thi = sb(name + f"{b}_thi", [128, 512], I32)
        p1sb = sb(name + f"{b}_p1sb", [128, T2], F32)
        r0bf = sb(name + f"{b}_r0bf", [128, 512], BF16)

        t_o128 = em.op("dve", lambda e: e.memset(on128[:], 1.0))
        t_om = t_o128
        t_smA = em.dma("sp", name + "_smld", lambda e, t0=t0: e.dma_start(
            out=smA[:], in_=smTd[0:64, t0:t0 + T2]))
        t_smB = em.dma("sp", name + "_smld", lambda e, t0=t0: e.dma_start(
            out=smB[:], in_=smTd[64:128, t0:t0 + T2]))

        def softplus(dst, src_sl, bias_col, toks):
            t1 = em.op("dve", lambda e, src_sl=src_sl: e.tensor_copy(sp1[:], src_sl),
                       after=toks + [t_free.get("sp")])
            if bias_col is not None:
                t1 = em.op("dve", lambda e, bias_col=bias_col: e.tensor_scalar(
                    sp1[:], sp1[:], bias_col, None, ALU.add), after=[t1, t_cd])
            t2 = em.op("dve", lambda e: e.tensor_scalar(sp2[:], sp1[:], -1.0, None, ALU.mult), after=[t1])
            t3 = em.op("dve", lambda e: e.tensor_tensor(sp2[:], sp1[:], sp2[:], ALU.min), after=[t2])
            t4 = em.op("act", lambda e: e.activation(sp2[:], sp2[:], AF.Exp), after=[t3])
            t5 = em.op("act", lambda e: e.activation(sp2[:], sp2[:], AF.Ln,
                                                     bias=cst[0:32, ONE_COL:ONE_COL + 1]),
                       after=[t4, cstk])
            t6 = em.op("dve", lambda e: e.tensor_scalar(sp1[:], sp1[:], 0.0, None, ALU.max), after=[t5])
            t7 = em.op("dve", lambda e, dst=dst: e.tensor_add(dst, sp1[:], sp2[:]), after=[t6])
            t_free["sp"] = t7
            return t7

        t_dt = softplus(dtT[:], smA[0:32, :], dtb[:, 0:1], [t_smA])
        t_sa = softplus(aT[:], smA[32:64, :], None, [t_smA])
        t_a = em.op("dve", lambda e: e.tensor_scalar(aT[:], aT[:], -1.0, -1e-4, ALU.mult, ALU.min),
                    after=[t_sa])
        t_tr = em.op("act", lambda e: e.activation(sp2[:], smB[0:32, :], AF.Sigmoid),
                     after=[t_smB, t_free.get("sp")])
        t_th = em.op("act", lambda e: e.activation(tanT[:], smB[32:64, :], AF.Tanh), after=[t_smB])
        t_li = em.op("dve", lambda e: e.tensor_mul(linc[:], aT[:], dtT[:]), after=[t_a, t_dt])
        t_al = em.op("act", lambda e: e.activation(alT[:], linc[:], AF.Exp), after=[t_li])
        t_ga = em.op("dve", lambda e: e.tensor_mul(gaT[:], sp2[:], dtT[:]), after=[t_tr, t_dt])
        t_free["sp"] = t_ga
        t_be = em.op("dve", lambda e: e.tensor_sub(beT[:], dtT[:], gaT[:]), after=[t_ga])
        t_be = em.op("dve", lambda e: e.tensor_mul(beT[:], beT[:], alT[:]), after=[t_be, t_al])
        t_l1 = em.op("dve", lambda e: e.tensor_tensor_scan(lam[:], on128[0:32, :], linc[:], 0.0,
                                                           ALU.mult, ALU.add),
                     after=[t_li, t_om])
        t_l2 = em.op("dve", lambda e: e.memset(sp1[:, 0:128], 0.0), after=[t_dt, t_sa])
        t_l3 = em.op("dve", lambda e: e.tensor_copy(
            sp1[:, 128:T2].rearrange("p (c q) -> p c q", q=128),
            lam[:, 127:T2 - 1:128].unsqueeze(2).broadcast_to((32, n_c - 1, 128))),
            after=[t_l1, t_l2])
        t_l4 = em.op("dve", lambda e: e.tensor_sub(lam[:], lam[:], sp1[:]), after=[t_l3])
        t_d1 = em.op("dve", lambda e: e.tensor_sub(
            dT[:].rearrange("p (c q) -> p c q", q=128),
            lam[:, 127::128].unsqueeze(2).broadcast_to((32, n_c, 128)),
            lam[:].rearrange("p (c q) -> p c q", q=128)), after=[t_l4])
        t_d2 = em.op("act", lambda e: e.activation(dT[:], dT[:], AF.Exp), after=[t_d1])
        t_p1 = em.op("act", lambda e: e.activation(piT[:], lam[:], AF.Exp), after=[t_l4])
        t_ae = em.op("act", lambda e: e.activation(aab[:, 0:n_c], lam[:, 127::128], AF.Exp),
                     after=[t_l4])
        t_at = em.op("pe", lambda e: e.transpose(PA[0:n_c, 0:32], aab[:, 0:n_c], idf[0:32, 0:32]),
                     after=[t_ae, pfree.get("PA")])
        t_at2 = em.op("dve", lambda e: e.tensor_copy(aab[0:n_c, 32:64], PA[0:n_c, 0:32]), after=[t_at])
        pfree["PA"] = t_at2
        t_ar = em.dma("pool", name + "_arow", lambda e: e.dma_start(
            out=arow[0:1, 0:n_c * 32], in_=aab[0:n_c, 32:64]), after=[t_at2, t_free.get("arow")])
        t_ln = em.op("dve", lambda e: e.tensor_scalar(lneg[:], lam[:], -1.0, None, ALU.mult),
                     after=[t_l4])
        t_t3 = em.dma("pool", name + "_tl", lambda e, b=b: e.dma_start(
            out=tld[0:1, b * H * T2:(b + 1) * H * T2], in_=lneg[:]), after=[t_ln])
        t_t4 = em.dma("pool", name + "_tr", lambda e, b=b: e.dma_start(
            out=trd[0:1, b * H * T2:(b + 1) * H * T2], in_=lam[:]), after=[t_l4])
        t_bg = t_free.get("bg_tok")
        t_pit = t_free.get("pit_tok")
        for c in range(n_c):
            specs = [(beT, t_be, bg_tok, c * 96, "bg"), (gaT, t_ga, bg_tok, c * 96 + 32, "bg"),
                     (dT, t_d2, bg_tok, c * 96 + 64, "bg"), (piT, t_p1, pit_tok, c * 32, "pit")]
            for srcT, stok_, dstT, off, kind in specs:
                wtr = em.op("pe", lambda e, srcT=srcT, c=c: e.transpose(
                    PB[:, 0:32], srcT[:, c * 128:(c + 1) * 128], idf[0:32, 0:32]),
                    after=[stok_, pfree.get("PB")])
                tevv = em.op("dve", lambda e, dstT=dstT, off=off: e.tensor_copy(
                    dstT[:, off:off + 32], PB[:, 0:32]),
                    after=[wtr, (t_bg if kind == "bg" else t_pit)])
                pfree["PB"] = tevv
                if kind == "bg":
                    t_bg = tevv
                else:
                    t_pit = tevv
        t_dp = em.op("dve", lambda e: e.tensor_scalar(alT[:], dtT[:], PI, None, ALU.mult),
                     after=[t_be])
        t_p1s = None
        for tt2 in range(T2 // 512):
            mm = em.op("pe", lambda e, tt2=tt2: e.matmul(
                PC[:, 0:512], s32f[:], tanT[:, tt2 * 512:(tt2 + 1) * 512],
                start=True, stop=True),
                after=[t_th, pfree.get("PC")])
            t_p1s = em.op("dve", lambda e, tt2=tt2: e.tensor_copy(
                p1sb[:, tt2 * 512:(tt2 + 1) * 512], PC[:, 0:512]), after=[mm, t_p1s])
            pfree["PC"] = t_p1s
        t_r0 = None
        t_st_r0 = None
        for j in range(8):
            t_i2 = None
            for tt2 in range(T2 // 512):
                mm = em.op("pe", lambda e, j=j, tt2=tt2: e.matmul(
                    PD[:, 0:512], shh[:, j * 128:(j + 1) * 128], alT[:, tt2 * 512:(tt2 + 1) * 512],
                    start=True, stop=True),
                    after=[t_dp, pfree.get("PD")])
                t_i2 = em.op("dve", lambda e, tt2=tt2: e.tensor_mul(
                    thb[:, tt2 * 512:(tt2 + 1) * 512], PD[:, 0:512],
                    p1sb[:, tt2 * 512:(tt2 + 1) * 512]),
                    after=[mm, t_p1s, t_i2, t_free.get("thb")])
                pfree["PD"] = t_i2
            t_s1 = em.op("dve", lambda e: e.tensor_tensor_scan(
                thc[:], on128[:], thb[:], 0.0, ALU.mult, ALU.add),
                after=[t_i2, t_o128, t_free.get("thc")])
            t_s2 = em.op("dve", lambda e: e.tensor_scalar(thb[:], thc[:], INV2PI, None, ALU.mult),
                         after=[t_s1])
            t_free["thb"] = t_s2
            t_s4 = None
            for tt2 in range(T2 // 512):
                t_s3 = em.op("dve", lambda e, tt2=tt2: e.tensor_copy(
                    thi[:], thb[:, tt2 * 512:(tt2 + 1) * 512]),
                    after=[t_s2, t_free.get("thi")])
                t_s4 = em.op("dve", lambda e, tt2=tt2: e.tensor_copy(
                    thc[:, tt2 * 512:(tt2 + 1) * 512], thi[:]), after=[t_s3])
                t_free["thi"] = t_s4
            t_free["thc"] = t_s4
            t_r5 = em.op("dve", lambda e: e.tensor_sub(thb[:], thb[:], thc[:]), after=[t_s4])
            t_r6 = em.op("dve", lambda e: e.tensor_scalar(thc[:], thb[:], 0.5, None, ALU.is_gt),
                         after=[t_r5])
            t_r7 = em.op("dve", lambda e: e.tensor_sub(thb[:], thb[:], thc[:]), after=[t_r6])
            t_r8 = em.op("dve", lambda e: e.tensor_scalar(thc[:], thb[:], -0.5, None, ALU.is_lt),
                         after=[t_r7])
            t_st_r0 = None
            for tt2 in range(T2 // 512):
                t_r0 = em.op("dve", lambda e, tt2=tt2: e.tensor_add(
                    r0bf[:], thb[:, tt2 * 512:(tt2 + 1) * 512],
                    thc[:, tt2 * 512:(tt2 + 1) * 512]),
                    after=[t_r8, t_free.get("r0bf")])
                t_st_r0 = em.dma("sp", name + "_r0st", lambda e, j=j, b=b, tt2=tt2: e.dma_start(
                    out=r0d[:, (b * 8 + j) * T2 + tt2 * 512:(b * 8 + j) * T2 + (tt2 + 1) * 512],
                    in_=r0bf[:]), after=[t_r0])
                t_free["r0bf"] = t_st_r0
            t_free["thb"] = t_r0
        em.wait("sp", [t_st_r0])
        em.wait("pool", [t_t3, t_t4, t_ar])
        em.pop_scope()

        em.push_scope()
        brT = sb(name + f"{b}_brT", [32, GW], BF16)
        bunT = sb(name + f"{b}_bunT", [96, GW], BF16)
        crT = sb(name + f"{b}_crT", [32, GW], BF16)
        cunT = sb(name + f"{b}_cunT", [96, GW], BF16)
        bdup = sb(name + f"{b}_bdup", [128, GW], BF16)
        cdup = sb(name + f"{b}_cdup", [128, GW], BF16)
        r0g = sb(name + f"{b}_r0g", [128, 8 * GW], BF16)
        cosg = sb(name + f"{b}_cosg", [128, 8 * GW], BF16)
        sing = sb(name + f"{b}_sing", [128, 8 * GW], BF16)
        brc = sb(name + f"{b}_brc", [128, 8 * GW], BF16)
        brs = sb(name + f"{b}_brs", [128, 8 * GW], BF16)
        crc = sb(name + f"{b}_crc", [128, 8 * GW], BF16)
        crs = sb(name + f"{b}_crs", [128, 8 * GW], BF16)
        trsc = sb(name + f"{b}_trsc", [128, GW], F32)
        trsc2 = sb(name + f"{b}_trsc2", [128, GW], F32)
        z_bf = sb(name + f"{b}_zbf", [128, DI], BF16)
        x_bf = sb(name + f"{b}_xbf", [128, DI], BF16)
        px_bf = sb(name + f"{b}_pxbf", [128, DI], BF16)
        xlast = sb(name + f"{b}_xlast", [1, DI], BF16)
        b_tok = sb(name + f"{b}_btok", [128, 128], BF16)
        u_f = sb(name + f"{b}_uf", [128, DI], F32)
        u_bf = sb(name + f"{b}_ubf", [128, DI], BF16)
        us_bf = sb(name + f"{b}_usbf", [128, DI], BF16)
        tlc = sb(name + f"{b}_tlc", [1, H * 128], F32)
        trc = sb(name + f"{b}_trc", [1, H * 128], F32)
        ctok = sb(name + f"{b}_ctok", [128, 1024], BF16)
        stok = sb(name + f"{b}_stok", [128, 1024], BF16)
        brctok = sb(name + f"{b}_brctok", [128, 1024], BF16)
        brstok = sb(name + f"{b}_brstok", [128, 1024], BF16)
        mexp = sb(name + f"{b}_mexp", [128, 512], F32)
        mt = sb(name + f"{b}_mt", [128, H * 128], BF16)
        ytok = sb(name + f"{b}_ytok", [128, DI], F32)
        yint = sb(name + f"{b}_yint", [128, 512], F32)
        abc = sb(name + f"{b}_abc", [128, 512], F32)
        abunh = sb(name + f"{b}_abunh", [128, 1024], F32)
        outbf = sb(name + f"{b}_outbf", [128, DI], BF16)
        outT = sb(name + f"{b}_outT", [128, DI], BF16)
        rest = sb(name + f"{b}_rest", [128, D], F32)
        outo = sb(name + f"{b}_outo", [128, D], F32)

        t_sc0 = em.op("dve", lambda e: e.memset(s_cos[:], 0.0), after=[t_free.get("s_cos")])
        t_ss0 = em.op("dve", lambda e: e.memset(s_sin[:], 0.0), after=[t_free.get("s_sin")])
        t_su0 = em.op("dve", lambda e: e.memset(s_un[:], 0.0), after=[t_free.get("s_un")])
        st_tok = {"cos": t_sc0, "sin": t_ss0, "un": t_su0}
        t_xlast = None

        for g in range(n_g):
            gt0 = t0 + g * GW
            em.dma("sp", name + "_fbld", lambda e, gt0=gt0: e.dma_start(
                out=brT[:], in_=bTd[0:32, gt0:gt0 + GW]), after=[t_free.get("fb")])
            em.dma("sp", name + "_fbld", lambda e, gt0=gt0: e.dma_start(
                out=bunT[:], in_=bTd[32:128, gt0:gt0 + GW]))
            em.dma("sp", name + "_fbld", lambda e, gt0=gt0: e.dma_start(
                out=crT[:], in_=cTd[0:32, gt0:gt0 + GW]))
            t_fb = em.dma("sp", name + "_fbld", lambda e, gt0=gt0: e.dma_start(
                out=cunT[:], in_=cTd[32:128, gt0:gt0 + GW]))
            t_r0g = em.dma("sp", name + "_r0ld", lambda e, g=g, b=b: e.dma_start(
                out=r0g[:].rearrange("p (j t) -> p j t", j=8),
                in_=bass.AP(r0d.tensor, b * 8 * T2 + g * GW, [[8 * T, 128], [T2, 8], [1, GW]])),
                after=[t_free.get("r0g")])
            mm = em.op("pe", lambda e: e.matmul(PB[:, 0:GW], s32b[:], brT[:], start=True, stop=True),
                       after=[t_fb, pfree.get("PB")])
            t_bd = em.op("dve", lambda e: e.tensor_copy(bdup[:], PB[:, 0:GW]),
                         after=[mm, t_free.get("bdup")])
            pfree["PB"] = t_bd
            mm = em.op("pe", lambda e: e.matmul(PB[:, 0:GW], s32b[:], crT[:], start=True, stop=True),
                       after=[t_fb, pfree.get("PB")])
            t_cd2 = em.op("dve", lambda e: e.tensor_copy(cdup[:], PB[:, 0:GW]),
                          after=[mm, t_free.get("cdup")])
            pfree["PB"] = t_cd2
            t_pl = None
            ts1 = tc1 = None
            for j in range(8):
                r0sl = r0g[:, j * GW:(j + 1) * GW]
                ts1 = em.op("act", lambda e, j=j, r0sl=r0sl: e.activation(
                    sing[:, j * GW:(j + 1) * GW], r0sl, AF.Sin, scale=cst[:, TPI_COL:TPI_COL + 1]),
                    after=[t_r0g, cstk, t_free.get("sing")])
                tc1 = em.op("dve", lambda e, r0sl=r0sl: e.tensor_scalar(trsc[:], r0sl, 0.25, None, ALU.add),
                            after=[t_r0g, t_free.get("trsc")])
                tc2 = em.op("dve", lambda e: e.tensor_scalar(trsc2[:], trsc[:], 0.5, None, ALU.is_gt),
                            after=[tc1, t_free.get("trsc2")])
                tc3 = em.op("dve", lambda e: e.tensor_sub(trsc[:], trsc[:], trsc2[:]), after=[tc2])
                t_free["trsc2"] = tc3
                tc4 = em.op("act", lambda e, j=j: e.activation(
                    cosg[:, j * GW:(j + 1) * GW], trsc[:], AF.Sin, scale=cst[:, TPI_COL:TPI_COL + 1]),
                    after=[tc3, cstk, t_free.get("cosg")])
                t_free["trsc"] = tc4
                p1 = em.op("dve", lambda e, j=j: e.tensor_mul(
                    brc[:, j * GW:(j + 1) * GW], bdup[:], cosg[:, j * GW:(j + 1) * GW]),
                    after=[t_bd, tc4, t_free.get("brc")])
                p2 = em.op("dve", lambda e, j=j: e.tensor_mul(
                    brs[:, j * GW:(j + 1) * GW], bdup[:], sing[:, j * GW:(j + 1) * GW]),
                    after=[t_bd, ts1, t_free.get("brs")])
                p3 = em.op("dve", lambda e, j=j: e.tensor_mul(
                    crc[:, j * GW:(j + 1) * GW], cdup[:], cosg[:, j * GW:(j + 1) * GW]),
                    after=[t_cd2, tc4, t_free.get("crc")])
                p4 = em.op("dve", lambda e, j=j: e.tensor_mul(
                    crs[:, j * GW:(j + 1) * GW], cdup[:], sing[:, j * GW:(j + 1) * GW]),
                    after=[t_cd2, ts1, t_free.get("crs")])
                t_pl = p4
            t_planes = t_pl

            for c in range(GQ):
                ci = g * GQ + c
                tglob = t0 + ci * Q
                cb1 = em.op("dve", lambda e: e.tensor_copy(s_cosb[:], s_cos[:]), after=[st_tok["cos"]])
                cb2 = em.op("dve", lambda e: e.tensor_copy(s_sinb[:], s_sin[:]), after=[st_tok["sin"]])
                cb3 = em.op("dve", lambda e: e.tensor_copy(s_unb[:], s_un[:]), after=[st_tok["un"]])
                t_z = em.dma("sp", name + "_zld", lambda e, tglob=tglob: e.dma_start(
                    out=z_bf[:], in_=proj[tglob:tglob + 128, ZC:ZC + DI]), after=[t_free.get("z_bf")])
                t_x = em.dma("sp", name + "_xld", lambda e, tglob=tglob: e.dma_start(
                    out=x_bf[:], in_=proj[tglob:tglob + 128, XC:XC + DI]),
                    after=[t_free.get("x_bf"), t_xlast])
                t_b = em.dma("sp", name + "_bld", lambda e, tglob=tglob: e.dma_start(
                    out=b_tok[:], in_=proj[tglob:tglob + 128, BRC:BRC + 128]),
                    after=[t_free.get("b_tok")])
                if ci == 0:
                    t_px0 = em.op("dve", lambda e: e.memset(px_bf[0:1, :], 0.0),
                                  after=[t_free.get("px0")])
                else:
                    t_px0 = em.dma("sp", name + "_px0", lambda e: e.dma_start(
                        out=px_bf[0:1, :], in_=xlast[0:1, :]), after=[t_xlast, t_free.get("px0")])
                t_px1 = em.dma("sp", name + "_px1", lambda e: e.dma_start(
                    out=px_bf[1:128, :], in_=x_bf[0:127, :]), after=[t_x, t_free.get("px1")])
                t_xlast = em.dma("sp", name + "_xl", lambda e: e.dma_start(
                    out=xlast[0:1, :], in_=x_bf[127:128, :]), after=[t_x, t_px0])
                bg = bg_tok[:, ci * 96:(ci + 1) * 96]
                u1 = em.op("dve", lambda e, bg=bg: e.tensor_tensor(
                    u_f[:].rearrange("p (h q) -> p h q", h=32),
                    x_bf[:].rearrange("p (h q) -> p h q", h=32),
                    bg[:, 32:64].unsqueeze(2).broadcast_to((128, 32, 64)), ALU.mult),
                    after=[t_x, t_bg, t_free.get("u_f")])
                u2 = em.op("dve", lambda e, bg=bg: e.tensor_tensor(
                    u_bf[:].rearrange("p (h q) -> p h q", h=32),
                    px_bf[:].rearrange("p (h q) -> p h q", h=32),
                    bg[:, 0:32].unsqueeze(2).broadcast_to((128, 32, 64)), ALU.mult),
                    after=[t_px0, t_px1, t_bg, t_free.get("u_bf")])
                u3 = em.op("dve", lambda e: e.tensor_add(u_f[:], u_f[:], u_bf[:]), after=[u1, u2])
                u4 = em.op("dve", lambda e: e.tensor_copy(u_bf[:], u_f[:]), after=[u3])
                u5 = em.op("dve", lambda e, bg=bg: e.tensor_tensor(
                    us_bf[:].rearrange("p (h q) -> p h q", h=32),
                    u_f[:].rearrange("p (h q) -> p h q", h=32),
                    bg[:, 64:96].unsqueeze(2).broadcast_to((128, 32, 64)), ALU.mult),
                    after=[u3, t_free.get("us_bf")])
                t_free["u_f"] = u5
                t_tl = em.dma("sp", name + "_tlld", lambda e, b=b, ci=ci: e.dma_start(
                    out=tlc[:].rearrange("p (h q) -> p h q", h=H),
                    in_=bass.AP(tld.tensor, b * H * T2 + ci * 128, [[1, 1], [T2, H], [1, 128]])),
                    after=[t_t3, t_free.get("tlc")])
                t_tr2 = em.dma("sp", name + "_trld", lambda e, b=b, ci=ci: e.dma_start(
                    out=trc[:].rearrange("p (h q) -> p h q", h=H),
                    in_=bass.AP(trd.tensor, b * H * T2 + ci * 128, [[1, 1], [T2, H], [1, 128]])),
                    after=[t_t4, t_free.get("trc")])
                t_ctok = t_stok = None
                for srcpl, dsttok, which in [(cosg, ctok, "c"), (sing, stok, "s")]:
                    for j in range(8):
                        tw0 = em.op("pe", lambda e, j=j, srcpl=srcpl, c=c: e.transpose(
                            ptr[:, 0:128],
                            srcpl[:, j * GW + c * 128: j * GW + (c + 1) * 128], idb[:]),
                            after=[t_planes, pfree.get("ptr")])
                        te = em.op("dve", lambda e, j=j, dsttok=dsttok: e.tensor_copy(
                            dsttok[:, j * 128:(j + 1) * 128], ptr[:, 0:128]),
                            after=[tw0, t_free.get("ctokstok"),
                                   (t_ctok if which == "c" else t_stok)])
                        pfree["ptr"] = te
                        if which == "c":
                            t_ctok = te
                        else:
                            t_stok = te
                bb1 = em.op("dve", lambda e: e.tensor_tensor(
                    brctok[:].rearrange("p (h r) -> p h r", h=32),
                    b_tok[:, 0:32].rearrange("p r -> p () r").broadcast_to((128, 32, 32)),
                    ctok[:].rearrange("p (h r) -> p h r", h=32), ALU.mult),
                    after=[t_b, t_ctok, t_free.get("brctok")])
                bb2 = em.op("dve", lambda e: e.tensor_tensor(
                    brstok[:].rearrange("p (h r) -> p h r", h=32),
                    b_tok[:, 0:32].rearrange("p r -> p () r").broadcast_to((128, 32, 32)),
                    stok[:].rearrange("p (h r) -> p h r", h=32), ALU.mult),
                    after=[t_b, t_stok, t_free.get("brstok")])
                t_mt = None
                for w in range(8):
                    gb_ = PA if w % 2 == 0 else PB
                    gb_nm = "PA" if w % 2 == 0 else "PB"
                    db_ = PC if w % 2 == 0 else PD
                    db_nm = "PC" if w % 2 == 0 else "PD"
                    gl = dl = None
                    for k4 in range(4):
                        h = w * 4 + k4
                        j = h // 4
                        hb = (h % 4) * 32
                        gl = em.op("pe", lambda e, j=j, hb=hb, c=c, gb_=gb_, k4=k4: e.matmul(
                            gb_[:, k4 * 128:(k4 + 1) * 128],
                            brc[hb:hb + 32, j * GW + c * 128: j * GW + (c + 1) * 128],
                            crc[hb:hb + 32, j * GW + c * 128: j * GW + (c + 1) * 128],
                            start=True, stop=False, tile_position=(hb, 0), skip_group_check=True),
                            after=[t_planes, pfree.get(gb_nm) if k4 == 0 else None])
                        gl = em.op("pe", lambda e, j=j, hb=hb, c=c, gb_=gb_, k4=k4: e.matmul(
                            gb_[:, k4 * 128:(k4 + 1) * 128],
                            brs[hb:hb + 32, j * GW + c * 128: j * GW + (c + 1) * 128],
                            crs[hb:hb + 32, j * GW + c * 128: j * GW + (c + 1) * 128],
                            start=False, stop=True, tile_position=(hb, 0), skip_group_check=True),
                            after=[])
                        gl = em.op("pe", lambda e, c=c, gb_=gb_, k4=k4: e.matmul(
                            gb_[:, k4 * 128:(k4 + 1) * 128],
                            bunT[0:96, c * 128:(c + 1) * 128],
                            cunT[0:96, c * 128:(c + 1) * 128],
                            start=False, stop=True, skip_group_check=True),
                            after=[t_fb])
                        dl = em.op("pe", lambda e, h=h, db_=db_, k4=k4: e.matmul(
                            db_[:, k4 * 128:(k4 + 1) * 128],
                            tlc[0:1, h * 128:(h + 1) * 128], onesr[0:1, :],
                            start=True, stop=False, skip_group_check=True),
                            after=[t_tl, t_or, pfree.get(db_nm) if k4 == 0 else None])
                        dl = em.op("pe", lambda e, h=h, db_=db_, k4=k4: e.matmul(
                            db_[:, k4 * 128:(k4 + 1) * 128],
                            onesr[0:1, :], trc[0:1, h * 128:(h + 1) * 128],
                            start=False, stop=True, skip_group_check=True),
                            after=[t_tr2])
                    tm1 = em.op("dve", lambda e, db_=db_: e.tensor_tensor(
                        mexp[:], db_[:, 0:512], m4[:], ALU.add),
                        after=[dl, t_free.get("mexp")])
                    pfree[db_nm] = tm1
                    tm2 = em.op("act", lambda e: e.activation(mexp[:], mexp[:], AF.Exp), after=[tm1])
                    tm3 = em.op("dve", lambda e, w=w, gb_=gb_: e.tensor_mul(
                        mt[:, w * 512:(w + 1) * 512], gb_[:, 0:512], mexp[:]),
                        after=[gl, tm2, t_free.get("mt")])
                    pfree[gb_nm] = tm3
                    t_free["mexp"] = tm3
                    t_mt = tm3
                t_ysl = []
                yl = il = None
                for w2 in range(4):
                    yb = PA if w2 % 2 == 0 else PB
                    yb_nm = "PA" if w2 % 2 == 0 else "PB"
                    ib = PC if w2 % 2 == 0 else PD
                    ib_nm = "PC" if w2 % 2 == 0 else "PD"
                    for k8 in range(8):
                        h = w2 * 8 + k8
                        j = h // 4
                        hb = (h % 4) * 32
                        yl = em.op("pe", lambda e, h=h, k8=k8, yb=yb: e.matmul(
                            yb[:, k8 * 64:(k8 + 1) * 64],
                            mt[:, h * 128:(h + 1) * 128],
                            u_bf[:, h * 64:(h + 1) * 64],
                            start=(k8 == 0), stop=(k8 == 7), skip_group_check=True),
                            after=[t_mt, u4, pfree.get(yb_nm) if k8 == 0 else None])
                        il = em.op("pe", lambda e, h=h, j=j, hb=hb, c=c, k8=k8, ib=ib: e.matmul(
                            ib[:, k8 * 64:(k8 + 1) * 64],
                            crc[hb:hb + 32, j * GW + c * 128: j * GW + (c + 1) * 128],
                            s_cosb[hb:hb + 32, (h // 4) * 64:(h // 4) * 64 + 64],
                            start=True, stop=False, tile_position=(hb, 0),
                            skip_group_check=True),
                            after=[t_planes, cb1, pfree.get(ib_nm) if k8 == 0 else None])
                        il = em.op("pe", lambda e, h=h, j=j, hb=hb, c=c, k8=k8, ib=ib: e.matmul(
                            ib[:, k8 * 64:(k8 + 1) * 64],
                            crs[hb:hb + 32, j * GW + c * 128: j * GW + (c + 1) * 128],
                            s_sinb[hb:hb + 32, (h // 4) * 64:(h // 4) * 64 + 64],
                            start=False, stop=True, tile_position=(hb, 0), skip_group_check=True),
                            after=[cb2])
                        il = em.op("pe", lambda e, h=h, c=c, k8=k8, ib=ib: e.matmul(
                            ib[:, k8 * 64:(k8 + 1) * 64],
                            cunT[0:96, c * 128:(c + 1) * 128],
                            s_unb[0:96, h * 64:(h + 1) * 64],
                            start=False, stop=True, skip_group_check=True),
                            after=[cb3])
                    ty1 = em.op("dve", lambda e, w2=w2, ib=ib, ci=ci: e.tensor_tensor(
                        yint[:].rearrange("p (h q) -> p h q", h=8),
                        ib[:, 0:512].rearrange("p (h q) -> p h q", h=8),
                        pit_tok[:, ci * 32 + w2 * 8: ci * 32 + (w2 + 1) * 8].unsqueeze(2).broadcast_to(
                            (128, 8, 64)), ALU.mult),
                        after=[il, t_pit, t_free.get("yint")])
                    pfree[ib_nm] = ty1
                    ty2 = em.op("dve", lambda e, w2=w2, yb=yb: e.tensor_tensor(
                        ytok[:, w2 * 512:(w2 + 1) * 512], yb[:, 0:512], yint[:], ALU.add),
                        after=[yl, ty1, t_free.get("ytok")])
                    pfree[yb_nm] = ty2
                    t_free["yint"] = ty2
                    t_ysl.append(ty2)
                t_y = t_ysl[-1]
                ta1 = em.dma("pool", name + "_a4", lambda e, ci=ci: e.dma_start(
                    out=a8[:], in_=arow[0:1, ci * 32:(ci + 1) * 32]),
                    after=[t_ar, t_free.get("a4"), t_free.get("a8")])
                ta1b = em.op("pe", lambda e: e.transpose(PF[0:4, 0:8], a8[:], idf[0:8, 0:8]),
                             after=[ta1, pfree.get("PF")])
                ta1c = em.op("dve", lambda e: e.tensor_copy(a4[:], PF[0:4, 0:8]),
                             after=[ta1b, t_free.get("a4")])
                pfree["PF"] = ta1c
                t_free["a8"] = ta1b
                mm = em.op("pe", lambda e: e.matmul(
                    PF[:, 0:512], sp4[:],
                    a4[:].rearrange("p q -> p q ()").broadcast_to((4, 8, 64)),
                    start=True, stop=True, skip_group_check=True),
                    after=[ta1c, pfree.get("PF")])
                ta2 = em.op("dve", lambda e: e.tensor_copy(abc[:], PF[:, 0:512]),
                            after=[mm, t_free.get("abc")])
                pfree["PF"] = ta2
                icl = isl = None
                for h in range(H):
                    hb = (h % 4) * 32
                    icl = em.op("pe", lambda e, h=h, hb=hb: e.matmul(
                        PA[hb:hb + 32, (h // 4) * 64:(h // 4) * 64 + 64],
                        brctok[:, h * 32:(h + 1) * 32],
                        us_bf[:, h * 64:(h + 1) * 64],
                        start=True, stop=True, tile_position=(0, hb),
                        skip_group_check=True),
                        after=[bb1, u5, pfree.get("PA") if h == 0 else None])
                    isl = em.op("pe", lambda e, h=h, hb=hb: e.matmul(
                        PB[hb:hb + 32, (h // 4) * 64:(h // 4) * 64 + 64],
                        brstok[:, h * 32:(h + 1) * 32],
                        us_bf[:, h * 64:(h + 1) * 64],
                        start=True, stop=True, tile_position=(0, hb),
                        skip_group_check=True),
                        after=[bb2, u5, pfree.get("PB") if h == 0 else None])
                su1 = em.op("dve", lambda e: e.tensor_mul(s_cos[:], s_cos[:], abc[:]),
                            after=[ta2, cb1, st_tok["cos"]])
                su2 = em.op("dve", lambda e: e.tensor_tensor(s_cos[:], s_cos[:], PA[:, 0:512], ALU.add),
                            after=[su1, icl])
                pfree["PA"] = su2
                st_tok["cos"] = su2
                su3 = em.op("dve", lambda e: e.tensor_mul(s_sin[:], s_sin[:], abc[:]),
                            after=[ta2, cb2, st_tok["sin"]])
                su4 = em.op("dve", lambda e: e.tensor_tensor(s_sin[:], s_sin[:], PB[:, 0:512], ALU.add),
                            after=[su3, isl])
                pfree["PB"] = su4
                st_tok["sin"] = su4
                t_free["abc"] = su4
                sun_last = None
                iul = {}
                tq = None
                for half in range(2):
                    for k16 in range(16):
                        h = half * 16 + k16
                        bk = PC if k16 < 8 else PD
                        bnm = "PC" if k16 < 8 else "PD"
                        iul[bnm] = em.op("pe", lambda e, h=h, k16=k16, bk=bk: e.matmul(
                            bk[0:96, (k16 % 8) * 64:((k16 % 8) + 1) * 64],
                            b_tok[:, 32:128],
                            us_bf[:, h * 64:(h + 1) * 64],
                            start=(k16 % 8 == 0), stop=(k16 % 8 == 7), skip_group_check=True),
                            after=[t_b, u5, pfree.get(bnm) if k16 % 8 == 0 else None])
                    for qd in range(2):
                        mmq = em.op("pe", lambda e, qd=qd, ci=ci, half=half: e.matmul(
                            PF[:, 0:512], onesr[0:1, :],
                            arow[0:1, ci * 32 + half * 16 + qd * 8:
                                 ci * 32 + half * 16 + (qd + 1) * 8].rearrange(
                                "p q -> p q ()").broadcast_to((1, 8, 64)),
                            start=True, stop=True, skip_group_check=True),
                            after=[t_ar, t_or, pfree.get("PF")])
                        tq = em.op("dve", lambda e, qd=qd: e.tensor_copy(
                            abunh[:, qd * 512:(qd + 1) * 512], PF[:, 0:512]),
                            after=[mmq, t_free.get("abunh")])
                        pfree["PF"] = tq
                    t_free["abunh"] = None
                    sv1 = em.op("dve", lambda e, half=half: e.tensor_mul(
                        s_un[:, half * 1024:(half + 1) * 1024],
                        s_un[:, half * 1024:(half + 1) * 1024],
                        abunh[0:96, 0:1024]),
                        after=[tq, cb3, st_tok["un"]])
                    t_free["abunh"] = sv1
                    sv2 = em.op("dve", lambda e, half=half: e.tensor_tensor(
                        s_un[:, half * 1024: half * 1024 + 512],
                        s_un[:, half * 1024: half * 1024 + 512],
                        PC[0:96, 0:512], ALU.add), after=[sv1, iul["PC"]])
                    pfree["PC"] = sv2
                    sv3 = em.op("dve", lambda e, half=half: e.tensor_tensor(
                        s_un[:, half * 1024 + 512: (half + 1) * 1024],
                        s_un[:, half * 1024 + 512: (half + 1) * 1024],
                        PD[0:96, 0:512], ALU.add), after=[sv1, iul["PD"]])
                    pfree["PD"] = sv3
                    st_tok["un"] = sv3
                    sun_last = sv3
                g1 = em.op("dve", lambda e: e.tensor_mul(u_f[:], x_bf[:], ddup[:]),
                           after=[t_x, u5, u4])
                g2 = em.op("dve", lambda e: e.tensor_add(ytok[:], ytok[:], u_f[:]), after=[g1, t_y])
                g3 = em.op("act", lambda e: e.activation(u_f[:], z_bf[:], AF.Silu), after=[t_z, g2])
                t_free["z_bf"] = g3
                g4 = em.op("dve", lambda e: e.tensor_mul(outbf[:], ytok[:], u_f[:]),
                           after=[g2, g3, t_free.get("outbf")])
                t_free["ytok"] = g4
                t_free["u_f"] = g4
                t_free["x_bf"] = g1
                t_free["px0"] = u3
                t_free["px1"] = u3
                t_free["u_bf"] = yl
                t_free["us_bf"] = isl
                t_ot = None
                tw = None
                for kc in range(16):
                    tw = em.op("pe", lambda e, kc=kc: e.transpose(
                        ptr[:, 0:128], outbf[:, kc * 128:(kc + 1) * 128], idb[:]),
                        after=[g4, pfree.get("ptr")])
                    t_ot = em.op("dve", lambda e, kc=kc: e.tensor_copy(
                        outT[:, kc * 128:(kc + 1) * 128], ptr[:, 0:128]),
                        after=[tw, t_free.get("outT"), t_ot])
                    pfree["ptr"] = t_ot
                t_free["outbf"] = tw
                base_w = (b * n_c + ci) * 16
                t_res = em.dma("sp", name + "_res", lambda e, tglob=tglob: e.dma_start(
                    out=rest[:], in_=res[tglob:tglob + 128, :]),
                    after=[last_store, t_free.get("rest")])
                ml = {}
                NW2 = min(512, D)
                for kc in range(16):
                    ensure_wos(base_w + kc + 2, NB * n_c * 16)
                    slot, wtok = wos_tok[base_w + kc]
                    for nd in range(max(1, D // 512)):
                        bk = PE1 if nd == 0 else PF
                        ml[nd] = em.op("pe", lambda e, kc=kc, nd=nd, slot=slot, bk=bk, NW2=NW2: e.matmul(
                            bk[:, 0:NW2],
                            outT[:, kc * 128:(kc + 1) * 128],
                            w_os[slot][:, nd * NW2:(nd + 1) * NW2],
                            start=(kc == 0), stop=(kc == 15), skip_group_check=True),
                            after=[wtok, t_ot,
                                   (pfree.get("PE1") if nd == 0 else pfree.get("PF"))
                                   if kc == 0 else None])
                    wos_use[base_w + kc] = ml[max(ml)]
                ev1 = em.op("dve", lambda e, NW2=NW2: e.tensor_tensor(
                    outo[:, 0:NW2], PE1[:, 0:NW2], rest[:, 0:NW2], ALU.add),
                    after=[ml[0], t_res, t_free.get("outo")])
                pfree["PE1"] = ev1
                toks_ev = [ev1]
                if D > 512:
                    ev2 = em.op("dve", lambda e: e.tensor_tensor(
                        outo[:, 512:1024], PF[:, 0:512], rest[:, 512:1024], ALU.add),
                        after=[ml.get(1), t_res, ev1])
                    pfree["PF"] = ev2
                    toks_ev.append(ev2)
                last_store = em.dma("sp", name + "_st", lambda e, tglob=tglob: e.dma_start(
                    out=out[tglob:tglob + 128, :], in_=outo[:]), after=toks_ev)
                t_free["rest"] = last_store
                t_free["outo"] = last_store
                t_free["mt"] = yl
                t_free["ctokstok"] = bb2
                t_free["brctok"] = icl
                t_free["brstok"] = isl
                t_free["b_tok"] = iul["PD"]
                t_free["tlc"] = dl
                t_free["trc"] = dl
                t_free["a4"] = mm
            t_free["fb"] = il
            t_free["r0g"] = tc1
            t_free["sing"] = tw
            t_free["cosg"] = tw
            t_free["brc"] = il
            t_free["brs"] = il
            t_free["crc"] = il
            t_free["crs"] = il
            t_free["bdup"] = p2
            t_free["cdup"] = p4
        em.wait("sp", [last_store, t_xlast])
        t_free["s_cos"] = su2
        t_free["s_sin"] = su4
        t_free["s_un"] = sun_last
        t_free["bg_tok"] = u5
        t_free["pit_tok"] = ty1
        t_free["arow"] = tq
        em.pop_scope()
    em.wait("sp", [last_store])
    return last_store



"""Plain layernorm pass (no affine): x [T, D] f32 -> xn [T, D] bf16, eps=1e-5."""

F32 = mybir.dt.float32
BF16 = mybir.dt.bfloat16
AF = mybir.ActivationFunctionType
ALU = mybir.AluOpType
X = mybir.AxisListType.X


def emit_ln(nc, em, cfg, x_in, xn_out, cst, name="ln"):
    T, D = cfg["T"], cfg["D"]
    EPS_COL = cfg["eps_col"]
    cstk = cfg["cst_tok"]
    sb = em.sb
    em.push_scope()
    xt = [sb(f"{name}_xt{i}", [128, D], F32) for i in range(2)]
    xo = [sb(f"{name}_xo{i}", [128, D], BF16) for i in range(2)]
    sq = sb(name + "_sq", [128, D], F32)
    stat = sb(name + "_stat", [128, 4], F32)
    t_free = {}
    last = None
    for nt in range(T // 128):
        i = nt % 2
        t_ld = em.dma("sp", name + "_ld", lambda e, nt=nt, i=i: e.dma_start(
            out=xt[i][:], in_=x_in[nt * 128:(nt + 1) * 128, :]),
            after=[t_free.get(f"xt{i}")])
        r1 = em.op("dve", lambda e, i=i: e.reduce_sum(stat[:, 0:1], xt[i][:], axis=X),
                   after=[t_ld, t_free.get("stat")])
        r2 = em.op("dve", lambda e: e.tensor_scalar(
            stat[:, 0:1], stat[:, 0:1], 1.0 / D, None, ALU.mult), after=[r1])
        s1 = em.op("dve", lambda e, i=i: e.tensor_scalar(
            xt[i][:], xt[i][:], stat[:, 0:1], None, ALU.subtract), after=[r2])
        a1 = em.op("act", lambda e, i=i: e.activation(sq[:], xt[i][:], AF.Square),
                   after=[s1, t_free.get("sq")])
        r3 = em.op("dve", lambda e: e.reduce_sum(stat[:, 1:2], sq[:], axis=X), after=[a1])
        t_free["sq"] = r3
        r4 = em.op("dve", lambda e: e.tensor_scalar(
            stat[:, 1:2], stat[:, 1:2], 1.0 / D, None, ALU.mult), after=[r3])
        a2 = em.op("act", lambda e: e.activation(
            stat[:, 2:3], stat[:, 1:2], AF.Sqrt, bias=cst[:, EPS_COL:EPS_COL + 1]),
            after=[r4, cstk])
        a3 = em.op("dve", lambda e: e.reciprocal(stat[:, 3:4], stat[:, 2:3]), after=[a2])
        t_free["stat"] = a3
        m1 = em.op("dve", lambda e, i=i: e.tensor_scalar(
            xo[i][:], xt[i][:], stat[:, 3:4], None, ALU.mult),
            after=[a3, t_free.get(f"xo{i}")])
        t_free[f"xt{i}"] = m1
        last = em.dma("sp", name + "_st", lambda e, nt=nt, i=i: e.dma_start(
            out=xn_out[nt * 128:(nt + 1) * 128, :], in_=xo[i][:]), after=[m1])
        t_free[f"xo{i}"] = last
    em.wait("sp", [last])
    em.pop_scope()
    return last


_F32 = mybir.dt.float32
_BF16 = mybir.dt.bfloat16
_NCORES = 8
_TC, _DM = 4096, 1024
_T2 = 2048
_N_LAUNCH = 4


def _np_prep(inputs):
    f32 = np.float32
    bf16 = ml_dtypes.bfloat16
    p = {}
    p["w_inT"] = np.ascontiguousarray(inputs["L0_attn_w_in"].T).astype(f32)
    p["w_outT"] = np.ascontiguousarray(inputs["L0_attn_w_out"].T).astype(f32)
    p["dtb"] = inputs["L0_attn_dt_bias"].reshape(32, 1).astype(f32)
    p["ddup"] = np.tile(np.repeat(inputs["L0_attn_D"].astype(f32), 64)[None, :], (128, 1))
    ii = np.arange(128)
    sel32 = (ii[None, :] % 32 == np.arange(32)[:, None]).astype(f32)
    p["s32f"] = sel32
    p["s32b"] = sel32.astype(bf16)
    selh = np.zeros((32, 8 * 128), f32)
    for j in range(8):
        selh[:, j * 128:(j + 1) * 128] = (ii[None, :] // 32 + 4 * j == np.arange(32)[:, None])
    p["selh"] = selh
    p["selp4"] = (ii[None, :] // 32 == np.arange(4)[:, None]).astype(f32)
    c128 = np.where(np.arange(128)[:, None] <= np.arange(128)[None, :], 0.0, -1e30).astype(f32)
    p["mask4"] = np.tile(c128, (1, 4))
    p["identf"] = np.eye(128, dtype=f32)
    p["identb"] = np.eye(128).astype(bf16)
    cst = np.zeros((128, 8), f32)
    cst[:, 0] = 1.0
    cst[:, 1] = 2 * np.pi
    cst[:, 2] = 1e-5
    cst[:, 3] = 1.0 / np.sqrt(96.0)
    p["cstd"] = cst
    p["wr"] = inputs["L0_ffn_router_kernel_DE"].astype(f32)
    p["wg"] = inputs["L0_ffn_kernel_gating_EDF"].astype(f32)
    p["wu"] = inputs["L0_ffn_kernel_up_EDF"].astype(f32)
    p["wd"] = inputs["L0_ffn_kernel_down_EFD"].astype(f32)
    H, DCH = 16, 4096
    p["wdq"] = inputs["L1_attn_W_DQ"].astype(f32)
    p["wuq"] = inputs["L1_attn_W_UQ"].astype(f32)
    p["wdkv"] = inputs["L1_attn_W_DKV"].astype(f32)
    p["wuk"] = inputs["L1_attn_W_UK"].astype(f32)
    p["wuv"] = inputs["L1_attn_W_UV"].astype(f32)
    wqr_pad = np.zeros((DCH, H * 64), f32)
    wqr_pad.reshape(DCH, H, 64)[:, :, 0:32] = inputs["L1_attn_W_QR"].astype(f32).reshape(DCH, H, 32)
    p["wqr"] = wqr_pad
    p["wkr"] = inputs["L1_attn_W_KR"].astype(f32)
    p["wo"] = inputs["L1_attn_W_O"].astype(f32)
    t = np.arange(_T2, dtype=f32)
    inv = 1.0 / 10000.0 ** (np.arange(0, 32, 2, dtype=f32) / 32)
    freqs = t[:, None] * inv[None, :]
    emb = np.concatenate([freqs, freqs], -1)
    scale = (np.cos(emb) + np.sin(emb)).astype(f32)
    scale_dup = np.zeros((H * 64, _T2), f32)
    scale_dup.reshape(H, 64, _T2)[:, 0:32, :] = scale.T[None, :, :]
    p["scale_dup"] = scale_dup
    QT = 512
    n_tc = QT // 128
    MEXT = QT + (n_tc - 1) * 128
    masks = np.zeros((128, MEXT), f32)
    for i in range(128):
        masks[i, :] = np.where(np.arange(MEXT) - (n_tc - 1) * 128 >= i, 0.0, -1e30)
    p["masks"] = masks
    p["wfcT"] = np.ascontiguousarray(inputs["L1_ffn_W_fc"].astype(f32).T)
    p["wprT"] = np.ascontiguousarray(inputs["L1_ffn_W_proj"].astype(f32).T)
    p["lnw"] = inputs["L1_ffn_ln_w"].astype(f32)
    p["lnb"] = inputs["L1_ffn_ln_b"].astype(f32)
    p["ident"] = np.eye(128, dtype=f32)
    p["sel2"] = np.concatenate(
        [np.stack([np.ones(128, f32), np.zeros(128, f32)]),
         np.stack([np.zeros(128, f32), np.ones(128, f32)])], axis=1)
    return p


def _din(nc, nm, shape, dt):
    return nc.dram_tensor(nm, shape, dt, kind="ExternalInput").ap()


def _dout(nc, nm, shape, dt):
    return nc.dram_tensor(nm, shape, dt, kind="ExternalOutput").ap()


def _dint(nc, nm, shape, dt):
    return nc.dram_tensor(nm, shape, dt, kind="Internal").ap()


def _mk_scr(nc, mk, T):
    return dict(
        proj=mk(nc, "s_proj", [T, 4224], _BF16),
        bT=mk(nc, "s_bT", [128, T], _BF16),
        cT=mk(nc, "s_cT", [128, T], _BF16),
        smT=mk(nc, "s_smT", [128, T], _F32),
        r0d=mk(nc, "s_r0d", [128, 8 * T], _BF16),
        tld=mk(nc, "s_tld", [1, 32 * T], _F32),
        trd=mk(nc, "s_trd", [1, 32 * T], _F32),
    )


def _mamba_dram(nc):
    d = {}
    d["w_inT"] = _din(nc, "w_inT", [_DM, 4480], _F32)
    d["w_outT"] = _din(nc, "w_outT", [2048, _DM], _F32)
    d["dtb"] = _din(nc, "dtb", [32, 1], _F32)
    d["ddup"] = _din(nc, "ddup", [128, 2048], _F32)
    d["s32f"] = _din(nc, "s32f", [32, 128], _F32)
    d["s32b"] = _din(nc, "s32b", [32, 128], _BF16)
    d["selh"] = _din(nc, "selh", [32, 1024], _F32)
    d["selp4"] = _din(nc, "selp4", [4, 128], _F32)
    d["mask4"] = _din(nc, "mask4", [128, 512], _F32)
    d["identf"] = _din(nc, "identf", [128, 128], _F32)
    d["identb"] = _din(nc, "identb", [128, 128], _BF16)
    return d


def _moe_dram(nc):
    d = {}
    d["wr"] = _din(nc, "wr", [_DM, 8], _F32)
    d["wg"] = _din(nc, "wg", [8, _DM, 2048], _F32)
    d["wu"] = _din(nc, "wu", [8, _DM, 2048], _F32)
    d["wd"] = _din(nc, "wd", [8, 2048, _DM], _F32)
    return d


def _attn_dram(nc):
    d = {}
    d["wdq"] = _din(nc, "wdq", [_DM, 4096], _F32)
    d["wuq"] = _din(nc, "wuq", [4096, _DM], _F32)
    d["wdkv"] = _din(nc, "wdkv", [_DM, 256], _F32)
    d["wuk"] = _din(nc, "wuk", [256, _DM], _F32)
    d["wuv"] = _din(nc, "wuv", [256, _DM], _F32)
    d["wqr"] = _din(nc, "wqr", [4096, 16 * 64], _F32)
    d["wkr"] = _din(nc, "wkr", [_DM, 32], _F32)
    d["wo"] = _din(nc, "wo", [_DM, _DM], _F32)
    d["scale_dup"] = _din(nc, "scale_dup", [16 * 64, _T2], _F32)
    d["masks"] = _din(nc, "masks", [128, 896], _F32)
    return d


def _ffn_dram(nc):
    d = {}
    d["wfcT"] = _din(nc, "wfcT", [_DM, 4096], _F32)
    d["wprT"] = _din(nc, "wprT", [4096, _DM], _F32)
    d["lnw"] = _din(nc, "lnw", [4096], _F32)
    d["lnb"] = _din(nc, "lnb", [4096], _F32)
    d["ident"] = _din(nc, "ident", [128, 128], _F32)
    d["sel2"] = _din(nc, "sel2", [2, 256], _F32)
    return d


def _emit_cst(nc, em):
    cstd = _din(nc, "cstd", [128, 8], _F32)
    cst = em.sb("cst", [128, 8], _F32)
    t_cst = em.dma("sp", "cstld", lambda e: e.dma_start(out=cst[:], in_=cstd))
    return cst, t_cst


def _emit_g1(nc, em, cst, t_cst, d, x0, xn0, xacc, scr):
    emit_ln(nc, em, dict(T=_TC, D=_DM, eps_col=2, cst_tok=t_cst), x0, xn0, cst, name="ln0")
    cfgm = dict(T=_TC, D=_DM, NB=2, one_col=0, tpi_col=1, cst_tok=t_cst)
    emit_mamba(nc, em, cfgm, xn0, d["w_inT"], d["w_outT"], d["dtb"], d["ddup"],
               d["s32f"], d["s32b"], d["selh"], d["selp4"], d["mask4"],
               d["identf"], d["identb"], x0, xacc, cst, scr)


def _emit_g2(nc, em, d, xn0, xacc, x1):
    cfg = dict(T=_TC, D=_DM, F=2048, E=8, ST=1024)
    emit_moe(nc, em, cfg, xn0, d["wr"], d["wg"], d["wu"], d["wd"], x1, res_in=xacc)


def _emit_g3(nc, em, cst, t_cst, d, x1, xn1, xa):
    emit_ln(nc, em, dict(T=_TC, D=_DM, eps_col=2, cst_tok=t_cst), x1, xn1, cst, name="ln1")
    cfga = dict(T=_TC, D=_DM, NB=2, H=16, DCH=4096, DKV=256, scl_col=3, cst_tok=t_cst)
    emit_attn(nc, em, cfga, xn1, d["wdq"], d["wuq"], d["wdkv"], d["wuk"], d["wuv"],
              d["wqr"], d["wkr"], d["wo"], d["scale_dup"], d["masks"], x1, xa, cst)


def _emit_g4(nc, em, cst, t_cst, d, xn1, xa, outt, wfcq, wprq):
    tq1 = emit_quant(nc, em, d["wfcT"], wfcq, _DM, 4096, "qfc")
    tq2 = emit_quant(nc, em, d["wprT"], wprq, 4096, _DM, "qpr")
    cfgf = dict(T=_TC, D=_DM, Hh=4096, eps_col=2, ident_dram=d["ident"],
                sel2_dram=d["sel2"], cst_tok=t_cst)
    emit_ffn(nc, em, cfgf, xn1, wfcq, wprq, d["lnw"], d["lnb"], xa, outt, cst,
             after_w=[tq1, tq2])


_CACHE = {}


def _build_unfused():
    if "u" in _CACHE:
        return _CACHE["u"]
    progs = []
    nc = bass.Bass("TRN2", target_bir_lowering=False)
    x0 = _din(nc, "x0", [_TC, _DM], _F32)
    xn0 = _dout(nc, "xn0", [_TC, _DM], _BF16)
    xacc = _dout(nc, "xacc", [_TC, _DM], _F32)
    scr = _mk_scr(nc, _dint, _TC)
    d = _mamba_dram(nc)
    em = Em(nc)
    cst, t_cst = _emit_cst(nc, em)
    _emit_g1(nc, em, cst, t_cst, d, x0, xn0, xacc, scr)
    em.finalize()
    progs.append((nc, ["w_inT", "w_outT", "dtb", "ddup", "s32f", "s32b", "selh",
                       "selp4", "mask4", "identf", "identb", "cstd"], ["x0"],
                  ["xn0", "xacc"]))
    nc = bass.Bass("TRN2", target_bir_lowering=False)
    xn0 = _din(nc, "xn0", [_TC, _DM], _BF16)
    xacc = _din(nc, "xacc", [_TC, _DM], _F32)
    x1 = _dout(nc, "x1", [_TC, _DM], _F32)
    d = _moe_dram(nc)
    em = Em(nc)
    _emit_g2(nc, em, d, xn0, xacc, x1)
    em.finalize()
    progs.append((nc, ["wr", "wg", "wu", "wd"], ["xn0", "xacc"], ["x1"]))
    nc = bass.Bass("TRN2", target_bir_lowering=False)
    x1 = _din(nc, "x1", [_TC, _DM], _F32)
    xn1 = _dout(nc, "xn1", [_TC, _DM], _BF16)
    xa = _dout(nc, "xa", [_TC, _DM], _F32)
    d = _attn_dram(nc)
    em = Em(nc)
    cst, t_cst = _emit_cst(nc, em)
    _emit_g3(nc, em, cst, t_cst, d, x1, xn1, xa)
    em.finalize()
    progs.append((nc, ["wdq", "wuq", "wdkv", "wuk", "wuv", "wqr", "wkr", "wo",
                       "scale_dup", "masks", "cstd"], ["x1"], ["xn1", "xa"]))
    nc = bass.Bass("TRN2", target_bir_lowering=False)
    xn1 = _din(nc, "xn1", [_TC, _DM], _BF16)
    xa = _din(nc, "xa", [_TC, _DM], _F32)
    outt = _dout(nc, "outt", [_TC, _DM], _F32)
    wfcq = _dint(nc, "wfcq", [_DM, 4096], _BF16)
    wprq = _dint(nc, "wprq", [4096, _DM], _BF16)
    d = _ffn_dram(nc)
    em = Em(nc)
    cst, t_cst = _emit_cst(nc, em)
    _emit_g4(nc, em, cst, t_cst, d, xn1, xa, outt, wfcq, wprq)
    em.finalize()
    progs.append((nc, ["wfcT", "wprT", "lnw", "lnb", "ident", "sel2", "cstd"],
                  ["xn1", "xa"], ["outt"]))
    _CACHE["u"] = progs
    return progs


def _build_fused():
    if "f" in _CACHE:
        return _CACHE["f"]
    nc = bass.Bass("TRN2", target_bir_lowering=False)
    x0 = _din(nc, "x0", [_TC, _DM], _F32)
    outt = _dout(nc, "outt", [_TC, _DM], _F32)
    xn0 = _dint(nc, "xn0", [_TC, _DM], _BF16)
    xacc = _dint(nc, "xacc", [_TC, _DM], _F32)
    x1 = _dint(nc, "x1", [_TC, _DM], _F32)
    xn1 = _dint(nc, "xn1", [_TC, _DM], _BF16)
    xa = _dint(nc, "xa", [_TC, _DM], _F32)
    wfcq = _dint(nc, "wfcq", [_DM, 4096], _BF16)
    wprq = _dint(nc, "wprq", [4096, _DM], _BF16)
    scr = _mk_scr(nc, _dint, _TC)
    dm = _mamba_dram(nc)
    dmo = _moe_dram(nc)
    da = _attn_dram(nc)
    df = _ffn_dram(nc)
    em = Em(nc)
    cst, t_cst = _emit_cst(nc, em)
    em.push_scope()
    _emit_g1(nc, em, cst, t_cst, dm, x0, xn0, xacc, scr)
    em.pop_scope()
    em.push_scope()
    _emit_g2(nc, em, dmo, xn0, xacc, x1)
    em.pop_scope()
    em.push_scope()
    _emit_g3(nc, em, cst, t_cst, da, x1, xn1, xa)
    em.pop_scope()
    em.push_scope()
    _emit_g4(nc, em, cst, t_cst, df, xn1, xa, outt, wfcq, wprq)
    em.pop_scope()
    em.finalize()
    w_names = ["w_inT", "w_outT", "dtb", "ddup", "s32f", "s32b", "selh", "selp4",
               "mask4", "identf", "identb", "cstd", "wr", "wg", "wu", "wd",
               "wdq", "wuq", "wdkv", "wuk", "wuv", "wqr", "wkr", "wo",
               "scale_dup", "masks", "wfcT", "wprT", "lnw", "lnb", "ident", "sel2"]
    _CACHE["f"] = (nc, w_names)
    return _CACHE["f"]


def kernel(**inputs):
    p = _np_prep(inputs)
    u = inputs["L0_attn_u"].astype(np.float32)
    shards = [np.ascontiguousarray(u[2 * i:2 * i + 2].reshape(_TC, _DM))
              for i in range(_NCORES)]
    cores = list(range(_NCORES))
    if _N_LAUNCH == 1:
        nc, w_names = _build_fused()
        in_maps = [dict({k: p[k] for k in w_names}, x0=shards[i]) for i in cores]
        r = run_bass_kernel_spmd(nc, in_maps, core_ids=cores)
        outs = [r.results[i]["outt"] for i in cores]
    else:
        progs = _build_unfused()
        state = [{"x0": shards[i]} for i in cores]
        for nc, wn, act_in, act_out in progs:
            in_maps = [dict({k: p[k] for k in wn},
                            **{k: state[i][k] for k in act_in}) for i in cores]
            r = run_bass_kernel_spmd(nc, in_maps, core_ids=cores)
            for i in cores:
                for k in act_out:
                    state[i][k] = r.results[i][k]
        outs = [state[i]["outt"] for i in cores]
    return np.concatenate([o.reshape(2, _T2, _DM) for o in outs], axis=0).astype(np.float32)


# version 03
# speedup vs baseline: 1.1661x; 1.1661x over previous
import contextlib
import numpy as np
import ml_dtypes
import concourse.bass as bass
import concourse.mybir as mybir
from concourse.bass_utils import run_bass_kernel_spmd




"""Raw-bass emission framework: per-engine queues, auto semaphore edges.

Tokens are (sem_name, value). Every op declares `after=[tokens...]`; the
framework emits wait_ge on the consuming engine (deduped by monotone
per-engine wait tracking) and puts then_inc on the producing instruction.

Rules encoded here (learned from sim/device probes):
- same-engine RAW/WAW edges need semaphores too (race detector / HW).
- DMA sems: SWDGE (pool) and HWDGE (sync/scalar) must not share a sem.
  A sem shared by several DMAs may only be waited at full cumulative value;
  we give each logical channel its own sem and consumers wait the channel's
  cumulative count at emission time.
- then_inc rides the real producing instruction.
"""

ENGINES = ("pe", "act", "dve", "pool", "sp")
ENG_ATTR = {"pe": "tensor", "act": "scalar", "dve": "vector", "pool": "gpsimd", "sp": "sync"}


class Em:
    def __init__(self, nc):
        self.nc = nc
        self.stack = contextlib.ExitStack()
        self.scopes = []
        self.q = {e: [] for e in ENGINES}
        self.sems = {}
        self.cnt = {}
        self.waited = {e: {} for e in ENGINES}
        self.is_chan = set()
        for e in ENGINES:
            self._mksem("c_" + e)

    def _mksem(self, name):
        self.sems[name] = self.stack.enter_context(self.nc.semaphore(name))
        self.cnt[name] = 0
        return name

    def chan(self, name):
        if name not in self.sems:
            self._mksem(name)
            self.is_chan.add(name)
        return name

    def sb(self, name, shape, dtype):
        st = self.scopes[-1] if self.scopes else self.stack
        return st.enter_context(self.nc.sbuf_tensor(name, shape, dtype))

    def ps(self, name, shape, dtype):
        st = self.scopes[-1] if self.scopes else self.stack
        return st.enter_context(self.nc.psum_tensor(name, shape, dtype))

    def push_scope(self):
        self.scopes.append(contextlib.ExitStack())

    def pop_scope(self):
        self.flush()
        self.scopes.pop().close()

    def flush(self):
        with self.nc.Block() as block:
            for e in ENGINES:
                if not self.q[e]:
                    continue
                def run(engobj, e=e):
                    for f in self.q[e]:
                        f(engobj)
                getattr(block, ENG_ATTR[e])(run)
        self.q = {e: [] for e in ENGINES}

    def pscratch(self):
        if not hasattr(self, "_pscr"):
            import concourse.mybir as mybir
            self._pscr = self.stack.enter_context(
                self.nc.psum_tensor("pscratch", [128, 16], mybir.dt.float32))
        return self._pscr

    def op(self, eng, build, after=(), sem=None, inc=1):
        semname = sem if sem is not None else ("c_" + eng)
        waits = []
        for tok in after:
            if tok is None:
                continue
            psem, pval = tok
            if psem in self.is_chan:
                pval = self.cnt[psem]
            cur = self.waited[eng].get(psem, 0)
            if pval > cur:
                self.waited[eng][psem] = pval
                waits.append((self.sems[psem], pval))
        self.cnt[semname] += inc
        val = self.cnt[semname]
        semh = self.sems[semname]

        def emit(engobj, waits=waits, build=build, semh=semh, inc=inc):
            for s, v in waits:
                engobj.wait_ge(s, v)
            ins = build(engobj)
            ins.then_inc(semh, inc)

        self.q[eng].append(emit)
        return (semname, val)

    def dma(self, eng, chan, build, after=()):
        return self.op(eng, build, after, sem=self.chan(chan), inc=16)

    def wait(self, eng, tokens):
        def emit(engobj, tokens=tuple(t for t in tokens if t is not None)):
            for psem, pval in tokens:
                engobj.wait_ge(self.sems[psem], pval)
        toks = []
        for tok in tokens:
            if tok is None:
                continue
            psem, pval = tok
            if psem in self.is_chan:
                pval = self.cnt[psem]
            cur = self.waited[eng].get(psem, 0)
            if pval > cur:
                self.waited[eng][psem] = pval
                toks.append(tok)

        def emit2(engobj, toks=toks):
            for psem, pval in toks:
                engobj.wait_ge(self.sems[psem], pval)
        self.q[eng].append(emit2)

    def finalize(self):
        self.flush()
        while self.scopes:
            self.scopes.pop().close()
        self.stack.close()



"""Dense MoE pass (L0 ffn): out = res + sum_e w_e * (silu(xn@Wg_e) * (xn@Wu_e)) @ Wd_e

Layouts:
- xn DRAM [T, D] bf16 token-major; read feature-major via HWDGE transpose-DMA.
- Wg, Wu [E, D, F] f32; Wd [E, F, D] f32; Wr [D, E] f32 (cast bf16 during SWDGE DMA)
- res_in (optional) DRAM [T, D] f32; out DRAM [T, D] f32 token-major.
"""

F32 = mybir.dt.float32
BF16 = mybir.dt.bfloat16
AF = mybir.ActivationFunctionType
ALU = mybir.AluOpType
X = mybir.AxisListType.X


def emit_moe(nc, em, cfg, xn, wr, wg, wu, wd, out, res_in=None, name="moe"):
    T, D, F, E, ST = cfg["T"], cfg["D"], cfg["F"], cfg["E"], cfg["ST"]
    n_dc = D // 128
    n_fc = F // 128
    FQ = min(512, F)
    n_fq = F // FQ
    fq_fc = FQ // 128
    n_st = T // ST
    TBW = min(512, ST)
    n_tb = ST // TBW
    n_tc = ST // 128
    WDH = min(512, F)
    n_wdh = F // WDH
    wdh_fc = WDH // 128
    NDW = min(512, D)
    n_nd = D // NDW

    sb, ps = em.sb, em.ps
    xnT = sb(name + "_xnT", [128, n_dc * ST], BF16)
    acc = sb(name + "_acc", [128, n_tc * D], F32)
    fuse = sb(name + "_fuse", [128, n_fc * ST], BF16)
    n_gu_slots = 2 * n_fq + 1
    gu = [sb(f"{name}_gu{i}", [128, n_dc * FQ], BF16) for i in range(n_gu_slots)]
    n_wd_slots = min(n_wdh + 1, 6)
    wdsb = [sb(f"{name}_wd{i}", [128, wdh_fc * D], BF16) for i in range(n_wd_slots)]
    wrb = sb(name + "_wrb", [128, n_dc * E], BF16)
    slog = sb(name + "_slog", [128, n_tc * E], F32)
    sstat = sb(name + "_sstat", [128, 8], F32)
    sse = sb(name + "_sse", [128, 2 * E], F32)
    wful = sb(name + "_wful", [128, n_tc * E], F32)
    sact = [sb(f"{name}_sact{i}", [128, TBW], F32) for i in range(2)]

    pg = ps(name + "_pg", [128, 512], F32)
    pu = ps(name + "_pu", [128, 512], F32)
    pg2 = ps(name + "_pg2", [128, 512], F32)
    pu2 = ps(name + "_pu2", [128, 512], F32)
    pd = ps(name + "_pd", [128, D], F32)
    pd2 = ps(name + "_pd2", [128, D], F32)
    pfree = {}

    t_wr = em.dma("pool", name + "_wr", lambda e: e.dma_start(
        out=wrb[:].rearrange("p (c e) -> p c e", c=n_dc),
        in_=wr.rearrange("(c p) e -> p c e", p=128)))

    def load_gu(slot, e_, kind, fq, after):
        src = (wg if kind == 0 else wu)[e_, :, :].rearrange("(c p) f -> p c f", p=128)
        src = src[:, :, fq * FQ:(fq + 1) * FQ]
        return em.dma("pool", f"{name}_gu{slot}", lambda e: e.dma_start(
            out=gu[slot][:].rearrange("p (c f) -> p c f", c=n_dc), in_=src), after=after)

    def load_wd(slot, e_, h, after):
        src = wd[e_, h * WDH:(h + 1) * WDH, :].rearrange("(c p) d -> p c d", p=128)
        return em.dma("pool", f"{name}_wd{slot}", lambda e: e.dma_start(
            out=wdsb[slot][:].rearrange("p (c d) -> p c d", c=wdh_fc), in_=src), after=after)

    units = []
    wd_units = []
    for st in range(n_st):
        for e_ in range(E):
            for fq in range(n_fq):
                units.append((e_, 0, fq))
                units.append((e_, 1, fq))
            for h in range(n_wdh):
                wd_units.append((e_, h))

    gu_last_use = [None] * n_gu_slots
    wd_last_use = [None] * n_wd_slots
    gu_tok = {}
    wd_tok = {}
    state = {"next_gu": 0, "next_wd": 0}

    def ensure_gu(upto):
        while state["next_gu"] <= min(upto, len(units) - 1):
            i = state["next_gu"]
            slot = i % n_gu_slots
            e_, kind, fq = units[i]
            tok = load_gu(slot, e_, kind, fq, after=[gu_last_use[slot]])
            gu_tok[i] = (slot, tok)
            state["next_gu"] = i + 1

    def ensure_wd(upto):
        while state["next_wd"] <= min(upto, len(wd_units) - 1):
            i = state["next_wd"]
            slot = i % n_wd_slots
            e_, h = wd_units[i]
            tok = load_wd(slot, e_, h, after=[wd_last_use[slot]])
            wd_tok[i] = (slot, tok)
            state["next_wd"] = i + 1

    t_xnT_prev_users = []
    t_acc_store = None
    gu_idx = 0
    wd_idx = 0
    sact_free = [None, None]
    prev_t17 = None

    for st in range(n_st):
        for dc in range(n_dc):
            t_xnT = em.dma("sp", name + "_xnT", lambda e, st=st, dc=dc: e.dma_start(
                out=xnT[:, dc * ST:(dc + 1) * ST],
                in_=xn[st * ST:(st + 1) * ST, dc * 128:(dc + 1) * 128],
                transpose=True), after=t_xnT_prev_users)

        if res_in is not None:
            t_acc0 = em.dma("sp", name + "_accld", lambda e, st=st: e.dma_start(
                out=acc[:].rearrange("p (c d) -> p c d", c=n_tc),
                in_=res_in[st * ST:(st + 1) * ST, :].rearrange("(c p) d -> p c d", p=128)),
                after=[t_acc_store])
        else:
            t_acc0 = em.op("dve", lambda e: e.memset(acc[:], 0.0), after=[t_acc_store])

        rt_first = True
        for tc in range(n_tc):
            for dc in range(n_dc):
                t_rt = em.op("pe", lambda e, tc=tc, dc=dc, first=rt_first: e.matmul(
                    pg[:, tc * E:(tc + 1) * E],
                    xnT[:, dc * ST + tc * 128: dc * ST + (tc + 1) * 128],
                    wrb[:, dc * E:(dc + 1) * E],
                    start=first, stop=(tc == n_tc - 1 and dc == n_dc - 1),
                    skip_group_check=True),
                    after=[t_xnT, t_wr, pfree.get("pg")])
                rt_first = False
        t_slog = em.op("dve", lambda e: e.tensor_copy(slog[:, 0:n_tc * E], pg[:, 0:n_tc * E]),
                       after=[t_rt, prev_t17])
        pfree["pg"] = t_slog

        for tc in range(n_tc):
            sl = slog[:, tc * E:(tc + 1) * E]
            t1 = em.op("dve", lambda e, sl=sl: e.reduce_max(sstat[:, 0:1], sl, axis=X, negate=True),
                       after=[t_slog, prev_t17])
            t2 = em.op("act", lambda e, sl=sl: e.activation(sse[:, 0:E], sl, AF.Exp, bias=sstat[:, 0:1]),
                       after=[t1, prev_t17])
            t3 = em.op("dve", lambda e: e.reduce_sum(sstat[:, 1:2], sse[:, 0:E], axis=X), after=[t2])
            t4 = em.op("dve", lambda e: e.reciprocal(sstat[:, 2:3], sstat[:, 1:2]), after=[t3])
            t5 = em.op("dve", lambda e: e.tensor_scalar(sse[:, E:2 * E], sse[:, 0:E], sstat[:, 2:3], None, ALU.mult), after=[t4])
            t6 = em.op("dve", lambda e: e.reduce_max(sstat[:, 3:4], sse[:, E:2 * E], axis=X), after=[t5])
            t7 = em.op("dve", lambda e: e.tensor_scalar(sse[:, 0:E], sse[:, E:2 * E], sstat[:, 3:4], None, ALU.is_ge), after=[t6])
            t8 = em.op("dve", lambda e: e.tensor_sub(sse[:, 0:E], sse[:, E:2 * E], sse[:, 0:E]), after=[t7])
            t9 = em.op("dve", lambda e: e.reduce_max(sstat[:, 4:5], sse[:, 0:E], axis=X), after=[t8])
            t10 = em.op("dve", lambda e: e.tensor_scalar(sse[:, 0:E], sse[:, E:2 * E], sstat[:, 4:5], None, ALU.is_ge), after=[t9])
            t11 = em.op("act", lambda e: e.activation(sse[:, E:2 * E], sse[:, E:2 * E], AF.Exp), after=[t10])
            t12 = em.op("act", lambda e: e.activation(sstat[:, 5:6], sstat[:, 3:4], AF.Exp), after=[t9])
            t13 = em.op("act", lambda e: e.activation(sstat[:, 6:7], sstat[:, 4:5], AF.Exp), after=[t12])
            t14 = em.op("dve", lambda e: e.tensor_add(sstat[:, 5:6], sstat[:, 5:6], sstat[:, 6:7]), after=[t13])
            t15 = em.op("dve", lambda e: e.reciprocal(sstat[:, 7:8], sstat[:, 5:6]), after=[t14])
            t16 = em.op("dve", lambda e, tc=tc: e.tensor_mul(wful[:, tc * E:(tc + 1) * E], sse[:, E:2 * E], sse[:, 0:E]), after=[t11, t15])
            t17 = em.op("dve", lambda e, tc=tc: e.tensor_scalar(wful[:, tc * E:(tc + 1) * E], wful[:, tc * E:(tc + 1) * E], sstat[:, 7:8], None, ALU.mult), after=[t16])
            prev_t17 = t17
        t_wful = prev_t17

        last_acc = [t_acc0] * n_tc
        prev_down_last = None
        for e_ in range(E):
            ensure_gu(gu_idx + 2 * n_fq)
            ensure_wd(wd_idx + n_wdh)
            fuse_last = None
            for fq in range(n_fq):
                gi, ui = gu_idx, gu_idx + 1
                gu_idx += 2
                gslot, gtok = gu_tok[gi]
                uslot, utok = gu_tok[ui]
                unit_last = None
                for fc in range(fq_fc):
                    fci = fq * fq_fc + fc
                    for tb in range(n_tb):
                        rot = (fci * n_tb + tb) % 2
                        pga, pua = (pg, pu) if rot == 0 else (pg2, pu2)
                        pga_nm = "pg" if rot == 0 else "pg2"
                        pua_nm = "pu" if rot == 0 else "pu2"
                        sa = sact[rot]
                        mm_last = None
                        for dc in range(n_dc):
                            mm_last = em.op("pe", lambda e, gslot=gslot, dc=dc, fc=fc, tb=tb, pga=pga: e.matmul(
                                pga[:, 0:TBW],
                                gu[gslot][:, dc * FQ + fc * 128: dc * FQ + (fc + 1) * 128],
                                xnT[:, dc * ST + tb * TBW: dc * ST + (tb + 1) * TBW],
                                start=(dc == 0), stop=(dc == n_dc - 1)),
                                after=[gtok, t_xnT, pfree.get(pga_nm) if dc == 0 else None])
                        mmg_last = mm_last
                        for dc in range(n_dc):
                            mm_last = em.op("pe", lambda e, uslot=uslot, dc=dc, fc=fc, tb=tb, pua=pua: e.matmul(
                                pua[:, 0:TBW],
                                gu[uslot][:, dc * FQ + fc * 128: dc * FQ + (fc + 1) * 128],
                                xnT[:, dc * ST + tb * TBW: dc * ST + (tb + 1) * TBW],
                                start=(dc == 0), stop=(dc == n_dc - 1)),
                                after=[utok, t_xnT, pfree.get(pua_nm) if dc == 0 else None])
                        ts1 = em.op("act", lambda e, pga=pga, sa=sa: e.activation(sa[:, 0:TBW], pga[:, 0:TBW], AF.Silu),
                                    after=[mmg_last, sact_free[rot]])
                        pfree[pga_nm] = ts1
                        ts2 = em.op("dve", lambda e, pua=pua, sa=sa, fci=fci, tb=tb: e.tensor_mul(
                            fuse[:, fci * ST + tb * TBW: fci * ST + (tb + 1) * TBW], pua[:, 0:TBW], sa[:, 0:TBW]),
                            after=[mm_last, ts1, prev_down_last])
                        pfree[pua_nm] = ts2
                        sact_free[rot] = ts2
                        unit_last = ts2
                        fuse_last = ts2
                gu_last_use[gslot] = unit_last
                gu_last_use[uslot] = unit_last
            t_fuse = fuse_last

            for tc in range(n_tc):
                pda = pd if tc % 2 == 0 else pd2
                pda_nm = "pd" if tc % 2 == 0 else "pd2"
                mmd_last = None
                for h in range(n_wdh):
                    wslot, wtok = wd_tok[wd_idx + h]
                    for fc in range(wdh_fc):
                        fci = h * wdh_fc + fc
                        for nd in range(n_nd):
                            mmd_last = em.op("pe", lambda e, fci=fci, tc=tc, wslot=wslot, fc=fc, nd=nd, pda=pda: e.matmul(
                                pda[:, nd * NDW:(nd + 1) * NDW],
                                fuse[:, fci * ST + tc * 128: fci * ST + (tc + 1) * 128],
                                wdsb[wslot][:, fc * D + nd * NDW: fc * D + (nd + 1) * NDW],
                                start=(fci == 0), stop=(fci == n_fc - 1)),
                                after=[wtok, t_fuse, pfree.get(pda_nm) if (fci == 0 and nd == 0) else None])
                tev = em.op("dve", lambda e, pda=pda, tc=tc, e_=e_: e.scalar_tensor_tensor(
                    acc[:, tc * D:(tc + 1) * D], pda[:, 0:D], wful[:, tc * E + e_: tc * E + e_ + 1],
                    acc[:, tc * D:(tc + 1) * D], ALU.mult, ALU.add),
                    after=[mmd_last, t_wful, last_acc[tc], t_acc0])
                last_acc[tc] = tev
                pfree[pda_nm] = tev
                for h in range(n_wdh):
                    wd_last_use[wd_tok[wd_idx + h][0]] = tev
                prev_down_last = mmd_last
            wd_idx += n_wdh

        t_acc_store = em.dma("sp", name + "_st", lambda e, st=st: e.dma_start(
            out=out[st * ST:(st + 1) * ST, :].rearrange("(c p) d -> p c d", p=128),
            in_=acc[:].rearrange("p (c d) -> p c d", c=n_tc)), after=last_acc)
        t_xnT_prev_users = last_acc[-1:]
    em.wait("sp", [t_acc_store])
    return t_acc_store



"""Bitnet FFN pass (L1 ffn):
  Wq = absmean-ternary(W);  h = gelu_exact(xn @ Wq_fc^T);  hn = LN(h)*ln_w + ln_b
  out = res + hn @ Wq_proj^T
LN folded into proj:  out = res + r*G - (r*m)*c1 + c2
  G = h @ Wpl (Wpl = lnw*Wq_projT), c1 = colsum(Wpl), c2 = lnb^T @ Wq_projT... note:
  hn@WqP^T = ((h-m)*r*lnw + lnb)@WqP^T = r*(h@Wpl) - r*m*colsum(Wpl) + lnb@WqP^T
  c2 uses the UNSCALED Wq_projT -> needs lnb^T@Wq_projT: we compute c2 from Wpl with
  lnb/lnw? lnw may be ~0 -> instead quantize stores BOTH: here we take c2 via a second
  lhsT column using lnb_over row trick: store Wq_projT unscaled, scale rows of G?!
  Simpler: quantizer writes Wpl (lnw-scaled) AND we get c2 = (lnb/lnw)ak... NO.
  => quantizer writes UNSCALED wprq; lnw folded into G's *rhs* here is impossible
  (lnw is per h = per contraction row) -> fold lnw into h instead? lnw per hh row =
  per-PARTITION in hT! hn_part = h*lnw is a per-partition scale on hbf... but stats
  need raw h. We apply lnw as per-partition scale DURING the gelu evac? stats use
  raw h though. Final choice: hbf stores gelu(h) raw; G-matmul lhsT = hbf, rhs = wprq
  (unscaled); fold lnw into... lnw multiplies rows of Wq_projT = contraction rows =
  lhsT-partition rows of rhs operand... rhs[k=hh, d]: scale per k row: pre-scale rhs
  tiles ONCE per load: wpl[slot] = wprq_tile * lnw_col (1 DVE per loaded tile, cheap).
  c1 = colsum(wpl) = ones^T@wpl, c2 = lnb_over_lnw... NO: c2 = lnb^T @ wprq (raw).
  Both c1, c2 via the [ones|lnb] K-column trick on raw wprq needs lnw-scaled for c1:
  c1 = lnw^T-weighted colsum of wprq = (lnw)^T @ wprq. So c1 = lnw^T@wprq, c2 =
  lnb^T@wprq -> one K=2-ish matmul per chunk with lhsT = [lnw_col | lnb_col]!
"""

F32 = mybir.dt.float32
BF16 = mybir.dt.bfloat16
AF = mybir.ActivationFunctionType
ALU = mybir.AluOpType
X = mybir.AxisListType.X


def emit_quant(nc, em, src, dst, R, C, name):
    sb, ps = em.sb, em.ps
    n_rc = R // 128
    CT = min(C, 1024)
    n_ct = C // CT
    bufs = [sb(name + "_w0", [128, CT], F32), sb(name + "_w1", [128, CT], F32)]
    accv = sb(name + "_accv", [128, 1], F32)
    red = sb(name + "_red", [128, 2], F32)
    ones = sb(name + "_ones", [128, 1], F32)
    onesrow = sb(name + "_onesrow", [1, 128], F32)
    gcol = sb(name + "_gcol", [128, 2], F32)
    qts = [sb(name + "_qt0", [128, CT], BF16), sb(name + "_qt1", [128, CT], BF16)]
    qneg = sb(name + "_qneg", [128, CT], F32)
    pq = em.pscratch()

    t_ones = em.op("dve", lambda e: e.memset(ones[:], 1.0))
    t_or = em.op("dve", lambda e: e.memset(onesrow[:], 1.0))
    t_acc = em.op("dve", lambda e: e.memset(accv[:], 0.0))
    tiles = [(rc, ct) for rc in range(n_rc) for ct in range(n_ct)]
    buf_free = [None, None]
    for i, (rc, ct) in enumerate(tiles):
        b = i % 2
        t_ld = em.dma("sp", f"{name}_ld{b}", lambda e, rc=rc, ct=ct, b=b: e.dma_start(
            out=bufs[b][:], in_=src[rc * 128:(rc + 1) * 128, ct * CT:(ct + 1) * CT]),
            after=[buf_free[b]])
        t_r = em.op("dve", lambda e, b=b: e.reduce_sum(red[:, 0:1], bufs[b][:], axis=X,
                                                       apply_absolute_value=True),
                    after=[t_ld, t_acc])
        t_acc = em.op("dve", lambda e: e.tensor_add(accv[:], accv[:], red[:, 0:1]), after=[t_r])
        buf_free[b] = t_acc
    t_tot = em.op("pe", lambda e: e.matmul(pq[0:1, 0:1], accv[:], ones[:], start=True, stop=True),
                  after=[t_acc, t_ones])
    t_g = em.op("dve", lambda e: e.tensor_scalar(red[0:1, 0:1], pq[0:1, 0:1], 0.5 / (R * C), 0.5e-6,
                                                 ALU.mult, ALU.add), after=[t_tot])
    t_g2 = em.op("dve", lambda e: e.tensor_scalar(red[0:1, 1:2], red[0:1, 0:1], -1.0, None, ALU.mult),
                 after=[t_g])
    t_gb = em.op("pe", lambda e: e.matmul(pq[:, 2:4], onesrow[0:1, :], red[0:1, 0:2], start=True, stop=True),
                 after=[t_g2, t_or])
    t_gc = em.op("dve", lambda e: e.tensor_copy(gcol[:], pq[:, 2:4]), after=[t_gb])

    last_st = None
    qfree = [None, None]
    for i, (rc, ct) in enumerate(tiles):
        b = i % 2
        t_ld2 = em.dma("sp", f"{name}_ld{b}", lambda e, rc=rc, ct=ct, b=b: e.dma_start(
            out=bufs[b][:], in_=src[rc * 128:(rc + 1) * 128, ct * CT:(ct + 1) * CT]),
            after=[buf_free[b]])
        tq1 = em.op("dve", lambda e, b=b: e.tensor_scalar(qneg[:], bufs[b][:], gcol[:, 1:2], None, ALU.is_lt),
                    after=[t_ld2, t_gc, qfree[b]])
        tq2 = em.op("dve", lambda e, b=b: e.tensor_scalar(qts[b][:], bufs[b][:], gcol[:, 0:1], None, ALU.is_gt),
                    after=[tq1])
        buf_free[b] = tq2
        tq3 = em.op("dve", lambda e, b=b: e.tensor_sub(qts[b][:], qts[b][:], qneg[:]), after=[tq2])
        last_st = em.dma("sp", name + "_qst", lambda e, rc=rc, ct=ct, b=b: e.dma_start(
            out=dst[rc * 128:(rc + 1) * 128, ct * CT:(ct + 1) * CT], in_=qts[b][:]), after=[tq3])
        qfree[b] = last_st
    em.wait("sp", [last_st])
    return last_st


def emit_ffn(nc, em, cfg, xn, wfcq, wprq, lnw, lnb, res, out, cst, after_w=(), name="ffn"):
    T, D, Hh = cfg["T"], cfg["D"], cfg["Hh"]
    EPS_COL = cfg["eps_col"]
    n_dc = D // 128
    n_hc = Hh // 128
    TT = min(512, T)
    n_tt = T // TT
    n_tc = TT // 128
    NDW = min(512, D)
    n_nd = D // NDW

    sb, ps = em.sb, em.ps
    xnT = sb(name + "_xnT", [128, n_dc * TT], BF16)
    hbf = sb(name + "_hbf", [128, n_hc * TT], BF16)
    hsq = sb(name + "_hsq", [128, TT], F32)
    wfc = [sb(f"{name}_wfc{i}", [128, n_dc * 128], BF16) for i in range(3)]
    wprf = sb(name + "_wprf", [128, n_hc * D], BF16)
    lnwc = sb(name + "_lnwc", [128, n_hc], F32)
    lnbc = sb(name + "_lnbc", [128, n_hc], F32)
    onescol = sb(name + "_onescol", [128, 1], F32)
    onescb = sb(name + "_onescb", [128, 1], BF16)
    onesrow = sb(name + "_onesrow", [1, 128], F32)
    ident = sb(name + "_ident", [128, 128], F32)
    c12 = sb(name + "_c12", [128, 2 * D], F32)
    c12r = sb(name + "_c12r", [2, D], F32)
    lcol = sb(name + "_lcol", [128, 2], BF16)
    statr = sb(name + "_statr", [1, 2 * TT], F32)
    statc = sb(name + "_statc", [128, 8], F32)
    rcols = sb(name + "_rcols", [128, 2 * n_tc], F32)
    btmp = sb(name + "_btmp", [128, D], F32)
    rest = sb(name + "_rest", [128, n_tc * D], F32)
    btmp2 = sb(name + "_btmp2", [128, D], F32)

    pA = [ps(name + "_pA0", [128, TT], F32), ps(name + "_pA1", [128, TT], F32)]
    pB = [ps(name + "_pB0", [128, D], F32)]
    pS1 = ps(name + "_pS1", [1, 512], F32)
    pS2 = ps(name + "_pS2", [1, 512], F32)
    pT = em.pscratch()
    pfree = {}

    t_ones = em.op("dve", lambda e: e.memset(onescol[:], 1.0))
    t_oneb = em.op("dve", lambda e: e.memset(onescb[:], 1.0))
    t_or = em.op("dve", lambda e: e.memset(onesrow[:], 1.0))
    t_id = em.dma("sp", name + "_id", lambda e: e.dma_start(out=ident[:], in_=cfg["ident_dram"]))
    t_lnw = em.dma("sp", name + "_lnw", lambda e: e.dma_start(
        out=lnwc[:], in_=lnw.rearrange("(c p) -> p c", p=128), allow_slow_non_contiguous=True))
    t_lnb = em.dma("sp", name + "_lnb", lambda e: e.dma_start(
        out=lnbc[:], in_=lnb.rearrange("(c p) -> p c", p=128), allow_slow_non_contiguous=True))

    t_wf = em.dma("pool", name + "_wprf", lambda e: e.dma_start(
        out=wprf[:].rearrange("p (c d) -> p c d", c=n_hc),
        in_=wprq.rearrange("(c p) d -> p c d", p=128)), after=list(after_w))
    t_cmm = None
    for hc in range(n_hc):
        t_l2 = em.op("dve", lambda e, hc=hc: e.tensor_copy(lcol[:, 0:1], lnwc[:, hc:hc + 1]),
                     after=[t_lnw, t_cmm])
        t_l3 = em.op("dve", lambda e, hc=hc: e.tensor_copy(lcol[:, 1:2], lnbc[:, hc:hc + 1]),
                     after=[t_lnb, t_l2])
        for nd in range(n_nd):
            t_cmm = em.op("pe", lambda e, nd=nd, hc=hc: e.matmul(
                pB[0][0:2, nd * NDW:(nd + 1) * NDW],
                lcol[:, 0:2], wprf[:, hc * D + nd * NDW: hc * D + (nd + 1) * NDW],
                start=(hc == 0), stop=(hc == n_hc - 1),
                skip_group_check=True),
                after=[t_wf, t_l3])
    t_wpl = None
    for hc in range(n_hc):
        t_wpl = em.op("dve", lambda e, hc=hc: e.tensor_scalar(
            wprf[:, hc * D:(hc + 1) * D], wprf[:, hc * D:(hc + 1) * D],
            lnwc[:, hc:hc + 1], None, ALU.mult), after=[t_wf, t_lnw, t_cmm])
    t_ce = em.op("dve", lambda e: e.tensor_copy(c12r[:], pB[0][0:2, 0:D]), after=[t_cmm])
    sel2 = sb(name + "_sel2", [2, 256], F32)
    t_sel = em.dma("sp", name + "_sel", lambda e: e.dma_start(out=sel2[:], in_=cfg["sel2_dram"]))
    t_cb = None
    for nd in range(n_nd):
        t_cb = em.op("pe", lambda e, nd=nd: e.matmul(
            pB[0][:, nd * NDW:(nd + 1) * NDW],
            sel2[:, 0:128], c12r[0:2, nd * NDW:(nd + 1) * NDW],
            start=True, stop=True, skip_group_check=True),
            after=[t_ce, t_sel])
    t_c1e = em.op("dve", lambda e: e.tensor_copy(c12[:, 0:D], pB[0][:, 0:D]), after=[t_cb])
    t_cb2 = None
    for nd in range(n_nd):
        t_cb2 = em.op("pe", lambda e, nd=nd: e.matmul(
            pB[0][:, nd * NDW:(nd + 1) * NDW],
            sel2[:, 128:256], c12r[0:2, nd * NDW:(nd + 1) * NDW],
            start=True, stop=True, skip_group_check=True),
            after=[t_c1e, t_sel])
    t_c2e = em.op("dve", lambda e: e.tensor_copy(c12[:, D:2 * D], pB[0][:, 0:D]), after=[t_cb2])
    pfree["pB0"] = t_c2e

    wfc_tok = {}
    wfc_use = {}
    state = {"next": 0}

    def ensure_wfc(upto):
        while state["next"] <= min(upto, n_hc * n_tt - 1):
            i = state["next"]
            hc = i % n_hc
            slot = i % 3
            prev = wfc_use.get(i - 3)
            tok = em.dma("pool", f"{name}_wfc{slot}", lambda e, hc=hc, slot=slot: e.dma_start(
                out=wfc[slot][:].rearrange("p (c h) -> p c h", c=n_dc),
                in_=wfcq[:, hc * 128:(hc + 1) * 128].rearrange("(c p) h -> p c h", p=128)),
                after=[prev] + list(after_w))
            wfc_tok[i] = (slot, tok)
            state["next"] = i + 1

    t_xnT_prev = []
    t_hbf_free = None
    t_statr_free = None
    last_store = None
    for tt in range(n_tt):
        for dc in range(n_dc):
            t_xnT = em.dma("sp", name + "_xnT", lambda e, tt=tt, dc=dc: e.dma_start(
                out=xnT[:, dc * TT:(dc + 1) * TT],
                in_=xn[tt * TT:(tt + 1) * TT, dc * 128:(dc + 1) * 128],
                transpose=True), after=t_xnT_prev)
        t_res = em.dma("sp", name + "_res", lambda e, tt=tt: e.dma_start(
            out=rest[:].rearrange("p (c d) -> p c d", c=n_tc),
            in_=res[tt * TT:(tt + 1) * TT, :].rearrange("(c p) d -> p c d", p=128)),
            after=[last_store])

        tm1 = tm2 = None
        for hc in range(n_hc):
            ensure_wfc(tt * n_hc + hc + 2)
            pha = pA[hc % 2]
            pha_nm = f"pA{hc % 2}"
            slot, wtok = wfc_tok[tt * n_hc + hc]
            mm = None
            for dc in range(n_dc):
                mm = em.op("pe", lambda e, hc=hc, dc=dc, slot=slot, pha=pha: e.matmul(
                    pha[:, 0:TT],
                    wfc[slot][:, dc * 128:(dc + 1) * 128],
                    xnT[:, dc * TT:(dc + 1) * TT],
                    start=(dc == 0), stop=(dc == n_dc - 1)),
                    after=[wtok, t_xnT, pfree.get(pha_nm) if dc == 0 else None])
            wfc_use[tt * n_hc + hc] = mm
            tg = em.op("act", lambda e, hc=hc, pha=pha: e.activation(
                hbf[:, hc * TT:(hc + 1) * TT], pha[:, 0:TT], AF.Gelu),
                after=[mm, t_hbf_free])
            pfree[pha_nm] = tg
            tsq = em.op("dve", lambda e, hc=hc: e.tensor_mul(
                hsq[:, 0:TT], hbf[:, hc * TT:(hc + 1) * TT],
                hbf[:, hc * TT:(hc + 1) * TT]),
                after=[tg, pfree.get("hsq")])
            tm1 = em.op("pe", lambda e, hc=hc: e.matmul(
                pS1[0:1, 0:TT], onescb[:], hbf[:, hc * TT:(hc + 1) * TT],
                start=(hc == 0), stop=(hc == n_hc - 1), skip_group_check=True),
                after=[tg, t_oneb, pfree.get("pS") if hc == 0 else None])
            tm2 = em.op("pe", lambda e, hc=hc: e.matmul(
                pS2[0:1, 0:TT], onescol[:], hsq[:, 0:TT],
                start=(hc == 0), stop=(hc == n_hc - 1), skip_group_check=True),
                after=[tsq, t_ones])
            pfree["hsq"] = tm2
        t_st0 = em.op("dve", lambda e: e.tensor_copy(statr[:, 0:TT], pS1[0:1, 0:TT]),
                      after=[tm1, t_statr_free])
        t_st = em.op("dve", lambda e: e.tensor_copy(statr[:, TT:2 * TT], pS2[0:1, 0:TT]),
                     after=[tm2, t_st0])
        pfree["pS"] = t_st
        t_rc = []
        for tc in range(n_tc):
            ttr0 = em.op("pe", lambda e, tc=tc: e.transpose(
                pT[:, 0:1], statr[0:1, tc * 128:(tc + 1) * 128], ident[0:1, 0:1]),
                after=[t_st, pfree.get("pT"), t_id])
            ttr = em.op("pe", lambda e, tc=tc: e.transpose(
                pT[:, 1:2], statr[0:1, TT + tc * 128: TT + (tc + 1) * 128], ident[0:1, 0:1]),
                after=[ttr0])
            a1 = em.op("dve", lambda e: e.tensor_scalar(statc[:, 0:2], pT[:, 0:2], 1.0 / Hh, None, ALU.mult),
                       after=[ttr] + (t_rc[-1:] if t_rc else []))
            pfree["pT"] = a1
            a2 = em.op("dve", lambda e: e.tensor_mul(statc[:, 2:3], statc[:, 0:1], statc[:, 0:1]),
                       after=[a1])
            a3 = em.op("dve", lambda e: e.tensor_sub(statc[:, 3:4], statc[:, 1:2], statc[:, 2:3]),
                       after=[a2])
            a4 = em.op("act", lambda e: e.activation(statc[:, 4:5], statc[:, 3:4], AF.Sqrt,
                                                     bias=cst[:, EPS_COL:EPS_COL + 1]),
                       after=[a3, cfg.get("cst_tok")])
            a5 = em.op("dve", lambda e, tc=tc: e.reciprocal(rcols[:, 2 * tc:2 * tc + 1], statc[:, 4:5]),
                       after=[a4])
            a6 = em.op("dve", lambda e, tc=tc: e.tensor_scalar(
                rcols[:, 2 * tc + 1:2 * tc + 2], statc[:, 0:1], rcols[:, 2 * tc:2 * tc + 1], -1.0,
                ALU.mult, ALU.mult), after=[a5])
            t_rc.append(a6)
        ev_toks = []
        for tc in range(n_tc):
            pb = pB[0]
            pb_nm = "pB0"
            mmp = None
            for hc in range(n_hc):
                for nd in range(n_nd):
                    mmp = em.op("pe", lambda e, hc=hc, tc=tc, nd=nd, pb=pb: e.matmul(
                        pb[:, nd * NDW:(nd + 1) * NDW],
                        hbf[:, hc * TT + tc * 128: hc * TT + (tc + 1) * 128],
                        wprf[:, hc * D + nd * NDW: hc * D + (nd + 1) * NDW],
                        start=(hc == 0), stop=(hc == n_hc - 1)),
                        after=[t_wpl, tg, pfree.get(pb_nm) if (hc == 0 and nd == 0) else None])
            tb1 = em.op("dve", lambda e, tc=tc: e.scalar_tensor_tensor(
                btmp[:], c12[:, 0:D], rcols[:, 2 * tc + 1:2 * tc + 2], c12[:, D:2 * D],
                ALU.mult, ALU.add), after=[t_rc[tc], t_c1e, t_c2e] + ([ev_toks[-1]] if ev_toks else []))
            tb2 = em.op("dve", lambda e, tc=tc, pb=pb: e.scalar_tensor_tensor(
                btmp2[:], pb[:, 0:D], rcols[:, 2 * tc:2 * tc + 1], btmp[:],
                ALU.mult, ALU.add), after=[mmp, tb1, last_store])
            pfree[pb_nm] = tb2
            tb3 = em.op("dve", lambda e, tc=tc: e.tensor_add(
                rest[:, tc * D:(tc + 1) * D], rest[:, tc * D:(tc + 1) * D], btmp2[:]),
                after=[tb2, t_res])
            ev_toks.append(tb3)
        last_store = em.dma("sp", name + "_st", lambda e, tt=tt: e.dma_start(
            out=out[tt * TT:(tt + 1) * TT, :].rearrange("(c p) d -> p c d", p=128),
            in_=rest[:].rearrange("p (c d) -> p c d", c=n_tc)), after=ev_toks)
        t_xnT_prev = [tm1]
        t_hbf_free = ev_toks[-1]
        t_statr_free = t_rc[-1]
    em.wait("sp", [last_store])
    return last_store



"""MLA-style attention pass (L1 attn), causal, per-sequence.

q = [c_Q@W_UQ | (c_Q@W_QR_pad)*scale]; k = [c_KV@W_UK | (xn@W_KR)*scale bcast];
v = c_KV@W_UV;  att = softmax(q.k/sqrt(96)) causal;  out = res + (att@v)@W_O.

Normalizer factored: P = exp(St*scl [+mask]), l = colsum(P), y = (V^T@P)*(1/l).
Host supplies:
  wqr_pad [DCH, H*64] f32: W_QR with each head's 32 cols zero-padded to 64.
  scale_dup [H*64, T2] f32: row (h*64+j) = scaleT[j, t] for j<32 else 0.
  masks [4, 128, 512] f32 additive causal masks for partial tiles.
Structure per sequence b: for ti: projections(ti) -> per-head attention(qt=ti)
over k-tiles 0..ti -> output tile. K/V persist per b; q transient per ti.
cst: consts sbuf col SCL_COL holds 1/sqrt(96).
"""

F32 = mybir.dt.float32
BF16 = mybir.dt.bfloat16
AF = mybir.ActivationFunctionType
ALU = mybir.AluOpType

SCL = 1.0 / (96.0 ** 0.5)


def emit_attn(nc, em, cfg, xn, wdq, wuq, wdkv, wuk, wuv, wqr_pad, wkr, wo,
              scale_dup, masks, res, out, cst, name="att"):
    T, D, NB, H = cfg["T"], cfg["D"], cfg["NB"], cfg["H"]
    T2 = T // NB
    DCH = cfg["DCH"]
    DKV = cfg["DKV"]
    SCL_COL = cfg["scl_col"]
    n_dc = D // 128
    n_qc = DCH // 128
    n_kv = DKV // 128
    n_hc = H * 64 // 128
    QT = min(512, T2)
    n_qt = T2 // QT
    n_tc = QT // 128
    PQ = QT
    n_pq = QT // PQ
    n_tcp = PQ // 128
    NDW = min(512, D)
    n_nd = D // NDW

    sb, ps = em.sb, em.ps
    xnT = sb(name + "_xnT", [128, n_dc * PQ], BF16)
    cqT = sb(name + "_cqT", [128, n_qc * PQ], BF16)
    ckvT = sb(name + "_ckvT", [128, n_kv * PQ], BF16)
    qCt = sb(name + "_qCt", [128, n_hc * QT], BF16)
    qRt = sb(name + "_qRt", [128, n_hc * QT], BF16)
    kC = sb(name + "_kC", [128, n_hc * T2], BF16)
    kR = sb(name + "_kR", [128, T2], BF16)
    vT = sb(name + "_vT", [128, (T2 // 128) * D], BF16)
    yTt = sb(name + "_yTt", [128, n_hc * QT], BF16)
    sclt = sb(name + "_sclt", [128, n_hc * QT], BF16)
    sclk = sb(name + "_sclk", [64, T2], BF16)
    MEXT = QT + (n_tc - 1) * 128
    msk = sb(name + "_msk", [128, MEXT], F32)
    sexp = sb(name + "_sexp", [128, QT], F32)
    pbf = [sb(f"{name}_pbf{i}", [128, QT], BF16) for i in range(2)]
    srow = sb(name + "_srow", [1, 2 * QT], F32)
    grow = sb(name + "_grow", [1, QT], BF16)
    gbuf = sb(name + "_gb", [128, QT], F32)
    onescb = sb(name + "_onescb", [128, 1], BF16)
    onesrb = sb(name + "_onesrb", [1, 128], BF16)
    outt = sb(name + "_outt", [128, D], F32)
    wu_dq = [sb(f"{name}_wdq{i}", [128, n_dc * 128], BF16) for i in range(3)]
    wu_uqr = [sb(f"{name}_wuqr{i}", [128, n_qc * 128], BF16) for i in range(2)]
    w_dkv = sb(name + "_wdkv", [128, n_dc * DKV], BF16)
    w_uk = sb(name + "_wuk", [128, n_kv * D], BF16)
    w_uv = sb(name + "_wuv", [128, n_kv * D], BF16)
    w_kr = sb(name + "_wkr", [128, n_dc * 32], BF16)
    w_o = [sb(f"{name}_wo{i}", [128, D], BF16) for i in range(3)]

    pS = [ps(name + "_pS0", [128, 512], F32), ps(name + "_pS1", [128, 512], F32)]
    pP0 = ps(name + "_pP0", [128, 512], F32)
    pL = ps(name + "_pL", [1, 512], F32)
    pY = ps(name + "_pY", [128, 512], F32)
    pO = ps(name + "_pO", [128, D], F32)
    pfree = {}
    t_free = {}

    t_ones = em.op("dve", lambda e: e.memset(onescb[:], 1.0))
    t_onr = em.op("dve", lambda e: e.memset(onesrb[:], 1.0))
    t_sclk = em.dma("pool", name + "_sclk", lambda e: e.dma_start(
        out=sclk[:], in_=scale_dup[0:64, :]))
    t_msk = em.dma("sp", name + "_msk", lambda e: e.dma_start(out=msk[:], in_=masks))
    t_wdkv = em.dma("pool", name + "_wdkv", lambda e: e.dma_start(
        out=w_dkv[:].rearrange("p (c n) -> p c n", c=n_dc),
        in_=wdkv.rearrange("(c p) n -> p c n", p=128)))
    t_wuk = em.dma("pool", name + "_wuk", lambda e: e.dma_start(
        out=w_uk[:].rearrange("p (c n) -> p c n", c=n_kv),
        in_=wuk.rearrange("(c p) n -> p c n", p=128)))
    t_wuv = em.dma("pool", name + "_wuv", lambda e: e.dma_start(
        out=w_uv[:].rearrange("p (c n) -> p c n", c=n_kv),
        in_=wuv.rearrange("(c p) n -> p c n", p=128)))
    t_wkr = em.dma("pool", name + "_wkr", lambda e: e.dma_start(
        out=w_kr[:].rearrange("p (c n) -> p c n", c=n_dc),
        in_=wkr.rearrange("(c p) n -> p c n", p=128)))
    wo_use = {}

    def mk_stream(wdram, n_k, n_oc, slots, tag):
        toks, uses, st = {}, {}, {"next": 0}

        def ensure(upto, total):
            while st["next"] <= min(upto, total - 1):
                i = st["next"]
                oc = i % n_oc
                slot = i % len(slots)
                prev = uses.get(i - len(slots))
                tok = em.dma("pool", f"{name}_{tag}{slot}", lambda e, oc=oc, slot=slot: e.dma_start(
                    out=slots[slot][:].rearrange("p (c n) -> p c n", c=n_k),
                    in_=wdram[:, oc * 128:(oc + 1) * 128].rearrange("(c p) n -> p c n", p=128)),
                    after=[prev])
                toks[i] = (slot, tok)
                st["next"] = i + 1
        return toks, uses, ensure

    dq_toks, dq_uses, dq_ensure = mk_stream(wdq, n_dc, n_qc, wu_dq, "sdq")

    last_store = None
    for b in range(NB):
        t0 = b * T2
        for ti in range(n_qt):
            trng = (t0 + ti * QT, t0 + (ti + 1) * QT)
            uqr_tok = {}
            for sub in range(n_pq):
                ts0 = trng[0] + sub * PQ
                for dc in range(n_dc):
                    t_xnT = em.dma("sp", name + "_xnT", lambda e, ts0=ts0, dc=dc: e.dma_start(
                        out=xnT[:, dc * PQ:(dc + 1) * PQ],
                        in_=xn[ts0:ts0 + PQ, dc * 128:(dc + 1) * 128],
                        transpose=True), after=[t_free.get("xnT")])
                if sub == 0:
                    t_sclt = em.dma("pool", name + "_sclt", lambda e, ti=ti: e.dma_start(
                        out=sclt[:].rearrange("p (c t) -> p c t", c=n_hc),
                        in_=scale_dup[:, ti * QT:(ti + 1) * QT].rearrange("(c p) t -> p c t", p=128)),
                        after=[t_free.get("sclt")])
                base_q = ((b * n_qt + ti) * n_pq + sub) * n_qc
                t_cq = None
                for oc in range(n_qc):
                    dq_ensure(base_q + oc + 2, NB * n_qt * n_pq * n_qc)
                    slot, wtok = dq_toks[base_q + oc]
                    mm = None
                    for dc in range(n_dc):
                        mm = em.op("pe", lambda e, oc=oc, dc=dc, slot=slot: e.matmul(
                            pS[oc % 2][:, 0:PQ],
                            wu_dq[slot][:, dc * 128:(dc + 1) * 128],
                            xnT[:, dc * PQ:(dc + 1) * PQ],
                            start=(dc == 0), stop=(dc == n_dc - 1)),
                            after=[wtok, t_xnT, pfree.get(f"pS{oc % 2}") if dc == 0 else None])
                    dq_uses[base_q + oc] = mm
                    t_cq = em.op("dve", lambda e, oc=oc: e.tensor_copy(
                        cqT[:, oc * PQ:(oc + 1) * PQ], pS[oc % 2][:, 0:PQ]),
                        after=[mm, t_free.get("cqT")])
                    pfree[f"pS{oc % 2}"] = t_cq
                for fam in range(2):
                    wsrc = wuq if fam == 0 else wqr_pad
                    for oc in range(n_hc):
                        ui = fam * n_hc + oc
                        slot = ui % 2
                        prev = uqr_tok.get(("use", ui - 3))
                        if ("ld", ui) not in uqr_tok or sub == 0:
                            pass
                        if sub == 0:
                            wtok = em.dma("pool", f"{name}_suqr{slot}", lambda e, oc=oc, slot=slot, wsrc=wsrc: e.dma_start(
                                out=wu_uqr[slot][:].rearrange("p (c n) -> p c n", c=n_qc),
                                in_=wsrc[:, oc * 128:(oc + 1) * 128].rearrange("(c p) n -> p c n", p=128)),
                                after=[uqr_tok.get(("use", ui - 2)), t_free.get(f"uqr{slot}")])
                            uqr_tok[("ld", ui)] = wtok
                        else:
                            wtok = uqr_tok[("ld", ui)]
                        mm = None
                        for kc in range(n_qc):
                            mm = em.op("pe", lambda e, oc=oc, kc=kc, slot=slot: e.matmul(
                                pS[oc % 2][:, 0:PQ],
                                wu_uqr[slot][:, kc * 128:(kc + 1) * 128],
                                cqT[:, kc * PQ:(kc + 1) * PQ],
                                start=(kc == 0), stop=(kc == n_qc - 1)),
                                after=[wtok, t_cq, pfree.get(f"pS{oc % 2}") if kc == 0 else None])
                        uqr_tok[("use", ui)] = mm
                        if fam == 0:
                            tev = em.op("dve", lambda e, oc=oc, sub=sub: e.tensor_copy(
                                qCt[:, oc * QT + sub * PQ: oc * QT + sub * PQ + PQ],
                                pS[oc % 2][:, 0:PQ]),
                                after=[mm, t_free.get("qCt")])
                        else:
                            tev = em.op("dve", lambda e, oc=oc, sub=sub: e.tensor_tensor(
                                qRt[:, oc * QT + sub * PQ: oc * QT + sub * PQ + PQ],
                                pS[oc % 2][:, 0:PQ],
                                sclt[:, oc * QT + sub * PQ: oc * QT + sub * PQ + PQ], ALU.mult),
                                after=[mm, t_sclt, t_free.get("qRt")])
                        pfree[f"pS{oc % 2}"] = tev
                t_ckv = None
                for oc in range(n_kv):
                    mm = None
                    for dc in range(n_dc):
                        mm = em.op("pe", lambda e, oc=oc, dc=dc: e.matmul(
                            pP0[:, 0:PQ],
                            w_dkv[:, dc * DKV + oc * 128: dc * DKV + (oc + 1) * 128],
                            xnT[:, dc * PQ:(dc + 1) * PQ],
                            start=(dc == 0), stop=(dc == n_dc - 1)),
                            after=[t_wdkv, t_xnT, pfree.get("pP0") if dc == 0 else None])
                    t_ckv = em.op("dve", lambda e, oc=oc: e.tensor_copy(
                        ckvT[:, oc * PQ:(oc + 1) * PQ], pP0[:, 0:PQ]),
                        after=[mm, t_free.get("ckvT")])
                    pfree["pP0"] = t_ckv
                for oc in range(n_hc):
                    mm = None
                    for kc in range(n_kv):
                        mm = em.op("pe", lambda e, oc=oc, kc=kc: e.matmul(
                            pS[oc % 2][:, 0:PQ],
                            w_uk[:, kc * D + oc * 128: kc * D + (oc + 1) * 128],
                            ckvT[:, kc * PQ:(kc + 1) * PQ],
                            start=(kc == 0), stop=(kc == n_kv - 1)),
                            after=[t_wuk, t_ckv, pfree.get(f"pS{oc % 2}") if kc == 0 else None])
                    tev = em.op("dve", lambda e, oc=oc, ti=ti, sub=sub: e.tensor_copy(
                        kC[:, oc * T2 + ti * QT + sub * PQ: oc * T2 + ti * QT + sub * PQ + PQ],
                        pS[oc % 2][:, 0:PQ]),
                        after=[mm, t_free.get("kC")])
                    pfree[f"pS{oc % 2}"] = tev
                mm = None
                for dc in range(n_dc):
                    mm = em.op("pe", lambda e, dc=dc: e.matmul(
                        pP0[0:32, 0:PQ],
                        w_kr[:, dc * 32:(dc + 1) * 32],
                        xnT[:, dc * PQ:(dc + 1) * PQ],
                        start=(dc == 0), stop=(dc == n_dc - 1)),
                        after=[t_wkr, t_xnT, pfree.get("pP0")])
                kslice = slice(ti * QT + sub * PQ, ti * QT + sub * PQ + PQ)
                tz = em.op("dve", lambda e, kslice=kslice: e.memset(kR[:, kslice], 0.0),
                           after=[t_free.get("kR")])
                t_kr0 = em.op("dve", lambda e, kslice=kslice: e.tensor_tensor(
                    kR[0:32, kslice], pP0[0:32, 0:PQ], sclk[0:32, kslice], ALU.mult),
                    after=[mm, tz, t_sclk])
                t_kr = em.op("dve", lambda e, kslice=kslice: e.tensor_tensor(
                    kR[64:96, kslice], pP0[0:32, 0:PQ], sclk[0:32, kslice], ALU.mult),
                    after=[t_kr0])
                pfree["pP0"] = t_kr
                t_v = None
                for tc in range(n_tcp):
                    mm = None
                    for kc in range(n_kv):
                        for nd in range(n_nd):
                            mm = em.op("pe", lambda e, tc=tc, kc=kc, nd=nd: e.matmul(
                                pO[:, nd * NDW:(nd + 1) * NDW],
                                ckvT[:, kc * PQ + tc * 128: kc * PQ + (tc + 1) * 128],
                                w_uv[:, kc * D + nd * NDW: kc * D + (nd + 1) * NDW],
                                start=(kc == 0), stop=(kc == n_kv - 1)),
                                after=[t_wuv, t_ckv, pfree.get("pO") if (kc == 0 and nd == 0) else None])
                    vchunk = ti * n_tc + sub * n_tcp + tc
                    t_v = em.op("dve", lambda e, vchunk=vchunk: e.tensor_copy(
                        vT[:, vchunk * D:(vchunk + 1) * D], pO[:, 0:D]),
                        after=[mm, t_free.get("vT")])
                    pfree["pO"] = t_v
                t_free["xnT"] = t_v
                t_free["cqT"] = uqr_tok[("use", 2 * n_hc - 1)]
                t_free["ckvT"] = t_v
            t_proj = t_v


            n_kc = n_tc * (ti + 1)
            y_last = None
            for h in range(H):
                hq = (h * 64) // 128
                hb = (h * 64) % 128
                t_pl = t_py = None
                for kc in range(n_kc):
                    rot = kc % 2
                    pSa = pS[rot]
                    m1 = em.op("pe", lambda e, kc=kc, ti=ti, hq=hq, hb=hb, pSa=pSa: e.matmul(
                        pSa[:, 0:QT],
                        kC[hb:hb + 64, hq * T2 + kc * 128: hq * T2 + (kc + 1) * 128],
                        qCt[hb:hb + 64, hq * QT: hq * QT + QT],
                        start=True, stop=False),
                        after=[t_proj, pfree.get(f"pS{rot}")])
                    m2 = em.op("pe", lambda e, kc=kc, hq=hq, hb=hb, pSa=pSa: e.matmul(
                        pSa[:, 0:QT],
                        kR[hb:hb + 64, kc * 128:(kc + 1) * 128],
                        qRt[hb:hb + 64, hq * QT: hq * QT + QT],
                        start=False, stop=True), after=[])
                    if kc >= n_tc * ti:
                        p_ = kc - n_tc * ti
                        tm = em.op("dve", lambda e, p_=p_, pSa=pSa: e.scalar_tensor_tensor(
                            sexp[:, 0:QT], pSa[:, 0:QT], SCL, msk[:, (n_tc - 1 - p_) * 128:(n_tc - 1 - p_) * 128 + QT],
                            ALU.mult, ALU.add), after=[m2, t_msk, pfree.get("sexp")])
                        te = em.op("act", lambda e, rot=rot: e.activation(
                            pbf[rot][:, 0:QT], sexp[:, 0:QT], AF.Exp),
                            after=[tm, t_free.get(f"pbf{rot}")])
                        pfree["sexp"] = te
                    else:
                        te = em.op("act", lambda e, rot=rot, pSa=pSa: e.activation(
                            pbf[rot][:, 0:QT], pSa[:, 0:QT], AF.Exp,
                            scale=cst[:, SCL_COL:SCL_COL + 1]),
                            after=[m2, t_free.get(f"pbf{rot}"), cfg.get("cst_tok")])
                    pfree[f"pS{rot}"] = te
                    t_pl = em.op("pe", lambda e, kc=kc, rot=rot, n_kc=n_kc: e.matmul(
                        pL[0:1, 0:QT], onescb[:], pbf[rot][:, 0:QT],
                        start=(kc == 0), stop=(kc == n_kc - 1), skip_group_check=True),
                        after=[te, t_ones, pfree.get("pL") if kc == 0 else None])
                    t_py = em.op("pe", lambda e, kc=kc, h=h, rot=rot, n_kc=n_kc: e.matmul(
                        pY[0:64, 0:QT], vT[:, kc * D + h * 64: kc * D + (h + 1) * 64],
                        pbf[rot][:, 0:QT],
                        start=(kc == 0), stop=(kc == n_kc - 1), skip_group_check=True),
                        after=[te, pfree.get("pY") if kc == 0 else None])
                    t_free[f"pbf{rot}"] = t_py
                f1 = em.op("dve", lambda e: e.tensor_copy(srow[0:1, 0:QT], pL[0:1, 0:QT]),
                           after=[t_pl, pfree.get("srow")])
                pfree["pL"] = f1
                f2 = em.op("dve", lambda e: e.reciprocal(srow[0:1, QT:2 * QT], srow[0:1, 0:QT]),
                           after=[f1])
                f3 = em.op("dve", lambda e: e.tensor_copy(grow[0:1, 0:QT], srow[0:1, QT:2 * QT]),
                           after=[f2, pfree.get("grow")])
                pfree["srow"] = f3
                f4 = em.op("pe", lambda e: e.matmul(
                    pP0[:, 0:QT], onesrb[0:1, :], grow[0:1, 0:QT], start=True, stop=True),
                    after=[f3, t_onr, pfree.get("pP0")])
                pfree["grow"] = f4
                f5 = em.op("dve", lambda e: e.tensor_copy(gbuf[:, 0:QT], pP0[:, 0:QT]),
                           after=[f4, pfree.get("gbuf")])
                pfree["pP0"] = f5
                f6 = em.op("dve", lambda e, h=h, hq=hq, hb=hb: e.tensor_tensor(
                    yTt[hb:hb + 64, hq * QT: hq * QT + QT],
                    pY[0:64, 0:QT], gbuf[0:64, 0:QT], ALU.mult),
                    after=[t_py, f5, t_free.get("yTt")])
                pfree["pY"] = f6
                pfree["gbuf"] = f6
                y_last = f6
            t_res = em.dma("sp", name + "_res", lambda e, trng=trng: e.dma_start(
                out=outt[:, 0:D],
                in_=res[trng[0]:trng[0] + 128, :]), after=[last_store, pfree.get("outt")])
            o_toks = []
            wo_toks = {}
            for kc in range(n_hc):
                slot = kc % 3
                wo_toks[kc] = em.dma("pool", f"{name}_wo{slot}", lambda e, kc=kc, slot=slot: e.dma_start(
                    out=w_o[slot][:], in_=wo[kc * 128:(kc + 1) * 128, :]),
                    after=[wo_use.get((b, ti, kc - 3) if False else None), wo_use.get(kc - 3)])
            for tc in range(n_tc):
                if tc > 0:
                    t_res = em.dma("sp", name + "_res", lambda e, trng=trng, tc=tc: e.dma_start(
                        out=outt[:, 0:D], in_=res[trng[0] + tc * 128: trng[0] + (tc + 1) * 128, :]),
                        after=[last_store, pfree.get("outt")])
                mm = None
                for kc in range(n_hc):
                    for nd in range(n_nd):
                        mm = em.op("pe", lambda e, kc=kc, nd=nd, tc=tc: e.matmul(
                            pO[:, nd * NDW:(nd + 1) * NDW],
                            yTt[:, kc * QT + tc * 128: kc * QT + (tc + 1) * 128],
                            w_o[kc % 3][:, nd * NDW:(nd + 1) * NDW],
                            start=(kc == 0), stop=(kc == n_hc - 1)),
                            after=[wo_toks[kc], y_last, pfree.get("pO") if (kc == 0 and nd == 0) else None])
                        if tc == n_tc - 1:
                            wo_use[kc] = mm
                tev = em.op("dve", lambda e, tc=tc: e.tensor_add(
                    outt[:], pO[:, 0:D], outt[:]),
                    after=[mm, t_res])
                pfree["pO"] = tev
                o_toks.append(tev)
                last_store = em.dma("sp", name + "_st", lambda e, trng=trng, tc=tc: e.dma_start(
                    out=out[trng[0] + tc * 128: trng[0] + (tc + 1) * 128, :], in_=outt[:]),
                    after=[tev])
                pfree["outt"] = last_store
            t_free["xnT"] = t_proj
            t_free["qCt"] = y_last
            t_free["qRt"] = y_last
            for sl in range(2):
                t_free[f"uqr{sl}"] = uqr_tok[("use", 2 * n_hc - 1)]
            t_free["ckvT"] = t_proj
            t_free["sclt"] = y_last
            t_free["yTt"] = o_toks[-1]
        t_free["kC"] = y_last
        t_free["kR"] = y_last
        t_free["vT"] = y_last
    em.wait("sp", [last_store])
    return last_store



"""Mamba-3 mixer pass (L0 attn branch), chunked SSD formulation, v2.

Two sub-passes (separate Blocks via em scopes):
  A) in_proj streamer: xn @ w_in^T -> DRAM proj [T, 4224] bf16 (z|x|B token-major)
     plus feature-major B^T/C^T -> bT/cT [128, T] bf16 and smallT -> smT [128, T] f32.
  B) per sequence: pass1 (small-family, lambda, theta->r0) then pass2 (chunk scan).

scr dict (internal DRAM): proj, bT, cT, smT, r0d [128, 8*T] bf16,
tld/trd [1, 32*T] f32 (per-b offset b*32*T2).
"""

F32 = mybir.dt.float32
BF16 = mybir.dt.bfloat16
I32 = mybir.dt.int32
AF = mybir.ActivationFunctionType
ALU = mybir.AluOpType

ZC, XC, BRC, CRC, DTC, TRC = 0, 2048, 4096, 4224, 4352, 4416
INV2PI = float(1.0 / (2.0 * 3.14159265358979323846))
PI = 3.14159265358979323846


def emit_mamba(nc, em, cfg, xn, w_inT, w_outT, dtb_col, d_dup, sel32f, sel32b, selh,
               selp4, mask4, identf, identb, res, out, cst, scr, name="mb"):
    T, D, NB = cfg["T"], cfg["D"], cfg["NB"]
    T2 = T // NB
    H, DI = 32, 2048
    ONE_COL, TPI_COL = cfg["one_col"], cfg["tpi_col"]
    n_dc = D // 128
    Q = 128
    n_c = T2 // Q
    GQ = min(2, n_c)
    n_g = n_c // GQ
    GW = GQ * Q
    cstk = cfg["cst_tok"]
    proj, bTd, cTd, smTd, r0d, tld, trd = (
        scr["proj"], scr["bT"], scr["cT"], scr["smT"], scr["r0d"], scr["tld"], scr["trd"])

    sb, ps = em.sb, em.ps
    ddup = sb(name + "_ddup", [128, DI], F32)
    s32f = sb(name + "_s32f", [32, 128], F32)
    s32b = sb(name + "_s32b", [32, 128], BF16)
    shh = sb(name + "_shh", [32, 8 * 128], F32)
    sp4 = sb(name + "_sp4", [4, 128], F32)
    m4 = sb(name + "_m4", [128, 512], F32)
    idf = sb(name + "_idf", [128, 128], F32)
    idb = sb(name + "_idb", [128, 128], BF16)
    dtb = sb(name + "_dtb", [32, 1], F32)
    onesr = sb(name + "_onesr", [1, 128], F32)
    bg_tok = sb(name + "_bgtok", [128, 16 * 96], F32)
    pit_tok = sb(name + "_pittok", [128, 16 * 32], F32)
    arow = sb(name + "_arow", [1, 16 * 32], F32)
    a4 = sb(name + "_a4", [4, 8], F32)
    a8 = sb(name + "_a8", [8, 4], F32)
    s_cos = sb(name + "_scos", [128, 512], F32)
    s_sin = sb(name + "_ssin", [128, 512], F32)
    s_un = sb(name + "_sun", [96, 2048], F32)
    s_cosb = sb(name + "_scosb", [128, 512], BF16)
    s_sinb = sb(name + "_ssinb", [128, 512], BF16)
    s_unb = sb(name + "_sunb", [96, 2048], BF16)
    w_os = [sb(f"{name}_wos{i}", [128, 1024], BF16) for i in range(3)]

    PA = ps(name + "_PA", [128, 512], F32)
    PB = ps(name + "_PB", [128, 512], F32)
    PC = ps(name + "_PC", [128, 512], F32)
    PD = ps(name + "_PD", [128, 512], F32)
    PE1 = ps(name + "_PE1", [128, 512], F32)
    PF = ps(name + "_PF", [128, 512], F32)
    ptr = ps(name + "_ptr", [128, 128], BF16)
    pfree = {}
    t_free = {}

    em.dma("sp", name + "_cd", lambda e: e.dma_start(out=ddup[:], in_=d_dup))
    em.dma("sp", name + "_cd", lambda e: e.dma_start(out=s32f[:], in_=sel32f))
    em.dma("sp", name + "_cd", lambda e: e.dma_start(out=s32b[:], in_=sel32b))
    em.dma("sp", name + "_cd", lambda e: e.dma_start(out=shh[:], in_=selh))
    em.dma("sp", name + "_cd", lambda e: e.dma_start(out=sp4[:], in_=selp4))
    em.dma("sp", name + "_cd", lambda e: e.dma_start(out=m4[:], in_=mask4))
    em.dma("sp", name + "_cd", lambda e: e.dma_start(out=idf[:], in_=identf))
    em.dma("sp", name + "_cd", lambda e: e.dma_start(out=idb[:], in_=identb))
    t_cd = em.dma("sp", name + "_cd", lambda e: e.dma_start(out=dtb[:], in_=dtb_col))
    t_or = em.op("dve", lambda e: e.memset(onesr[:], 1.0))

    em.push_scope()
    xnT = sb(name + "_xnT", [128, n_dc * 512], BF16)
    wu = [sb(f"{name}_wu{i}", [128, n_dc * 512], BF16) for i in range(3)]
    pbuf = sb(name + "_pbuf", [128, 4 * 4224], BF16)
    fbuf = sb(name + "_fbuf", [128, 2 * 512], BF16)
    fbuf2 = sb(name + "_fbuf2", [128, 512], F32)
    units = [(ZC + 512 * k, 512, "tok") for k in range(4)] + \
            [(XC + 512 * k, 512, "tok") for k in range(4)] + \
            [(BRC, 128, "both"), (CRC, 128, "feat"), (DTC, 128, "feat")]
    n_ttA = T // 512
    wu_tok = {}
    wu_use = {}
    wu_state = {"next": 0}

    def ensure_wu(upto):
        while wu_state["next"] <= min(upto, n_ttA * len(units) - 1):
            i = wu_state["next"]
            ui = i % len(units)
            co, nw, kind = units[ui]
            slot = i % 3
            prev = wu_use.get(i - 3)
            tok = em.dma("pool", f"{name}_wu{slot}", lambda e, co=co, nw=nw, slot=slot: e.dma_start(
                out=wu[slot][:, 0:n_dc * nw].rearrange("p (c n) -> p c n", c=n_dc),
                in_=w_inT[:, co:co + nw].rearrange("(c p) n -> p c n", p=128)),
                after=[prev])
            wu_tok[i] = (slot, tok)
            wu_state["next"] = i + 1

    t_pst = None
    t_fst = None
    for tt in range(n_ttA):
        for dc in range(n_dc):
            t_xn = em.dma("sp", name + "_xnT", lambda e, tt=tt, dc=dc: e.dma_start(
                out=xnT[:, dc * 512:(dc + 1) * 512],
                in_=xn[tt * 512:(tt + 1) * 512, dc * 128:(dc + 1) * 128],
                transpose=True), after=[t_free.get("xnT")])
        ev_toks = []
        for ui, (co, nw, kind) in enumerate(units):
            gi = tt * len(units) + ui
            ensure_wu(gi + 2)
            slot, wtok = wu_tok[gi]
            if kind in ("tok", "both"):
                for tc in range(4):
                    pband = PA if tc % 2 == 0 else PB
                    pnm = "PA" if tc % 2 == 0 else "PB"
                    mm = None
                    for dc in range(n_dc):
                        mm = em.op("pe", lambda e, nw=nw, dc=dc, tc=tc, slot=slot, pband=pband: e.matmul(
                            pband[:, 0:nw],
                            xnT[:, dc * 512 + tc * 128: dc * 512 + (tc + 1) * 128],
                            wu[slot][:, dc * nw:(dc + 1) * nw],
                            start=(dc == 0), stop=(dc == n_dc - 1)),
                            after=[wtok, t_xn, pfree.get(pnm) if dc == 0 else None])
                    tev = em.op("dve", lambda e, co=co, nw=nw, tc=tc, pband=pband: e.tensor_copy(
                        pbuf[:, tc * 4224 + co: tc * 4224 + co + nw], pband[:, 0:nw]),
                        after=[mm, t_pst])
                    pfree[pnm] = tev
                    ev_toks.append(tev)
            if kind in ("feat", "both"):
                mm = None
                for dc in range(n_dc):
                    mm = em.op("pe", lambda e, dc=dc, slot=slot: e.matmul(
                        PC[:, 0:512],
                        wu[slot][:, dc * 128:(dc + 1) * 128],
                        xnT[:, dc * 512:(dc + 1) * 512],
                        start=(dc == 0), stop=(dc == n_dc - 1)),
                        after=[wtok, t_xn, pfree.get("PC") if dc == 0 else None])
                if co == DTC:
                    tev = em.op("dve", lambda e: e.tensor_copy(fbuf2[:], PC[:, 0:512]),
                                after=[mm, t_fst])
                else:
                    fo = 0 if co == BRC else 512
                    tev = em.op("dve", lambda e, fo=fo: e.tensor_copy(
                        fbuf[:, fo:fo + 512], PC[:, 0:512]), after=[mm, t_fst])
                pfree["PC"] = tev
                ev_toks.append(tev)
            wu_use[gi] = ev_toks[-1]
        for tc in range(4):
            t_pst = em.dma("sp", name + "_pst", lambda e, tt=tt, tc=tc: e.dma_start(
                out=proj[tt * 512 + tc * 128: tt * 512 + (tc + 1) * 128, :],
                in_=pbuf[:, tc * 4224:(tc + 1) * 4224]), after=ev_toks)
        em.dma("sp", name + "_fst", lambda e, tt=tt: e.dma_start(
            out=bTd[:, tt * 512:(tt + 1) * 512], in_=fbuf[:, 0:512]), after=ev_toks)
        em.dma("sp", name + "_fst", lambda e, tt=tt: e.dma_start(
            out=cTd[:, tt * 512:(tt + 1) * 512], in_=fbuf[:, 512:1024]), after=ev_toks)
        t_fst = em.dma("sp", name + "_fst", lambda e, tt=tt: e.dma_start(
            out=smTd[:, tt * 512:(tt + 1) * 512], in_=fbuf2[:]), after=ev_toks)
        t_free["xnT"] = ev_toks[-1]
    em.wait("sp", [t_pst, t_fst])
    em.pop_scope()

    wos_tok = {}
    wos_use = {}
    wos_state = {"next": 0}

    def ensure_wos(upto, total):
        while wos_state["next"] <= min(upto, total - 1):
            i = wos_state["next"]
            kc = i % 16
            slot = i % 3
            prev = wos_use.get(i - 3)
            tok = em.dma("pool", f"{name}_wos{slot}", lambda e, kc=kc, slot=slot: e.dma_start(
                out=w_os[slot][:, 0:D], in_=w_outT[kc * 128:(kc + 1) * 128, :]), after=[prev])
            wos_tok[i] = (slot, tok)
            wos_state["next"] = i + 1

    last_store = None
    for b in range(NB):
        t0 = b * T2
        em.push_scope()
        smA = sb(name + f"{b}_smA", [64, T2], F32)
        smB = sb(name + f"{b}_smB", [64, T2], F32)
        dtT = sb(name + f"{b}_dtT", [32, T2], F32)
        aT = sb(name + f"{b}_aTT", [32, T2], F32)
        gaT = sb(name + f"{b}_gaT", [32, T2], F32)
        beT = sb(name + f"{b}_beT", [32, T2], F32)
        linc = sb(name + f"{b}_linc", [32, T2], F32)
        alT = sb(name + f"{b}_alT", [32, T2], F32)
        sp1 = sb(name + f"{b}_sp1", [32, T2], F32)
        sp2 = sb(name + f"{b}_sp2", [32, T2], F32)
        tanT = sb(name + f"{b}_tanT", [32, T2], F32)
        lam = sb(name + f"{b}_lam", [32, T2], F32)
        lneg = sb(name + f"{b}_lneg", [32, T2], F32)
        dT = sb(name + f"{b}_dTT", [32, T2], F32)
        piT = sb(name + f"{b}_piT", [32, T2], F32)
        aab = sb(name + f"{b}_aab", [32, 64], F32)
        on128 = sb(name + f"{b}_on128", [128, T2], F32)
        thb = sb(name + f"{b}_thb", [128, T2], F32)
        thc = sb(name + f"{b}_thc", [128, T2], F32)
        thi = sb(name + f"{b}_thi", [128, 512], I32)
        p1sb = sb(name + f"{b}_p1sb", [128, T2], F32)
        r0bf = sb(name + f"{b}_r0bf", [128, 512], BF16)

        t_o128 = em.op("dve", lambda e: e.memset(on128[:], 1.0))
        t_om = t_o128
        t_smA = em.dma("sp", name + "_smld", lambda e, t0=t0: e.dma_start(
            out=smA[:], in_=smTd[0:64, t0:t0 + T2]))
        t_smB = em.dma("sp", name + "_smld", lambda e, t0=t0: e.dma_start(
            out=smB[:], in_=smTd[64:128, t0:t0 + T2]))

        def softplus(dst, src_sl, bias_col, toks):
            t1 = em.op("dve", lambda e, src_sl=src_sl: e.tensor_copy(sp1[:], src_sl),
                       after=toks + [t_free.get("sp")])
            if bias_col is not None:
                t1 = em.op("dve", lambda e, bias_col=bias_col: e.tensor_scalar(
                    sp1[:], sp1[:], bias_col, None, ALU.add), after=[t1, t_cd])
            t2 = em.op("dve", lambda e: e.tensor_scalar(sp2[:], sp1[:], -1.0, None, ALU.mult), after=[t1])
            t3 = em.op("dve", lambda e: e.tensor_tensor(sp2[:], sp1[:], sp2[:], ALU.min), after=[t2])
            t4 = em.op("act", lambda e: e.activation(sp2[:], sp2[:], AF.Exp), after=[t3])
            t5 = em.op("act", lambda e: e.activation(sp2[:], sp2[:], AF.Ln,
                                                     bias=cst[0:32, ONE_COL:ONE_COL + 1]),
                       after=[t4, cstk])
            t6 = em.op("dve", lambda e: e.tensor_scalar(sp1[:], sp1[:], 0.0, None, ALU.max), after=[t5])
            t7 = em.op("dve", lambda e, dst=dst: e.tensor_add(dst, sp1[:], sp2[:]), after=[t6])
            t_free["sp"] = t7
            return t7

        t_dt = softplus(dtT[:], smA[0:32, :], dtb[:, 0:1], [t_smA])
        t_sa = softplus(aT[:], smA[32:64, :], None, [t_smA])
        t_a = em.op("dve", lambda e: e.tensor_scalar(aT[:], aT[:], -1.0, -1e-4, ALU.mult, ALU.min),
                    after=[t_sa])
        HALF_COL = cfg["half_col"]
        t_tr = em.op("act", lambda e: e.activation(
            sp2[:], smB[0:32, :], AF.Tanh, scale=cst[0:32, HALF_COL:HALF_COL + 1]),
            after=[t_smB, t_free.get("sp"), cstk])
        t_tr = em.op("dve", lambda e: e.tensor_scalar(
            sp2[:], sp2[:], 0.5, 0.5, ALU.mult, ALU.add), after=[t_tr])
        t_th = em.op("act", lambda e: e.activation(tanT[:], smB[32:64, :], AF.Tanh), after=[t_smB])
        t_li = em.op("dve", lambda e: e.tensor_mul(linc[:], aT[:], dtT[:]), after=[t_a, t_dt])
        t_al = em.op("act", lambda e: e.activation(alT[:], linc[:], AF.Exp), after=[t_li])
        t_ga = em.op("dve", lambda e: e.tensor_mul(gaT[:], sp2[:], dtT[:]), after=[t_tr, t_dt])
        t_free["sp"] = t_ga
        t_be = em.op("dve", lambda e: e.tensor_sub(beT[:], dtT[:], gaT[:]), after=[t_ga])
        t_be = em.op("dve", lambda e: e.tensor_mul(beT[:], beT[:], alT[:]), after=[t_be, t_al])
        t_l1 = em.op("dve", lambda e: e.tensor_tensor_scan(lam[:], on128[0:32, :], linc[:], 0.0,
                                                           ALU.mult, ALU.add),
                     after=[t_li, t_om])
        t_l2 = em.op("dve", lambda e: e.memset(sp1[:, 0:128], 0.0), after=[t_dt, t_sa])
        t_l3 = em.op("dve", lambda e: e.tensor_copy(
            sp1[:, 128:T2].rearrange("p (c q) -> p c q", q=128),
            lam[:, 127:T2 - 1:128].unsqueeze(2).broadcast_to((32, n_c - 1, 128))),
            after=[t_l1, t_l2])
        t_l4 = em.op("dve", lambda e: e.tensor_sub(lam[:], lam[:], sp1[:]), after=[t_l3])
        t_d1 = em.op("dve", lambda e: e.tensor_sub(
            dT[:].rearrange("p (c q) -> p c q", q=128),
            lam[:, 127::128].unsqueeze(2).broadcast_to((32, n_c, 128)),
            lam[:].rearrange("p (c q) -> p c q", q=128)), after=[t_l4])
        t_d2 = em.op("act", lambda e: e.activation(dT[:], dT[:], AF.Exp), after=[t_d1])
        t_p1 = em.op("act", lambda e: e.activation(piT[:], lam[:], AF.Exp), after=[t_l4])
        t_ae = em.op("act", lambda e: e.activation(aab[:, 0:n_c], lam[:, 127::128], AF.Exp),
                     after=[t_l4])
        t_at = em.op("pe", lambda e: e.transpose(PA[0:n_c, 0:32], aab[:, 0:n_c], idf[0:32, 0:32]),
                     after=[t_ae, pfree.get("PA")])
        t_at2 = em.op("dve", lambda e: e.tensor_copy(aab[0:n_c, 32:64], PA[0:n_c, 0:32]), after=[t_at])
        pfree["PA"] = t_at2
        t_ar = em.dma("pool", name + "_arow", lambda e: e.dma_start(
            out=arow[0:1, 0:n_c * 32], in_=aab[0:n_c, 32:64]), after=[t_at2, t_free.get("arow")])
        t_ln = em.op("dve", lambda e: e.tensor_scalar(lneg[:], lam[:], -1.0, None, ALU.mult),
                     after=[t_l4])
        t_t3 = em.dma("pool", name + "_tl", lambda e, b=b: e.dma_start(
            out=tld[0:1, b * H * T2:(b + 1) * H * T2], in_=lneg[:]), after=[t_ln])
        t_t4 = em.dma("pool", name + "_tr", lambda e, b=b: e.dma_start(
            out=trd[0:1, b * H * T2:(b + 1) * H * T2], in_=lam[:]), after=[t_l4])
        t_bg = t_free.get("bg_tok")
        t_pit = t_free.get("pit_tok")
        for c in range(n_c):
            specs = [(beT, t_be, bg_tok, c * 96, "bg"), (gaT, t_ga, bg_tok, c * 96 + 32, "bg"),
                     (dT, t_d2, bg_tok, c * 96 + 64, "bg"), (piT, t_p1, pit_tok, c * 32, "pit")]
            for srcT, stok_, dstT, off, kind in specs:
                wtr = em.op("pe", lambda e, srcT=srcT, c=c: e.transpose(
                    PB[:, 0:32], srcT[:, c * 128:(c + 1) * 128], idf[0:32, 0:32]),
                    after=[stok_, pfree.get("PB")])
                tevv = em.op("dve", lambda e, dstT=dstT, off=off: e.tensor_copy(
                    dstT[:, off:off + 32], PB[:, 0:32]),
                    after=[wtr, (t_bg if kind == "bg" else t_pit)])
                pfree["PB"] = tevv
                if kind == "bg":
                    t_bg = tevv
                else:
                    t_pit = tevv
        t_dp = em.op("dve", lambda e: e.tensor_scalar(alT[:], dtT[:], PI, None, ALU.mult),
                     after=[t_be])
        t_p1s = None
        for tt2 in range(T2 // 512):
            mm = em.op("pe", lambda e, tt2=tt2: e.matmul(
                PC[:, 0:512], s32f[:], tanT[:, tt2 * 512:(tt2 + 1) * 512],
                start=True, stop=True),
                after=[t_th, pfree.get("PC")])
            t_p1s = em.op("dve", lambda e, tt2=tt2: e.tensor_copy(
                p1sb[:, tt2 * 512:(tt2 + 1) * 512], PC[:, 0:512]), after=[mm, t_p1s])
            pfree["PC"] = t_p1s
        t_r0 = None
        t_st_r0 = None
        for j in range(8):
            t_i2 = None
            for tt2 in range(T2 // 512):
                mm = em.op("pe", lambda e, j=j, tt2=tt2: e.matmul(
                    PD[:, 0:512], shh[:, j * 128:(j + 1) * 128], alT[:, tt2 * 512:(tt2 + 1) * 512],
                    start=True, stop=True),
                    after=[t_dp, pfree.get("PD")])
                t_i2 = em.op("dve", lambda e, tt2=tt2: e.tensor_mul(
                    thb[:, tt2 * 512:(tt2 + 1) * 512], PD[:, 0:512],
                    p1sb[:, tt2 * 512:(tt2 + 1) * 512]),
                    after=[mm, t_p1s, t_i2, t_free.get("thb")])
                pfree["PD"] = t_i2
            t_s1 = em.op("dve", lambda e: e.tensor_tensor_scan(
                thc[:], on128[:], thb[:], 0.0, ALU.mult, ALU.add),
                after=[t_i2, t_o128, t_free.get("thc")])
            t_s2 = em.op("dve", lambda e: e.tensor_scalar(thb[:], thc[:], INV2PI, None, ALU.mult),
                         after=[t_s1])
            t_free["thb"] = t_s2
            t_s4 = None
            for tt2 in range(T2 // 512):
                t_s3 = em.op("dve", lambda e, tt2=tt2: e.tensor_copy(
                    thi[:], thb[:, tt2 * 512:(tt2 + 1) * 512]),
                    after=[t_s2, t_free.get("thi")])
                t_s4 = em.op("dve", lambda e, tt2=tt2: e.tensor_copy(
                    thc[:, tt2 * 512:(tt2 + 1) * 512], thi[:]), after=[t_s3])
                t_free["thi"] = t_s4
            t_free["thc"] = t_s4
            t_r5 = em.op("dve", lambda e: e.tensor_sub(thb[:], thb[:], thc[:]), after=[t_s4])
            t_r6 = em.op("dve", lambda e: e.tensor_scalar(thc[:], thb[:], 0.5, None, ALU.is_gt),
                         after=[t_r5])
            t_r7 = em.op("dve", lambda e: e.tensor_sub(thb[:], thb[:], thc[:]), after=[t_r6])
            t_r8 = em.op("dve", lambda e: e.tensor_scalar(thc[:], thb[:], -0.5, None, ALU.is_lt),
                         after=[t_r7])
            t_st_r0 = None
            for tt2 in range(T2 // 512):
                t_r0 = em.op("dve", lambda e, tt2=tt2: e.tensor_add(
                    r0bf[:], thb[:, tt2 * 512:(tt2 + 1) * 512],
                    thc[:, tt2 * 512:(tt2 + 1) * 512]),
                    after=[t_r8, t_free.get("r0bf")])
                t_st_r0 = em.dma("sp", name + "_r0st", lambda e, j=j, b=b, tt2=tt2: e.dma_start(
                    out=r0d[:, (b * 8 + j) * T2 + tt2 * 512:(b * 8 + j) * T2 + (tt2 + 1) * 512],
                    in_=r0bf[:]), after=[t_r0])
                t_free["r0bf"] = t_st_r0
            t_free["thb"] = t_r0
        em.wait("sp", [t_st_r0])
        em.wait("pool", [t_t3, t_t4, t_ar])
        em.pop_scope()

        em.push_scope()
        brT = sb(name + f"{b}_brT", [32, GW], BF16)
        bunT = sb(name + f"{b}_bunT", [96, GW], BF16)
        crT = sb(name + f"{b}_crT", [32, GW], BF16)
        cunT = sb(name + f"{b}_cunT", [96, GW], BF16)
        bdup = sb(name + f"{b}_bdup", [128, GW], BF16)
        cdup = sb(name + f"{b}_cdup", [128, GW], BF16)
        r0g = sb(name + f"{b}_r0g", [128, 8 * GW], BF16)
        cosg = sb(name + f"{b}_cosg", [128, 8 * GW], BF16)
        sing = sb(name + f"{b}_sing", [128, 8 * GW], BF16)
        brc = sb(name + f"{b}_brc", [128, 8 * GW], BF16)
        brs = sb(name + f"{b}_brs", [128, 8 * GW], BF16)
        crc = sb(name + f"{b}_crc", [128, 8 * GW], BF16)
        crs = sb(name + f"{b}_crs", [128, 8 * GW], BF16)
        trsc = sb(name + f"{b}_trsc", [128, GW], F32)
        trsc2 = sb(name + f"{b}_trsc2", [128, GW], F32)
        z_bf = sb(name + f"{b}_zbf", [128, DI], BF16)
        x_bf = sb(name + f"{b}_xbf", [128, DI], BF16)
        px_bf = sb(name + f"{b}_pxbf", [128, DI], BF16)
        xlast = sb(name + f"{b}_xlast", [1, DI], BF16)
        b_tok = sb(name + f"{b}_btok", [128, 128], BF16)
        u_f = sb(name + f"{b}_uf", [128, DI], F32)
        u_bf = sb(name + f"{b}_ubf", [128, DI], BF16)
        us_bf = sb(name + f"{b}_usbf", [128, DI], BF16)
        tlc = sb(name + f"{b}_tlc", [1, H * 128], F32)
        trc = sb(name + f"{b}_trc", [1, H * 128], F32)
        ctok = sb(name + f"{b}_ctok", [128, 1024], BF16)
        stok = sb(name + f"{b}_stok", [128, 1024], BF16)
        brctok = sb(name + f"{b}_brctok", [128, 1024], BF16)
        brstok = sb(name + f"{b}_brstok", [128, 1024], BF16)
        mexp = sb(name + f"{b}_mexp", [128, 512], F32)
        mt = sb(name + f"{b}_mt", [128, H * 128], BF16)
        ytok = sb(name + f"{b}_ytok", [128, DI], F32)
        yint = sb(name + f"{b}_yint", [128, 512], F32)
        abc = sb(name + f"{b}_abc", [128, 512], F32)
        abunh = sb(name + f"{b}_abunh", [128, 1024], F32)
        outbf = sb(name + f"{b}_outbf", [128, DI], BF16)
        outT = sb(name + f"{b}_outT", [128, DI], BF16)
        rest = sb(name + f"{b}_rest", [128, D], F32)
        outo = sb(name + f"{b}_outo", [128, D], F32)

        t_sc0 = em.op("dve", lambda e: e.memset(s_cos[:], 0.0), after=[t_free.get("s_cos")])
        t_ss0 = em.op("dve", lambda e: e.memset(s_sin[:], 0.0), after=[t_free.get("s_sin")])
        t_su0 = em.op("dve", lambda e: e.memset(s_un[:], 0.0), after=[t_free.get("s_un")])
        st_tok = {"cos": t_sc0, "sin": t_ss0, "un": t_su0}
        t_xlast = None

        for g in range(n_g):
            gt0 = t0 + g * GW
            em.dma("sp", name + "_fbld", lambda e, gt0=gt0: e.dma_start(
                out=brT[:], in_=bTd[0:32, gt0:gt0 + GW]), after=[t_free.get("fb")])
            em.dma("sp", name + "_fbld", lambda e, gt0=gt0: e.dma_start(
                out=bunT[:], in_=bTd[32:128, gt0:gt0 + GW]))
            em.dma("sp", name + "_fbld", lambda e, gt0=gt0: e.dma_start(
                out=crT[:], in_=cTd[0:32, gt0:gt0 + GW]))
            t_fb = em.dma("sp", name + "_fbld", lambda e, gt0=gt0: e.dma_start(
                out=cunT[:], in_=cTd[32:128, gt0:gt0 + GW]))
            t_r0g = em.dma("sp", name + "_r0ld", lambda e, g=g, b=b: e.dma_start(
                out=r0g[:].rearrange("p (j t) -> p j t", j=8),
                in_=bass.AP(r0d.tensor, b * 8 * T2 + g * GW, [[8 * T, 128], [T2, 8], [1, GW]])),
                after=[t_free.get("r0g")])
            mm = em.op("pe", lambda e: e.matmul(PB[:, 0:GW], s32b[:], brT[:], start=True, stop=True),
                       after=[t_fb, pfree.get("PB")])
            t_bd = em.op("dve", lambda e: e.tensor_copy(bdup[:], PB[:, 0:GW]),
                         after=[mm, t_free.get("bdup")])
            pfree["PB"] = t_bd
            mm = em.op("pe", lambda e: e.matmul(PB[:, 0:GW], s32b[:], crT[:], start=True, stop=True),
                       after=[t_fb, pfree.get("PB")])
            t_cd2 = em.op("dve", lambda e: e.tensor_copy(cdup[:], PB[:, 0:GW]),
                          after=[mm, t_free.get("cdup")])
            pfree["PB"] = t_cd2
            t_pl = None
            ts1 = tc1 = None
            for j in range(8):
                r0sl = r0g[:, j * GW:(j + 1) * GW]
                ts1 = em.op("act", lambda e, j=j, r0sl=r0sl: e.activation(
                    sing[:, j * GW:(j + 1) * GW], r0sl, AF.Sin, scale=cst[:, TPI_COL:TPI_COL + 1]),
                    after=[t_r0g, cstk, t_free.get("sing")])
                tc1 = em.op("dve", lambda e, r0sl=r0sl: e.tensor_scalar(trsc[:], r0sl, 0.25, None, ALU.add),
                            after=[t_r0g, t_free.get("trsc")])
                tc2 = em.op("dve", lambda e: e.tensor_scalar(trsc2[:], trsc[:], 0.5, None, ALU.is_gt),
                            after=[tc1, t_free.get("trsc2")])
                tc3 = em.op("dve", lambda e: e.tensor_sub(trsc[:], trsc[:], trsc2[:]), after=[tc2])
                t_free["trsc2"] = tc3
                tc4 = em.op("act", lambda e, j=j: e.activation(
                    cosg[:, j * GW:(j + 1) * GW], trsc[:], AF.Sin, scale=cst[:, TPI_COL:TPI_COL + 1]),
                    after=[tc3, cstk, t_free.get("cosg")])
                t_free["trsc"] = tc4
                p1 = em.op("dve", lambda e, j=j: e.tensor_mul(
                    brc[:, j * GW:(j + 1) * GW], bdup[:], cosg[:, j * GW:(j + 1) * GW]),
                    after=[t_bd, tc4, t_free.get("brc")])
                p2 = em.op("dve", lambda e, j=j: e.tensor_mul(
                    brs[:, j * GW:(j + 1) * GW], bdup[:], sing[:, j * GW:(j + 1) * GW]),
                    after=[t_bd, ts1, t_free.get("brs")])
                p3 = em.op("dve", lambda e, j=j: e.tensor_mul(
                    crc[:, j * GW:(j + 1) * GW], cdup[:], cosg[:, j * GW:(j + 1) * GW]),
                    after=[t_cd2, tc4, t_free.get("crc")])
                p4 = em.op("dve", lambda e, j=j: e.tensor_mul(
                    crs[:, j * GW:(j + 1) * GW], cdup[:], sing[:, j * GW:(j + 1) * GW]),
                    after=[t_cd2, ts1, t_free.get("crs")])
                t_pl = p4
            t_planes = t_pl

            for c in range(GQ):
                ci = g * GQ + c
                tglob = t0 + ci * Q
                cb1 = em.op("dve", lambda e: e.tensor_copy(s_cosb[:], s_cos[:]), after=[st_tok["cos"]])
                cb2 = em.op("dve", lambda e: e.tensor_copy(s_sinb[:], s_sin[:]), after=[st_tok["sin"]])
                cb3 = em.op("dve", lambda e: e.tensor_copy(s_unb[:], s_un[:]), after=[st_tok["un"]])
                t_z = em.dma("sp", name + "_zld", lambda e, tglob=tglob: e.dma_start(
                    out=z_bf[:], in_=proj[tglob:tglob + 128, ZC:ZC + DI]), after=[t_free.get("z_bf")])
                t_x = em.dma("sp", name + "_xld", lambda e, tglob=tglob: e.dma_start(
                    out=x_bf[:], in_=proj[tglob:tglob + 128, XC:XC + DI]),
                    after=[t_free.get("x_bf"), t_xlast])
                t_b = em.dma("sp", name + "_bld", lambda e, tglob=tglob: e.dma_start(
                    out=b_tok[:], in_=proj[tglob:tglob + 128, BRC:BRC + 128]),
                    after=[t_free.get("b_tok")])
                if ci == 0:
                    t_px0 = em.op("dve", lambda e: e.memset(px_bf[0:1, :], 0.0),
                                  after=[t_free.get("px0")])
                else:
                    t_px0 = em.dma("sp", name + "_px0", lambda e: e.dma_start(
                        out=px_bf[0:1, :], in_=xlast[0:1, :]), after=[t_xlast, t_free.get("px0")])
                t_px1 = em.dma("sp", name + "_px1", lambda e: e.dma_start(
                    out=px_bf[1:128, :], in_=x_bf[0:127, :]), after=[t_x, t_free.get("px1")])
                t_xlast = em.dma("sp", name + "_xl", lambda e: e.dma_start(
                    out=xlast[0:1, :], in_=x_bf[127:128, :]), after=[t_x, t_px0])
                bg = bg_tok[:, ci * 96:(ci + 1) * 96]
                u1 = em.op("dve", lambda e, bg=bg: e.tensor_tensor(
                    u_f[:].rearrange("p (h q) -> p h q", h=32),
                    x_bf[:].rearrange("p (h q) -> p h q", h=32),
                    bg[:, 32:64].unsqueeze(2).broadcast_to((128, 32, 64)), ALU.mult),
                    after=[t_x, t_bg, t_free.get("u_f")])
                u2 = em.op("dve", lambda e, bg=bg: e.tensor_tensor(
                    u_bf[:].rearrange("p (h q) -> p h q", h=32),
                    px_bf[:].rearrange("p (h q) -> p h q", h=32),
                    bg[:, 0:32].unsqueeze(2).broadcast_to((128, 32, 64)), ALU.mult),
                    after=[t_px0, t_px1, t_bg, t_free.get("u_bf")])
                u3 = em.op("dve", lambda e: e.tensor_add(u_f[:], u_f[:], u_bf[:]), after=[u1, u2])
                u4 = em.op("dve", lambda e: e.tensor_copy(u_bf[:], u_f[:]), after=[u3])
                u5 = em.op("dve", lambda e, bg=bg: e.tensor_tensor(
                    us_bf[:].rearrange("p (h q) -> p h q", h=32),
                    u_f[:].rearrange("p (h q) -> p h q", h=32),
                    bg[:, 64:96].unsqueeze(2).broadcast_to((128, 32, 64)), ALU.mult),
                    after=[u3, t_free.get("us_bf")])
                t_free["u_f"] = u5
                t_tl = em.dma("sp", name + "_tlld", lambda e, b=b, ci=ci: e.dma_start(
                    out=tlc[:].rearrange("p (h q) -> p h q", h=H),
                    in_=bass.AP(tld.tensor, b * H * T2 + ci * 128, [[1, 1], [T2, H], [1, 128]])),
                    after=[t_t3, t_free.get("tlc")])
                t_tr2 = em.dma("sp", name + "_trld", lambda e, b=b, ci=ci: e.dma_start(
                    out=trc[:].rearrange("p (h q) -> p h q", h=H),
                    in_=bass.AP(trd.tensor, b * H * T2 + ci * 128, [[1, 1], [T2, H], [1, 128]])),
                    after=[t_t4, t_free.get("trc")])
                t_ctok = t_stok = None
                for srcpl, dsttok, which in [(cosg, ctok, "c"), (sing, stok, "s")]:
                    for j in range(8):
                        tw0 = em.op("pe", lambda e, j=j, srcpl=srcpl, c=c: e.transpose(
                            ptr[:, 0:128],
                            srcpl[:, j * GW + c * 128: j * GW + (c + 1) * 128], idb[:]),
                            after=[t_planes, pfree.get("ptr")])
                        te = em.op("dve", lambda e, j=j, dsttok=dsttok: e.tensor_copy(
                            dsttok[:, j * 128:(j + 1) * 128], ptr[:, 0:128]),
                            after=[tw0, t_free.get("ctokstok"),
                                   (t_ctok if which == "c" else t_stok)])
                        pfree["ptr"] = te
                        if which == "c":
                            t_ctok = te
                        else:
                            t_stok = te
                bb1 = em.op("dve", lambda e: e.tensor_tensor(
                    brctok[:].rearrange("p (h r) -> p h r", h=32),
                    b_tok[:, 0:32].rearrange("p r -> p () r").broadcast_to((128, 32, 32)),
                    ctok[:].rearrange("p (h r) -> p h r", h=32), ALU.mult),
                    after=[t_b, t_ctok, t_free.get("brctok")])
                bb2 = em.op("dve", lambda e: e.tensor_tensor(
                    brstok[:].rearrange("p (h r) -> p h r", h=32),
                    b_tok[:, 0:32].rearrange("p r -> p () r").broadcast_to((128, 32, 32)),
                    stok[:].rearrange("p (h r) -> p h r", h=32), ALU.mult),
                    after=[t_b, t_stok, t_free.get("brstok")])
                t_mt = None
                for w in range(8):
                    gb_ = PA if w % 2 == 0 else PB
                    gb_nm = "PA" if w % 2 == 0 else "PB"
                    db_ = PC if w % 2 == 0 else PD
                    db_nm = "PC" if w % 2 == 0 else "PD"
                    gl = dl = None
                    for k4 in range(4):
                        h = w * 4 + k4
                        j = h // 4
                        hb = (h % 4) * 32
                        gl = em.op("pe", lambda e, j=j, hb=hb, c=c, gb_=gb_, k4=k4: e.matmul(
                            gb_[:, k4 * 128:(k4 + 1) * 128],
                            brc[hb:hb + 32, j * GW + c * 128: j * GW + (c + 1) * 128],
                            crc[hb:hb + 32, j * GW + c * 128: j * GW + (c + 1) * 128],
                            start=True, stop=False, tile_position=(hb, 0), skip_group_check=True),
                            after=[t_planes, pfree.get(gb_nm) if k4 == 0 else None])
                        gl = em.op("pe", lambda e, j=j, hb=hb, c=c, gb_=gb_, k4=k4: e.matmul(
                            gb_[:, k4 * 128:(k4 + 1) * 128],
                            brs[hb:hb + 32, j * GW + c * 128: j * GW + (c + 1) * 128],
                            crs[hb:hb + 32, j * GW + c * 128: j * GW + (c + 1) * 128],
                            start=False, stop=True, tile_position=(hb, 0), skip_group_check=True),
                            after=[])
                        gl = em.op("pe", lambda e, c=c, gb_=gb_, k4=k4: e.matmul(
                            gb_[:, k4 * 128:(k4 + 1) * 128],
                            bunT[0:96, c * 128:(c + 1) * 128],
                            cunT[0:96, c * 128:(c + 1) * 128],
                            start=False, stop=True, skip_group_check=True),
                            after=[t_fb])
                        dl = em.op("pe", lambda e, h=h, db_=db_, k4=k4: e.matmul(
                            db_[:, k4 * 128:(k4 + 1) * 128],
                            tlc[0:1, h * 128:(h + 1) * 128], onesr[0:1, :],
                            start=True, stop=False, skip_group_check=True),
                            after=[t_tl, t_or, pfree.get(db_nm) if k4 == 0 else None])
                        dl = em.op("pe", lambda e, h=h, db_=db_, k4=k4: e.matmul(
                            db_[:, k4 * 128:(k4 + 1) * 128],
                            onesr[0:1, :], trc[0:1, h * 128:(h + 1) * 128],
                            start=False, stop=True, skip_group_check=True),
                            after=[t_tr2])
                    tm1 = em.op("dve", lambda e, db_=db_: e.tensor_tensor(
                        mexp[:], db_[:, 0:512], m4[:], ALU.add),
                        after=[dl, t_free.get("mexp")])
                    pfree[db_nm] = tm1
                    tm2 = em.op("act", lambda e: e.activation(mexp[:], mexp[:], AF.Exp), after=[tm1])
                    tm3 = em.op("dve", lambda e, w=w, gb_=gb_: e.tensor_mul(
                        mt[:, w * 512:(w + 1) * 512], gb_[:, 0:512], mexp[:]),
                        after=[gl, tm2, t_free.get("mt")])
                    pfree[gb_nm] = tm3
                    t_free["mexp"] = tm3
                    t_mt = tm3
                t_ysl = []
                yl = il = None
                for w2 in range(4):
                    yb = PA if w2 % 2 == 0 else PB
                    yb_nm = "PA" if w2 % 2 == 0 else "PB"
                    ib = PC if w2 % 2 == 0 else PD
                    ib_nm = "PC" if w2 % 2 == 0 else "PD"
                    for k8 in range(8):
                        h = w2 * 8 + k8
                        j = h // 4
                        hb = (h % 4) * 32
                        yl = em.op("pe", lambda e, h=h, k8=k8, yb=yb: e.matmul(
                            yb[:, k8 * 64:(k8 + 1) * 64],
                            mt[:, h * 128:(h + 1) * 128],
                            u_bf[:, h * 64:(h + 1) * 64],
                            start=(k8 == 0), stop=(k8 == 7), skip_group_check=True),
                            after=[t_mt, u4, pfree.get(yb_nm) if k8 == 0 else None])
                        il = em.op("pe", lambda e, h=h, j=j, hb=hb, c=c, k8=k8, ib=ib: e.matmul(
                            ib[:, k8 * 64:(k8 + 1) * 64],
                            crc[hb:hb + 32, j * GW + c * 128: j * GW + (c + 1) * 128],
                            s_cosb[hb:hb + 32, (h // 4) * 64:(h // 4) * 64 + 64],
                            start=True, stop=False, tile_position=(hb, 0),
                            skip_group_check=True),
                            after=[t_planes, cb1, pfree.get(ib_nm) if k8 == 0 else None])
                        il = em.op("pe", lambda e, h=h, j=j, hb=hb, c=c, k8=k8, ib=ib: e.matmul(
                            ib[:, k8 * 64:(k8 + 1) * 64],
                            crs[hb:hb + 32, j * GW + c * 128: j * GW + (c + 1) * 128],
                            s_sinb[hb:hb + 32, (h // 4) * 64:(h // 4) * 64 + 64],
                            start=False, stop=True, tile_position=(hb, 0), skip_group_check=True),
                            after=[cb2])
                        il = em.op("pe", lambda e, h=h, c=c, k8=k8, ib=ib: e.matmul(
                            ib[:, k8 * 64:(k8 + 1) * 64],
                            cunT[0:96, c * 128:(c + 1) * 128],
                            s_unb[0:96, h * 64:(h + 1) * 64],
                            start=False, stop=True, skip_group_check=True),
                            after=[cb3])
                    ty1 = em.op("dve", lambda e, w2=w2, ib=ib, ci=ci: e.tensor_tensor(
                        yint[:].rearrange("p (h q) -> p h q", h=8),
                        ib[:, 0:512].rearrange("p (h q) -> p h q", h=8),
                        pit_tok[:, ci * 32 + w2 * 8: ci * 32 + (w2 + 1) * 8].unsqueeze(2).broadcast_to(
                            (128, 8, 64)), ALU.mult),
                        after=[il, t_pit, t_free.get("yint")])
                    pfree[ib_nm] = ty1
                    ty2 = em.op("dve", lambda e, w2=w2, yb=yb: e.tensor_tensor(
                        ytok[:, w2 * 512:(w2 + 1) * 512], yb[:, 0:512], yint[:], ALU.add),
                        after=[yl, ty1, t_free.get("ytok")])
                    pfree[yb_nm] = ty2
                    t_free["yint"] = ty2
                    t_ysl.append(ty2)
                t_y = t_ysl[-1]
                ta1 = em.dma("pool", name + "_a4", lambda e, ci=ci: e.dma_start(
                    out=a8[:], in_=arow[0:1, ci * 32:(ci + 1) * 32]),
                    after=[t_ar, t_free.get("a4"), t_free.get("a8")])
                ta1b = em.op("pe", lambda e: e.transpose(PF[0:4, 0:8], a8[:], idf[0:8, 0:8]),
                             after=[ta1, pfree.get("PF")])
                ta1c = em.op("dve", lambda e: e.tensor_copy(a4[:], PF[0:4, 0:8]),
                             after=[ta1b, t_free.get("a4")])
                pfree["PF"] = ta1c
                t_free["a8"] = ta1b
                mm = em.op("pe", lambda e: e.matmul(
                    PF[:, 0:512], sp4[:],
                    a4[:].rearrange("p q -> p q ()").broadcast_to((4, 8, 64)),
                    start=True, stop=True, skip_group_check=True),
                    after=[ta1c, pfree.get("PF")])
                ta2 = em.op("dve", lambda e: e.tensor_copy(abc[:], PF[:, 0:512]),
                            after=[mm, t_free.get("abc")])
                pfree["PF"] = ta2
                icl = isl = None
                for h in range(H):
                    hb = (h % 4) * 32
                    icl = em.op("pe", lambda e, h=h, hb=hb: e.matmul(
                        PA[hb:hb + 32, (h // 4) * 64:(h // 4) * 64 + 64],
                        brctok[:, h * 32:(h + 1) * 32],
                        us_bf[:, h * 64:(h + 1) * 64],
                        start=True, stop=True, tile_position=(0, hb),
                        skip_group_check=True),
                        after=[bb1, u5, pfree.get("PA") if h == 0 else None])
                    isl = em.op("pe", lambda e, h=h, hb=hb: e.matmul(
                        PB[hb:hb + 32, (h // 4) * 64:(h // 4) * 64 + 64],
                        brstok[:, h * 32:(h + 1) * 32],
                        us_bf[:, h * 64:(h + 1) * 64],
                        start=True, stop=True, tile_position=(0, hb),
                        skip_group_check=True),
                        after=[bb2, u5, pfree.get("PB") if h == 0 else None])
                su1 = em.op("dve", lambda e: e.tensor_mul(s_cos[:], s_cos[:], abc[:]),
                            after=[ta2, cb1, st_tok["cos"]])
                su2 = em.op("dve", lambda e: e.tensor_tensor(s_cos[:], s_cos[:], PA[:, 0:512], ALU.add),
                            after=[su1, icl])
                pfree["PA"] = su2
                st_tok["cos"] = su2
                su3 = em.op("dve", lambda e: e.tensor_mul(s_sin[:], s_sin[:], abc[:]),
                            after=[ta2, cb2, st_tok["sin"]])
                su4 = em.op("dve", lambda e: e.tensor_tensor(s_sin[:], s_sin[:], PB[:, 0:512], ALU.add),
                            after=[su3, isl])
                pfree["PB"] = su4
                st_tok["sin"] = su4
                t_free["abc"] = su4
                sun_last = None
                iul = {}
                tq = None
                for half in range(2):
                    for k16 in range(16):
                        h = half * 16 + k16
                        bk = PC if k16 < 8 else PD
                        bnm = "PC" if k16 < 8 else "PD"
                        iul[bnm] = em.op("pe", lambda e, h=h, k16=k16, bk=bk: e.matmul(
                            bk[0:96, (k16 % 8) * 64:((k16 % 8) + 1) * 64],
                            b_tok[:, 32:128],
                            us_bf[:, h * 64:(h + 1) * 64],
                            start=(k16 % 8 == 0), stop=(k16 % 8 == 7), skip_group_check=True),
                            after=[t_b, u5, pfree.get(bnm) if k16 % 8 == 0 else None])
                    for qd in range(2):
                        mmq = em.op("pe", lambda e, qd=qd, ci=ci, half=half: e.matmul(
                            PF[:, 0:512], onesr[0:1, :],
                            arow[0:1, ci * 32 + half * 16 + qd * 8:
                                 ci * 32 + half * 16 + (qd + 1) * 8].rearrange(
                                "p q -> p q ()").broadcast_to((1, 8, 64)),
                            start=True, stop=True, skip_group_check=True),
                            after=[t_ar, t_or, pfree.get("PF")])
                        tq = em.op("dve", lambda e, qd=qd: e.tensor_copy(
                            abunh[:, qd * 512:(qd + 1) * 512], PF[:, 0:512]),
                            after=[mmq, t_free.get("abunh")])
                        pfree["PF"] = tq
                    t_free["abunh"] = None
                    sv1 = em.op("dve", lambda e, half=half: e.tensor_mul(
                        s_un[:, half * 1024:(half + 1) * 1024],
                        s_un[:, half * 1024:(half + 1) * 1024],
                        abunh[0:96, 0:1024]),
                        after=[tq, cb3, st_tok["un"]])
                    t_free["abunh"] = sv1
                    sv2 = em.op("dve", lambda e, half=half: e.tensor_tensor(
                        s_un[:, half * 1024: half * 1024 + 512],
                        s_un[:, half * 1024: half * 1024 + 512],
                        PC[0:96, 0:512], ALU.add), after=[sv1, iul["PC"]])
                    pfree["PC"] = sv2
                    sv3 = em.op("dve", lambda e, half=half: e.tensor_tensor(
                        s_un[:, half * 1024 + 512: (half + 1) * 1024],
                        s_un[:, half * 1024 + 512: (half + 1) * 1024],
                        PD[0:96, 0:512], ALU.add), after=[sv1, iul["PD"]])
                    pfree["PD"] = sv3
                    st_tok["un"] = sv3
                    sun_last = sv3
                g1 = em.op("dve", lambda e: e.tensor_mul(u_f[:], x_bf[:], ddup[:]),
                           after=[t_x, u5, u4])
                g2 = em.op("dve", lambda e: e.tensor_add(ytok[:], ytok[:], u_f[:]), after=[g1, t_y])
                g3 = em.op("act", lambda e: e.activation(u_f[:], z_bf[:], AF.Silu), after=[t_z, g2])
                t_free["z_bf"] = g3
                g4 = em.op("dve", lambda e: e.tensor_mul(outbf[:], ytok[:], u_f[:]),
                           after=[g2, g3, t_free.get("outbf")])
                t_free["ytok"] = g4
                t_free["u_f"] = g4
                t_free["x_bf"] = g1
                t_free["px0"] = u3
                t_free["px1"] = u3
                t_free["u_bf"] = yl
                t_free["us_bf"] = isl
                t_ot = None
                tw = None
                for kc in range(16):
                    tw = em.op("pe", lambda e, kc=kc: e.transpose(
                        ptr[:, 0:128], outbf[:, kc * 128:(kc + 1) * 128], idb[:]),
                        after=[g4, pfree.get("ptr")])
                    t_ot = em.op("dve", lambda e, kc=kc: e.tensor_copy(
                        outT[:, kc * 128:(kc + 1) * 128], ptr[:, 0:128]),
                        after=[tw, t_free.get("outT"), t_ot])
                    pfree["ptr"] = t_ot
                t_free["outbf"] = tw
                base_w = (b * n_c + ci) * 16
                t_res = em.dma("sp", name + "_res", lambda e, tglob=tglob: e.dma_start(
                    out=rest[:], in_=res[tglob:tglob + 128, :]),
                    after=[last_store, t_free.get("rest")])
                ml = {}
                NW2 = min(512, D)
                for kc in range(16):
                    ensure_wos(base_w + kc + 2, NB * n_c * 16)
                    slot, wtok = wos_tok[base_w + kc]
                    for nd in range(max(1, D // 512)):
                        bk = PE1 if nd == 0 else PF
                        ml[nd] = em.op("pe", lambda e, kc=kc, nd=nd, slot=slot, bk=bk, NW2=NW2: e.matmul(
                            bk[:, 0:NW2],
                            outT[:, kc * 128:(kc + 1) * 128],
                            w_os[slot][:, nd * NW2:(nd + 1) * NW2],
                            start=(kc == 0), stop=(kc == 15), skip_group_check=True),
                            after=[wtok, t_ot,
                                   (pfree.get("PE1") if nd == 0 else pfree.get("PF"))
                                   if kc == 0 else None])
                    wos_use[base_w + kc] = ml[max(ml)]
                ev1 = em.op("dve", lambda e, NW2=NW2: e.tensor_tensor(
                    outo[:, 0:NW2], PE1[:, 0:NW2], rest[:, 0:NW2], ALU.add),
                    after=[ml[0], t_res, t_free.get("outo")])
                pfree["PE1"] = ev1
                toks_ev = [ev1]
                if D > 512:
                    ev2 = em.op("dve", lambda e: e.tensor_tensor(
                        outo[:, 512:1024], PF[:, 0:512], rest[:, 512:1024], ALU.add),
                        after=[ml.get(1), t_res, ev1])
                    pfree["PF"] = ev2
                    toks_ev.append(ev2)
                last_store = em.dma("sp", name + "_st", lambda e, tglob=tglob: e.dma_start(
                    out=out[tglob:tglob + 128, :], in_=outo[:]), after=toks_ev)
                t_free["rest"] = last_store
                t_free["outo"] = last_store
                t_free["mt"] = yl
                t_free["ctokstok"] = bb2
                t_free["brctok"] = icl
                t_free["brstok"] = isl
                t_free["b_tok"] = iul["PD"]
                t_free["tlc"] = dl
                t_free["trc"] = dl
                t_free["a4"] = mm
            t_free["fb"] = il
            t_free["r0g"] = tc1
            t_free["sing"] = tw
            t_free["cosg"] = tw
            t_free["brc"] = il
            t_free["brs"] = il
            t_free["crc"] = il
            t_free["crs"] = il
            t_free["bdup"] = p2
            t_free["cdup"] = p4
        em.wait("sp", [last_store, t_xlast])
        t_free["s_cos"] = su2
        t_free["s_sin"] = su4
        t_free["s_un"] = sun_last
        t_free["bg_tok"] = u5
        t_free["pit_tok"] = ty1
        t_free["arow"] = tq
        em.pop_scope()
    em.wait("sp", [last_store])
    return last_store



"""Plain layernorm pass (no affine): x [T, D] f32 -> xn [T, D] bf16, eps=1e-5."""

F32 = mybir.dt.float32
BF16 = mybir.dt.bfloat16
AF = mybir.ActivationFunctionType
ALU = mybir.AluOpType
X = mybir.AxisListType.X


def emit_ln(nc, em, cfg, x_in, xn_out, cst, name="ln"):
    T, D = cfg["T"], cfg["D"]
    EPS_COL = cfg["eps_col"]
    cstk = cfg["cst_tok"]
    sb = em.sb
    em.push_scope()
    xt = [sb(f"{name}_xt{i}", [128, D], F32) for i in range(2)]
    xo = [sb(f"{name}_xo{i}", [128, D], BF16) for i in range(2)]
    sq = sb(name + "_sq", [128, D], F32)
    stat = sb(name + "_stat", [128, 4], F32)
    t_free = {}
    last = None
    for nt in range(T // 128):
        i = nt % 2
        t_ld = em.dma("sp", name + "_ld", lambda e, nt=nt, i=i: e.dma_start(
            out=xt[i][:], in_=x_in[nt * 128:(nt + 1) * 128, :]),
            after=[t_free.get(f"xt{i}")])
        r1 = em.op("dve", lambda e, i=i: e.reduce_sum(stat[:, 0:1], xt[i][:], axis=X),
                   after=[t_ld, t_free.get("stat")])
        r2 = em.op("dve", lambda e: e.tensor_scalar(
            stat[:, 0:1], stat[:, 0:1], 1.0 / D, None, ALU.mult), after=[r1])
        s1 = em.op("dve", lambda e, i=i: e.tensor_scalar(
            xt[i][:], xt[i][:], stat[:, 0:1], None, ALU.subtract), after=[r2])
        a1 = em.op("dve", lambda e, i=i: e.tensor_mul(sq[:], xt[i][:], xt[i][:]),
                   after=[s1, t_free.get("sq")])
        r3 = em.op("dve", lambda e: e.reduce_sum(stat[:, 1:2], sq[:], axis=X), after=[a1])
        t_free["sq"] = r3
        r4 = em.op("dve", lambda e: e.tensor_scalar(
            stat[:, 1:2], stat[:, 1:2], 1.0 / D, None, ALU.mult), after=[r3])
        a2 = em.op("act", lambda e: e.activation(
            stat[:, 2:3], stat[:, 1:2], AF.Sqrt, bias=cst[:, EPS_COL:EPS_COL + 1]),
            after=[r4, cstk])
        a3 = em.op("dve", lambda e: e.reciprocal(stat[:, 3:4], stat[:, 2:3]), after=[a2])
        t_free["stat"] = a3
        m1 = em.op("dve", lambda e, i=i: e.tensor_scalar(
            xo[i][:], xt[i][:], stat[:, 3:4], None, ALU.mult),
            after=[a3, t_free.get(f"xo{i}")])
        t_free[f"xt{i}"] = m1
        last = em.dma("sp", name + "_st", lambda e, nt=nt, i=i: e.dma_start(
            out=xn_out[nt * 128:(nt + 1) * 128, :], in_=xo[i][:]), after=[m1])
        t_free[f"xo{i}"] = last
    em.wait("sp", [last])
    em.pop_scope()
    return last


_F32 = mybir.dt.float32
_BF16 = mybir.dt.bfloat16
_NCORES = 8
_TC, _DM = 4096, 1024
_T2 = 2048
_N_LAUNCH = 1


def _np_prep(inputs):
    f32 = np.float32
    bf16 = ml_dtypes.bfloat16
    p = {}
    p["w_inT"] = np.ascontiguousarray(inputs["L0_attn_w_in"].T).astype(f32)
    p["w_outT"] = np.ascontiguousarray(inputs["L0_attn_w_out"].T).astype(f32)
    p["dtb"] = inputs["L0_attn_dt_bias"].reshape(32, 1).astype(f32)
    p["ddup"] = np.tile(np.repeat(inputs["L0_attn_D"].astype(f32), 64)[None, :], (128, 1))
    ii = np.arange(128)
    sel32 = (ii[None, :] % 32 == np.arange(32)[:, None]).astype(f32)
    p["s32f"] = sel32
    p["s32b"] = sel32.astype(bf16)
    selh = np.zeros((32, 8 * 128), f32)
    for j in range(8):
        selh[:, j * 128:(j + 1) * 128] = (ii[None, :] // 32 + 4 * j == np.arange(32)[:, None])
    p["selh"] = selh
    p["selp4"] = (ii[None, :] // 32 == np.arange(4)[:, None]).astype(f32)
    c128 = np.where(np.arange(128)[:, None] <= np.arange(128)[None, :], 0.0, -1e30).astype(f32)
    p["mask4"] = np.tile(c128, (1, 4))
    p["identf"] = np.eye(128, dtype=f32)
    p["identb"] = np.eye(128).astype(bf16)
    cst = np.zeros((128, 8), f32)
    cst[:, 0] = 1.0
    cst[:, 1] = 2 * np.pi
    cst[:, 2] = 1e-5
    cst[:, 3] = 1.0 / np.sqrt(96.0)
    cst[:, 4] = 0.5
    p["cstd"] = cst
    p["wr"] = inputs["L0_ffn_router_kernel_DE"].astype(f32)
    p["wg"] = inputs["L0_ffn_kernel_gating_EDF"].astype(f32)
    p["wu"] = inputs["L0_ffn_kernel_up_EDF"].astype(f32)
    p["wd"] = inputs["L0_ffn_kernel_down_EFD"].astype(f32)
    H, DCH = 16, 4096
    p["wdq"] = inputs["L1_attn_W_DQ"].astype(f32)
    p["wuq"] = inputs["L1_attn_W_UQ"].astype(f32)
    p["wdkv"] = inputs["L1_attn_W_DKV"].astype(f32)
    p["wuk"] = inputs["L1_attn_W_UK"].astype(f32)
    p["wuv"] = inputs["L1_attn_W_UV"].astype(f32)
    wqr_pad = np.zeros((DCH, H * 64), f32)
    wqr_pad.reshape(DCH, H, 64)[:, :, 0:32] = inputs["L1_attn_W_QR"].astype(f32).reshape(DCH, H, 32)
    p["wqr"] = wqr_pad
    p["wkr"] = inputs["L1_attn_W_KR"].astype(f32)
    p["wo"] = inputs["L1_attn_W_O"].astype(f32)
    t = np.arange(_T2, dtype=f32)
    inv = 1.0 / 10000.0 ** (np.arange(0, 32, 2, dtype=f32) / 32)
    freqs = t[:, None] * inv[None, :]
    emb = np.concatenate([freqs, freqs], -1)
    scale = (np.cos(emb) + np.sin(emb)).astype(f32)
    scale_dup = np.zeros((H * 64, _T2), f32)
    scale_dup.reshape(H, 64, _T2)[:, 0:32, :] = scale.T[None, :, :]
    p["scale_dup"] = scale_dup
    QT = 512
    n_tc = QT // 128
    MEXT = QT + (n_tc - 1) * 128
    masks = np.zeros((128, MEXT), f32)
    for i in range(128):
        masks[i, :] = np.where(np.arange(MEXT) - (n_tc - 1) * 128 >= i, 0.0, -1e30)
    p["masks"] = masks
    p["wfcT"] = np.ascontiguousarray(inputs["L1_ffn_W_fc"].astype(f32).T)
    p["wprT"] = np.ascontiguousarray(inputs["L1_ffn_W_proj"].astype(f32).T)
    p["lnw"] = inputs["L1_ffn_ln_w"].astype(f32)
    p["lnb"] = inputs["L1_ffn_ln_b"].astype(f32)
    p["ident"] = np.eye(128, dtype=f32)
    p["sel2"] = np.concatenate(
        [np.stack([np.ones(128, f32), np.zeros(128, f32)]),
         np.stack([np.zeros(128, f32), np.ones(128, f32)])], axis=1)
    return p


def _din(nc, nm, shape, dt):
    return nc.dram_tensor(nm, shape, dt, kind="ExternalInput").ap()


def _dout(nc, nm, shape, dt):
    return nc.dram_tensor(nm, shape, dt, kind="ExternalOutput").ap()


def _dint(nc, nm, shape, dt):
    return nc.dram_tensor(nm, shape, dt, kind="Internal").ap()


def _mk_scr(nc, mk, T):
    return dict(
        proj=mk(nc, "s_proj", [T, 4224], _BF16),
        bT=mk(nc, "s_bT", [128, T], _BF16),
        cT=mk(nc, "s_cT", [128, T], _BF16),
        smT=mk(nc, "s_smT", [128, T], _F32),
        r0d=mk(nc, "s_r0d", [128, 8 * T], _BF16),
        tld=mk(nc, "s_tld", [1, 32 * T], _F32),
        trd=mk(nc, "s_trd", [1, 32 * T], _F32),
    )


def _mamba_dram(nc):
    d = {}
    d["w_inT"] = _din(nc, "w_inT", [_DM, 4480], _F32)
    d["w_outT"] = _din(nc, "w_outT", [2048, _DM], _F32)
    d["dtb"] = _din(nc, "dtb", [32, 1], _F32)
    d["ddup"] = _din(nc, "ddup", [128, 2048], _F32)
    d["s32f"] = _din(nc, "s32f", [32, 128], _F32)
    d["s32b"] = _din(nc, "s32b", [32, 128], _BF16)
    d["selh"] = _din(nc, "selh", [32, 1024], _F32)
    d["selp4"] = _din(nc, "selp4", [4, 128], _F32)
    d["mask4"] = _din(nc, "mask4", [128, 512], _F32)
    d["identf"] = _din(nc, "identf", [128, 128], _F32)
    d["identb"] = _din(nc, "identb", [128, 128], _BF16)
    return d


def _moe_dram(nc):
    d = {}
    d["wr"] = _din(nc, "wr", [_DM, 8], _F32)
    d["wg"] = _din(nc, "wg", [8, _DM, 2048], _F32)
    d["wu"] = _din(nc, "wu", [8, _DM, 2048], _F32)
    d["wd"] = _din(nc, "wd", [8, 2048, _DM], _F32)
    return d


def _attn_dram(nc):
    d = {}
    d["wdq"] = _din(nc, "wdq", [_DM, 4096], _F32)
    d["wuq"] = _din(nc, "wuq", [4096, _DM], _F32)
    d["wdkv"] = _din(nc, "wdkv", [_DM, 256], _F32)
    d["wuk"] = _din(nc, "wuk", [256, _DM], _F32)
    d["wuv"] = _din(nc, "wuv", [256, _DM], _F32)
    d["wqr"] = _din(nc, "wqr", [4096, 16 * 64], _F32)
    d["wkr"] = _din(nc, "wkr", [_DM, 32], _F32)
    d["wo"] = _din(nc, "wo", [_DM, _DM], _F32)
    d["scale_dup"] = _din(nc, "scale_dup", [16 * 64, _T2], _F32)
    d["masks"] = _din(nc, "masks", [128, 896], _F32)
    return d


def _ffn_dram(nc):
    d = {}
    d["wfcT"] = _din(nc, "wfcT", [_DM, 4096], _F32)
    d["wprT"] = _din(nc, "wprT", [4096, _DM], _F32)
    d["lnw"] = _din(nc, "lnw", [4096], _F32)
    d["lnb"] = _din(nc, "lnb", [4096], _F32)
    d["ident"] = _din(nc, "ident", [128, 128], _F32)
    d["sel2"] = _din(nc, "sel2", [2, 256], _F32)
    return d


def _emit_cst(nc, em):
    cstd = _din(nc, "cstd", [128, 8], _F32)
    cst = em.sb("cst", [128, 8], _F32)
    t_cst = em.dma("sp", "cstld", lambda e: e.dma_start(out=cst[:], in_=cstd))
    return cst, t_cst


def _emit_act_preloads(em):
    AFv = mybir.ActivationFunctionType
    for f in (AFv.Exp, AFv.Ln, AFv.Sigmoid, AFv.Sqrt, AFv.Sin, AFv.Gelu, AFv.Silu):
        em.op("act", lambda e, f=f: e.preload_act_table(f))


def _emit_g1(nc, em, cst, t_cst, d, x0, xn0, xacc, scr):
    emit_ln(nc, em, dict(T=_TC, D=_DM, eps_col=2, cst_tok=t_cst), x0, xn0, cst, name="ln0")
    cfgm = dict(T=_TC, D=_DM, NB=2, one_col=0, tpi_col=1, half_col=4, cst_tok=t_cst)
    emit_mamba(nc, em, cfgm, xn0, d["w_inT"], d["w_outT"], d["dtb"], d["ddup"],
               d["s32f"], d["s32b"], d["selh"], d["selp4"], d["mask4"],
               d["identf"], d["identb"], x0, xacc, cst, scr)


def _emit_g2(nc, em, d, xn0, xacc, x1):
    cfg = dict(T=_TC, D=_DM, F=2048, E=8, ST=1024)
    emit_moe(nc, em, cfg, xn0, d["wr"], d["wg"], d["wu"], d["wd"], x1, res_in=xacc)


def _emit_g3(nc, em, cst, t_cst, d, x1, xn1, xa):
    emit_ln(nc, em, dict(T=_TC, D=_DM, eps_col=2, cst_tok=t_cst), x1, xn1, cst, name="ln1")
    cfga = dict(T=_TC, D=_DM, NB=2, H=16, DCH=4096, DKV=256, scl_col=3, cst_tok=t_cst)
    emit_attn(nc, em, cfga, xn1, d["wdq"], d["wuq"], d["wdkv"], d["wuk"], d["wuv"],
              d["wqr"], d["wkr"], d["wo"], d["scale_dup"], d["masks"], x1, xa, cst)


def _emit_g4(nc, em, cst, t_cst, d, xn1, xa, outt, wfcq, wprq):
    tq1 = emit_quant(nc, em, d["wfcT"], wfcq, _DM, 4096, "qfc")
    tq2 = emit_quant(nc, em, d["wprT"], wprq, 4096, _DM, "qpr")
    cfgf = dict(T=_TC, D=_DM, Hh=4096, eps_col=2, ident_dram=d["ident"],
                sel2_dram=d["sel2"], cst_tok=t_cst)
    emit_ffn(nc, em, cfgf, xn1, wfcq, wprq, d["lnw"], d["lnb"], xa, outt, cst,
             after_w=[tq1, tq2])


_CACHE = {}


def _build_unfused():
    if "u" in _CACHE:
        return _CACHE["u"]
    progs = []
    nc = bass.Bass("TRN2", target_bir_lowering=False)
    x0 = _din(nc, "x0", [_TC, _DM], _F32)
    xn0 = _dout(nc, "xn0", [_TC, _DM], _BF16)
    xacc = _dout(nc, "xacc", [_TC, _DM], _F32)
    scr = _mk_scr(nc, _dint, _TC)
    d = _mamba_dram(nc)
    em = Em(nc)
    cst, t_cst = _emit_cst(nc, em)
    _emit_g1(nc, em, cst, t_cst, d, x0, xn0, xacc, scr)
    em.finalize()
    progs.append((nc, ["w_inT", "w_outT", "dtb", "ddup", "s32f", "s32b", "selh",
                       "selp4", "mask4", "identf", "identb", "cstd"], ["x0"],
                  ["xn0", "xacc"]))
    nc = bass.Bass("TRN2", target_bir_lowering=False)
    xn0 = _din(nc, "xn0", [_TC, _DM], _BF16)
    xacc = _din(nc, "xacc", [_TC, _DM], _F32)
    x1 = _dout(nc, "x1", [_TC, _DM], _F32)
    d = _moe_dram(nc)
    em = Em(nc)
    _emit_g2(nc, em, d, xn0, xacc, x1)
    em.finalize()
    progs.append((nc, ["wr", "wg", "wu", "wd"], ["xn0", "xacc"], ["x1"]))
    nc = bass.Bass("TRN2", target_bir_lowering=False)
    x1 = _din(nc, "x1", [_TC, _DM], _F32)
    xn1 = _dout(nc, "xn1", [_TC, _DM], _BF16)
    xa = _dout(nc, "xa", [_TC, _DM], _F32)
    d = _attn_dram(nc)
    em = Em(nc)
    cst, t_cst = _emit_cst(nc, em)
    _emit_g3(nc, em, cst, t_cst, d, x1, xn1, xa)
    em.finalize()
    progs.append((nc, ["wdq", "wuq", "wdkv", "wuk", "wuv", "wqr", "wkr", "wo",
                       "scale_dup", "masks", "cstd"], ["x1"], ["xn1", "xa"]))
    nc = bass.Bass("TRN2", target_bir_lowering=False)
    xn1 = _din(nc, "xn1", [_TC, _DM], _BF16)
    xa = _din(nc, "xa", [_TC, _DM], _F32)
    outt = _dout(nc, "outt", [_TC, _DM], _F32)
    wfcq = _dint(nc, "wfcq", [_DM, 4096], _BF16)
    wprq = _dint(nc, "wprq", [4096, _DM], _BF16)
    d = _ffn_dram(nc)
    em = Em(nc)
    cst, t_cst = _emit_cst(nc, em)
    _emit_g4(nc, em, cst, t_cst, d, xn1, xa, outt, wfcq, wprq)
    em.finalize()
    progs.append((nc, ["wfcT", "wprT", "lnw", "lnb", "ident", "sel2", "cstd"],
                  ["xn1", "xa"], ["outt"]))
    _CACHE["u"] = progs
    return progs


def _build_fused():
    if "f" in _CACHE:
        return _CACHE["f"]
    nc = bass.Bass("TRN2", target_bir_lowering=False)
    x0 = _din(nc, "x0", [_TC, _DM], _F32)
    outt = _dout(nc, "outt", [_TC, _DM], _F32)
    xn0 = _dint(nc, "xn0", [_TC, _DM], _BF16)
    xacc = _dint(nc, "xacc", [_TC, _DM], _F32)
    x1 = _dint(nc, "x1", [_TC, _DM], _F32)
    xn1 = _dint(nc, "xn1", [_TC, _DM], _BF16)
    xa = _dint(nc, "xa", [_TC, _DM], _F32)
    wfcq = _dint(nc, "wfcq", [_DM, 4096], _BF16)
    wprq = _dint(nc, "wprq", [4096, _DM], _BF16)
    scr = _mk_scr(nc, _dint, _TC)
    dm = _mamba_dram(nc)
    dmo = _moe_dram(nc)
    da = _attn_dram(nc)
    df = _ffn_dram(nc)
    em = Em(nc)
    cst, t_cst = _emit_cst(nc, em)
    em.push_scope()
    _emit_g1(nc, em, cst, t_cst, dm, x0, xn0, xacc, scr)
    em.pop_scope()
    em.push_scope()
    _emit_g2(nc, em, dmo, xn0, xacc, x1)
    em.pop_scope()
    em.push_scope()
    _emit_g3(nc, em, cst, t_cst, da, x1, xn1, xa)
    em.pop_scope()
    em.push_scope()
    _emit_g4(nc, em, cst, t_cst, df, xn1, xa, outt, wfcq, wprq)
    em.pop_scope()
    em.finalize()
    w_names = ["w_inT", "w_outT", "dtb", "ddup", "s32f", "s32b", "selh", "selp4",
               "mask4", "identf", "identb", "cstd", "wr", "wg", "wu", "wd",
               "wdq", "wuq", "wdkv", "wuk", "wuv", "wqr", "wkr", "wo",
               "scale_dup", "masks", "wfcT", "wprT", "lnw", "lnb", "ident", "sel2"]
    _CACHE["f"] = (nc, w_names)
    return _CACHE["f"]


def kernel(**inputs):
    p = _np_prep(inputs)
    u = inputs["L0_attn_u"].astype(np.float32)
    shards = [np.ascontiguousarray(u[2 * i:2 * i + 2].reshape(_TC, _DM))
              for i in range(_NCORES)]
    cores = list(range(_NCORES))
    if _N_LAUNCH == 1:
        nc, w_names = _build_fused()
        in_maps = [dict({k: p[k] for k in w_names}, x0=shards[i]) for i in cores]
        r = run_bass_kernel_spmd(nc, in_maps, core_ids=cores)
        outs = [r.results[i]["outt"] for i in cores]
    else:
        progs = _build_unfused()
        state = [{"x0": shards[i]} for i in cores]
        for nc, wn, act_in, act_out in progs:
            in_maps = [dict({k: p[k] for k in wn},
                            **{k: state[i][k] for k in act_in}) for i in cores]
            r = run_bass_kernel_spmd(nc, in_maps, core_ids=cores)
            for i in cores:
                for k in act_out:
                    state[i][k] = r.results[i][k]
        outs = [state[i]["outt"] for i in cores]
    return np.concatenate([o.reshape(2, _T2, _DM) for o in outs], axis=0).astype(np.float32)


# version 04
# speedup vs baseline: 1.2611x; 1.0815x over previous
import contextlib
import numpy as np
import ml_dtypes
import concourse.bass as bass
import concourse.mybir as mybir
from concourse.bass_utils import run_bass_kernel_spmd




"""Raw-bass emission framework: per-engine queues, auto semaphore edges.

Tokens are (sem_name, value). Every op declares `after=[tokens...]`; the
framework emits wait_ge on the consuming engine (deduped by monotone
per-engine wait tracking) and puts then_inc on the producing instruction.

Rules encoded here (learned from sim/device probes):
- same-engine RAW/WAW edges need semaphores too (race detector / HW).
- DMA sems: SWDGE (pool) and HWDGE (sync/scalar) must not share a sem.
  A sem shared by several DMAs may only be waited at full cumulative value;
  we give each logical channel its own sem and consumers wait the channel's
  cumulative count at emission time.
- then_inc rides the real producing instruction.
"""

ENGINES = ("pe", "act", "dve", "pool", "sp")
ENG_ATTR = {"pe": "tensor", "act": "scalar", "dve": "vector", "pool": "gpsimd", "sp": "sync"}


class Em:
    def __init__(self, nc):
        self.nc = nc
        self.stack = contextlib.ExitStack()
        self.scopes = []
        self.q = {e: [] for e in ENGINES}
        self.sems = {}
        self.cnt = {}
        self.waited = {e: {} for e in ENGINES}
        self.is_chan = set()
        for e in ENGINES:
            self._mksem("c_" + e)

    def _mksem(self, name):
        self.sems[name] = self.stack.enter_context(self.nc.semaphore(name))
        self.cnt[name] = 0
        return name

    def chan(self, name):
        if name not in self.sems:
            self._mksem(name)
            self.is_chan.add(name)
        return name

    def sb(self, name, shape, dtype):
        st = self.scopes[-1] if self.scopes else self.stack
        return st.enter_context(self.nc.sbuf_tensor(name, shape, dtype))

    def ps(self, name, shape, dtype):
        st = self.scopes[-1] if self.scopes else self.stack
        return st.enter_context(self.nc.psum_tensor(name, shape, dtype))

    def push_scope(self):
        self.scopes.append(contextlib.ExitStack())

    def pop_scope(self):
        self.flush()
        self.scopes.pop().close()

    def flush(self):
        with self.nc.Block() as block:
            for e in ENGINES:
                if not self.q[e]:
                    continue
                def run(engobj, e=e):
                    for f in self.q[e]:
                        f(engobj)
                getattr(block, ENG_ATTR[e])(run)
        self.q = {e: [] for e in ENGINES}

    def pscratch(self):
        if not hasattr(self, "_pscr"):
            import concourse.mybir as mybir
            self._pscr = self.stack.enter_context(
                self.nc.psum_tensor("pscratch", [128, 16], mybir.dt.float32))
        return self._pscr

    def op(self, eng, build, after=(), sem=None, inc=1):
        semname = sem if sem is not None else ("c_" + eng)
        waits = []
        for tok in after:
            if tok is None:
                continue
            psem, pval = tok
            if psem in self.is_chan:
                pval = self.cnt[psem]
            cur = self.waited[eng].get(psem, 0)
            if pval > cur:
                self.waited[eng][psem] = pval
                waits.append((self.sems[psem], pval))
        self.cnt[semname] += inc
        val = self.cnt[semname]
        semh = self.sems[semname]

        def emit(engobj, waits=waits, build=build, semh=semh, inc=inc):
            for s, v in waits:
                engobj.wait_ge(s, v)
            ins = build(engobj)
            ins.then_inc(semh, inc)

        self.q[eng].append(emit)
        return (semname, val)

    def dma(self, eng, chan, build, after=()):
        return self.op(eng, build, after, sem=self.chan(chan), inc=16)

    def wait(self, eng, tokens):
        def emit(engobj, tokens=tuple(t for t in tokens if t is not None)):
            for psem, pval in tokens:
                engobj.wait_ge(self.sems[psem], pval)
        toks = []
        for tok in tokens:
            if tok is None:
                continue
            psem, pval = tok
            if psem in self.is_chan:
                pval = self.cnt[psem]
            cur = self.waited[eng].get(psem, 0)
            if pval > cur:
                self.waited[eng][psem] = pval
                toks.append(tok)

        def emit2(engobj, toks=toks):
            for psem, pval in toks:
                engobj.wait_ge(self.sems[psem], pval)
        self.q[eng].append(emit2)

    def finalize(self):
        self.flush()
        while self.scopes:
            self.scopes.pop().close()
        self.stack.close()



"""Dense MoE pass (L0 ffn): out = res + sum_e w_e * (silu(xn@Wg_e) * (xn@Wu_e)) @ Wd_e

Layouts:
- xn DRAM [T, D] bf16 token-major; read feature-major via HWDGE transpose-DMA.
- Wg, Wu [E, D, F] f32; Wd [E, F, D] f32; Wr [D, E] f32 (cast bf16 during SWDGE DMA)
- res_in (optional) DRAM [T, D] f32; out DRAM [T, D] f32 token-major.
"""

F32 = mybir.dt.float32
BF16 = mybir.dt.bfloat16
AF = mybir.ActivationFunctionType
ALU = mybir.AluOpType
X = mybir.AxisListType.X


def emit_moe(nc, em, cfg, xn, wr, wg, wu, wd, out, res_in=None, name="moe"):
    T, D, F, E, ST = cfg["T"], cfg["D"], cfg["F"], cfg["E"], cfg["ST"]
    n_dc = D // 128
    n_fc = F // 128
    FQ = min(512, F)
    n_fq = F // FQ
    fq_fc = FQ // 128
    n_st = T // ST
    TBW = min(512, ST)
    n_tb = ST // TBW
    n_tc = ST // 128
    WDH = min(512, F)
    n_wdh = F // WDH
    wdh_fc = WDH // 128
    NDW = min(512, D)
    n_nd = D // NDW

    sb, ps = em.sb, em.ps
    xnT = sb(name + "_xnT", [128, n_dc * ST], BF16)
    acc = sb(name + "_acc", [128, n_tc * D], F32)
    fuse = sb(name + "_fuse", [128, n_fc * ST], BF16)
    n_gu_slots = 2 * n_fq + 1
    gu = [sb(f"{name}_gu{i}", [128, n_dc * FQ], BF16) for i in range(n_gu_slots)]
    n_wd_slots = min(n_wdh + 1, 6)
    wdsb = [sb(f"{name}_wd{i}", [128, wdh_fc * D], BF16) for i in range(n_wd_slots)]
    wrb = sb(name + "_wrb", [128, n_dc * E], BF16)
    slog = sb(name + "_slog", [128, n_tc * E], F32)
    sstat = sb(name + "_sstat", [128, 8], F32)
    sse = sb(name + "_sse", [128, 2 * E], F32)
    wful = sb(name + "_wful", [128, n_tc * E], F32)
    sact = [sb(f"{name}_sact{i}", [128, TBW], F32) for i in range(2)]

    pg = ps(name + "_pg", [128, 512], F32)
    pu = ps(name + "_pu", [128, 512], F32)
    pg2 = ps(name + "_pg2", [128, 512], F32)
    pu2 = ps(name + "_pu2", [128, 512], F32)
    pd = ps(name + "_pd", [128, D], F32)
    pd2 = ps(name + "_pd2", [128, D], F32)
    pfree = {}

    t_wr = em.dma("pool", name + "_wr", lambda e: e.dma_start(
        out=wrb[:].rearrange("p (c e) -> p c e", c=n_dc),
        in_=wr.rearrange("(c p) e -> p c e", p=128)))

    def load_gu(slot, e_, kind, fq, after):
        src = (wg if kind == 0 else wu)[e_, :, :].rearrange("(c p) f -> p c f", p=128)
        src = src[:, :, fq * FQ:(fq + 1) * FQ]
        return em.dma("pool", f"{name}_gu{slot}", lambda e: e.dma_start(
            out=gu[slot][:].rearrange("p (c f) -> p c f", c=n_dc), in_=src), after=after)

    def load_wd(slot, e_, h, after):
        src = wd[e_, h * WDH:(h + 1) * WDH, :].rearrange("(c p) d -> p c d", p=128)
        return em.dma("pool", f"{name}_wd{slot}", lambda e: e.dma_start(
            out=wdsb[slot][:].rearrange("p (c d) -> p c d", c=wdh_fc), in_=src), after=after)

    units = []
    wd_units = []
    for st in range(n_st):
        for e_ in range(E):
            for fq in range(n_fq):
                units.append((e_, 0, fq))
                units.append((e_, 1, fq))
            for h in range(n_wdh):
                wd_units.append((e_, h))

    gu_last_use = [None] * n_gu_slots
    wd_last_use = [None] * n_wd_slots
    gu_tok = {}
    wd_tok = {}
    state = {"next_gu": 0, "next_wd": 0}

    def ensure_gu(upto):
        while state["next_gu"] <= min(upto, len(units) - 1):
            i = state["next_gu"]
            slot = i % n_gu_slots
            e_, kind, fq = units[i]
            tok = load_gu(slot, e_, kind, fq, after=[gu_last_use[slot]])
            gu_tok[i] = (slot, tok)
            state["next_gu"] = i + 1

    def ensure_wd(upto):
        while state["next_wd"] <= min(upto, len(wd_units) - 1):
            i = state["next_wd"]
            slot = i % n_wd_slots
            e_, h = wd_units[i]
            tok = load_wd(slot, e_, h, after=[wd_last_use[slot]])
            wd_tok[i] = (slot, tok)
            state["next_wd"] = i + 1

    t_xnT_prev_users = []
    t_acc_store = None
    gu_idx = 0
    wd_idx = 0
    sact_free = [None, None]
    prev_t17 = None

    for st in range(n_st):
        for dc in range(n_dc):
            t_xnT = em.dma("sp", name + "_xnT", lambda e, st=st, dc=dc: e.dma_start(
                out=xnT[:, dc * ST:(dc + 1) * ST],
                in_=xn[st * ST:(st + 1) * ST, dc * 128:(dc + 1) * 128],
                transpose=True), after=t_xnT_prev_users)

        if res_in is not None:
            t_acc0 = em.dma("sp", name + "_accld", lambda e, st=st: e.dma_start(
                out=acc[:].rearrange("p (c d) -> p c d", c=n_tc),
                in_=res_in[st * ST:(st + 1) * ST, :].rearrange("(c p) d -> p c d", p=128)),
                after=[t_acc_store])
        else:
            t_acc0 = em.op("dve", lambda e: e.memset(acc[:], 0.0), after=[t_acc_store])

        rt_first = True
        for tc in range(n_tc):
            for dc in range(n_dc):
                t_rt = em.op("pe", lambda e, tc=tc, dc=dc, first=rt_first: e.matmul(
                    pg[:, tc * E:(tc + 1) * E],
                    xnT[:, dc * ST + tc * 128: dc * ST + (tc + 1) * 128],
                    wrb[:, dc * E:(dc + 1) * E],
                    start=first, stop=(tc == n_tc - 1 and dc == n_dc - 1),
                    skip_group_check=True),
                    after=[t_xnT, t_wr, pfree.get("pg")])
                rt_first = False
        t_slog = em.op("dve", lambda e: e.tensor_copy(slog[:, 0:n_tc * E], pg[:, 0:n_tc * E]),
                       after=[t_rt, prev_t17])
        pfree["pg"] = t_slog

        for tc in range(n_tc):
            sl = slog[:, tc * E:(tc + 1) * E]
            t1 = em.op("dve", lambda e, sl=sl: e.reduce_max(sstat[:, 0:1], sl, axis=X, negate=True),
                       after=[t_slog, prev_t17])
            t2 = em.op("act", lambda e, sl=sl: e.activation(sse[:, 0:E], sl, AF.Exp, bias=sstat[:, 0:1]),
                       after=[t1, prev_t17])
            t3 = em.op("dve", lambda e: e.reduce_sum(sstat[:, 1:2], sse[:, 0:E], axis=X), after=[t2])
            t4 = em.op("dve", lambda e: e.reciprocal(sstat[:, 2:3], sstat[:, 1:2]), after=[t3])
            t5 = em.op("dve", lambda e: e.tensor_scalar(sse[:, E:2 * E], sse[:, 0:E], sstat[:, 2:3], None, ALU.mult), after=[t4])
            t6 = em.op("dve", lambda e: e.reduce_max(sstat[:, 3:4], sse[:, E:2 * E], axis=X), after=[t5])
            t7 = em.op("dve", lambda e: e.tensor_scalar(sse[:, 0:E], sse[:, E:2 * E], sstat[:, 3:4], None, ALU.is_ge), after=[t6])
            t8 = em.op("dve", lambda e: e.tensor_sub(sse[:, 0:E], sse[:, E:2 * E], sse[:, 0:E]), after=[t7])
            t9 = em.op("dve", lambda e: e.reduce_max(sstat[:, 4:5], sse[:, 0:E], axis=X), after=[t8])
            t10 = em.op("dve", lambda e: e.tensor_scalar(sse[:, 0:E], sse[:, E:2 * E], sstat[:, 4:5], None, ALU.is_ge), after=[t9])
            t11 = em.op("act", lambda e: e.activation(sse[:, E:2 * E], sse[:, E:2 * E], AF.Exp), after=[t10])
            t12 = em.op("act", lambda e: e.activation(sstat[:, 5:6], sstat[:, 3:4], AF.Exp), after=[t9])
            t13 = em.op("act", lambda e: e.activation(sstat[:, 6:7], sstat[:, 4:5], AF.Exp), after=[t12])
            t14 = em.op("dve", lambda e: e.tensor_add(sstat[:, 5:6], sstat[:, 5:6], sstat[:, 6:7]), after=[t13])
            t15 = em.op("dve", lambda e: e.reciprocal(sstat[:, 7:8], sstat[:, 5:6]), after=[t14])
            t16 = em.op("dve", lambda e, tc=tc: e.tensor_mul(wful[:, tc * E:(tc + 1) * E], sse[:, E:2 * E], sse[:, 0:E]), after=[t11, t15])
            t17 = em.op("dve", lambda e, tc=tc: e.tensor_scalar(wful[:, tc * E:(tc + 1) * E], wful[:, tc * E:(tc + 1) * E], sstat[:, 7:8], None, ALU.mult), after=[t16])
            prev_t17 = t17
        t_wful = prev_t17

        last_acc = [t_acc0] * n_tc
        prev_down_last = None
        for e_ in range(E):
            ensure_gu(gu_idx + 2 * n_fq)
            ensure_wd(wd_idx + n_wdh)
            fuse_last = None
            for fq in range(n_fq):
                gi, ui = gu_idx, gu_idx + 1
                gu_idx += 2
                gslot, gtok = gu_tok[gi]
                uslot, utok = gu_tok[ui]
                unit_last = None
                for fc in range(fq_fc):
                    fci = fq * fq_fc + fc
                    for tb in range(n_tb):
                        rot = (fci * n_tb + tb) % 2
                        pga, pua = (pg, pu) if rot == 0 else (pg2, pu2)
                        pga_nm = "pg" if rot == 0 else "pg2"
                        pua_nm = "pu" if rot == 0 else "pu2"
                        sa = sact[rot]
                        mm_last = None
                        for dc in range(n_dc):
                            mm_last = em.op("pe", lambda e, gslot=gslot, dc=dc, fc=fc, tb=tb, pga=pga: e.matmul(
                                pga[:, 0:TBW],
                                gu[gslot][:, dc * FQ + fc * 128: dc * FQ + (fc + 1) * 128],
                                xnT[:, dc * ST + tb * TBW: dc * ST + (tb + 1) * TBW],
                                start=(dc == 0), stop=(dc == n_dc - 1)),
                                after=[gtok, t_xnT, pfree.get(pga_nm) if dc == 0 else None])
                        mmg_last = mm_last
                        for dc in range(n_dc):
                            mm_last = em.op("pe", lambda e, uslot=uslot, dc=dc, fc=fc, tb=tb, pua=pua: e.matmul(
                                pua[:, 0:TBW],
                                gu[uslot][:, dc * FQ + fc * 128: dc * FQ + (fc + 1) * 128],
                                xnT[:, dc * ST + tb * TBW: dc * ST + (tb + 1) * TBW],
                                start=(dc == 0), stop=(dc == n_dc - 1)),
                                after=[utok, t_xnT, pfree.get(pua_nm) if dc == 0 else None])
                        ts1 = em.op("act", lambda e, pga=pga, sa=sa: e.activation(sa[:, 0:TBW], pga[:, 0:TBW], AF.Silu),
                                    after=[mmg_last, sact_free[rot]])
                        pfree[pga_nm] = ts1
                        ts2 = em.op("dve", lambda e, pua=pua, sa=sa, fci=fci, tb=tb: e.tensor_mul(
                            fuse[:, fci * ST + tb * TBW: fci * ST + (tb + 1) * TBW], pua[:, 0:TBW], sa[:, 0:TBW]),
                            after=[mm_last, ts1, prev_down_last])
                        pfree[pua_nm] = ts2
                        sact_free[rot] = ts2
                        unit_last = ts2
                        fuse_last = ts2
                gu_last_use[gslot] = unit_last
                gu_last_use[uslot] = unit_last
            t_fuse = fuse_last

            for tc in range(n_tc):
                pda = pd if tc % 2 == 0 else pd2
                pda_nm = "pd" if tc % 2 == 0 else "pd2"
                mmd_last = None
                for h in range(n_wdh):
                    wslot, wtok = wd_tok[wd_idx + h]
                    for fc in range(wdh_fc):
                        fci = h * wdh_fc + fc
                        for nd in range(n_nd):
                            mmd_last = em.op("pe", lambda e, fci=fci, tc=tc, wslot=wslot, fc=fc, nd=nd, pda=pda: e.matmul(
                                pda[:, nd * NDW:(nd + 1) * NDW],
                                fuse[:, fci * ST + tc * 128: fci * ST + (tc + 1) * 128],
                                wdsb[wslot][:, fc * D + nd * NDW: fc * D + (nd + 1) * NDW],
                                start=(fci == 0), stop=(fci == n_fc - 1)),
                                after=[wtok, t_fuse, pfree.get(pda_nm) if (fci == 0 and nd == 0) else None])
                tev = em.op("dve", lambda e, pda=pda, tc=tc, e_=e_: e.scalar_tensor_tensor(
                    acc[:, tc * D:(tc + 1) * D], pda[:, 0:D], wful[:, tc * E + e_: tc * E + e_ + 1],
                    acc[:, tc * D:(tc + 1) * D], ALU.mult, ALU.add),
                    after=[mmd_last, t_wful, last_acc[tc], t_acc0])
                last_acc[tc] = tev
                pfree[pda_nm] = tev
                for h in range(n_wdh):
                    wd_last_use[wd_tok[wd_idx + h][0]] = tev
                prev_down_last = mmd_last
            wd_idx += n_wdh

        t_acc_store = em.dma("sp", name + "_st", lambda e, st=st: e.dma_start(
            out=out[st * ST:(st + 1) * ST, :].rearrange("(c p) d -> p c d", p=128),
            in_=acc[:].rearrange("p (c d) -> p c d", c=n_tc)), after=last_acc)
        t_xnT_prev_users = last_acc[-1:]
    em.wait("sp", [t_acc_store])
    return t_acc_store



"""Bitnet FFN pass (L1 ffn):
  Wq = absmean-ternary(W);  h = gelu_exact(xn @ Wq_fc^T);  hn = LN(h)*ln_w + ln_b
  out = res + hn @ Wq_proj^T
LN folded into proj:  out = res + r*G - (r*m)*c1 + c2
  G = h @ Wpl (Wpl = lnw*Wq_projT), c1 = colsum(Wpl), c2 = lnb^T @ Wq_projT... note:
  hn@WqP^T = ((h-m)*r*lnw + lnb)@WqP^T = r*(h@Wpl) - r*m*colsum(Wpl) + lnb@WqP^T
  c2 uses the UNSCALED Wq_projT -> needs lnb^T@Wq_projT: we compute c2 from Wpl with
  lnb/lnw? lnw may be ~0 -> instead quantize stores BOTH: here we take c2 via a second
  lhsT column using lnb_over row trick: store Wq_projT unscaled, scale rows of G?!
  Simpler: quantizer writes Wpl (lnw-scaled) AND we get c2 = (lnb/lnw)ak... NO.
  => quantizer writes UNSCALED wprq; lnw folded into G's *rhs* here is impossible
  (lnw is per h = per contraction row) -> fold lnw into h instead? lnw per hh row =
  per-PARTITION in hT! hn_part = h*lnw is a per-partition scale on hbf... but stats
  need raw h. We apply lnw as per-partition scale DURING the gelu evac? stats use
  raw h though. Final choice: hbf stores gelu(h) raw; G-matmul lhsT = hbf, rhs = wprq
  (unscaled); fold lnw into... lnw multiplies rows of Wq_projT = contraction rows =
  lhsT-partition rows of rhs operand... rhs[k=hh, d]: scale per k row: pre-scale rhs
  tiles ONCE per load: wpl[slot] = wprq_tile * lnw_col (1 DVE per loaded tile, cheap).
  c1 = colsum(wpl) = ones^T@wpl, c2 = lnb_over_lnw... NO: c2 = lnb^T @ wprq (raw).
  Both c1, c2 via the [ones|lnb] K-column trick on raw wprq needs lnw-scaled for c1:
  c1 = lnw^T-weighted colsum of wprq = (lnw)^T @ wprq. So c1 = lnw^T@wprq, c2 =
  lnb^T@wprq -> one K=2-ish matmul per chunk with lhsT = [lnw_col | lnb_col]!
"""

F32 = mybir.dt.float32
BF16 = mybir.dt.bfloat16
AF = mybir.ActivationFunctionType
ALU = mybir.AluOpType
X = mybir.AxisListType.X


def emit_quant(nc, em, src, dst, R, C, name):
    sb, ps = em.sb, em.ps
    n_rc = R // 128
    CT = min(C, 1024)
    n_ct = C // CT
    bufs = [sb(name + "_w0", [128, CT], F32), sb(name + "_w1", [128, CT], F32)]
    accv = sb(name + "_accv", [128, 1], F32)
    red = sb(name + "_red", [128, 2], F32)
    ones = sb(name + "_ones", [128, 1], F32)
    onesrow = sb(name + "_onesrow", [1, 128], F32)
    gcol = sb(name + "_gcol", [128, 2], F32)
    qts = [sb(name + "_qt0", [128, CT], BF16), sb(name + "_qt1", [128, CT], BF16)]
    qneg = sb(name + "_qneg", [128, CT], F32)
    pq = em.pscratch()

    t_ones = em.op("dve", lambda e: e.memset(ones[:], 1.0))
    t_or = em.op("dve", lambda e: e.memset(onesrow[:], 1.0))
    t_acc = em.op("dve", lambda e: e.memset(accv[:], 0.0))
    tiles = [(rc, ct) for rc in range(n_rc) for ct in range(n_ct)]
    buf_free = [None, None]
    for i, (rc, ct) in enumerate(tiles):
        b = i % 2
        t_ld = em.dma("sp", f"{name}_ld{b}", lambda e, rc=rc, ct=ct, b=b: e.dma_start(
            out=bufs[b][:], in_=src[rc * 128:(rc + 1) * 128, ct * CT:(ct + 1) * CT]),
            after=[buf_free[b]])
        t_r = em.op("dve", lambda e, b=b: e.reduce_sum(red[:, 0:1], bufs[b][:], axis=X,
                                                       apply_absolute_value=True),
                    after=[t_ld, t_acc])
        t_acc = em.op("dve", lambda e: e.tensor_add(accv[:], accv[:], red[:, 0:1]), after=[t_r])
        buf_free[b] = t_acc
    t_tot = em.op("pe", lambda e: e.matmul(pq[0:1, 0:1], accv[:], ones[:], start=True, stop=True),
                  after=[t_acc, t_ones])
    t_g = em.op("dve", lambda e: e.tensor_scalar(red[0:1, 0:1], pq[0:1, 0:1], 0.5 / (R * C), 0.5e-6,
                                                 ALU.mult, ALU.add), after=[t_tot])
    t_g2 = em.op("dve", lambda e: e.tensor_scalar(red[0:1, 1:2], red[0:1, 0:1], -1.0, None, ALU.mult),
                 after=[t_g])
    t_gb = em.op("pe", lambda e: e.matmul(pq[:, 2:4], onesrow[0:1, :], red[0:1, 0:2], start=True, stop=True),
                 after=[t_g2, t_or])
    t_gc = em.op("dve", lambda e: e.tensor_copy(gcol[:], pq[:, 2:4]), after=[t_gb])

    last_st = None
    qfree = [None, None]
    for i, (rc, ct) in enumerate(tiles):
        b = i % 2
        t_ld2 = em.dma("sp", f"{name}_ld{b}", lambda e, rc=rc, ct=ct, b=b: e.dma_start(
            out=bufs[b][:], in_=src[rc * 128:(rc + 1) * 128, ct * CT:(ct + 1) * CT]),
            after=[buf_free[b]])
        tq1 = em.op("dve", lambda e, b=b: e.tensor_scalar(qneg[:], bufs[b][:], gcol[:, 1:2], None, ALU.is_lt),
                    after=[t_ld2, t_gc, qfree[b]])
        tq2 = em.op("dve", lambda e, b=b: e.tensor_scalar(qts[b][:], bufs[b][:], gcol[:, 0:1], None, ALU.is_gt),
                    after=[tq1])
        buf_free[b] = tq2
        tq3 = em.op("dve", lambda e, b=b: e.tensor_sub(qts[b][:], qts[b][:], qneg[:]), after=[tq2])
        last_st = em.dma("sp", name + "_qst", lambda e, rc=rc, ct=ct, b=b: e.dma_start(
            out=dst[rc * 128:(rc + 1) * 128, ct * CT:(ct + 1) * CT], in_=qts[b][:]), after=[tq3])
        qfree[b] = last_st
    em.wait("sp", [last_st])
    return last_st


def emit_ffn(nc, em, cfg, xn, wfcq, wprq, lnw, lnb, res, out, cst, after_w=(), name="ffn"):
    T, D, Hh = cfg["T"], cfg["D"], cfg["Hh"]
    EPS_COL = cfg["eps_col"]
    n_dc = D // 128
    n_hc = Hh // 128
    TT = min(512, T)
    n_tt = T // TT
    n_tc = TT // 128
    NDW = min(512, D)
    n_nd = D // NDW

    sb, ps = em.sb, em.ps
    xnT = sb(name + "_xnT", [128, n_dc * TT], BF16)
    hbf = sb(name + "_hbf", [128, n_hc * TT], BF16)
    hsq = sb(name + "_hsq", [128, TT], F32)
    wfc = [sb(f"{name}_wfc{i}", [128, n_dc * 128], BF16) for i in range(3)]
    wprf = sb(name + "_wprf", [128, n_hc * D], BF16)
    lnwc = sb(name + "_lnwc", [128, n_hc], F32)
    lnbc = sb(name + "_lnbc", [128, n_hc], F32)
    onescol = sb(name + "_onescol", [128, 1], F32)
    onescb = sb(name + "_onescb", [128, 1], BF16)
    onesrow = sb(name + "_onesrow", [1, 128], F32)
    ident = sb(name + "_ident", [128, 128], F32)
    c12 = sb(name + "_c12", [128, 2 * D], F32)
    c12r = sb(name + "_c12r", [2, D], F32)
    lcol = sb(name + "_lcol", [128, 2], BF16)
    statr = sb(name + "_statr", [1, 2 * TT], F32)
    statc = sb(name + "_statc", [128, 8], F32)
    rcols = sb(name + "_rcols", [128, 2 * n_tc], F32)
    btmp = sb(name + "_btmp", [128, D], F32)
    rest = sb(name + "_rest", [128, n_tc * D], F32)
    btmp2 = sb(name + "_btmp2", [128, D], F32)

    pA = [ps(name + "_pA0", [128, TT], F32), ps(name + "_pA1", [128, TT], F32)]
    pB = [ps(name + "_pB0", [128, D], F32)]
    pS1 = ps(name + "_pS1", [1, 512], F32)
    pS2 = ps(name + "_pS2", [1, 512], F32)
    pT = em.pscratch()
    pfree = {}

    t_ones = em.op("dve", lambda e: e.memset(onescol[:], 1.0))
    t_oneb = em.op("dve", lambda e: e.memset(onescb[:], 1.0))
    t_or = em.op("dve", lambda e: e.memset(onesrow[:], 1.0))
    t_id = em.dma("sp", name + "_id", lambda e: e.dma_start(out=ident[:], in_=cfg["ident_dram"]))
    t_lnw = em.dma("sp", name + "_lnw", lambda e: e.dma_start(
        out=lnwc[:], in_=lnw.rearrange("(c p) -> p c", p=128), allow_slow_non_contiguous=True))
    t_lnb = em.dma("sp", name + "_lnb", lambda e: e.dma_start(
        out=lnbc[:], in_=lnb.rearrange("(c p) -> p c", p=128), allow_slow_non_contiguous=True))

    t_wf = em.dma("pool", name + "_wprf", lambda e: e.dma_start(
        out=wprf[:].rearrange("p (c d) -> p c d", c=n_hc),
        in_=wprq.rearrange("(c p) d -> p c d", p=128)), after=list(after_w))
    t_cmm = None
    for hc in range(n_hc):
        t_l2 = em.op("dve", lambda e, hc=hc: e.tensor_copy(lcol[:, 0:1], lnwc[:, hc:hc + 1]),
                     after=[t_lnw, t_cmm])
        t_l3 = em.op("dve", lambda e, hc=hc: e.tensor_copy(lcol[:, 1:2], lnbc[:, hc:hc + 1]),
                     after=[t_lnb, t_l2])
        for nd in range(n_nd):
            t_cmm = em.op("pe", lambda e, nd=nd, hc=hc: e.matmul(
                pB[0][0:2, nd * NDW:(nd + 1) * NDW],
                lcol[:, 0:2], wprf[:, hc * D + nd * NDW: hc * D + (nd + 1) * NDW],
                start=(hc == 0), stop=(hc == n_hc - 1),
                skip_group_check=True),
                after=[t_wf, t_l3])
    t_wpl = None
    for hc in range(n_hc):
        t_wpl = em.op("dve", lambda e, hc=hc: e.tensor_scalar(
            wprf[:, hc * D:(hc + 1) * D], wprf[:, hc * D:(hc + 1) * D],
            lnwc[:, hc:hc + 1], None, ALU.mult), after=[t_wf, t_lnw, t_cmm])
    t_ce = em.op("dve", lambda e: e.tensor_copy(c12r[:], pB[0][0:2, 0:D]), after=[t_cmm])
    sel2 = sb(name + "_sel2", [2, 256], F32)
    t_sel = em.dma("sp", name + "_sel", lambda e: e.dma_start(out=sel2[:], in_=cfg["sel2_dram"]))
    t_cb = None
    for nd in range(n_nd):
        t_cb = em.op("pe", lambda e, nd=nd: e.matmul(
            pB[0][:, nd * NDW:(nd + 1) * NDW],
            sel2[:, 0:128], c12r[0:2, nd * NDW:(nd + 1) * NDW],
            start=True, stop=True, skip_group_check=True),
            after=[t_ce, t_sel])
    t_c1e = em.op("dve", lambda e: e.tensor_copy(c12[:, 0:D], pB[0][:, 0:D]), after=[t_cb])
    t_cb2 = None
    for nd in range(n_nd):
        t_cb2 = em.op("pe", lambda e, nd=nd: e.matmul(
            pB[0][:, nd * NDW:(nd + 1) * NDW],
            sel2[:, 128:256], c12r[0:2, nd * NDW:(nd + 1) * NDW],
            start=True, stop=True, skip_group_check=True),
            after=[t_c1e, t_sel])
    t_c2e = em.op("dve", lambda e: e.tensor_copy(c12[:, D:2 * D], pB[0][:, 0:D]), after=[t_cb2])
    pfree["pB0"] = t_c2e

    wfc_tok = {}
    wfc_use = {}
    state = {"next": 0}

    def ensure_wfc(upto):
        while state["next"] <= min(upto, n_hc * n_tt - 1):
            i = state["next"]
            hc = i % n_hc
            slot = i % 3
            prev = wfc_use.get(i - 3)
            tok = em.dma("pool", f"{name}_wfc{slot}", lambda e, hc=hc, slot=slot: e.dma_start(
                out=wfc[slot][:].rearrange("p (c h) -> p c h", c=n_dc),
                in_=wfcq[:, hc * 128:(hc + 1) * 128].rearrange("(c p) h -> p c h", p=128)),
                after=[prev] + list(after_w))
            wfc_tok[i] = (slot, tok)
            state["next"] = i + 1

    t_xnT_prev = []
    t_hbf_free = None
    t_statr_free = None
    last_store = None
    for tt in range(n_tt):
        for dc in range(n_dc):
            t_xnT = em.dma("sp", name + "_xnT", lambda e, tt=tt, dc=dc: e.dma_start(
                out=xnT[:, dc * TT:(dc + 1) * TT],
                in_=xn[tt * TT:(tt + 1) * TT, dc * 128:(dc + 1) * 128],
                transpose=True), after=t_xnT_prev)
        t_res = em.dma("sp", name + "_res", lambda e, tt=tt: e.dma_start(
            out=rest[:].rearrange("p (c d) -> p c d", c=n_tc),
            in_=res[tt * TT:(tt + 1) * TT, :].rearrange("(c p) d -> p c d", p=128)),
            after=[last_store])

        tm1 = tm2 = None
        for hc in range(n_hc):
            ensure_wfc(tt * n_hc + hc + 2)
            pha = pA[hc % 2]
            pha_nm = f"pA{hc % 2}"
            slot, wtok = wfc_tok[tt * n_hc + hc]
            mm = None
            for dc in range(n_dc):
                mm = em.op("pe", lambda e, hc=hc, dc=dc, slot=slot, pha=pha: e.matmul(
                    pha[:, 0:TT],
                    wfc[slot][:, dc * 128:(dc + 1) * 128],
                    xnT[:, dc * TT:(dc + 1) * TT],
                    start=(dc == 0), stop=(dc == n_dc - 1)),
                    after=[wtok, t_xnT, pfree.get(pha_nm) if dc == 0 else None])
            wfc_use[tt * n_hc + hc] = mm
            tg = em.op("act", lambda e, hc=hc, pha=pha: e.activation(
                hbf[:, hc * TT:(hc + 1) * TT], pha[:, 0:TT], AF.Gelu),
                after=[mm, t_hbf_free])
            pfree[pha_nm] = tg
            tsq = em.op("dve", lambda e, hc=hc: e.tensor_mul(
                hsq[:, 0:TT], hbf[:, hc * TT:(hc + 1) * TT],
                hbf[:, hc * TT:(hc + 1) * TT]),
                after=[tg, pfree.get("hsq")])
            tm1 = em.op("pe", lambda e, hc=hc: e.matmul(
                pS1[0:1, 0:TT], onescb[:], hbf[:, hc * TT:(hc + 1) * TT],
                start=(hc == 0), stop=(hc == n_hc - 1), skip_group_check=True),
                after=[tg, t_oneb, pfree.get("pS") if hc == 0 else None])
            tm2 = em.op("pe", lambda e, hc=hc: e.matmul(
                pS2[0:1, 0:TT], onescol[:], hsq[:, 0:TT],
                start=(hc == 0), stop=(hc == n_hc - 1), skip_group_check=True),
                after=[tsq, t_ones])
            pfree["hsq"] = tm2
        t_st0 = em.op("dve", lambda e: e.tensor_copy(statr[:, 0:TT], pS1[0:1, 0:TT]),
                      after=[tm1, t_statr_free])
        t_st = em.op("dve", lambda e: e.tensor_copy(statr[:, TT:2 * TT], pS2[0:1, 0:TT]),
                     after=[tm2, t_st0])
        pfree["pS"] = t_st
        t_rc = []
        for tc in range(n_tc):
            ttr0 = em.op("pe", lambda e, tc=tc: e.transpose(
                pT[:, 0:1], statr[0:1, tc * 128:(tc + 1) * 128], ident[0:1, 0:1]),
                after=[t_st, pfree.get("pT"), t_id])
            ttr = em.op("pe", lambda e, tc=tc: e.transpose(
                pT[:, 1:2], statr[0:1, TT + tc * 128: TT + (tc + 1) * 128], ident[0:1, 0:1]),
                after=[ttr0])
            a1 = em.op("dve", lambda e: e.tensor_scalar(statc[:, 0:2], pT[:, 0:2], 1.0 / Hh, None, ALU.mult),
                       after=[ttr] + (t_rc[-1:] if t_rc else []))
            pfree["pT"] = a1
            a2 = em.op("dve", lambda e: e.tensor_mul(statc[:, 2:3], statc[:, 0:1], statc[:, 0:1]),
                       after=[a1])
            a3 = em.op("dve", lambda e: e.tensor_sub(statc[:, 3:4], statc[:, 1:2], statc[:, 2:3]),
                       after=[a2])
            a4 = em.op("act", lambda e: e.activation(statc[:, 4:5], statc[:, 3:4], AF.Sqrt,
                                                     bias=cst[:, EPS_COL:EPS_COL + 1]),
                       after=[a3, cfg.get("cst_tok")])
            a5 = em.op("dve", lambda e, tc=tc: e.reciprocal(rcols[:, 2 * tc:2 * tc + 1], statc[:, 4:5]),
                       after=[a4])
            a6 = em.op("dve", lambda e, tc=tc: e.tensor_scalar(
                rcols[:, 2 * tc + 1:2 * tc + 2], statc[:, 0:1], rcols[:, 2 * tc:2 * tc + 1], -1.0,
                ALU.mult, ALU.mult), after=[a5])
            t_rc.append(a6)
        ev_toks = []
        for tc in range(n_tc):
            pb = pB[0]
            pb_nm = "pB0"
            mmp = None
            for hc in range(n_hc):
                for nd in range(n_nd):
                    mmp = em.op("pe", lambda e, hc=hc, tc=tc, nd=nd, pb=pb: e.matmul(
                        pb[:, nd * NDW:(nd + 1) * NDW],
                        hbf[:, hc * TT + tc * 128: hc * TT + (tc + 1) * 128],
                        wprf[:, hc * D + nd * NDW: hc * D + (nd + 1) * NDW],
                        start=(hc == 0), stop=(hc == n_hc - 1)),
                        after=[t_wpl, tg, pfree.get(pb_nm) if (hc == 0 and nd == 0) else None])
            tb1 = em.op("dve", lambda e, tc=tc: e.scalar_tensor_tensor(
                btmp[:], c12[:, 0:D], rcols[:, 2 * tc + 1:2 * tc + 2], c12[:, D:2 * D],
                ALU.mult, ALU.add), after=[t_rc[tc], t_c1e, t_c2e] + ([ev_toks[-1]] if ev_toks else []))
            tb2 = em.op("dve", lambda e, tc=tc, pb=pb: e.scalar_tensor_tensor(
                btmp2[:], pb[:, 0:D], rcols[:, 2 * tc:2 * tc + 1], btmp[:],
                ALU.mult, ALU.add), after=[mmp, tb1, last_store])
            pfree[pb_nm] = tb2
            tb3 = em.op("dve", lambda e, tc=tc: e.tensor_add(
                rest[:, tc * D:(tc + 1) * D], rest[:, tc * D:(tc + 1) * D], btmp2[:]),
                after=[tb2, t_res])
            ev_toks.append(tb3)
        last_store = em.dma("sp", name + "_st", lambda e, tt=tt: e.dma_start(
            out=out[tt * TT:(tt + 1) * TT, :].rearrange("(c p) d -> p c d", p=128),
            in_=rest[:].rearrange("p (c d) -> p c d", c=n_tc)), after=ev_toks)
        t_xnT_prev = [tm1]
        t_hbf_free = ev_toks[-1]
        t_statr_free = t_rc[-1]
    em.wait("sp", [last_store])
    return last_store



"""MLA-style attention pass (L1 attn), causal, per-sequence.

q = [c_Q@W_UQ | (c_Q@W_QR_pad)*scale]; k = [c_KV@W_UK | (xn@W_KR)*scale bcast];
v = c_KV@W_UV;  att = softmax(q.k/sqrt(96)) causal;  out = res + (att@v)@W_O.

Normalizer factored: P = exp(St*scl [+mask]), l = colsum(P), y = (V^T@P)*(1/l).
Host supplies:
  wqr_pad [DCH, H*64] f32: W_QR with each head's 32 cols zero-padded to 64.
  scale_dup [H*64, T2] f32: row (h*64+j) = scaleT[j, t] for j<32 else 0.
  masks [4, 128, 512] f32 additive causal masks for partial tiles.
Structure per sequence b: for ti: projections(ti) -> per-head attention(qt=ti)
over k-tiles 0..ti -> output tile. K/V persist per b; q transient per ti.
cst: consts sbuf col SCL_COL holds 1/sqrt(96).
"""

F32 = mybir.dt.float32
BF16 = mybir.dt.bfloat16
AF = mybir.ActivationFunctionType
ALU = mybir.AluOpType

SCL = 1.0 / (96.0 ** 0.5)


def emit_attn(nc, em, cfg, xn, wdq, wuq, wdkv, wuk, wuv, wqr_pad, wkr, wo,
              scale_dup, masks, res, out, cst, name="att"):
    T, D, NB, H = cfg["T"], cfg["D"], cfg["NB"], cfg["H"]
    T2 = T // NB
    DCH = cfg["DCH"]
    DKV = cfg["DKV"]
    SCL_COL = cfg["scl_col"]
    n_dc = D // 128
    n_qc = DCH // 128
    n_kv = DKV // 128
    n_hc = H * 64 // 128
    QT = min(512, T2)
    n_qt = T2 // QT
    n_tc = QT // 128
    PQ = QT
    n_pq = QT // PQ
    n_tcp = PQ // 128
    NDW = min(512, D)
    n_nd = D // NDW

    sb, ps = em.sb, em.ps
    xnT = sb(name + "_xnT", [128, n_dc * PQ], BF16)
    cqT = sb(name + "_cqT", [128, n_qc * PQ], BF16)
    ckvT = sb(name + "_ckvT", [128, n_kv * PQ], BF16)
    qCt = sb(name + "_qCt", [128, n_hc * QT], BF16)
    qRt = sb(name + "_qRt", [128, n_hc * QT], BF16)
    kC = sb(name + "_kC", [128, n_hc * T2], BF16)
    kR = sb(name + "_kR", [128, T2], BF16)
    VS = 65
    vT = sb(name + "_vT", [128, (T2 // 128) * H * VS], BF16)
    yTt = sb(name + "_yTt", [128, n_hc * QT], BF16)
    sclt = sb(name + "_sclt", [128, n_hc * QT], BF16)
    sclk = sb(name + "_sclk", [64, T2], BF16)
    MEXT = QT + (n_tc - 1) * 128
    msk = sb(name + "_msk", [128, MEXT], F32)
    sexp = sb(name + "_sexp", [128, QT], F32)
    pbf = [sb(f"{name}_pbf{i}", [128, QT], BF16) for i in range(2)]
    srow = sb(name + "_srow", [1, 2 * QT], F32)
    grow = sb(name + "_grow", [1, QT], BF16)
    gbuf = sb(name + "_gb", [128, QT], F32)
    onescb = sb(name + "_onescb", [128, 1], BF16)
    onesrb = sb(name + "_onesrb", [1, 128], BF16)
    outt = sb(name + "_outt", [128, D], F32)
    wu_dq = [sb(f"{name}_wdq{i}", [128, n_dc * 128], BF16) for i in range(3)]
    wu_uqr = [sb(f"{name}_wuqr{i}", [128, n_qc * 128], BF16) for i in range(2)]
    w_dkv = sb(name + "_wdkv", [128, n_dc * DKV], BF16)
    w_uk = sb(name + "_wuk", [128, n_kv * D], BF16)
    w_uv = sb(name + "_wuv", [128, n_kv * D], BF16)
    w_kr = sb(name + "_wkr", [128, n_dc * 32], BF16)
    w_o = [sb(f"{name}_wo{i}", [128, D], BF16) for i in range(3)]

    pS = [ps(name + "_pS0", [128, 512], F32), ps(name + "_pS1", [128, 512], F32),
          ps(name + "_pS2", [128, 512], F32)]
    pP0 = ps(name + "_pP0", [128, 512], F32)
    pY = ps(name + "_pY", [128, 512], F32)
    pO = ps(name + "_pO", [128, D], F32)
    pfree = {}
    t_free = {}

    t_ones = em.op("dve", lambda e: e.memset(onescb[:], 1.0))
    t_onr = em.op("dve", lambda e: e.memset(onesrb[:], 1.0))
    t_vones = em.op("dve", lambda e: e.memset(vT[:], 1.0))
    t_sclk = em.dma("pool", name + "_sclk", lambda e: e.dma_start(
        out=sclk[:], in_=scale_dup[0:64, :]))
    t_msk = em.dma("sp", name + "_msk", lambda e: e.dma_start(out=msk[:], in_=masks))
    t_wdkv = em.dma("pool", name + "_wdkv", lambda e: e.dma_start(
        out=w_dkv[:].rearrange("p (c n) -> p c n", c=n_dc),
        in_=wdkv.rearrange("(c p) n -> p c n", p=128)))
    t_wuk = em.dma("pool", name + "_wuk", lambda e: e.dma_start(
        out=w_uk[:].rearrange("p (c n) -> p c n", c=n_kv),
        in_=wuk.rearrange("(c p) n -> p c n", p=128)))
    t_wuv = em.dma("pool", name + "_wuv", lambda e: e.dma_start(
        out=w_uv[:].rearrange("p (c n) -> p c n", c=n_kv),
        in_=wuv.rearrange("(c p) n -> p c n", p=128)))
    t_wkr = em.dma("pool", name + "_wkr", lambda e: e.dma_start(
        out=w_kr[:].rearrange("p (c n) -> p c n", c=n_dc),
        in_=wkr.rearrange("(c p) n -> p c n", p=128)))
    wo_use = {}

    def mk_stream(wdram, n_k, n_oc, slots, tag):
        toks, uses, st = {}, {}, {"next": 0}

        def ensure(upto, total):
            while st["next"] <= min(upto, total - 1):
                i = st["next"]
                oc = i % n_oc
                slot = i % len(slots)
                prev = uses.get(i - len(slots))
                tok = em.dma("pool", f"{name}_{tag}{slot}", lambda e, oc=oc, slot=slot: e.dma_start(
                    out=slots[slot][:].rearrange("p (c n) -> p c n", c=n_k),
                    in_=wdram[:, oc * 128:(oc + 1) * 128].rearrange("(c p) n -> p c n", p=128)),
                    after=[prev])
                toks[i] = (slot, tok)
                st["next"] = i + 1
        return toks, uses, ensure

    dq_toks, dq_uses, dq_ensure = mk_stream(wdq, n_dc, n_qc, wu_dq, "sdq")

    last_store = None
    for b in range(NB):
        t0 = b * T2
        for ti in range(n_qt):
            trng = (t0 + ti * QT, t0 + (ti + 1) * QT)
            uqr_tok = {}
            for sub in range(n_pq):
                ts0 = trng[0] + sub * PQ
                for dc in range(n_dc):
                    t_xnT = em.dma("sp", name + "_xnT", lambda e, ts0=ts0, dc=dc: e.dma_start(
                        out=xnT[:, dc * PQ:(dc + 1) * PQ],
                        in_=xn[ts0:ts0 + PQ, dc * 128:(dc + 1) * 128],
                        transpose=True), after=[t_free.get("xnT")])
                if sub == 0:
                    t_sclt = em.dma("pool", name + "_sclt", lambda e, ti=ti: e.dma_start(
                        out=sclt[:].rearrange("p (c t) -> p c t", c=n_hc),
                        in_=scale_dup[:, ti * QT:(ti + 1) * QT].rearrange("(c p) t -> p c t", p=128)),
                        after=[t_free.get("sclt")])
                base_q = ((b * n_qt + ti) * n_pq + sub) * n_qc
                t_cq = None
                for oc in range(n_qc):
                    dq_ensure(base_q + oc + 2, NB * n_qt * n_pq * n_qc)
                    slot, wtok = dq_toks[base_q + oc]
                    mm = None
                    for dc in range(n_dc):
                        mm = em.op("pe", lambda e, oc=oc, dc=dc, slot=slot: e.matmul(
                            pS[oc % 3][:, 0:PQ],
                            wu_dq[slot][:, dc * 128:(dc + 1) * 128],
                            xnT[:, dc * PQ:(dc + 1) * PQ],
                            start=(dc == 0), stop=(dc == n_dc - 1)),
                            after=[wtok, t_xnT, pfree.get(f"pS{oc % 3}") if dc == 0 else None])
                    dq_uses[base_q + oc] = mm
                    t_cq = em.op("dve", lambda e, oc=oc: e.tensor_copy(
                        cqT[:, oc * PQ:(oc + 1) * PQ], pS[oc % 3][:, 0:PQ]),
                        after=[mm, t_free.get("cqT")])
                    pfree[f"pS{oc % 3}"] = t_cq
                for fam in range(2):
                    wsrc = wuq if fam == 0 else wqr_pad
                    for oc in range(n_hc):
                        ui = fam * n_hc + oc
                        slot = ui % 2
                        prev = uqr_tok.get(("use", ui - 3))
                        if ("ld", ui) not in uqr_tok or sub == 0:
                            pass
                        if sub == 0:
                            wtok = em.dma("pool", f"{name}_suqr{slot}", lambda e, oc=oc, slot=slot, wsrc=wsrc: e.dma_start(
                                out=wu_uqr[slot][:].rearrange("p (c n) -> p c n", c=n_qc),
                                in_=wsrc[:, oc * 128:(oc + 1) * 128].rearrange("(c p) n -> p c n", p=128)),
                                after=[uqr_tok.get(("use", ui - 2)), t_free.get(f"uqr{slot}")])
                            uqr_tok[("ld", ui)] = wtok
                        else:
                            wtok = uqr_tok[("ld", ui)]
                        mm = None
                        for kc in range(n_qc):
                            mm = em.op("pe", lambda e, oc=oc, kc=kc, slot=slot: e.matmul(
                                pS[oc % 3][:, 0:PQ],
                                wu_uqr[slot][:, kc * 128:(kc + 1) * 128],
                                cqT[:, kc * PQ:(kc + 1) * PQ],
                                start=(kc == 0), stop=(kc == n_qc - 1)),
                                after=[wtok, t_cq, pfree.get(f"pS{oc % 3}") if kc == 0 else None])
                        uqr_tok[("use", ui)] = mm
                        if fam == 0:
                            tev = em.op("dve", lambda e, oc=oc, sub=sub: e.tensor_copy(
                                qCt[:, oc * QT + sub * PQ: oc * QT + sub * PQ + PQ],
                                pS[oc % 3][:, 0:PQ]),
                                after=[mm, t_free.get("qCt")])
                        else:
                            tev = em.op("dve", lambda e, oc=oc, sub=sub: e.tensor_tensor(
                                qRt[:, oc * QT + sub * PQ: oc * QT + sub * PQ + PQ],
                                pS[oc % 3][:, 0:PQ],
                                sclt[:, oc * QT + sub * PQ: oc * QT + sub * PQ + PQ], ALU.mult),
                                after=[mm, t_sclt, t_free.get("qRt")])
                        pfree[f"pS{oc % 3}"] = tev
                t_ckv = None
                for oc in range(n_kv):
                    mm = None
                    for dc in range(n_dc):
                        mm = em.op("pe", lambda e, oc=oc, dc=dc: e.matmul(
                            pP0[:, 0:PQ],
                            w_dkv[:, dc * DKV + oc * 128: dc * DKV + (oc + 1) * 128],
                            xnT[:, dc * PQ:(dc + 1) * PQ],
                            start=(dc == 0), stop=(dc == n_dc - 1)),
                            after=[t_wdkv, t_xnT, pfree.get("pP0") if dc == 0 else None])
                    t_ckv = em.op("dve", lambda e, oc=oc: e.tensor_copy(
                        ckvT[:, oc * PQ:(oc + 1) * PQ], pP0[:, 0:PQ]),
                        after=[mm, t_free.get("ckvT")])
                    pfree["pP0"] = t_ckv
                for oc in range(n_hc):
                    mm = None
                    for kc in range(n_kv):
                        mm = em.op("pe", lambda e, oc=oc, kc=kc: e.matmul(
                            pS[oc % 3][:, 0:PQ],
                            w_uk[:, kc * D + oc * 128: kc * D + (oc + 1) * 128],
                            ckvT[:, kc * PQ:(kc + 1) * PQ],
                            start=(kc == 0), stop=(kc == n_kv - 1)),
                            after=[t_wuk, t_ckv, pfree.get(f"pS{oc % 3}") if kc == 0 else None])
                    tev = em.op("dve", lambda e, oc=oc, ti=ti, sub=sub: e.tensor_copy(
                        kC[:, oc * T2 + ti * QT + sub * PQ: oc * T2 + ti * QT + sub * PQ + PQ],
                        pS[oc % 3][:, 0:PQ]),
                        after=[mm, t_free.get("kC")])
                    pfree[f"pS{oc % 3}"] = tev
                mm = None
                for dc in range(n_dc):
                    mm = em.op("pe", lambda e, dc=dc: e.matmul(
                        pP0[0:32, 0:PQ],
                        w_kr[:, dc * 32:(dc + 1) * 32],
                        xnT[:, dc * PQ:(dc + 1) * PQ],
                        start=(dc == 0), stop=(dc == n_dc - 1)),
                        after=[t_wkr, t_xnT, pfree.get("pP0")])
                kslice = slice(ti * QT + sub * PQ, ti * QT + sub * PQ + PQ)
                tz = em.op("dve", lambda e, kslice=kslice: e.memset(kR[:, kslice], 0.0),
                           after=[t_free.get("kR")])
                t_kr0 = em.op("dve", lambda e, kslice=kslice: e.tensor_tensor(
                    kR[0:32, kslice], pP0[0:32, 0:PQ], sclk[0:32, kslice], ALU.mult),
                    after=[mm, tz, t_sclk])
                t_kr = em.op("dve", lambda e, kslice=kslice: e.tensor_tensor(
                    kR[64:96, kslice], pP0[0:32, 0:PQ], sclk[0:32, kslice], ALU.mult),
                    after=[t_kr0])
                pfree["pP0"] = t_kr
                t_v = None
                for tc in range(n_tcp):
                    mm = None
                    for kc in range(n_kv):
                        for nd in range(n_nd):
                            mm = em.op("pe", lambda e, tc=tc, kc=kc, nd=nd: e.matmul(
                                pO[:, nd * NDW:(nd + 1) * NDW],
                                ckvT[:, kc * PQ + tc * 128: kc * PQ + (tc + 1) * 128],
                                w_uv[:, kc * D + nd * NDW: kc * D + (nd + 1) * NDW],
                                start=(kc == 0), stop=(kc == n_kv - 1)),
                                after=[t_wuv, t_ckv, pfree.get("pO") if (kc == 0 and nd == 0) else None])
                    vchunk = ti * n_tc + sub * n_tcp + tc
                    t_v = em.op("dve", lambda e, vchunk=vchunk: e.tensor_copy(
                        vT[:, vchunk * H * VS:(vchunk + 1) * H * VS].rearrange(
                            "p (h u) -> p h u", u=VS)[:, :, 0:64],
                        pO[:, 0:D].rearrange("p (h u) -> p h u", u=64)),
                        after=[mm, t_free.get("vT"), t_vones])
                    pfree["pO"] = t_v
                t_free["xnT"] = t_v
                t_free["cqT"] = uqr_tok[("use", 2 * n_hc - 1)]
                t_free["ckvT"] = t_v
            t_proj = t_v


            n_kc = n_tc * (ti + 1)
            y_last = None
            for h in range(H):
                hq = (h * 64) // 128
                hb = (h * 64) % 128
                t_py = None
                te_k = {}
                rot_k = {}

                def emit_py(kc, h=h, n_kc=n_kc):
                    rot = rot_k[kc]
                    return em.op("pe", lambda e, kc=kc, h=h, rot=rot, n_kc=n_kc: e.matmul(
                        pY[0:VS, 0:QT],
                        vT[:, kc * H * VS + h * VS: kc * H * VS + (h + 1) * VS],
                        pbf[rot][:, 0:QT],
                        start=(kc == 0), stop=(kc == n_kc - 1), skip_group_check=True),
                        after=[te_k[kc], pfree.get("pY") if kc == 0 else None])

                for kc in range(n_kc):
                    srot = kc % 3
                    rot = kc % 2
                    rot_k[kc] = rot
                    pSa = pS[srot]
                    m1 = em.op("pe", lambda e, kc=kc, ti=ti, hq=hq, hb=hb, pSa=pSa: e.matmul(
                        pSa[:, 0:QT],
                        kC[hb:hb + 64, hq * T2 + kc * 128: hq * T2 + (kc + 1) * 128],
                        qCt[hb:hb + 64, hq * QT: hq * QT + QT],
                        start=True, stop=False),
                        after=[t_proj, pfree.get(f"pS{srot}")])
                    m2 = em.op("pe", lambda e, kc=kc, hq=hq, hb=hb, pSa=pSa: e.matmul(
                        pSa[:, 0:QT],
                        kR[hb:hb + 64, kc * 128:(kc + 1) * 128],
                        qRt[hb:hb + 64, hq * QT: hq * QT + QT],
                        start=False, stop=True), after=[])
                    if kc > 0:
                        t_py = emit_py(kc - 1)
                        t_free[f"pbf{rot_k[kc - 1]}"] = t_py
                    if kc >= n_tc * ti:
                        p_ = kc - n_tc * ti
                        tm = em.op("dve", lambda e, p_=p_, pSa=pSa: e.scalar_tensor_tensor(
                            sexp[:, 0:QT], pSa[:, 0:QT], SCL, msk[:, (n_tc - 1 - p_) * 128:(n_tc - 1 - p_) * 128 + QT],
                            ALU.mult, ALU.add), after=[m2, t_msk, pfree.get("sexp")])
                        te = em.op("act", lambda e, rot=rot: e.activation(
                            pbf[rot][:, 0:QT], sexp[:, 0:QT], AF.Exp),
                            after=[tm, t_free.get(f"pbf{rot}")])
                        pfree["sexp"] = te
                    else:
                        te = em.op("act", lambda e, rot=rot, pSa=pSa: e.activation(
                            pbf[rot][:, 0:QT], pSa[:, 0:QT], AF.Exp,
                            scale=cst[:, SCL_COL:SCL_COL + 1]),
                            after=[m2, t_free.get(f"pbf{rot}"), cfg.get("cst_tok")])
                    te_k[kc] = te
                    pfree[f"pS{srot}"] = te
                t_py = emit_py(n_kc - 1)
                t_free[f"pbf{rot_k[n_kc - 1]}"] = t_py
                f2 = em.op("dve", lambda e: e.reciprocal(srow[0:1, 0:QT], pY[64:65, 0:QT]),
                           after=[t_py, pfree.get("srow")])
                f3 = em.op("dve", lambda e: e.tensor_copy(grow[0:1, 0:QT], srow[0:1, 0:QT]),
                           after=[f2, pfree.get("grow")])
                pfree["srow"] = f3
                f4 = em.op("pe", lambda e: e.matmul(
                    pP0[:, 0:QT], onesrb[0:1, :], grow[0:1, 0:QT], start=True, stop=True),
                    after=[f3, t_onr, pfree.get("pP0")])
                pfree["grow"] = f4
                f5 = em.op("dve", lambda e: e.tensor_copy(gbuf[:, 0:QT], pP0[:, 0:QT]),
                           after=[f4, pfree.get("gbuf")])
                pfree["pP0"] = f5
                f6 = em.op("dve", lambda e, h=h, hq=hq, hb=hb: e.tensor_tensor(
                    yTt[hb:hb + 64, hq * QT: hq * QT + QT],
                    pY[0:64, 0:QT], gbuf[0:64, 0:QT], ALU.mult),
                    after=[f2, f5, t_free.get("yTt")])
                pfree["pY"] = f6
                pfree["gbuf"] = f6
                y_last = f6
            t_res = em.dma("sp", name + "_res", lambda e, trng=trng: e.dma_start(
                out=outt[:, 0:D],
                in_=res[trng[0]:trng[0] + 128, :]), after=[last_store, pfree.get("outt")])
            o_toks = []
            wo_toks = {}
            for kc in range(n_hc):
                slot = kc % 3
                wo_toks[kc] = em.dma("pool", f"{name}_wo{slot}", lambda e, kc=kc, slot=slot: e.dma_start(
                    out=w_o[slot][:], in_=wo[kc * 128:(kc + 1) * 128, :]),
                    after=[wo_use.get((b, ti, kc - 3) if False else None), wo_use.get(kc - 3)])
            for tc in range(n_tc):
                if tc > 0:
                    t_res = em.dma("sp", name + "_res", lambda e, trng=trng, tc=tc: e.dma_start(
                        out=outt[:, 0:D], in_=res[trng[0] + tc * 128: trng[0] + (tc + 1) * 128, :]),
                        after=[last_store, pfree.get("outt")])
                mm = None
                for kc in range(n_hc):
                    for nd in range(n_nd):
                        mm = em.op("pe", lambda e, kc=kc, nd=nd, tc=tc: e.matmul(
                            pO[:, nd * NDW:(nd + 1) * NDW],
                            yTt[:, kc * QT + tc * 128: kc * QT + (tc + 1) * 128],
                            w_o[kc % 3][:, nd * NDW:(nd + 1) * NDW],
                            start=(kc == 0), stop=(kc == n_hc - 1)),
                            after=[wo_toks[kc], y_last, pfree.get("pO") if (kc == 0 and nd == 0) else None])
                        if tc == n_tc - 1:
                            wo_use[kc] = mm
                tev = em.op("dve", lambda e, tc=tc: e.tensor_add(
                    outt[:], pO[:, 0:D], outt[:]),
                    after=[mm, t_res])
                pfree["pO"] = tev
                o_toks.append(tev)
                last_store = em.dma("sp", name + "_st", lambda e, trng=trng, tc=tc: e.dma_start(
                    out=out[trng[0] + tc * 128: trng[0] + (tc + 1) * 128, :], in_=outt[:]),
                    after=[tev])
                pfree["outt"] = last_store
            t_free["xnT"] = t_proj
            t_free["qCt"] = y_last
            t_free["qRt"] = y_last
            for sl in range(2):
                t_free[f"uqr{sl}"] = uqr_tok[("use", 2 * n_hc - 1)]
            t_free["ckvT"] = t_proj
            t_free["sclt"] = y_last
            t_free["yTt"] = o_toks[-1]
        t_free["kC"] = y_last
        t_free["kR"] = y_last
        t_free["vT"] = y_last
    em.wait("sp", [last_store])
    return last_store



"""Mamba-3 mixer pass (L0 attn branch), chunked SSD formulation, v2.

Two sub-passes (separate Blocks via em scopes):
  A) in_proj streamer: xn @ w_in^T -> DRAM proj [T, 4224] bf16 (z|x|B token-major)
     plus feature-major B^T/C^T -> bT/cT [128, T] bf16 and smallT -> smT [128, T] f32.
  B) per sequence: pass1 (small-family, lambda, theta->r0) then pass2 (chunk scan).

scr dict (internal DRAM): proj, bT, cT, smT, r0d [128, 8*T] bf16,
tld/trd [1, 32*T] f32 (per-b offset b*32*T2).
"""

F32 = mybir.dt.float32
BF16 = mybir.dt.bfloat16
I32 = mybir.dt.int32
AF = mybir.ActivationFunctionType
ALU = mybir.AluOpType

ZC, XC, BRC, CRC, DTC, TRC = 0, 2048, 4096, 4224, 4352, 4416
INV2PI = float(1.0 / (2.0 * 3.14159265358979323846))
PI = 3.14159265358979323846


def emit_mamba(nc, em, cfg, xn, w_inT, w_outT, dtb_col, d_dup, sel32f, sel32b, selh,
               selp4, mask4, identf, identb, res, out, cst, scr, name="mb"):
    T, D, NB = cfg["T"], cfg["D"], cfg["NB"]
    T2 = T // NB
    H, DI = 32, 2048
    ONE_COL, TPI_COL = cfg["one_col"], cfg["tpi_col"]
    n_dc = D // 128
    Q = 128
    n_c = T2 // Q
    GQ = min(2, n_c)
    n_g = n_c // GQ
    GW = GQ * Q
    cstk = cfg["cst_tok"]
    proj, bTd, cTd, smTd, r0d, tld, trd = (
        scr["proj"], scr["bT"], scr["cT"], scr["smT"], scr["r0d"], scr["tld"], scr["trd"])

    sb, ps = em.sb, em.ps
    ddup = sb(name + "_ddup", [128, DI], F32)
    s32f = sb(name + "_s32f", [32, 128], F32)
    s32b = sb(name + "_s32b", [32, 128], BF16)
    shh = sb(name + "_shh", [32, 8 * 128], F32)
    sp4 = sb(name + "_sp4", [4, 128], F32)
    m4 = sb(name + "_m4", [128, 512], F32)
    idf = sb(name + "_idf", [128, 128], F32)
    idb = sb(name + "_idb", [128, 128], BF16)
    dtb = sb(name + "_dtb", [32, 1], F32)
    onesr = sb(name + "_onesr", [1, 128], F32)
    bg_tok = sb(name + "_bgtok", [128, 16 * 96], F32)
    pit_tok = sb(name + "_pittok", [128, 16 * 32], F32)
    arow = sb(name + "_arow", [1, 16 * 32], F32)
    a4 = sb(name + "_a4", [4, 8], F32)
    a8 = sb(name + "_a8", [8, 4], F32)
    s_cos = sb(name + "_scos", [128, 512], F32)
    s_sin = sb(name + "_ssin", [128, 512], F32)
    s_un = sb(name + "_sun", [96, 2048], F32)
    s_cosb = sb(name + "_scosb", [128, 512], BF16)
    s_sinb = sb(name + "_ssinb", [128, 512], BF16)
    s_unb = sb(name + "_sunb", [96, 2048], BF16)
    w_os = [sb(f"{name}_wos{i}", [128, 1024], BF16) for i in range(3)]

    PA = ps(name + "_PA", [128, 512], F32)
    PB = ps(name + "_PB", [128, 512], F32)
    PC = ps(name + "_PC", [128, 512], F32)
    PD = ps(name + "_PD", [128, 512], F32)
    PE1 = ps(name + "_PE1", [128, 512], F32)
    PF = ps(name + "_PF", [128, 512], F32)
    ptrs = [ps(name + "_ptr", [128, 128], BF16), ps(name + "_ptr2", [128, 128], BF16)]
    pfree = {}
    t_free = {}

    em.dma("sp", name + "_cd", lambda e: e.dma_start(out=ddup[:], in_=d_dup))
    em.dma("sp", name + "_cd", lambda e: e.dma_start(out=s32f[:], in_=sel32f))
    em.dma("sp", name + "_cd", lambda e: e.dma_start(out=s32b[:], in_=sel32b))
    em.dma("sp", name + "_cd", lambda e: e.dma_start(out=shh[:], in_=selh))
    em.dma("sp", name + "_cd", lambda e: e.dma_start(out=sp4[:], in_=selp4))
    em.dma("sp", name + "_cd", lambda e: e.dma_start(out=m4[:], in_=mask4))
    em.dma("sp", name + "_cd", lambda e: e.dma_start(out=idf[:], in_=identf))
    em.dma("sp", name + "_cd", lambda e: e.dma_start(out=idb[:], in_=identb))
    t_cd = em.dma("sp", name + "_cd", lambda e: e.dma_start(out=dtb[:], in_=dtb_col))
    t_or = em.op("dve", lambda e: e.memset(onesr[:], 1.0))

    em.push_scope()
    xnT = sb(name + "_xnT", [128, n_dc * 512], BF16)
    wu = [sb(f"{name}_wu{i}", [128, n_dc * 512], BF16) for i in range(3)]
    pbuf = sb(name + "_pbuf", [128, 4 * 4224], BF16)
    fbuf = sb(name + "_fbuf", [128, 2 * 512], BF16)
    fbuf2 = sb(name + "_fbuf2", [128, 512], F32)
    units = [(ZC + 512 * k, 512, "tok") for k in range(4)] + \
            [(XC + 512 * k, 512, "tok") for k in range(4)] + \
            [(BRC, 128, "both"), (CRC, 128, "feat"), (DTC, 128, "feat")]
    n_ttA = T // 512
    wu_tok = {}
    wu_use = {}
    wu_state = {"next": 0}

    def ensure_wu(upto):
        while wu_state["next"] <= min(upto, n_ttA * len(units) - 1):
            i = wu_state["next"]
            ui = i % len(units)
            co, nw, kind = units[ui]
            slot = i % 3
            prev = wu_use.get(i - 3)
            tok = em.dma("pool", f"{name}_wu{slot}", lambda e, co=co, nw=nw, slot=slot: e.dma_start(
                out=wu[slot][:, 0:n_dc * nw].rearrange("p (c n) -> p c n", c=n_dc),
                in_=w_inT[:, co:co + nw].rearrange("(c p) n -> p c n", p=128)),
                after=[prev])
            wu_tok[i] = (slot, tok)
            wu_state["next"] = i + 1

    t_pst = None
    t_fst = None
    for tt in range(n_ttA):
        for dc in range(n_dc):
            t_xn = em.dma("sp", name + "_xnT", lambda e, tt=tt, dc=dc: e.dma_start(
                out=xnT[:, dc * 512:(dc + 1) * 512],
                in_=xn[tt * 512:(tt + 1) * 512, dc * 128:(dc + 1) * 128],
                transpose=True), after=[t_free.get("xnT")])
        ev_toks = []
        for ui, (co, nw, kind) in enumerate(units):
            gi = tt * len(units) + ui
            ensure_wu(gi + 2)
            slot, wtok = wu_tok[gi]
            if kind in ("tok", "both"):
                for tc in range(4):
                    pband = PA if tc % 2 == 0 else PB
                    pnm = "PA" if tc % 2 == 0 else "PB"
                    mm = None
                    for dc in range(n_dc):
                        mm = em.op("pe", lambda e, nw=nw, dc=dc, tc=tc, slot=slot, pband=pband: e.matmul(
                            pband[:, 0:nw],
                            xnT[:, dc * 512 + tc * 128: dc * 512 + (tc + 1) * 128],
                            wu[slot][:, dc * nw:(dc + 1) * nw],
                            start=(dc == 0), stop=(dc == n_dc - 1)),
                            after=[wtok, t_xn, pfree.get(pnm) if dc == 0 else None])
                    tev = em.op("dve", lambda e, co=co, nw=nw, tc=tc, pband=pband: e.tensor_copy(
                        pbuf[:, tc * 4224 + co: tc * 4224 + co + nw], pband[:, 0:nw]),
                        after=[mm, t_pst])
                    pfree[pnm] = tev
                    ev_toks.append(tev)
            if kind in ("feat", "both"):
                mm = None
                for dc in range(n_dc):
                    mm = em.op("pe", lambda e, dc=dc, slot=slot: e.matmul(
                        PC[:, 0:512],
                        wu[slot][:, dc * 128:(dc + 1) * 128],
                        xnT[:, dc * 512:(dc + 1) * 512],
                        start=(dc == 0), stop=(dc == n_dc - 1)),
                        after=[wtok, t_xn, pfree.get("PC") if dc == 0 else None])
                if co == DTC:
                    tev = em.op("dve", lambda e: e.tensor_copy(fbuf2[:], PC[:, 0:512]),
                                after=[mm, t_fst])
                else:
                    fo = 0 if co == BRC else 512
                    tev = em.op("dve", lambda e, fo=fo: e.tensor_copy(
                        fbuf[:, fo:fo + 512], PC[:, 0:512]), after=[mm, t_fst])
                pfree["PC"] = tev
                ev_toks.append(tev)
            wu_use[gi] = ev_toks[-1]
        for tc in range(4):
            t_pst = em.dma("sp", name + "_pst", lambda e, tt=tt, tc=tc: e.dma_start(
                out=proj[tt * 512 + tc * 128: tt * 512 + (tc + 1) * 128, :],
                in_=pbuf[:, tc * 4224:(tc + 1) * 4224]), after=ev_toks)
        em.dma("sp", name + "_fst", lambda e, tt=tt: e.dma_start(
            out=bTd[:, tt * 512:(tt + 1) * 512], in_=fbuf[:, 0:512]), after=ev_toks)
        em.dma("sp", name + "_fst", lambda e, tt=tt: e.dma_start(
            out=cTd[:, tt * 512:(tt + 1) * 512], in_=fbuf[:, 512:1024]), after=ev_toks)
        t_fst = em.dma("sp", name + "_fst", lambda e, tt=tt: e.dma_start(
            out=smTd[:, tt * 512:(tt + 1) * 512], in_=fbuf2[:]), after=ev_toks)
        t_free["xnT"] = ev_toks[-1]
    em.wait("sp", [t_pst, t_fst])
    em.pop_scope()

    wos_tok = {}
    wos_use = {}
    wos_state = {"next": 0}

    def ensure_wos(upto, total):
        while wos_state["next"] <= min(upto, total - 1):
            i = wos_state["next"]
            kc = i % 16
            slot = i % 3
            prev = wos_use.get(i - 3)
            tok = em.dma("pool", f"{name}_wos{slot}", lambda e, kc=kc, slot=slot: e.dma_start(
                out=w_os[slot][:, 0:D], in_=w_outT[kc * 128:(kc + 1) * 128, :]), after=[prev])
            wos_tok[i] = (slot, tok)
            wos_state["next"] = i + 1

    last_store = None
    for b in range(NB):
        t0 = b * T2
        em.push_scope()
        smA = sb(name + f"{b}_smA", [64, T2], F32)
        smB = sb(name + f"{b}_smB", [64, T2], F32)
        dtT = sb(name + f"{b}_dtT", [32, T2], F32)
        aT = sb(name + f"{b}_aTT", [32, T2], F32)
        gaT = sb(name + f"{b}_gaT", [32, T2], F32)
        beT = sb(name + f"{b}_beT", [32, T2], F32)
        linc = sb(name + f"{b}_linc", [32, T2], F32)
        alT = sb(name + f"{b}_alT", [32, T2], F32)
        sp1 = sb(name + f"{b}_sp1", [32, T2], F32)
        sp2 = sb(name + f"{b}_sp2", [32, T2], F32)
        tanT = sb(name + f"{b}_tanT", [32, T2], F32)
        lam = sb(name + f"{b}_lam", [32, T2], F32)
        lneg = sb(name + f"{b}_lneg", [32, T2], F32)
        dT = sb(name + f"{b}_dTT", [32, T2], F32)
        piT = sb(name + f"{b}_piT", [32, T2], F32)
        aab = sb(name + f"{b}_aab", [32, 64], F32)
        on128 = sb(name + f"{b}_on128", [128, T2], F32)
        thb = sb(name + f"{b}_thb", [128, T2], F32)
        thc = sb(name + f"{b}_thc", [128, T2], F32)
        thi = sb(name + f"{b}_thi", [128, 512], I32)
        p1sb = sb(name + f"{b}_p1sb", [128, T2], F32)
        r0bf = sb(name + f"{b}_r0bf", [128, 512], BF16)

        t_o128 = em.op("dve", lambda e: e.memset(on128[:], 1.0))
        t_om = t_o128
        t_smA = em.dma("sp", name + "_smld", lambda e, t0=t0: e.dma_start(
            out=smA[:], in_=smTd[0:64, t0:t0 + T2]))
        t_smB = em.dma("sp", name + "_smld", lambda e, t0=t0: e.dma_start(
            out=smB[:], in_=smTd[64:128, t0:t0 + T2]))

        def softplus(dst, src_sl, bias_col, toks):
            t1 = em.op("dve", lambda e, src_sl=src_sl: e.tensor_copy(sp1[:], src_sl),
                       after=toks + [t_free.get("sp")])
            if bias_col is not None:
                t1 = em.op("dve", lambda e, bias_col=bias_col: e.tensor_scalar(
                    sp1[:], sp1[:], bias_col, None, ALU.add), after=[t1, t_cd])
            t2 = em.op("dve", lambda e: e.tensor_scalar(sp2[:], sp1[:], -1.0, None, ALU.mult), after=[t1])
            t3 = em.op("dve", lambda e: e.tensor_tensor(sp2[:], sp1[:], sp2[:], ALU.min), after=[t2])
            t4 = em.op("act", lambda e: e.activation(sp2[:], sp2[:], AF.Exp), after=[t3])
            t5 = em.op("act", lambda e: e.activation(sp2[:], sp2[:], AF.Ln,
                                                     bias=cst[0:32, ONE_COL:ONE_COL + 1]),
                       after=[t4, cstk])
            t6 = em.op("dve", lambda e: e.tensor_scalar(sp1[:], sp1[:], 0.0, None, ALU.max), after=[t5])
            t7 = em.op("dve", lambda e, dst=dst: e.tensor_add(dst, sp1[:], sp2[:]), after=[t6])
            t_free["sp"] = t7
            return t7

        t_dt = softplus(dtT[:], smA[0:32, :], dtb[:, 0:1], [t_smA])
        t_sa = softplus(aT[:], smA[32:64, :], None, [t_smA])
        t_a = em.op("dve", lambda e: e.tensor_scalar(aT[:], aT[:], -1.0, -1e-4, ALU.mult, ALU.min),
                    after=[t_sa])
        HALF_COL = cfg["half_col"]
        t_tr = em.op("act", lambda e: e.activation(
            sp2[:], smB[0:32, :], AF.Tanh, scale=cst[0:32, HALF_COL:HALF_COL + 1]),
            after=[t_smB, t_free.get("sp"), cstk])
        t_tr = em.op("dve", lambda e: e.tensor_scalar(
            sp2[:], sp2[:], 0.5, 0.5, ALU.mult, ALU.add), after=[t_tr])
        t_th = em.op("act", lambda e: e.activation(tanT[:], smB[32:64, :], AF.Tanh), after=[t_smB])
        t_li = em.op("dve", lambda e: e.tensor_mul(linc[:], aT[:], dtT[:]), after=[t_a, t_dt])
        t_al = em.op("act", lambda e: e.activation(alT[:], linc[:], AF.Exp), after=[t_li])
        t_ga = em.op("dve", lambda e: e.tensor_mul(gaT[:], sp2[:], dtT[:]), after=[t_tr, t_dt])
        t_free["sp"] = t_ga
        t_be = em.op("dve", lambda e: e.tensor_sub(beT[:], dtT[:], gaT[:]), after=[t_ga])
        t_be = em.op("dve", lambda e: e.tensor_mul(beT[:], beT[:], alT[:]), after=[t_be, t_al])
        t_l1 = em.op("dve", lambda e: e.tensor_tensor_scan(lam[:], on128[0:32, :], linc[:], 0.0,
                                                           ALU.mult, ALU.add),
                     after=[t_li, t_om])
        t_l2 = em.op("dve", lambda e: e.memset(sp1[:, 0:128], 0.0), after=[t_dt, t_sa])
        t_l3 = em.op("dve", lambda e: e.tensor_copy(
            sp1[:, 128:T2].rearrange("p (c q) -> p c q", q=128),
            lam[:, 127:T2 - 1:128].unsqueeze(2).broadcast_to((32, n_c - 1, 128))),
            after=[t_l1, t_l2])
        t_l4 = em.op("dve", lambda e: e.tensor_sub(lam[:], lam[:], sp1[:]), after=[t_l3])
        t_d1 = em.op("dve", lambda e: e.tensor_sub(
            dT[:].rearrange("p (c q) -> p c q", q=128),
            lam[:, 127::128].unsqueeze(2).broadcast_to((32, n_c, 128)),
            lam[:].rearrange("p (c q) -> p c q", q=128)), after=[t_l4])
        t_d2 = em.op("act", lambda e: e.activation(dT[:], dT[:], AF.Exp), after=[t_d1])
        t_p1 = em.op("act", lambda e: e.activation(piT[:], lam[:], AF.Exp), after=[t_l4])
        t_ae = em.op("act", lambda e: e.activation(aab[:, 0:n_c], lam[:, 127::128], AF.Exp),
                     after=[t_l4])
        t_at = em.op("pe", lambda e: e.transpose(PA[0:n_c, 0:32], aab[:, 0:n_c], idf[0:32, 0:32]),
                     after=[t_ae, pfree.get("PA")])
        t_at2 = em.op("dve", lambda e: e.tensor_copy(aab[0:n_c, 32:64], PA[0:n_c, 0:32]), after=[t_at])
        pfree["PA"] = t_at2
        t_ar = em.dma("pool", name + "_arow", lambda e: e.dma_start(
            out=arow[0:1, 0:n_c * 32], in_=aab[0:n_c, 32:64]), after=[t_at2, t_free.get("arow")])
        t_ln = em.op("dve", lambda e: e.tensor_scalar(lneg[:], lam[:], -1.0, None, ALU.mult),
                     after=[t_l4])
        t_t3 = em.dma("pool", name + "_tl", lambda e, b=b: e.dma_start(
            out=tld[0:1, b * H * T2:(b + 1) * H * T2], in_=lneg[:]), after=[t_ln])
        t_t4 = em.dma("pool", name + "_tr", lambda e, b=b: e.dma_start(
            out=trd[0:1, b * H * T2:(b + 1) * H * T2], in_=lam[:]), after=[t_l4])
        t_bg = t_free.get("bg_tok")
        t_pit = t_free.get("pit_tok")
        for c in range(n_c):
            specs = [(beT, t_be, bg_tok, c * 96, "bg"), (gaT, t_ga, bg_tok, c * 96 + 32, "bg"),
                     (dT, t_d2, bg_tok, c * 96 + 64, "bg"), (piT, t_p1, pit_tok, c * 32, "pit")]
            for srcT, stok_, dstT, off, kind in specs:
                wtr = em.op("pe", lambda e, srcT=srcT, c=c: e.transpose(
                    PB[:, 0:32], srcT[:, c * 128:(c + 1) * 128], idf[0:32, 0:32]),
                    after=[stok_, pfree.get("PB")])
                tevv = em.op("dve", lambda e, dstT=dstT, off=off: e.tensor_copy(
                    dstT[:, off:off + 32], PB[:, 0:32]),
                    after=[wtr, (t_bg if kind == "bg" else t_pit)])
                pfree["PB"] = tevv
                if kind == "bg":
                    t_bg = tevv
                else:
                    t_pit = tevv
        t_dp = em.op("dve", lambda e: e.tensor_scalar(alT[:], dtT[:], PI, None, ALU.mult),
                     after=[t_be])
        t_p1s = None
        for tt2 in range(T2 // 512):
            mm = em.op("pe", lambda e, tt2=tt2: e.matmul(
                PC[:, 0:512], s32f[:], tanT[:, tt2 * 512:(tt2 + 1) * 512],
                start=True, stop=True),
                after=[t_th, pfree.get("PC")])
            t_p1s = em.op("dve", lambda e, tt2=tt2: e.tensor_copy(
                p1sb[:, tt2 * 512:(tt2 + 1) * 512], PC[:, 0:512]), after=[mm, t_p1s])
            pfree["PC"] = t_p1s
        t_r0 = None
        t_st_r0 = None
        for j in range(8):
            t_i2 = None
            for tt2 in range(T2 // 512):
                mm = em.op("pe", lambda e, j=j, tt2=tt2: e.matmul(
                    PD[:, 0:512], shh[:, j * 128:(j + 1) * 128], alT[:, tt2 * 512:(tt2 + 1) * 512],
                    start=True, stop=True),
                    after=[t_dp, pfree.get("PD")])
                t_i2 = em.op("dve", lambda e, tt2=tt2: e.tensor_mul(
                    thb[:, tt2 * 512:(tt2 + 1) * 512], PD[:, 0:512],
                    p1sb[:, tt2 * 512:(tt2 + 1) * 512]),
                    after=[mm, t_p1s, t_i2, t_free.get("thb")])
                pfree["PD"] = t_i2
            t_s1 = em.op("dve", lambda e: e.tensor_tensor_scan(
                thc[:], on128[:], thb[:], 0.0, ALU.mult, ALU.add),
                after=[t_i2, t_o128, t_free.get("thc")])
            t_s2 = em.op("dve", lambda e: e.tensor_scalar(thb[:], thc[:], INV2PI, None, ALU.mult),
                         after=[t_s1])
            t_free["thb"] = t_s2
            t_s4 = None
            for tt2 in range(T2 // 512):
                t_s3 = em.op("dve", lambda e, tt2=tt2: e.tensor_copy(
                    thi[:], thb[:, tt2 * 512:(tt2 + 1) * 512]),
                    after=[t_s2, t_free.get("thi")])
                t_s4 = em.op("dve", lambda e, tt2=tt2: e.tensor_copy(
                    thc[:, tt2 * 512:(tt2 + 1) * 512], thi[:]), after=[t_s3])
                t_free["thi"] = t_s4
            t_free["thc"] = t_s4
            t_r5 = em.op("dve", lambda e: e.tensor_sub(thb[:], thb[:], thc[:]), after=[t_s4])
            t_r6 = em.op("dve", lambda e: e.tensor_scalar(thc[:], thb[:], 0.5, None, ALU.is_gt),
                         after=[t_r5])
            t_r7 = em.op("dve", lambda e: e.tensor_sub(thb[:], thb[:], thc[:]), after=[t_r6])
            t_r8 = em.op("dve", lambda e: e.tensor_scalar(thc[:], thb[:], -0.5, None, ALU.is_lt),
                         after=[t_r7])
            t_st_r0 = None
            for tt2 in range(T2 // 512):
                t_r0 = em.op("dve", lambda e, tt2=tt2: e.tensor_add(
                    r0bf[:], thb[:, tt2 * 512:(tt2 + 1) * 512],
                    thc[:, tt2 * 512:(tt2 + 1) * 512]),
                    after=[t_r8, t_free.get("r0bf")])
                t_st_r0 = em.dma("sp", name + "_r0st", lambda e, j=j, b=b, tt2=tt2: e.dma_start(
                    out=r0d[:, (b * 8 + j) * T2 + tt2 * 512:(b * 8 + j) * T2 + (tt2 + 1) * 512],
                    in_=r0bf[:]), after=[t_r0])
                t_free["r0bf"] = t_st_r0
            t_free["thb"] = t_r0
        em.wait("sp", [t_st_r0])
        em.wait("pool", [t_t3, t_t4, t_ar])
        em.pop_scope()

        em.push_scope()
        brT = sb(name + f"{b}_brT", [32, GW], BF16)
        bunT = sb(name + f"{b}_bunT", [96, GW], BF16)
        crT = sb(name + f"{b}_crT", [32, GW], BF16)
        cunT = sb(name + f"{b}_cunT", [96, GW], BF16)
        bdup = sb(name + f"{b}_bdup", [128, GW], BF16)
        cdup = sb(name + f"{b}_cdup", [128, GW], BF16)
        r0g = sb(name + f"{b}_r0g", [128, 8 * GW], BF16)
        cosg = sb(name + f"{b}_cosg", [128, 8 * GW], BF16)
        sing = sb(name + f"{b}_sing", [128, 8 * GW], BF16)
        brc = sb(name + f"{b}_brc", [128, 8 * GW], BF16)
        brs = sb(name + f"{b}_brs", [128, 8 * GW], BF16)
        crc = sb(name + f"{b}_crc", [128, 8 * GW], BF16)
        crs = sb(name + f"{b}_crs", [128, 8 * GW], BF16)
        trsc = sb(name + f"{b}_trsc", [128, GW], F32)
        trsc2 = sb(name + f"{b}_trsc2", [128, GW], F32)
        z_bf = sb(name + f"{b}_zbf", [128, DI], BF16)
        x_bf = sb(name + f"{b}_xbf", [128, DI], BF16)
        px_bf = sb(name + f"{b}_pxbf", [128, DI], BF16)
        xlast = sb(name + f"{b}_xlast", [1, DI], BF16)
        b_tok = sb(name + f"{b}_btok", [128, 128], BF16)
        u_f = sb(name + f"{b}_uf", [128, DI], F32)
        u_bf = sb(name + f"{b}_ubf", [128, DI], BF16)
        us_bf = sb(name + f"{b}_usbf", [128, DI], BF16)
        tlc = sb(name + f"{b}_tlc", [1, H * 128], F32)
        trc = sb(name + f"{b}_trc", [1, H * 128], F32)
        ctok = sb(name + f"{b}_ctok", [128, 1024], BF16)
        stok = sb(name + f"{b}_stok", [128, 1024], BF16)
        brctok = sb(name + f"{b}_brctok", [128, 1024], BF16)
        brstok = sb(name + f"{b}_brstok", [128, 1024], BF16)
        mexp = sb(name + f"{b}_mexp", [128, 512], F32)
        mt = sb(name + f"{b}_mt", [128, H * 128], BF16)
        ytok = sb(name + f"{b}_ytok", [128, DI], F32)
        yint = sb(name + f"{b}_yint", [128, 512], F32)
        abc = sb(name + f"{b}_abc", [128, 512], F32)
        abunh = sb(name + f"{b}_abunh", [128, 1024], F32)
        outbf = sb(name + f"{b}_outbf", [128, DI], BF16)
        outT = sb(name + f"{b}_outT", [128, DI], BF16)
        rest = sb(name + f"{b}_rest", [128, D], F32)
        outo = sb(name + f"{b}_outo", [128, D], F32)

        t_sc0 = em.op("dve", lambda e: e.memset(s_cos[:], 0.0), after=[t_free.get("s_cos")])
        t_ss0 = em.op("dve", lambda e: e.memset(s_sin[:], 0.0), after=[t_free.get("s_sin")])
        t_su0 = em.op("dve", lambda e: e.memset(s_un[:], 0.0), after=[t_free.get("s_un")])
        st_tok = {"cos": t_sc0, "sin": t_ss0, "un": t_su0}
        t_xlast = None

        for g in range(n_g):
            gt0 = t0 + g * GW
            em.dma("sp", name + "_fbld", lambda e, gt0=gt0: e.dma_start(
                out=brT[:], in_=bTd[0:32, gt0:gt0 + GW]), after=[t_free.get("fb")])
            em.dma("sp", name + "_fbld", lambda e, gt0=gt0: e.dma_start(
                out=bunT[:], in_=bTd[32:128, gt0:gt0 + GW]))
            em.dma("sp", name + "_fbld", lambda e, gt0=gt0: e.dma_start(
                out=crT[:], in_=cTd[0:32, gt0:gt0 + GW]))
            t_fb = em.dma("sp", name + "_fbld", lambda e, gt0=gt0: e.dma_start(
                out=cunT[:], in_=cTd[32:128, gt0:gt0 + GW]))
            t_r0g = em.dma("sp", name + "_r0ld", lambda e, g=g, b=b: e.dma_start(
                out=r0g[:].rearrange("p (j t) -> p j t", j=8),
                in_=bass.AP(r0d.tensor, b * 8 * T2 + g * GW, [[8 * T, 128], [T2, 8], [1, GW]])),
                after=[t_free.get("r0g")])
            mm = em.op("pe", lambda e: e.matmul(PB[:, 0:GW], s32b[:], brT[:], start=True, stop=True),
                       after=[t_fb, pfree.get("PB")])
            t_bd = em.op("dve", lambda e: e.tensor_copy(bdup[:], PB[:, 0:GW]),
                         after=[mm, t_free.get("bdup")])
            pfree["PB"] = t_bd
            mm = em.op("pe", lambda e: e.matmul(PB[:, 0:GW], s32b[:], crT[:], start=True, stop=True),
                       after=[t_fb, pfree.get("PB")])
            t_cd2 = em.op("dve", lambda e: e.tensor_copy(cdup[:], PB[:, 0:GW]),
                          after=[mm, t_free.get("cdup")])
            pfree["PB"] = t_cd2
            t_pl = None
            ts1 = tc1 = None
            for j in range(8):
                r0sl = r0g[:, j * GW:(j + 1) * GW]
                ts1 = em.op("act", lambda e, j=j, r0sl=r0sl: e.activation(
                    sing[:, j * GW:(j + 1) * GW], r0sl, AF.Sin, scale=cst[:, TPI_COL:TPI_COL + 1]),
                    after=[t_r0g, cstk, t_free.get("sing")])
                tc1 = em.op("dve", lambda e, r0sl=r0sl: e.tensor_scalar(trsc[:], r0sl, 0.25, None, ALU.add),
                            after=[t_r0g, t_free.get("trsc")])
                tc2 = em.op("dve", lambda e: e.tensor_scalar(trsc2[:], trsc[:], 0.5, None, ALU.is_gt),
                            after=[tc1, t_free.get("trsc2")])
                tc3 = em.op("dve", lambda e: e.tensor_sub(trsc[:], trsc[:], trsc2[:]), after=[tc2])
                t_free["trsc2"] = tc3
                tc4 = em.op("act", lambda e, j=j: e.activation(
                    cosg[:, j * GW:(j + 1) * GW], trsc[:], AF.Sin, scale=cst[:, TPI_COL:TPI_COL + 1]),
                    after=[tc3, cstk, t_free.get("cosg")])
                t_free["trsc"] = tc4
                p1 = em.op("dve", lambda e, j=j: e.tensor_mul(
                    brc[:, j * GW:(j + 1) * GW], bdup[:], cosg[:, j * GW:(j + 1) * GW]),
                    after=[t_bd, tc4, t_free.get("brc")])
                p2 = em.op("dve", lambda e, j=j: e.tensor_mul(
                    brs[:, j * GW:(j + 1) * GW], bdup[:], sing[:, j * GW:(j + 1) * GW]),
                    after=[t_bd, ts1, t_free.get("brs")])
                p3 = em.op("dve", lambda e, j=j: e.tensor_mul(
                    crc[:, j * GW:(j + 1) * GW], cdup[:], cosg[:, j * GW:(j + 1) * GW]),
                    after=[t_cd2, tc4, t_free.get("crc")])
                p4 = em.op("dve", lambda e, j=j: e.tensor_mul(
                    crs[:, j * GW:(j + 1) * GW], cdup[:], sing[:, j * GW:(j + 1) * GW]),
                    after=[t_cd2, ts1, t_free.get("crs")])
                t_pl = p4
            t_planes = t_pl

            for c in range(GQ):
                ci = g * GQ + c
                tglob = t0 + ci * Q
                cb1 = em.op("dve", lambda e: e.tensor_copy(s_cosb[:], s_cos[:]), after=[st_tok["cos"]])
                cb2 = em.op("dve", lambda e: e.tensor_copy(s_sinb[:], s_sin[:]), after=[st_tok["sin"]])
                cb3 = em.op("dve", lambda e: e.tensor_copy(s_unb[:], s_un[:]), after=[st_tok["un"]])
                t_z = em.dma("sp", name + "_zld", lambda e, tglob=tglob: e.dma_start(
                    out=z_bf[:], in_=proj[tglob:tglob + 128, ZC:ZC + DI]), after=[t_free.get("z_bf")])
                t_x = em.dma("sp", name + "_xld", lambda e, tglob=tglob: e.dma_start(
                    out=x_bf[:], in_=proj[tglob:tglob + 128, XC:XC + DI]),
                    after=[t_free.get("x_bf"), t_xlast])
                t_b = em.dma("sp", name + "_bld", lambda e, tglob=tglob: e.dma_start(
                    out=b_tok[:], in_=proj[tglob:tglob + 128, BRC:BRC + 128]),
                    after=[t_free.get("b_tok")])
                if ci == 0:
                    t_px0 = em.op("dve", lambda e: e.memset(px_bf[0:1, :], 0.0),
                                  after=[t_free.get("px0")])
                else:
                    t_px0 = em.dma("sp", name + "_px0", lambda e: e.dma_start(
                        out=px_bf[0:1, :], in_=xlast[0:1, :]), after=[t_xlast, t_free.get("px0")])
                t_px1 = em.dma("sp", name + "_px1", lambda e: e.dma_start(
                    out=px_bf[1:128, :], in_=x_bf[0:127, :]), after=[t_x, t_free.get("px1")])
                t_xlast = em.dma("sp", name + "_xl", lambda e: e.dma_start(
                    out=xlast[0:1, :], in_=x_bf[127:128, :]), after=[t_x, t_px0])
                bg = bg_tok[:, ci * 96:(ci + 1) * 96]
                u1 = em.op("dve", lambda e, bg=bg: e.tensor_tensor(
                    u_f[:].rearrange("p (h q) -> p h q", h=32),
                    x_bf[:].rearrange("p (h q) -> p h q", h=32),
                    bg[:, 32:64].unsqueeze(2).broadcast_to((128, 32, 64)), ALU.mult),
                    after=[t_x, t_bg, t_free.get("u_f")])
                u2 = em.op("dve", lambda e, bg=bg: e.tensor_tensor(
                    u_bf[:].rearrange("p (h q) -> p h q", h=32),
                    px_bf[:].rearrange("p (h q) -> p h q", h=32),
                    bg[:, 0:32].unsqueeze(2).broadcast_to((128, 32, 64)), ALU.mult),
                    after=[t_px0, t_px1, t_bg, t_free.get("u_bf")])
                u3 = em.op("dve", lambda e: e.tensor_add(u_f[:], u_f[:], u_bf[:]), after=[u1, u2])
                u4 = em.op("dve", lambda e: e.tensor_copy(u_bf[:], u_f[:]), after=[u3])
                u5 = em.op("dve", lambda e, bg=bg: e.tensor_tensor(
                    us_bf[:].rearrange("p (h q) -> p h q", h=32),
                    u_f[:].rearrange("p (h q) -> p h q", h=32),
                    bg[:, 64:96].unsqueeze(2).broadcast_to((128, 32, 64)), ALU.mult),
                    after=[u3, t_free.get("us_bf")])
                t_free["u_f"] = u5
                t_tl = em.dma("sp", name + "_tlld", lambda e, b=b, ci=ci: e.dma_start(
                    out=tlc[:].rearrange("p (h q) -> p h q", h=H),
                    in_=bass.AP(tld.tensor, b * H * T2 + ci * 128, [[1, 1], [T2, H], [1, 128]])),
                    after=[t_t3, t_free.get("tlc")])
                t_tr2 = em.dma("sp", name + "_trld", lambda e, b=b, ci=ci: e.dma_start(
                    out=trc[:].rearrange("p (h q) -> p h q", h=H),
                    in_=bass.AP(trd.tensor, b * H * T2 + ci * 128, [[1, 1], [T2, H], [1, 128]])),
                    after=[t_t4, t_free.get("trc")])
                t_ctok = t_stok = None
                for srcpl, dsttok, which in [(cosg, ctok, "c"), (sing, stok, "s")]:
                    for j in range(8):
                        pr = ptrs[j % 2]
                        tw0 = em.op("pe", lambda e, j=j, srcpl=srcpl, c=c, pr=pr: e.transpose(
                            pr[:, 0:128],
                            srcpl[:, j * GW + c * 128: j * GW + (c + 1) * 128], idb[:]),
                            after=[t_planes, pfree.get(f"ptr{j % 2}")])
                        te = em.op("dve", lambda e, j=j, dsttok=dsttok, pr=pr: e.tensor_copy(
                            dsttok[:, j * 128:(j + 1) * 128], pr[:, 0:128]),
                            after=[tw0, t_free.get("ctokstok"),
                                   (t_ctok if which == "c" else t_stok)])
                        pfree[f"ptr{j % 2}"] = te
                        if which == "c":
                            t_ctok = te
                        else:
                            t_stok = te
                bb1 = em.op("dve", lambda e: e.tensor_tensor(
                    brctok[:].rearrange("p (h r) -> p h r", h=32),
                    b_tok[:, 0:32].rearrange("p r -> p () r").broadcast_to((128, 32, 32)),
                    ctok[:].rearrange("p (h r) -> p h r", h=32), ALU.mult),
                    after=[t_b, t_ctok, t_free.get("brctok")])
                bb2 = em.op("dve", lambda e: e.tensor_tensor(
                    brstok[:].rearrange("p (h r) -> p h r", h=32),
                    b_tok[:, 0:32].rearrange("p r -> p () r").broadcast_to((128, 32, 32)),
                    stok[:].rearrange("p (h r) -> p h r", h=32), ALU.mult),
                    after=[t_b, t_stok, t_free.get("brstok")])
                t_mt = None
                for w in range(8):
                    gb_ = PA if w % 2 == 0 else PB
                    gb_nm = "PA" if w % 2 == 0 else "PB"
                    db_ = PC if w % 2 == 0 else PD
                    db_nm = "PC" if w % 2 == 0 else "PD"
                    gl = dl = None
                    for k4 in range(4):
                        h = w * 4 + k4
                        j = h // 4
                        hb = (h % 4) * 32
                        gl = em.op("pe", lambda e, j=j, hb=hb, c=c, gb_=gb_, k4=k4: e.matmul(
                            gb_[:, k4 * 128:(k4 + 1) * 128],
                            brc[hb:hb + 32, j * GW + c * 128: j * GW + (c + 1) * 128],
                            crc[hb:hb + 32, j * GW + c * 128: j * GW + (c + 1) * 128],
                            start=True, stop=False, tile_position=(hb, 0), skip_group_check=True),
                            after=[t_planes, pfree.get(gb_nm) if k4 == 0 else None])
                        gl = em.op("pe", lambda e, j=j, hb=hb, c=c, gb_=gb_, k4=k4: e.matmul(
                            gb_[:, k4 * 128:(k4 + 1) * 128],
                            brs[hb:hb + 32, j * GW + c * 128: j * GW + (c + 1) * 128],
                            crs[hb:hb + 32, j * GW + c * 128: j * GW + (c + 1) * 128],
                            start=False, stop=True, tile_position=(hb, 0), skip_group_check=True),
                            after=[])
                        gl = em.op("pe", lambda e, c=c, gb_=gb_, k4=k4: e.matmul(
                            gb_[:, k4 * 128:(k4 + 1) * 128],
                            bunT[0:96, c * 128:(c + 1) * 128],
                            cunT[0:96, c * 128:(c + 1) * 128],
                            start=False, stop=True, skip_group_check=True),
                            after=[t_fb])
                        dl = em.op("pe", lambda e, h=h, db_=db_, k4=k4: e.matmul(
                            db_[:, k4 * 128:(k4 + 1) * 128],
                            tlc[0:1, h * 128:(h + 1) * 128], onesr[0:1, :],
                            start=True, stop=False, skip_group_check=True),
                            after=[t_tl, t_or, pfree.get(db_nm) if k4 == 0 else None])
                        dl = em.op("pe", lambda e, h=h, db_=db_, k4=k4: e.matmul(
                            db_[:, k4 * 128:(k4 + 1) * 128],
                            onesr[0:1, :], trc[0:1, h * 128:(h + 1) * 128],
                            start=False, stop=True, skip_group_check=True),
                            after=[t_tr2])
                    tm1 = em.op("dve", lambda e, db_=db_: e.tensor_tensor(
                        mexp[:], db_[:, 0:512], m4[:], ALU.add),
                        after=[dl, t_free.get("mexp")])
                    pfree[db_nm] = tm1
                    tm2 = em.op("act", lambda e: e.activation(mexp[:], mexp[:], AF.Exp), after=[tm1])
                    tm3 = em.op("dve", lambda e, w=w, gb_=gb_: e.tensor_mul(
                        mt[:, w * 512:(w + 1) * 512], gb_[:, 0:512], mexp[:]),
                        after=[gl, tm2, t_free.get("mt")])
                    pfree[gb_nm] = tm3
                    t_free["mexp"] = tm3
                    t_mt = tm3
                t_ysl = []
                yl = il = None
                for w2 in range(4):
                    yb = PA if w2 % 2 == 0 else PB
                    yb_nm = "PA" if w2 % 2 == 0 else "PB"
                    ib = PC if w2 % 2 == 0 else PD
                    ib_nm = "PC" if w2 % 2 == 0 else "PD"
                    for k8 in range(8):
                        h = w2 * 8 + k8
                        j = h // 4
                        hb = (h % 4) * 32
                        yl = em.op("pe", lambda e, h=h, k8=k8, yb=yb: e.matmul(
                            yb[:, k8 * 64:(k8 + 1) * 64],
                            mt[:, h * 128:(h + 1) * 128],
                            u_bf[:, h * 64:(h + 1) * 64],
                            start=(k8 == 0), stop=(k8 == 7), skip_group_check=True),
                            after=[t_mt, u4, pfree.get(yb_nm) if k8 == 0 else None])
                        il = em.op("pe", lambda e, h=h, j=j, hb=hb, c=c, k8=k8, ib=ib: e.matmul(
                            ib[:, k8 * 64:(k8 + 1) * 64],
                            crc[hb:hb + 32, j * GW + c * 128: j * GW + (c + 1) * 128],
                            s_cosb[hb:hb + 32, (h // 4) * 64:(h // 4) * 64 + 64],
                            start=True, stop=False, tile_position=(hb, 0),
                            skip_group_check=True),
                            after=[t_planes, cb1, pfree.get(ib_nm) if k8 == 0 else None])
                        il = em.op("pe", lambda e, h=h, j=j, hb=hb, c=c, k8=k8, ib=ib: e.matmul(
                            ib[:, k8 * 64:(k8 + 1) * 64],
                            crs[hb:hb + 32, j * GW + c * 128: j * GW + (c + 1) * 128],
                            s_sinb[hb:hb + 32, (h // 4) * 64:(h // 4) * 64 + 64],
                            start=False, stop=True, tile_position=(hb, 0), skip_group_check=True),
                            after=[cb2])
                        il = em.op("pe", lambda e, h=h, c=c, k8=k8, ib=ib: e.matmul(
                            ib[:, k8 * 64:(k8 + 1) * 64],
                            cunT[0:96, c * 128:(c + 1) * 128],
                            s_unb[0:96, h * 64:(h + 1) * 64],
                            start=False, stop=True, skip_group_check=True),
                            after=[cb3])
                    ty1 = em.op("dve", lambda e, w2=w2, ib=ib, ci=ci: e.tensor_tensor(
                        yint[:].rearrange("p (h q) -> p h q", h=8),
                        ib[:, 0:512].rearrange("p (h q) -> p h q", h=8),
                        pit_tok[:, ci * 32 + w2 * 8: ci * 32 + (w2 + 1) * 8].unsqueeze(2).broadcast_to(
                            (128, 8, 64)), ALU.mult),
                        after=[il, t_pit, t_free.get("yint")])
                    pfree[ib_nm] = ty1
                    ty2 = em.op("dve", lambda e, w2=w2, yb=yb: e.tensor_tensor(
                        ytok[:, w2 * 512:(w2 + 1) * 512], yb[:, 0:512], yint[:], ALU.add),
                        after=[yl, ty1, t_free.get("ytok")])
                    pfree[yb_nm] = ty2
                    t_free["yint"] = ty2
                    t_ysl.append(ty2)
                t_y = t_ysl[-1]
                ta1 = em.dma("pool", name + "_a4", lambda e, ci=ci: e.dma_start(
                    out=a8[:], in_=arow[0:1, ci * 32:(ci + 1) * 32]),
                    after=[t_ar, t_free.get("a4"), t_free.get("a8")])
                ta1b = em.op("pe", lambda e: e.transpose(PF[0:4, 0:8], a8[:], idf[0:8, 0:8]),
                             after=[ta1, pfree.get("PF")])
                ta1c = em.op("dve", lambda e: e.tensor_copy(a4[:], PF[0:4, 0:8]),
                             after=[ta1b, t_free.get("a4")])
                pfree["PF"] = ta1c
                t_free["a8"] = ta1b
                mm = em.op("pe", lambda e: e.matmul(
                    PF[:, 0:512], sp4[:],
                    a4[:].rearrange("p q -> p q ()").broadcast_to((4, 8, 64)),
                    start=True, stop=True, skip_group_check=True),
                    after=[ta1c, pfree.get("PF")])
                ta2 = em.op("dve", lambda e: e.tensor_copy(abc[:], PF[:, 0:512]),
                            after=[mm, t_free.get("abc")])
                pfree["PF"] = ta2
                icl = isl = None
                for h in range(H):
                    hb = (h % 4) * 32
                    icl = em.op("pe", lambda e, h=h, hb=hb: e.matmul(
                        PA[hb:hb + 32, (h // 4) * 64:(h // 4) * 64 + 64],
                        brctok[:, h * 32:(h + 1) * 32],
                        us_bf[:, h * 64:(h + 1) * 64],
                        start=True, stop=True, tile_position=(0, hb),
                        skip_group_check=True),
                        after=[bb1, u5, pfree.get("PA") if h == 0 else None])
                    isl = em.op("pe", lambda e, h=h, hb=hb: e.matmul(
                        PB[hb:hb + 32, (h // 4) * 64:(h // 4) * 64 + 64],
                        brstok[:, h * 32:(h + 1) * 32],
                        us_bf[:, h * 64:(h + 1) * 64],
                        start=True, stop=True, tile_position=(0, hb),
                        skip_group_check=True),
                        after=[bb2, u5, pfree.get("PB") if h == 0 else None])
                su1 = em.op("dve", lambda e: e.tensor_mul(s_cos[:], s_cos[:], abc[:]),
                            after=[ta2, cb1, st_tok["cos"]])
                su2 = em.op("dve", lambda e: e.tensor_tensor(s_cos[:], s_cos[:], PA[:, 0:512], ALU.add),
                            after=[su1, icl])
                pfree["PA"] = su2
                st_tok["cos"] = su2
                su3 = em.op("dve", lambda e: e.tensor_mul(s_sin[:], s_sin[:], abc[:]),
                            after=[ta2, cb2, st_tok["sin"]])
                su4 = em.op("dve", lambda e: e.tensor_tensor(s_sin[:], s_sin[:], PB[:, 0:512], ALU.add),
                            after=[su3, isl])
                pfree["PB"] = su4
                st_tok["sin"] = su4
                t_free["abc"] = su4
                sun_last = None
                iul = {}
                tq = None
                for half in range(2):
                    for k16 in range(16):
                        h = half * 16 + k16
                        bk = PC if k16 < 8 else PD
                        bnm = "PC" if k16 < 8 else "PD"
                        iul[bnm] = em.op("pe", lambda e, h=h, k16=k16, bk=bk: e.matmul(
                            bk[0:96, (k16 % 8) * 64:((k16 % 8) + 1) * 64],
                            b_tok[:, 32:128],
                            us_bf[:, h * 64:(h + 1) * 64],
                            start=(k16 % 8 == 0), stop=(k16 % 8 == 7), skip_group_check=True),
                            after=[t_b, u5, pfree.get(bnm) if k16 % 8 == 0 else None])
                    for qd in range(2):
                        mmq = em.op("pe", lambda e, qd=qd, ci=ci, half=half: e.matmul(
                            PF[:, 0:512], onesr[0:1, :],
                            arow[0:1, ci * 32 + half * 16 + qd * 8:
                                 ci * 32 + half * 16 + (qd + 1) * 8].rearrange(
                                "p q -> p q ()").broadcast_to((1, 8, 64)),
                            start=True, stop=True, skip_group_check=True),
                            after=[t_ar, t_or, pfree.get("PF")])
                        tq = em.op("dve", lambda e, qd=qd: e.tensor_copy(
                            abunh[:, qd * 512:(qd + 1) * 512], PF[:, 0:512]),
                            after=[mmq, t_free.get("abunh")])
                        pfree["PF"] = tq
                    t_free["abunh"] = None
                    sv1 = em.op("dve", lambda e, half=half: e.tensor_mul(
                        s_un[:, half * 1024:(half + 1) * 1024],
                        s_un[:, half * 1024:(half + 1) * 1024],
                        abunh[0:96, 0:1024]),
                        after=[tq, cb3, st_tok["un"]])
                    t_free["abunh"] = sv1
                    sv2 = em.op("dve", lambda e, half=half: e.tensor_tensor(
                        s_un[:, half * 1024: half * 1024 + 512],
                        s_un[:, half * 1024: half * 1024 + 512],
                        PC[0:96, 0:512], ALU.add), after=[sv1, iul["PC"]])
                    pfree["PC"] = sv2
                    sv3 = em.op("dve", lambda e, half=half: e.tensor_tensor(
                        s_un[:, half * 1024 + 512: (half + 1) * 1024],
                        s_un[:, half * 1024 + 512: (half + 1) * 1024],
                        PD[0:96, 0:512], ALU.add), after=[sv1, iul["PD"]])
                    pfree["PD"] = sv3
                    st_tok["un"] = sv3
                    sun_last = sv3
                g1 = em.op("dve", lambda e: e.tensor_mul(u_f[:], x_bf[:], ddup[:]),
                           after=[t_x, u5, u4])
                g2 = em.op("dve", lambda e: e.tensor_add(ytok[:], ytok[:], u_f[:]), after=[g1, t_y])
                g3 = em.op("act", lambda e: e.activation(u_f[:], z_bf[:], AF.Silu), after=[t_z, g2])
                t_free["z_bf"] = g3
                g4 = em.op("dve", lambda e: e.tensor_mul(outbf[:], ytok[:], u_f[:]),
                           after=[g2, g3, t_free.get("outbf")])
                t_free["ytok"] = g4
                t_free["u_f"] = g4
                t_free["x_bf"] = g1
                t_free["px0"] = u3
                t_free["px1"] = u3
                t_free["u_bf"] = yl
                t_free["us_bf"] = isl
                t_ot = None
                tw = None
                for kc in range(16):
                    pr = ptrs[kc % 2]
                    tw = em.op("pe", lambda e, kc=kc, pr=pr: e.transpose(
                        pr[:, 0:128], outbf[:, kc * 128:(kc + 1) * 128], idb[:]),
                        after=[g4, pfree.get(f"ptr{kc % 2}")])
                    t_ot = em.op("dve", lambda e, kc=kc, pr=pr: e.tensor_copy(
                        outT[:, kc * 128:(kc + 1) * 128], pr[:, 0:128]),
                        after=[tw, t_free.get("outT"), t_ot])
                    pfree[f"ptr{kc % 2}"] = t_ot
                t_free["outbf"] = tw
                base_w = (b * n_c + ci) * 16
                t_res = em.dma("sp", name + "_res", lambda e, tglob=tglob: e.dma_start(
                    out=rest[:], in_=res[tglob:tglob + 128, :]),
                    after=[last_store, t_free.get("rest")])
                ml = {}
                NW2 = min(512, D)
                for kc in range(16):
                    ensure_wos(base_w + kc + 2, NB * n_c * 16)
                    slot, wtok = wos_tok[base_w + kc]
                    for nd in range(max(1, D // 512)):
                        bk = PE1 if nd == 0 else PF
                        ml[nd] = em.op("pe", lambda e, kc=kc, nd=nd, slot=slot, bk=bk, NW2=NW2: e.matmul(
                            bk[:, 0:NW2],
                            outT[:, kc * 128:(kc + 1) * 128],
                            w_os[slot][:, nd * NW2:(nd + 1) * NW2],
                            start=(kc == 0), stop=(kc == 15), skip_group_check=True),
                            after=[wtok, t_ot,
                                   (pfree.get("PE1") if nd == 0 else pfree.get("PF"))
                                   if kc == 0 else None])
                    wos_use[base_w + kc] = ml[max(ml)]
                ev1 = em.op("dve", lambda e, NW2=NW2: e.tensor_tensor(
                    outo[:, 0:NW2], PE1[:, 0:NW2], rest[:, 0:NW2], ALU.add),
                    after=[ml[0], t_res, t_free.get("outo")])
                pfree["PE1"] = ev1
                toks_ev = [ev1]
                if D > 512:
                    ev2 = em.op("dve", lambda e: e.tensor_tensor(
                        outo[:, 512:1024], PF[:, 0:512], rest[:, 512:1024], ALU.add),
                        after=[ml.get(1), t_res, ev1])
                    pfree["PF"] = ev2
                    toks_ev.append(ev2)
                last_store = em.dma("sp", name + "_st", lambda e, tglob=tglob: e.dma_start(
                    out=out[tglob:tglob + 128, :], in_=outo[:]), after=toks_ev)
                t_free["rest"] = last_store
                t_free["outo"] = last_store
                t_free["mt"] = yl
                t_free["ctokstok"] = bb2
                t_free["brctok"] = icl
                t_free["brstok"] = isl
                t_free["b_tok"] = iul["PD"]
                t_free["tlc"] = dl
                t_free["trc"] = dl
                t_free["a4"] = mm
            t_free["fb"] = il
            t_free["r0g"] = tc1
            t_free["sing"] = tw
            t_free["cosg"] = tw
            t_free["brc"] = il
            t_free["brs"] = il
            t_free["crc"] = il
            t_free["crs"] = il
            t_free["bdup"] = p2
            t_free["cdup"] = p4
        em.wait("sp", [last_store, t_xlast])
        t_free["s_cos"] = su2
        t_free["s_sin"] = su4
        t_free["s_un"] = sun_last
        t_free["bg_tok"] = u5
        t_free["pit_tok"] = ty1
        t_free["arow"] = tq
        em.pop_scope()
    em.wait("sp", [last_store])
    return last_store



"""Plain layernorm pass (no affine): x [T, D] f32 -> xn [T, D] bf16, eps=1e-5."""

F32 = mybir.dt.float32
BF16 = mybir.dt.bfloat16
AF = mybir.ActivationFunctionType
ALU = mybir.AluOpType
X = mybir.AxisListType.X


def emit_ln(nc, em, cfg, x_in, xn_out, cst, name="ln"):
    T, D = cfg["T"], cfg["D"]
    EPS_COL = cfg["eps_col"]
    cstk = cfg["cst_tok"]
    sb = em.sb
    em.push_scope()
    xt = [sb(f"{name}_xt{i}", [128, D], F32) for i in range(2)]
    xo = [sb(f"{name}_xo{i}", [128, D], BF16) for i in range(2)]
    sq = sb(name + "_sq", [128, D], F32)
    stat = sb(name + "_stat", [128, 4], F32)
    t_free = {}
    last = None
    for nt in range(T // 128):
        i = nt % 2
        t_ld = em.dma("sp", name + "_ld", lambda e, nt=nt, i=i: e.dma_start(
            out=xt[i][:], in_=x_in[nt * 128:(nt + 1) * 128, :]),
            after=[t_free.get(f"xt{i}")])
        r1 = em.op("dve", lambda e, i=i: e.reduce_sum(stat[:, 0:1], xt[i][:], axis=X),
                   after=[t_ld, t_free.get("stat")])
        r2 = em.op("dve", lambda e: e.tensor_scalar(
            stat[:, 0:1], stat[:, 0:1], 1.0 / D, None, ALU.mult), after=[r1])
        s1 = em.op("dve", lambda e, i=i: e.tensor_scalar(
            xt[i][:], xt[i][:], stat[:, 0:1], None, ALU.subtract), after=[r2])
        a1 = em.op("dve", lambda e, i=i: e.tensor_mul(sq[:], xt[i][:], xt[i][:]),
                   after=[s1, t_free.get("sq")])
        r3 = em.op("dve", lambda e: e.reduce_sum(stat[:, 1:2], sq[:], axis=X), after=[a1])
        t_free["sq"] = r3
        r4 = em.op("dve", lambda e: e.tensor_scalar(
            stat[:, 1:2], stat[:, 1:2], 1.0 / D, None, ALU.mult), after=[r3])
        a2 = em.op("act", lambda e: e.activation(
            stat[:, 2:3], stat[:, 1:2], AF.Sqrt, bias=cst[:, EPS_COL:EPS_COL + 1]),
            after=[r4, cstk])
        a3 = em.op("dve", lambda e: e.reciprocal(stat[:, 3:4], stat[:, 2:3]), after=[a2])
        t_free["stat"] = a3
        m1 = em.op("dve", lambda e, i=i: e.tensor_scalar(
            xo[i][:], xt[i][:], stat[:, 3:4], None, ALU.mult),
            after=[a3, t_free.get(f"xo{i}")])
        t_free[f"xt{i}"] = m1
        last = em.dma("sp", name + "_st", lambda e, nt=nt, i=i: e.dma_start(
            out=xn_out[nt * 128:(nt + 1) * 128, :], in_=xo[i][:]), after=[m1])
        t_free[f"xo{i}"] = last
    em.wait("sp", [last])
    em.pop_scope()
    return last


_F32 = mybir.dt.float32
_BF16 = mybir.dt.bfloat16
_NCORES = 8
_TC, _DM = 4096, 1024
_T2 = 2048
_N_LAUNCH = 1


def _np_prep(inputs):
    f32 = np.float32
    bf16 = ml_dtypes.bfloat16
    p = {}
    p["w_inT"] = np.ascontiguousarray(inputs["L0_attn_w_in"].T).astype(f32)
    p["w_outT"] = np.ascontiguousarray(inputs["L0_attn_w_out"].T).astype(f32)
    p["dtb"] = inputs["L0_attn_dt_bias"].reshape(32, 1).astype(f32)
    p["ddup"] = np.tile(np.repeat(inputs["L0_attn_D"].astype(f32), 64)[None, :], (128, 1))
    ii = np.arange(128)
    sel32 = (ii[None, :] % 32 == np.arange(32)[:, None]).astype(f32)
    p["s32f"] = sel32
    p["s32b"] = sel32.astype(bf16)
    selh = np.zeros((32, 8 * 128), f32)
    for j in range(8):
        selh[:, j * 128:(j + 1) * 128] = (ii[None, :] // 32 + 4 * j == np.arange(32)[:, None])
    p["selh"] = selh
    p["selp4"] = (ii[None, :] // 32 == np.arange(4)[:, None]).astype(f32)
    c128 = np.where(np.arange(128)[:, None] <= np.arange(128)[None, :], 0.0, -1e30).astype(f32)
    p["mask4"] = np.tile(c128, (1, 4))
    p["identf"] = np.eye(128, dtype=f32)
    p["identb"] = np.eye(128).astype(bf16)
    cst = np.zeros((128, 8), f32)
    cst[:, 0] = 1.0
    cst[:, 1] = 2 * np.pi
    cst[:, 2] = 1e-5
    cst[:, 3] = 1.0 / np.sqrt(96.0)
    cst[:, 4] = 0.5
    p["cstd"] = cst
    p["wr"] = inputs["L0_ffn_router_kernel_DE"].astype(f32)
    p["wg"] = inputs["L0_ffn_kernel_gating_EDF"].astype(f32)
    p["wu"] = inputs["L0_ffn_kernel_up_EDF"].astype(f32)
    p["wd"] = inputs["L0_ffn_kernel_down_EFD"].astype(f32)
    H, DCH = 16, 4096
    p["wdq"] = inputs["L1_attn_W_DQ"].astype(f32)
    p["wuq"] = inputs["L1_attn_W_UQ"].astype(f32)
    p["wdkv"] = inputs["L1_attn_W_DKV"].astype(f32)
    p["wuk"] = inputs["L1_attn_W_UK"].astype(f32)
    p["wuv"] = inputs["L1_attn_W_UV"].astype(f32)
    wqr_pad = np.zeros((DCH, H * 64), f32)
    wqr_pad.reshape(DCH, H, 64)[:, :, 0:32] = inputs["L1_attn_W_QR"].astype(f32).reshape(DCH, H, 32)
    p["wqr"] = wqr_pad
    p["wkr"] = inputs["L1_attn_W_KR"].astype(f32)
    p["wo"] = inputs["L1_attn_W_O"].astype(f32)
    t = np.arange(_T2, dtype=f32)
    inv = 1.0 / 10000.0 ** (np.arange(0, 32, 2, dtype=f32) / 32)
    freqs = t[:, None] * inv[None, :]
    emb = np.concatenate([freqs, freqs], -1)
    scale = (np.cos(emb) + np.sin(emb)).astype(f32)
    scale_dup = np.zeros((H * 64, _T2), f32)
    scale_dup.reshape(H, 64, _T2)[:, 0:32, :] = scale.T[None, :, :]
    p["scale_dup"] = scale_dup
    QT = 512
    n_tc = QT // 128
    MEXT = QT + (n_tc - 1) * 128
    masks = np.zeros((128, MEXT), f32)
    for i in range(128):
        masks[i, :] = np.where(np.arange(MEXT) - (n_tc - 1) * 128 >= i, 0.0, -1e30)
    p["masks"] = masks
    p["wfcT"] = np.ascontiguousarray(inputs["L1_ffn_W_fc"].astype(f32).T)
    p["wprT"] = np.ascontiguousarray(inputs["L1_ffn_W_proj"].astype(f32).T)
    p["lnw"] = inputs["L1_ffn_ln_w"].astype(f32)
    p["lnb"] = inputs["L1_ffn_ln_b"].astype(f32)
    p["ident"] = np.eye(128, dtype=f32)
    p["sel2"] = np.concatenate(
        [np.stack([np.ones(128, f32), np.zeros(128, f32)]),
         np.stack([np.zeros(128, f32), np.ones(128, f32)])], axis=1)
    return p


def _din(nc, nm, shape, dt):
    return nc.dram_tensor(nm, shape, dt, kind="ExternalInput").ap()


def _dout(nc, nm, shape, dt):
    return nc.dram_tensor(nm, shape, dt, kind="ExternalOutput").ap()


def _dint(nc, nm, shape, dt):
    return nc.dram_tensor(nm, shape, dt, kind="Internal").ap()


def _mk_scr(nc, mk, T):
    return dict(
        proj=mk(nc, "s_proj", [T, 4224], _BF16),
        bT=mk(nc, "s_bT", [128, T], _BF16),
        cT=mk(nc, "s_cT", [128, T], _BF16),
        smT=mk(nc, "s_smT", [128, T], _F32),
        r0d=mk(nc, "s_r0d", [128, 8 * T], _BF16),
        tld=mk(nc, "s_tld", [1, 32 * T], _F32),
        trd=mk(nc, "s_trd", [1, 32 * T], _F32),
    )


def _mamba_dram(nc):
    d = {}
    d["w_inT"] = _din(nc, "w_inT", [_DM, 4480], _F32)
    d["w_outT"] = _din(nc, "w_outT", [2048, _DM], _F32)
    d["dtb"] = _din(nc, "dtb", [32, 1], _F32)
    d["ddup"] = _din(nc, "ddup", [128, 2048], _F32)
    d["s32f"] = _din(nc, "s32f", [32, 128], _F32)
    d["s32b"] = _din(nc, "s32b", [32, 128], _BF16)
    d["selh"] = _din(nc, "selh", [32, 1024], _F32)
    d["selp4"] = _din(nc, "selp4", [4, 128], _F32)
    d["mask4"] = _din(nc, "mask4", [128, 512], _F32)
    d["identf"] = _din(nc, "identf", [128, 128], _F32)
    d["identb"] = _din(nc, "identb", [128, 128], _BF16)
    return d


def _moe_dram(nc):
    d = {}
    d["wr"] = _din(nc, "wr", [_DM, 8], _F32)
    d["wg"] = _din(nc, "wg", [8, _DM, 2048], _F32)
    d["wu"] = _din(nc, "wu", [8, _DM, 2048], _F32)
    d["wd"] = _din(nc, "wd", [8, 2048, _DM], _F32)
    return d


def _attn_dram(nc):
    d = {}
    d["wdq"] = _din(nc, "wdq", [_DM, 4096], _F32)
    d["wuq"] = _din(nc, "wuq", [4096, _DM], _F32)
    d["wdkv"] = _din(nc, "wdkv", [_DM, 256], _F32)
    d["wuk"] = _din(nc, "wuk", [256, _DM], _F32)
    d["wuv"] = _din(nc, "wuv", [256, _DM], _F32)
    d["wqr"] = _din(nc, "wqr", [4096, 16 * 64], _F32)
    d["wkr"] = _din(nc, "wkr", [_DM, 32], _F32)
    d["wo"] = _din(nc, "wo", [_DM, _DM], _F32)
    d["scale_dup"] = _din(nc, "scale_dup", [16 * 64, _T2], _F32)
    d["masks"] = _din(nc, "masks", [128, 896], _F32)
    return d


def _ffn_dram(nc):
    d = {}
    d["wfcT"] = _din(nc, "wfcT", [_DM, 4096], _F32)
    d["wprT"] = _din(nc, "wprT", [4096, _DM], _F32)
    d["lnw"] = _din(nc, "lnw", [4096], _F32)
    d["lnb"] = _din(nc, "lnb", [4096], _F32)
    d["ident"] = _din(nc, "ident", [128, 128], _F32)
    d["sel2"] = _din(nc, "sel2", [2, 256], _F32)
    return d


def _emit_cst(nc, em):
    cstd = _din(nc, "cstd", [128, 8], _F32)
    cst = em.sb("cst", [128, 8], _F32)
    t_cst = em.dma("sp", "cstld", lambda e: e.dma_start(out=cst[:], in_=cstd))
    return cst, t_cst


def _emit_act_preloads(em):
    AFv = mybir.ActivationFunctionType
    for f in (AFv.Exp, AFv.Ln, AFv.Sigmoid, AFv.Sqrt, AFv.Sin, AFv.Gelu, AFv.Silu):
        em.op("act", lambda e, f=f: e.preload_act_table(f))


def _emit_g1(nc, em, cst, t_cst, d, x0, xn0, xacc, scr):
    emit_ln(nc, em, dict(T=_TC, D=_DM, eps_col=2, cst_tok=t_cst), x0, xn0, cst, name="ln0")
    cfgm = dict(T=_TC, D=_DM, NB=2, one_col=0, tpi_col=1, half_col=4, cst_tok=t_cst)
    emit_mamba(nc, em, cfgm, xn0, d["w_inT"], d["w_outT"], d["dtb"], d["ddup"],
               d["s32f"], d["s32b"], d["selh"], d["selp4"], d["mask4"],
               d["identf"], d["identb"], x0, xacc, cst, scr)


def _emit_g2(nc, em, d, xn0, xacc, x1):
    cfg = dict(T=_TC, D=_DM, F=2048, E=8, ST=1024)
    emit_moe(nc, em, cfg, xn0, d["wr"], d["wg"], d["wu"], d["wd"], x1, res_in=xacc)


def _emit_g3(nc, em, cst, t_cst, d, x1, xn1, xa):
    emit_ln(nc, em, dict(T=_TC, D=_DM, eps_col=2, cst_tok=t_cst), x1, xn1, cst, name="ln1")
    cfga = dict(T=_TC, D=_DM, NB=2, H=16, DCH=4096, DKV=256, scl_col=3, cst_tok=t_cst)
    emit_attn(nc, em, cfga, xn1, d["wdq"], d["wuq"], d["wdkv"], d["wuk"], d["wuv"],
              d["wqr"], d["wkr"], d["wo"], d["scale_dup"], d["masks"], x1, xa, cst)


def _emit_g4(nc, em, cst, t_cst, d, xn1, xa, outt, wfcq, wprq):
    tq1 = emit_quant(nc, em, d["wfcT"], wfcq, _DM, 4096, "qfc")
    tq2 = emit_quant(nc, em, d["wprT"], wprq, 4096, _DM, "qpr")
    cfgf = dict(T=_TC, D=_DM, Hh=4096, eps_col=2, ident_dram=d["ident"],
                sel2_dram=d["sel2"], cst_tok=t_cst)
    emit_ffn(nc, em, cfgf, xn1, wfcq, wprq, d["lnw"], d["lnb"], xa, outt, cst,
             after_w=[tq1, tq2])


_CACHE = {}


def _build_unfused():
    if "u" in _CACHE:
        return _CACHE["u"]
    progs = []
    nc = bass.Bass("TRN2", target_bir_lowering=False)
    x0 = _din(nc, "x0", [_TC, _DM], _F32)
    xn0 = _dout(nc, "xn0", [_TC, _DM], _BF16)
    xacc = _dout(nc, "xacc", [_TC, _DM], _F32)
    scr = _mk_scr(nc, _dint, _TC)
    d = _mamba_dram(nc)
    em = Em(nc)
    cst, t_cst = _emit_cst(nc, em)
    _emit_g1(nc, em, cst, t_cst, d, x0, xn0, xacc, scr)
    em.finalize()
    progs.append((nc, ["w_inT", "w_outT", "dtb", "ddup", "s32f", "s32b", "selh",
                       "selp4", "mask4", "identf", "identb", "cstd"], ["x0"],
                  ["xn0", "xacc"]))
    nc = bass.Bass("TRN2", target_bir_lowering=False)
    xn0 = _din(nc, "xn0", [_TC, _DM], _BF16)
    xacc = _din(nc, "xacc", [_TC, _DM], _F32)
    x1 = _dout(nc, "x1", [_TC, _DM], _F32)
    d = _moe_dram(nc)
    em = Em(nc)
    _emit_g2(nc, em, d, xn0, xacc, x1)
    em.finalize()
    progs.append((nc, ["wr", "wg", "wu", "wd"], ["xn0", "xacc"], ["x1"]))
    nc = bass.Bass("TRN2", target_bir_lowering=False)
    x1 = _din(nc, "x1", [_TC, _DM], _F32)
    xn1 = _dout(nc, "xn1", [_TC, _DM], _BF16)
    xa = _dout(nc, "xa", [_TC, _DM], _F32)
    d = _attn_dram(nc)
    em = Em(nc)
    cst, t_cst = _emit_cst(nc, em)
    _emit_g3(nc, em, cst, t_cst, d, x1, xn1, xa)
    em.finalize()
    progs.append((nc, ["wdq", "wuq", "wdkv", "wuk", "wuv", "wqr", "wkr", "wo",
                       "scale_dup", "masks", "cstd"], ["x1"], ["xn1", "xa"]))
    nc = bass.Bass("TRN2", target_bir_lowering=False)
    xn1 = _din(nc, "xn1", [_TC, _DM], _BF16)
    xa = _din(nc, "xa", [_TC, _DM], _F32)
    outt = _dout(nc, "outt", [_TC, _DM], _F32)
    wfcq = _dint(nc, "wfcq", [_DM, 4096], _BF16)
    wprq = _dint(nc, "wprq", [4096, _DM], _BF16)
    d = _ffn_dram(nc)
    em = Em(nc)
    cst, t_cst = _emit_cst(nc, em)
    _emit_g4(nc, em, cst, t_cst, d, xn1, xa, outt, wfcq, wprq)
    em.finalize()
    progs.append((nc, ["wfcT", "wprT", "lnw", "lnb", "ident", "sel2", "cstd"],
                  ["xn1", "xa"], ["outt"]))
    _CACHE["u"] = progs
    return progs


def _build_fused():
    if "f" in _CACHE:
        return _CACHE["f"]
    nc = bass.Bass("TRN2", target_bir_lowering=False)
    x0 = _din(nc, "x0", [_TC, _DM], _F32)
    outt = _dout(nc, "outt", [_TC, _DM], _F32)
    xn0 = _dint(nc, "xn0", [_TC, _DM], _BF16)
    xacc = _dint(nc, "xacc", [_TC, _DM], _F32)
    x1 = _dint(nc, "x1", [_TC, _DM], _F32)
    xn1 = _dint(nc, "xn1", [_TC, _DM], _BF16)
    xa = _dint(nc, "xa", [_TC, _DM], _F32)
    wfcq = _dint(nc, "wfcq", [_DM, 4096], _BF16)
    wprq = _dint(nc, "wprq", [4096, _DM], _BF16)
    scr = _mk_scr(nc, _dint, _TC)
    dm = _mamba_dram(nc)
    dmo = _moe_dram(nc)
    da = _attn_dram(nc)
    df = _ffn_dram(nc)
    em = Em(nc)
    cst, t_cst = _emit_cst(nc, em)
    em.push_scope()
    _emit_g1(nc, em, cst, t_cst, dm, x0, xn0, xacc, scr)
    em.pop_scope()
    em.push_scope()
    _emit_g2(nc, em, dmo, xn0, xacc, x1)
    em.pop_scope()
    em.push_scope()
    _emit_g3(nc, em, cst, t_cst, da, x1, xn1, xa)
    em.pop_scope()
    em.push_scope()
    _emit_g4(nc, em, cst, t_cst, df, xn1, xa, outt, wfcq, wprq)
    em.pop_scope()
    em.finalize()
    w_names = ["w_inT", "w_outT", "dtb", "ddup", "s32f", "s32b", "selh", "selp4",
               "mask4", "identf", "identb", "cstd", "wr", "wg", "wu", "wd",
               "wdq", "wuq", "wdkv", "wuk", "wuv", "wqr", "wkr", "wo",
               "scale_dup", "masks", "wfcT", "wprT", "lnw", "lnb", "ident", "sel2"]
    _CACHE["f"] = (nc, w_names)
    return _CACHE["f"]


def kernel(**inputs):
    p = _np_prep(inputs)
    u = inputs["L0_attn_u"].astype(np.float32)
    shards = [np.ascontiguousarray(u[2 * i:2 * i + 2].reshape(_TC, _DM))
              for i in range(_NCORES)]
    cores = list(range(_NCORES))
    if _N_LAUNCH == 1:
        nc, w_names = _build_fused()
        in_maps = [dict({k: p[k] for k in w_names}, x0=shards[i]) for i in cores]
        r = run_bass_kernel_spmd(nc, in_maps, core_ids=cores)
        outs = [r.results[i]["outt"] for i in cores]
    else:
        progs = _build_unfused()
        state = [{"x0": shards[i]} for i in cores]
        for nc, wn, act_in, act_out in progs:
            in_maps = [dict({k: p[k] for k in wn},
                            **{k: state[i][k] for k in act_in}) for i in cores]
            r = run_bass_kernel_spmd(nc, in_maps, core_ids=cores)
            for i in cores:
                for k in act_out:
                    state[i][k] = r.results[i][k]
        outs = [state[i]["outt"] for i in cores]
    return np.concatenate([o.reshape(2, _T2, _DM) for o in outs], axis=0).astype(np.float32)
